# version 02
# baseline (speedup 1.0000x reference)
import numpy as np
import ml_dtypes
import concourse.bass as bass
import concourse.mybir as mybir
from concourse.bass_utils import run_bass_kernel_spmd

F32 = mybir.dt.float32
F32R = mybir.dt.float32r
BF16 = mybir.dt.bfloat16
AF = mybir.ActivationFunctionType
ALU = mybir.AluOpType

NCORES = 8
B_TOT, S, D = 16, 2048, 1024
B_LOC = B_TOT // NCORES
T = B_LOC * S
TC = 512
NCH = T // TC
MH, MP, MN, MK = 32, 64, 128, 4
INTER = MH * MP
CONV = INTER + 2 * MN
PROJ = INTER + CONV + MH
CS = 128
EPS = 1e-5
FF = 4096
NH, HD, KF = 16, 64, 4
NSLOT = S // KF
LORA = 64

ENGINES = ("sync", "vector", "scalar", "tensor", "gpsimd")


def _spike_breakpoints():
    BETA, TH, NS = 0.9, 1.0, 5

    def out_of(y):
        mem = spk = acc = 0.0
        for _ in range(NS):
            mem = BETA * mem + y - spk * TH
            spk = 1.0 if mem > TH else 0.0
            acc += spk
        return acc / NS

    bps, lo = [], -1.0
    for k in range(NS):
        target = (k + 1) / NS
        a, b = lo, 8.0
        for _ in range(200):
            m = 0.5 * (a + b)
            if out_of(m) >= target - 1e-9:
                b = m
            else:
                a = m
        bps.append(0.5 * (a + b))
        lo = bps[-1]
    return bps


SPIKE_BPS = _spike_breakpoints()


class Sched:
    def __init__(self):
        self.c = {}

    def inc(self, sem, n):
        k = sem.name
        self.c[k] = self.c.get(k, 0) + n
        return self.c[k]

    def val(self, sem):
        return self.c.get(sem.name, 0)


class _Instr:
    def __init__(self, ctx, inst, fence):
        self.ctx = ctx
        self.inst = inst
        self.fence = fence

    def inc(self, sem, n=1):
        self.ctx.s.inc(sem, n)
        if self.inst is not None:
            if self.fence:
                self.ctx.handle.drain().then_inc(sem, n)
            else:
                self.inst.then_inc(sem, n)
        return self


import os as _os
_FENCE_ENGINES = () if _os.environ.get("DEV_NOFENCE") else ("vector", "scalar", "gpsimd")


class Eng:

    def __init__(self, ctx, name):
        self._ctx = ctx
        self._name = name

    @property
    def live(self):
        return self._ctx.active == self._name

    def __getattr__(self, op):
        ctx = self._ctx
        name = self._name

        def call(*args, **kwargs):
            fence = name in _FENCE_ENGINES and not op.startswith("dma")
            if self.live:
                return _Instr(ctx, getattr(ctx.handle, op)(*args, **kwargs), fence)
            return _Instr(ctx, None, fence)

        return call

    def wait(self, sem, val=None):
        if val is None:
            val = self._ctx.s.val(sem)
        if self.live:
            self._ctx.handle.wait_ge(sem, val)

    def dmat(self, out, in_):
        if self.live:
            return _Instr(self._ctx, self._ctx.handle.dma_start_transpose(out=out, in_=in_), False)
        return _Instr(self._ctx, None, False)


class Ctx:
    def __init__(self, nc):
        self.nc = nc
        self.s = None
        self.active = None
        self.handle = None
        self.Y = Eng(self, "sync")
        self.V = Eng(self, "vector")
        self.S = Eng(self, "scalar")
        self.T = Eng(self, "tensor")
        self.G = Eng(self, "gpsimd")

    def barrier(self, bar_sem):
        for e in (self.Y, self.V, self.S, self.T, self.G):
            e.nop().inc(bar_sem, 1)
        tgt = self.s.val(bar_sem)
        for e in (self.Y, self.V, self.S, self.T, self.G):
            e.wait(bar_sem, tgt)


class PA:

    def __init__(self, nc):
        from contextlib import ExitStack
        self.nc = nc
        self.es = ExitStack()

    def sb(self, name, shape, dt):
        return self.es.enter_context(self.nc.sbuf_tensor(name, list(shape), dt))

    def ps(self, name, shape, dt=F32):
        return self.es.enter_context(self.nc.psum_tensor(name, list(shape), dt))

    def close(self):
        self.es.close()


def _ap3(t, ap_dims, offset=0):
    base = t if hasattr(t, "tensor") else t.ap()
    return bass.AP(tensor=base.tensor, offset=base.offset + offset, ap=list(ap_dims))


def emit_ln_chunk(C, b, sm, xch, tc):
    s = C.s
    nd = D // 128
    C.T.wait(sm["ve"], s.val(sm["ve"]))
    C.V.wait(sm["mm"], s.val(sm["mm"]))
    C.V.tensor_copy(out=b.xbf[:, 0:nd * tc], in_=xch[:, 0:nd * tc])
    C.V.tensor_mul(out=b.xsqbf[:, 0:nd * tc], in0=b.xbf[:, 0:nd * tc],
                   in1=b.xbf[:, 0:nd * tc]).inc(sm["ve"], 1)
    v_sq = s.val(sm["ve"])
    C.T.wait(sm["ve"], v_sq)
    for j in range(nd):
        C.T.matmul(b.pA[0:1, 0:tc], b.ones_bf[:, 0:1], b.xbf[:, j * tc:(j + 1) * tc],
                   start=(j == 0), stop=(j == nd - 1))
    for j in range(nd):
        mm = C.T.matmul(b.pB[0:1, 0:tc], b.ones_bf[:, 0:1], b.xsqbf[:, j * tc:(j + 1) * tc],
                        start=(j == 0), stop=(j == nd - 1))
        if j == nd - 1:
            mm.inc(sm["mm"], 1)
    v_stats = s.val(sm["mm"])
    C.V.wait(sm["mm"], v_stats)
    r_mean = b.lnrow[0:1, 0 * TC:0 * TC + tc]
    r_ms = b.lnrow[0:1, 1 * TC:1 * TC + tc]
    r_m2 = b.lnrow[0:1, 2 * TC:2 * TC + tc]
    r_var = r_ms
    r_lnv = r_m2
    C.V.tensor_scalar_mul(out=r_mean, in0=b.pA[0:1, 0:tc], scalar1=1.0 / D)
    C.V.tensor_scalar_mul(out=r_ms, in0=b.pB[0:1, 0:tc], scalar1=1.0 / D)
    C.V.drain()
    C.V.tensor_mul(out=r_m2, in0=r_mean, in1=r_mean)
    C.V.drain()
    C.V.tensor_sub(out=r_var, in0=r_ms, in1=r_m2).inc(sm["ve"], 1)
    v_var = s.val(sm["ve"])
    C.S.wait(sm["ve"], v_var)
    C.S.activation(out=r_lnv, in_=r_var, func=AF.Ln, bias=b.cpool[0:1, 0:1]).inc(sm["ac"], 1)
    C.S.wait(sm["ac"], s.val(sm["ac"]))
    C.S.activation(out=b.lnrow_r[0:1, 0:tc], in_=r_lnv, func=AF.Exp, scale=-0.5).inc(sm["ac"], 1)
    v_r = s.val(sm["ac"])
    C.V.wait(sm["ac"], v_r)
    C.V.drain()
    C.V.tensor_mul(out=b.lnrow_r2[0:1, 0:tc], in0=r_mean, in1=b.lnrow_r[0:1, 0:tc]).inc(sm["ve"], 1)
    v_mr = s.val(sm["ve"])
    C.T.wait(sm["ac"], v_r)
    C.T.matmul(b.pC[:, 0:tc], b.onesrow_f[0:1, :], b.lnrow_r[0:1, 0:tc], start=True, stop=True)
    C.T.wait(sm["ve"], v_mr)
    C.T.matmul(b.pD[:, 0:tc], b.onesrow_f[0:1, :], b.lnrow_r2[0:1, 0:tc],
               start=True, stop=True).inc(sm["mm"], 1)
    v_out = s.val(sm["mm"])
    C.V.wait(sm["mm"], v_out)
    pR_rep = _ap3(b.pC, [[b.pC.ap().ap[0][0], 128], [0, nd], [1, tc]])
    pMR_rep = _ap3(b.pD, [[b.pD.ap().ap[0][0], 128], [0, nd], [1, tc]])
    C.V.tensor_tensor(out=b.scrA[:, 0:nd * tc], in0=xch[:, 0:nd * tc], in1=pR_rep, op=ALU.mult)
    C.V.tensor_tensor(out=b.xn[:, 0:nd * tc], in0=b.scrA[:, 0:nd * tc], in1=pMR_rep,
                      op=ALU.subtract).inc(sm["ve"], 1)
    return s.val(sm["ve"])


class NS:
    pass


DEV_FFN_LEVEL = 3
DEV_FFN_LEVEL2 = None


def emit_ffn_phase(C, dr, bufs, sm, src, dst, first_ffn, dbg=None):
    s = C.s
    b = bufs
    nd = D // 128
    NF = FF // 128
    LV = DEV_FFN_LEVEL if (first_ffn or DEV_FFN_LEVEL2 is None) else DEV_FFN_LEVEL2
    if LV == 3:
        LV = 99

    for j in range(nd):
        src_ap = _ap3(dr["wfcT"], [[FF, 128], [1, FF]], offset=j * 128 * FF)
        C.G.dma_start(out=b.wfc[:, j * FF:(j + 1) * FF], in_=src_ap).inc(sm["gld"], 16)
    if first_ffn:
        for i in range(NF):
            src_ap = _ap3(dr["wprojT"], [[D, 128], [1, D]], offset=i * 128 * D)
            C.G.dma_start(out=b.wpst[:, 0:D], in_=src_ap).inc(sm["gld"], 16)
            v = s.val(sm["gld"])
            C.G.wait(sm["gld"], v)
            dst_ap = _ap3(dr["wproj_bf"], [[D, 128], [1, D]], offset=i * 128 * D)
            C.G.dma_start(out=dst_ap, in_=b.wpst[:, 0:D]).inc(sm["gld"], 16)
    v_wload = s.val(sm["gld"])

    for c in range(NCH):
        cb = c % 2
        xch = b.xch[:, cb * nd * TC:(cb + 1) * nd * TC]
        if c >= 2:
            C.Y.wait(sm["ve"], b.ffn_cons[c - 2])
        src_ap = _ap3(src, [[T, 128], [128 * T, nd], [1, TC]], offset=c * TC)
        C.Y.dma_start(out=xch, in_=src_ap).inc(sm["x"], 16)
        v_x = s.val(sm["x"])
        C.S.wait(sm["x"], v_x)
        C.T.wait(sm["x"], v_x)
        C.V.wait(sm["x"], v_x)

        v_xn = emit_ln_chunk(C, b, sm, xch, TC)
        C.T.wait(sm["ve"], v_xn)
        if c == 0:
            C.T.wait(sm["gld"], v_wload)

        for fi in range(NF if LV >= 1 else 0):
            pf = b.pE if fi % 2 == 0 else b.pF
            if fi >= 2:
                C.T.wait(sm["ac"], s.val(sm["ac"]))
            for j in range(nd):
                mm = C.T.matmul(pf[:, 0:TC], b.wfc[:, j * FF + fi * 128:j * FF + (fi + 1) * 128],
                                b.xn[:, j * TC:(j + 1) * TC], start=(j == 0), stop=(j == nd - 1))
                if j == nd - 1:
                    mm.inc(sm["mm"], 1)
            C.S.wait(sm["mm"], s.val(sm["mm"]))
            C.S.wait(sm["ve"], s.val(sm["ve"]))
            C.S.activation(out=b.uff[:, (fi % 2) * TC:(fi % 2 + 1) * TC],
                           in_=pf[:, 0:TC], func=AF.Relu).inc(sm["ac"], 1)
            C.V.wait(sm["ac"], s.val(sm["ac"]))
            uslice = b.uff[:, (fi % 2) * TC:(fi % 2 + 1) * TC]
            C.V.tensor_mul(out=b.hbuf[:, fi * TC:(fi + 1) * TC], in0=uslice, in1=uslice).inc(sm["ve"], 1)
        v_h = s.val(sm["ve"])

        if c >= 1:
            C.V.wait(sm["st"], b.ffn_stored[c - 1])
        if LV < 2:
            C.V.tensor_copy(out=b.rout[:, 0:nd * TC], in_=xch).inc(sm["ve"], 1)
            b.ffn_cons[c] = s.val(sm["ve"])
            C.Y.wait(sm["ve"], b.ffn_cons[c])
            dst_ap = _ap3(dst, [[T, 128], [128 * T, nd], [1, TC]], offset=c * TC)
            C.Y.dma_start(out=dst_ap, in_=b.rout[:, 0:nd * TC]).inc(sm["st"], 16)
            b.ffn_stored[c] = s.val(sm["st"])
            continue
        for g in range(2):
            C.T.wait(sm["ve"], s.val(sm["ve"]))
            for fi in range(NF):
                wb = b.wpb0 if fi % 2 == 0 else b.wpb1
                if len(b.ffn_wpmarks) >= 2:
                    C.Y.wait(sm["mm"], b.ffn_wpmarks[-2])
                if c == 0 and g == 0 and fi == 0:
                    C.Y.wait(sm["gld"], s.val(sm["gld"]))
                ld_ap = _ap3(dr["wproj_bf"], [[D, 128], [1, TC]],
                             offset=fi * 128 * D + g * TC)
                C.Y.dma_start(out=wb[:, 0:TC], in_=ld_ap).inc(sm["sld"], 16)
                v_w = s.val(sm["sld"])
                C.T.wait(sm["sld"], v_w)
                if fi == 0:
                    C.T.wait(sm["ve"], v_h)
                if LV != 15:
                    for jo in range(4):
                        mm = C.T.matmul(b.py[jo][:, 0:TC], wb[:, jo * 128:(jo + 1) * 128],
                                        b.hbuf[:, fi * TC:(fi + 1) * TC],
                                        start=(fi == 0), stop=(fi == NF - 1))
                        if jo == 3:
                            mm.inc(sm["mm"], 1)
                else:
                    C.T.nop().inc(sm["mm"], 1)
                b.ffn_wpmarks.append(s.val(sm["mm"]))
            v_y = s.val(sm["mm"])
            if LV in (15, 17):
                continue
            if LV == 20:
                for jo in range(4):
                    jd = g * 4 + jo
                    C.V.wait(sm["mm"], v_y)
                    C.V.tensor_copy(out=b.rout[:, jd * TC:(jd + 1) * TC],
                                    in_=xch[:, jd * TC:(jd + 1) * TC]).inc(sm["ve"], 1)
                continue
            if LV == 18:
                for jo in range(4):
                    jd = g * 4 + jo
                    C.S.wait(sm["mm"], v_y)
                    C.S.activation(out=b.rout[:, jd * TC:(jd + 1) * TC],
                                   in_=b.py[jo][:, 0:TC], func=AF.Identity).inc(sm["ac"], 1)
                if True:
                    C.V.wait(sm["ac"], s.val(sm["ac"]))
                    C.V.engine_nop().inc(sm["ve"], 1)
                continue
            if LV < 3:
                for jo in range(4):
                    jd = g * 4 + jo
                    C.V.wait(sm["mm"], v_y)
                    C.V.tensor_copy(out=b.rout[:, jd * TC:(jd + 1) * TC],
                                    in_=b.py[jo][:, 0:TC]).inc(sm["ve"], 1)
                continue
            for jo in range(4):
                jd = g * 4 + jo
                pyj = b.py[jo]
                if dbg is not None and "dbg_y" in dbg:
                    C.V.wait(sm["mm"], v_y)
                    C.V.tensor_copy(out=b.rout[:, jd * TC:(jd + 1) * TC],
                                    in_=pyj[:, 0:TC]).inc(sm["ve"], 1)
                    C.Y.wait(sm["ve"], s.val(sm["ve"]))
                    dba = _ap3(dbg["dbg_y"], [[T, 128], [1, TC]],
                               offset=(jd * 128) * T + c * TC)
                    C.Y.dma_start(out=dba, in_=b.rout[:, jd * TC:(jd + 1) * TC]).inc(sm["st"], 16)
                    C.V.wait(sm["st"], s.val(sm["st"]))
                C.S.wait(sm["mm"], v_y)
                C.S.wait(sm["ve"], s.val(sm["ve"]))
                if LV >= 25:
                    for k in range(3):
                        act = C.S.activation(out=b.sgn[k][:, 0:TC], in_=pyj[:, 0:TC], func=AF.Sign,
                                             bias=b.cpool[:, 1 + k:2 + k])
                        if k == 2:
                            act.inc(sm["ac"], 1)
                else:
                    C.S.nop().inc(sm["ac"], 1)
                v_sgn = s.val(sm["ac"])
                C.V.wait(sm["mm"], v_y)
                C.V.wait(sm["ac"], v_sgn)
                if LV >= 26:
                    C.V.tensor_single_scalar(out=b.ind4[:, 0:TC], in_=pyj[:, 0:TC],
                                             scalar=b.cpool[:, 6:7], op=ALU.is_gt)
                    C.V.tensor_single_scalar(out=b.ind5[:, 0:TC], in_=pyj[:, 0:TC],
                                             scalar=b.cpool[:, 7:8], op=ALU.is_gt)
                if LV >= 27:
                    C.V.tensor_add(out=b.sa, in0=b.sgn[0][:, 0:TC], in1=b.sgn[1][:, 0:TC])
                    C.V.tensor_add(out=b.sa, in0=b.sa, in1=b.sgn[2][:, 0:TC])
                    C.V.tensor_add(out=b.sb_, in0=b.ind4[:, 0:TC], in1=b.ind5[:, 0:TC])
                xsl = xch[:, jd * TC:(jd + 1) * TC]
                rsl = b.rout[:, jd * TC:(jd + 1) * TC]
                if LV >= 28:
                    C.V.scalar_tensor_tensor(out=rsl, in0=b.sa, scalar=0.1,
                                             in1=xsl, op0=ALU.mult, op1=ALU.add)
                    C.V.scalar_tensor_tensor(out=rsl, in0=b.sb_, scalar=0.2,
                                             in1=rsl, op0=ALU.mult, op1=ALU.add)
                    C.V.tensor_scalar_add(out=rsl, in0=rsl, scalar1=0.3).inc(sm["ve"], 1)
                else:
                    C.V.tensor_copy(out=rsl, in_=xsl).inc(sm["ve"], 1)
        if LV in (15, 17):
            C.V.wait(sm["mm"], s.val(sm["mm"]))
            C.V.tensor_copy(out=b.rout[:, 0:nd * TC], in_=xch).inc(sm["ve"], 1)
        b.ffn_cons[c] = s.val(sm["ve"])
        C.Y.wait(sm["ve"], b.ffn_cons[c])
        dst_ap = _ap3(dst, [[T, 128], [128 * T, nd], [1, TC]], offset=c * TC)
        C.Y.dma_start(out=dst_ap, in_=b.rout[:, 0:nd * TC]).inc(sm["st"], 16)
        b.ffn_stored[c] = s.val(sm["st"])
        if dbg is not None and c == 0:
            pass
    C.Y.wait(sm["st"], b.ffn_stored[NCH - 1])


def alloc_ffn_bufs(pa, shared):
    b = shared
    b.wfc = pa.sb("wfc", [128, 8 * FF], BF16)
    b.wpst = pa.sb("wpst", [128, D], BF16)
    b.wpb0 = pa.sb("wpb0", [128, TC], BF16)
    b.wpb1 = pa.sb("wpb1", [128, TC], BF16)
    b.hbuf = pa.sb("hbuf", [128, FF // 128 * TC], BF16)
    b.uff = pa.sb("uff", [128, 2 * TC], BF16)
    b.sgn = [pa.sb(f"sgn{k}", [128, TC], BF16) for k in range(3)]
    b.ind4 = pa.sb("ind4", [128, TC], BF16)
    b.ind5 = pa.sb("ind5", [128, TC], BF16)
    import os
    if os.environ.get("DEV_NOALIAS2"):
        b.sa = pa.sb("sa2", [128, TC], F32)[:, 0:TC]
        b.sb_ = pa.sb("sb2", [128, TC], F32)[:, 0:TC]
    else:
        b.sa = b.scrA[:, 0:TC]
        b.sb_ = b.scrA[:, TC:2 * TC]
    b.py = [b.pA, b.pB, b.pC, b.pD]
    b.ffn_wpmarks = []
    b.ffn_cons = {}
    b.ffn_stored = {}
    return b


def alloc_shared_bufs(pa):
    b = NS()
    nd = D // 128
    b.xch = pa.sb("xch", [128, 2 * nd * TC], F32)
    b.scrA = pa.sb("scrA", [128, nd * TC], F32)
    b.xbf = pa.sb("xbf", [128, nd * TC], BF16)
    b.xn = pa.sb("xn", [128, nd * TC], BF16)
    import os
    if os.environ.get("DEV_NOALIAS"):
        b.xsqbf = pa.sb("xsqbf2", [128, nd * TC], BF16)
    else:
        b.xsqbf = b.xn
    b.rout = pa.sb("rout", [128, nd * TC], F32)
    b.lnrow = pa.sb("lnrow", [1, 3 * TC], F32)
    b.lnrow_r = pa.sb("lnrow_r", [1, TC], F32)
    b.lnrow_r2 = pa.sb("lnrow_r2", [1, TC], F32)
    b.cpool = pa.sb("sb_cpool", [128, 16], F32)
    b.ident_sb = pa.sb("sb_ident", [128, 128], F32)
    b.triu_sb = pa.sb("sb_triu", [128, 128], F32)
    b.dtbrep = pa.sb("sb_dtbrep", [128, MH], F32)
    b.ones_bf = pa.sb("sb_ones_bf", [128, 1], BF16)
    b.onesrow_f = pa.sb("sb_onesrow_f", [1, 128], F32)
    b.onesrow_bf = pa.sb("sb_onesrow_bf", [1, 512], BF16)
    b.pA = pa.ps("pA", [128, TC])
    b.pB = pa.ps("pB", [128, TC])
    b.pC = pa.ps("pC", [128, TC])
    b.pD = pa.ps("pD", [128, TC])
    b.pE = pa.ps("pE", [128, TC])
    b.pF = pa.ps("pF", [128, TC])
    return b


def emit_const_loads(C, dr, b, sm):
    C.Y.dma_start(out=b.cpool[:], in_=dr["consts"]).inc(sm["ld"], 16)
    C.Y.dma_start(out=b.ones_bf[:], in_=dr["ones_bf"]).inc(sm["ld"], 16)
    C.Y.dma_start(out=b.onesrow_f[:], in_=dr["onesrow_f"]).inc(sm["ld"], 16)
    C.Y.dma_start(out=b.onesrow_bf[:], in_=dr["onesrow_bf"]).inc(sm["ld"], 16)
    C.Y.dma_start(out=b.ident_sb[:], in_=dr["ident"]).inc(sm["ld"], 16)
    C.Y.dma_start(out=b.triu_sb[:], in_=dr["triu01"]).inc(sm["ld"], 16)
    C.Y.dma_start(out=b.dtbrep[:], in_=dr["dtb_rep"]).inc(sm["ld"], 16)
    v = C.s.val(sm["ld"])
    for e in (C.V, C.S, C.T, C.G):
        e.wait(sm["ld"], v)


NJT = CONV // 128
NHT = INTER // 128


def alloc_mamba_a_bufs(pa, b):
    b.win = pa.sb("win", [128, 8 * PROJ], BF16)
    b.binrow = pa.sb("binrow", [1, PROJ], BF16)
    b.cw = pa.sb("cw", [128, NJT * MK], F32)
    b.hbct = pa.sb("hbct", [128, MK - 1 + TC], F32)
    b.tails = pa.sb("tails", [128, NJT * (MK - 1)], F32)
    b.convo = pa.sb("convo", [128, TC], F32)
    b.hsp = pa.sb("hsp", [128, NHT * TC // 2], BF16)
    b.bcsp = pa.sb("bcsp", [128, 2 * TC], BF16)
    b.gbuf = pa.sb("gbuf", [128, INTER], BF16)
    b.dtt = pa.sb("dtt", [128, 4 * MH], F32)
    b.dts1 = pa.sb("dts1", [128, MH], F32)
    b.dts2 = pa.sb("dts2", [128, MH], F32)
    b.aexp = pa.sb("aexp", [MH, 1], F32)
    b.alog = pa.sb("alog", [MH, 1], F32)
    b.dtc = pa.sb("dtc", [32, TC], F32)
    b.acumc = pa.sb("acumc", [32, TC], F32)
    b.zrow = pa.sb("zrow", [32, 128], F32)
    b.pG = pa.ps("pG", [128, 128])


def alloc_mamba_b_bufs(pa, b):
    b.htok = [pa.sb(f"htok{i}", [128, INTER], BF16) for i in range(2)]
    b.btok = [pa.sb(f"btok{i}", [128, 128], BF16) for i in range(2)]
    b.bchan = [pa.sb(f"bchan{i}", [128, 128], BF16) for i in range(2)]
    b.cchan = [pa.sb(f"cchan{i}", [128, 128], BF16) for i in range(2)]
    b.gchk = [pa.sb(f"gchk{i}", [128, INTER], BF16) for i in range(2)]
    b.dtk = [pa.sb(f"dtk{i}", [128, MH], F32) for i in range(2)]
    b.acu = [pa.sb(f"acu{i}", [MH, 128], F32) for i in range(2)]
    b.arep = [pa.sb("arep0", [128, MH * 128], F32)]
    b.acumT = pa.sb("acumT", [128, MH], F32)
    b.decc = pa.sb("decc", [MH, 128], F32)
    b.sdoc = pa.sb("sdoc", [MH, 128], F32)
    b.decT = pa.sb("decT", [128, MH], F32)
    b.sdoT = pa.sb("sdoT", [128, MH], F32)
    b.escrow = pa.sb("escrow", [1, MH], F32)
    b.escM = pa.sb("escM", [128, MH], F32)
    b.expbf = pa.sb("expbf", [128, MH * 128], BF16)
    b.gmbf = pa.sb("gmbf", [128, 128], BF16)
    b.xx = pa.sb("xx", [128, INTER], BF16)
    b.xx2 = pa.sb("xx2", [128, INTER], BF16)
    b.ssb = [pa.sb(f"ssb{i}", [128, INTER], BF16) for i in range(2)]
    b.stmp = pa.sb("stmp", [128, INTER], BF16)
    b.gbig = pa.sb("gbig", [128, INTER], BF16)
    b.gsq = pa.sb("gsq", [128, 1], F32)
    b.drep = pa.sb("drep", [128, INTER], F32)
    b.nwrep = pa.sb("nwrep", [128, INTER], F32)
    b.droww = pa.sb("droww", [1, INTER], F32)
    b.nwrow = b.droww
    b.pG = pa.ps("pGb", [128, 128])
    b.pE2 = pa.ps("pE2", [128, 128])


def alloc_mamba_c_bufs(pa, b):
    b.wout = pa.sb("wout", [128, NHT * D], BF16)
    b.boutrow = pa.sb("boutrow", [1, D], BF16)
    b.gT = [pa.sb(f"gT{i}", [128, TC], BF16) for i in range(NHT)]


def emit_mamba_a(C, dr, b, sm, src):
    s = C.s
    nd = D // 128
    for j in range(nd):
        sap = _ap3(dr["winT"], [[PROJ, 128], [1, PROJ]], offset=j * 128 * PROJ)
        C.G.dma_start(out=b.win[:, j * PROJ:(j + 1) * PROJ], in_=sap).inc(sm["gld"], 16)
    C.G.dma_start(out=b.binrow[:], in_=dr["b_in"]).inc(sm["gld"], 16)
    C.Y.dma_start(out=b.cw[:], in_=dr["conv_w_t"]).inc(sm["ld"], 16)
    C.Y.dma_start(out=b.alog[:], in_=dr["a_log"]).inc(sm["ld"], 16)
    v_w = s.val(sm["gld"])
    v_l = s.val(sm["ld"])
    C.T.wait(sm["gld"], v_w)
    C.S.wait(sm["ld"], v_l)
    C.S.activation(out=b.aexp[:], in_=b.alog[:], func=AF.Exp).inc(sm["ac"], 1)
    C.V.wait(sm["ac"], s.val(sm["ac"]))
    C.V.tensor_scalar_mul(out=b.aexp[:], in0=b.aexp[:], scalar1=-1.0).inc(sm["ve"], 1)
    C.V.memset(b.zrow[:], 0.0)
    C.V.memset(b.tails[:], 0.0).inc(sm["ve"], 1)

    for c in range(NCH):
        cb = c % 2
        xch = b.xch[:, cb * nd * TC:(cb + 1) * nd * TC]
        if c >= 2:
            C.Y.wait(sm["ve"], b.ma_cons[c - 2])
        sap = _ap3(src, [[T, 128], [128 * T, nd], [1, TC]], offset=c * TC)
        C.Y.dma_start(out=xch, in_=sap).inc(sm["x"], 16)
        v_x = s.val(sm["x"])
        C.S.wait(sm["x"], v_x)
        C.T.wait(sm["x"], v_x)
        C.V.wait(sm["x"], v_x)
        v_xn = emit_ln_chunk(C, b, sm, xch, TC)
        b.ma_cons[c] = v_xn
        C.T.wait(sm["ve"], v_xn)
        if c == 0:
            C.T.wait(sm["gld"], v_w)

        if c >= 1:
            C.T.wait(sm["st"], b.ma_stored[c - 1])
            C.S.wait(sm["st"], b.ma_stored[c - 1])
            C.V.wait(sm["st"], b.ma_stored[c - 1])
        for jt in range(NJT):
            pf = b.pE if jt % 2 == 0 else b.pF
            co = INTER + jt * 128
            if jt >= 2:
                C.T.wait(sm["ve"], s.val(sm["ve"]))
            for j in range(nd):
                C.T.matmul(pf[:, 0:TC], b.win[:, j * PROJ + co:j * PROJ + co + 128],
                           b.xn[:, j * TC:(j + 1) * TC],
                           start=(j == 0), stop=False)
            C.T.matmul(pf[:, 0:TC], b.binrow[0:1, co:co + 128], b.onesrow_bf[0:1, 0:TC],
                       start=False, stop=True).inc(sm["mm"], 1)
            v_p = s.val(sm["mm"])
            C.V.wait(sm["mm"], v_p)
            C.V.wait(sm["ac"], s.val(sm["ac"]))
            C.V.tensor_copy(out=b.hbct[:, MK - 1:MK - 1 + TC], in_=pf[:, 0:TC])
            C.V.tensor_copy(out=b.hbct[:, 0:MK - 1], in_=b.tails[:, jt * (MK - 1):(jt + 1) * (MK - 1)])
            C.V.drain()
            C.V.tensor_scalar(out=b.convo[:, 0:TC], in0=b.hbct[:, 0:TC],
                              scalar1=b.cw[:, jt * MK:jt * MK + 1], scalar2=None, op0=ALU.mult)
            for i in range(1, MK):
                C.V.scalar_tensor_tensor(out=b.convo[:, 0:TC], in0=b.hbct[:, i:i + TC],
                                         scalar=b.cw[:, jt * MK + i:jt * MK + i + 1],
                                         in1=b.convo[:, 0:TC], op0=ALU.mult, op1=ALU.add)
            C.V.tensor_copy(out=b.tails[:, jt * (MK - 1):(jt + 1) * (MK - 1)],
                            in_=b.hbct[:, TC:TC + MK - 1]).inc(sm["ve"], 1)
            v_cv = s.val(sm["ve"])
            C.S.wait(sm["ve"], v_cv)
            if jt < NHT:
                if jt == 8 and True:
                    C.S.wait(sm["st"], s.val(sm["st"]))
                dst = b.hsp[:, (jt % 8) * TC:(jt % 8 + 1) * TC]
            elif jt == NHT:
                dst = b.bcsp[:, 0:TC]
            else:
                dst = b.bcsp[:, TC:2 * TC]
            C.S.activation(out=dst, in_=b.convo[:, 0:TC], func=AF.Silu).inc(sm["ac"], 1)
            if jt == 7 or jt == 15:
                vh = s.val(sm["ac"])
                C.Y.wait(sm["ac"], vh)
                half = 0 if jt == 7 else 1
                hap = _ap3(dr["h_scr"], [[T, 128], [128 * T, 8], [1, TC]],
                           offset=half * 8 * 128 * T + c * TC)
                C.Y.dma_start(out=hap, in_=b.hsp[:, 0:8 * TC]).inc(sm["st"], 16)
        v_sta = s.val(sm["ac"])

        if (c + 1) % (S // TC) == 0:
            C.V.wait(sm["ac"], v_sta)
            C.V.memset(b.tails[:], 0.0).inc(sm["ve"], 1)

        for tt in range(4):
            pg = [b.pA, b.pB, b.pC, b.pD]
            C.T.wait(sm["ac"], s.val(sm["ac"]))
            C.T.wait(sm["ve"], s.val(sm["ve"]))
            for bi in range(4):
                for j in range(nd):
                    C.T.matmul(pg[bi][:, 0:TC], b.xn[:, j * TC + tt * 128:j * TC + tt * 128 + 128],
                               b.win[:, j * PROJ + bi * TC:j * PROJ + (bi + 1) * TC],
                               start=(j == 0), stop=False)
                C.T.matmul(pg[bi][:, 0:TC], b.onesrow_bf[0:1, tt * 128:tt * 128 + 128],
                           b.binrow[0:1, bi * TC:(bi + 1) * TC],
                           start=False, stop=True).inc(sm["mm"], 1)
            for j in range(nd):
                C.T.matmul(b.pG[:, 0:MH], b.xn[:, j * TC + tt * 128:j * TC + tt * 128 + 128],
                           b.win[:, j * PROJ + INTER + CONV:j * PROJ + PROJ],
                           start=(j == 0), stop=False)
            C.T.matmul(b.pG[:, 0:MH], b.onesrow_bf[0:1, tt * 128:tt * 128 + 128],
                       b.binrow[0:1, INTER + CONV:PROJ], start=False, stop=True).inc(sm["mm"], 1)
            v_g = s.val(sm["mm"])
            C.S.wait(sm["mm"], v_g)
            if c > 0 or tt > 0:
                C.S.wait(sm["st"], s.val(sm["st"]))
            for bi in range(4):
                C.S.activation(out=b.gbuf[:, bi * TC:(bi + 1) * TC],
                               in_=pg[bi][:, 0:TC], func=AF.Silu).inc(sm["ac"], 1)
            v_gb = s.val(sm["ac"])
            C.Y.wait(sm["ac"], v_gb)
            gap = _ap3(dr["gate_scr"], [[INTER, 128], [1, INTER]],
                       offset=(c * 4 + tt) * 128 * INTER)
            C.Y.dma_start(out=gap, in_=b.gbuf[:, 0:INTER]).inc(sm["st"], 16)
            C.V.wait(sm["mm"], v_g)
            dsl = b.dtt[:, tt * MH:(tt + 1) * MH]
            C.V.tensor_tensor(out=b.dts2[:, 0:MH], in0=b.pG[:, 0:MH], in1=b.dtbrep[:, 0:MH],
                              op=ALU.add)
            C.V.drain()
            C.V.tensor_scalar_max(out=b.dts1[:, 0:MH], in0=b.dts2[:, 0:MH], scalar1=0.0).inc(sm["ve"], 1)
            v_d1 = s.val(sm["ve"])
            C.S.wait(sm["ve"], v_d1)
            C.S.activation(out=b.dts2[:, 0:MH], in_=b.dts2[:, 0:MH], func=AF.Abs).inc(sm["ac"], 1)
            C.S.wait(sm["ac"], s.val(sm["ac"]))
            C.S.activation(out=b.dts2[:, 0:MH], in_=b.dts2[:, 0:MH], func=AF.Exp,
                           scale=-1.0).inc(sm["ac"], 1)
            C.S.wait(sm["ac"], s.val(sm["ac"]))
            C.S.activation(out=b.dts2[:, 0:MH], in_=b.dts2[:, 0:MH], func=AF.Ln,
                           bias=b.cpool[:, 8:9]).inc(sm["ac"], 1)
            v_d2 = s.val(sm["ac"])
            C.V.wait(sm["ac"], v_d2)
            C.V.drain()
            C.V.tensor_add(out=dsl, in0=b.dts1[:, 0:MH], in1=b.dts2[:, 0:MH]).inc(sm["ve"], 1)
            v_dt = s.val(sm["ve"])
            C.T.wait(sm["ve"], v_dt)
            C.T.transpose(b.pG[0:MH, 0:128], dsl, identity=b.ident_sb[:, 0:128]).inc(sm["mm"], 1)
            v_tr = s.val(sm["mm"])
            C.V.wait(sm["mm"], v_tr)
            C.V.tensor_copy(out=b.dtc[:, tt * 128:(tt + 1) * 128], in_=b.pG[0:MH, 0:128])
            C.V.drain()
            C.V.tensor_scalar_mul(out=b.dtc[:, tt * 128:(tt + 1) * 128],
                                  in0=b.dtc[:, tt * 128:(tt + 1) * 128],
                                  scalar1=b.aexp[0:MH, 0:1])
            C.V.drain()
            C.V.tensor_tensor_scan(out=b.acumc[0:MH, tt * 128:(tt + 1) * 128],
                                   data0=b.zrow[0:MH, 0:128],
                                   data1=b.dtc[0:MH, tt * 128:(tt + 1) * 128],
                                   initial=0.0, op0=ALU.add, op1=ALU.add).inc(sm["ve"], 1)
        v_chunk = s.val(sm["ve"])

        C.Y.wait(sm["ac"], v_sta)
        bap = _ap3(dr["b_scr"], [[T, 128], [1, TC]], offset=c * TC)
        C.Y.dma_start(out=bap, in_=b.bcsp[:, 0:TC]).inc(sm["st"], 16)
        cap = _ap3(dr["c_scr"], [[T, 128], [1, TC]], offset=c * TC)
        C.Y.dma_start(out=cap, in_=b.bcsp[:, TC:2 * TC]).inc(sm["st"], 16)
        C.Y.wait(sm["ve"], v_chunk)
        dap = _ap3(dr["dt_scr"], [[MH, 128], [128 * MH, 4], [1, MH]], offset=c * 4 * 128 * MH)
        C.Y.dma_start(out=dap, in_=b.dtt[:, 0:4 * MH]).inc(sm["st"], 16)
        aap = _ap3(dr["acum_scr"], [[T, MH], [1, TC]], offset=c * TC)
        C.Y.dma_start(out=aap, in_=b.acumc[0:MH, 0:TC]).inc(sm["st"], 16)
        b.ma_stored[c] = s.val(sm["st"])
    C.Y.wait(sm["st"], b.ma_stored[NCH - 1])


def emit_mamba_b(C, dr, b, sm, dbg=None):
    s = C.s
    NU = S // CS
    C.Y.dma_start(out=b.droww[:], in_=dr["d_row"]).inc(sm["ld"], 16)
    v_l = s.val(sm["ld"])
    C.T.wait(sm["ld"], v_l)
    C.T.matmul(b.pA[:, 0:TC], b.onesrow_f[0:1, :], b.droww[0:1, 0:INTER // 4], start=True, stop=True)
    C.T.matmul(b.pB[:, 0:TC], b.onesrow_f[0:1, :], b.droww[0:1, INTER // 4:INTER // 2], start=True, stop=True)
    C.T.matmul(b.pC[:, 0:TC], b.onesrow_f[0:1, :], b.droww[0:1, INTER // 2:3 * INTER // 4], start=True, stop=True)
    C.T.matmul(b.pD[:, 0:TC], b.onesrow_f[0:1, :], b.droww[0:1, 3 * INTER // 4:INTER], start=True, stop=True).inc(sm["mm"], 1)
    v_mm = s.val(sm["mm"])
    C.V.wait(sm["mm"], v_mm)
    for q in range(4):
        cp = C.V.tensor_copy(out=b.drep[:, q * TC:(q + 1) * TC],
                             in_=[b.pA, b.pB, b.pC, b.pD][q][:, 0:TC])
        if q == 3:
            cp.inc(sm["ve"], 1)
    v_dr = s.val(sm["ve"])
    C.Y.wait(sm["ve"], v_dr)
    C.Y.dma_start(out=b.nwrow[:], in_=dr["normw_row"]).inc(sm["ld"], 16)
    v_l2 = s.val(sm["ld"])
    C.T.wait(sm["ld"], v_l2)
    C.T.wait(sm["ve"], v_dr)
    C.T.matmul(b.pA[:, 0:TC], b.onesrow_f[0:1, :], b.nwrow[0:1, 0:TC], start=True, stop=True)
    C.T.matmul(b.pB[:, 0:TC], b.onesrow_f[0:1, :], b.nwrow[0:1, TC:2 * TC], start=True, stop=True)
    C.T.matmul(b.pC[:, 0:TC], b.onesrow_f[0:1, :], b.nwrow[0:1, 2 * TC:3 * TC], start=True, stop=True)
    C.T.matmul(b.pD[:, 0:TC], b.onesrow_f[0:1, :], b.nwrow[0:1, 3 * TC:INTER], start=True, stop=True).inc(sm["mm"], 1)
    v_mm2 = s.val(sm["mm"])
    C.V.wait(sm["mm"], v_mm2)
    for q in range(4):
        C.V.tensor_copy(out=b.nwrep[:, q * TC:(q + 1) * TC], in_=[b.pA, b.pB, b.pC, b.pD][q][:, 0:TC]).inc(sm["ve"], 1)
    v_nw = s.val(sm["ve"])

    for bi in range(B_LOC):
        C.V.wait(sm["mm"], s.val(sm["mm"]))
        C.V.memset(b.ssb[0][:], 0.0).inc(sm["ve"], 1)
        for u in range(NU):
            ub = u % 2
            tof = bi * S + u * CS
            sp, sn = b.ssb[u % 2], b.ssb[(u + 1) % 2]
            if u >= 2:
                C.Y.wait(sm["ve"], b.mb_cons[(bi, u - 2)])
                C.S.wait(sm["ve"], b.mb_cons[(bi, u - 2)])
            hin = _ap3(dr["h_scr"], [[T, INTER], [1, CS]], offset=tof)
            C.Y.dmat(out=b.htok[ub][:, 0:INTER], in_=hin).inc(sm["tr"], 16)
            bin_ = _ap3(dr["b_scr"], [[T, 128], [1, CS]], offset=tof)
            C.Y.dma_start(out=b.bchan[ub][:], in_=bin_).inc(sm["x"], 16)
            C.Y.dmat(out=b.btok[ub][:], in_=bin_).inc(sm["tr"], 16)
            cin = _ap3(dr["c_scr"], [[T, 128], [1, CS]], offset=tof)
            C.Y.dma_start(out=b.cchan[ub][:], in_=cin).inc(sm["x"], 16)
            gin = _ap3(dr["gate_scr"], [[INTER, 128], [1, INTER]], offset=tof * INTER)
            C.Y.dma_start(out=b.gchk[ub][:], in_=gin).inc(sm["x"], 16)
            din = _ap3(dr["dt_scr"], [[MH, 128], [1, MH]], offset=tof * MH)
            C.Y.dma_start(out=b.dtk[ub][:], in_=din).inc(sm["x"], 16)
            ain = _ap3(dr["acum_scr"], [[T, MH], [1, CS]], offset=tof)
            C.Y.dma_start(out=b.acu[ub][:], in_=ain).inc(sm["x"], 16)
            arin = _ap3(dr["acum_scr"], [[0, 128], [T, MH], [1, CS]], offset=tof)
            C.S.wait(sm["ve"], b.mb_difdone)
            C.S.dma_start(out=b.arep[0][:, 0:MH * 128], in_=arin).inc(sm["sld"], 16)
            v_ld = s.val(sm["x"])
            v_tr = s.val(sm["tr"])
            v_ar = s.val(sm["sld"])

            C.T.wait(sm["x"], v_ld)
            C.T.wait(sm["ve"], s.val(sm["ve"]))
            C.T.transpose(b.pG[0:128, 0:MH], b.acu[ub][0:MH, 0:128],
                          identity=b.ident_sb[0:MH, 0:MH]).inc(sm["mm"], 1)
            vt1 = s.val(sm["mm"])
            C.V.wait(sm["mm"], vt1)
            C.V.tensor_copy(out=b.acumT[:, 0:MH], in_=b.pG[0:128, 0:MH]).inc(sm["ve"], 1)
            vacT = s.val(sm["ve"])
            C.S.wait(sm["x"], v_ld)
            C.S.activation(out=b.decc[0:MH, 0:128], in_=b.acu[ub][0:MH, 0:128],
                           func=AF.Exp, scale=-1.0, bias=b.acu[ub][0:MH, CS - 1:CS])
            C.S.activation(out=b.sdoc[0:MH, 0:128], in_=b.acu[ub][0:MH, 0:128],
                           func=AF.Exp).inc(sm["ac"], 1)
            vexp = s.val(sm["ac"])
            C.T.wait(sm["ac"], vexp)
            C.T.wait(sm["ve"], vacT)
            C.T.transpose(b.pG[0:128, 0:MH], b.decc[0:MH, 0:128],
                          identity=b.ident_sb[0:MH, 0:MH]).inc(sm["mm"], 1)
            C.V.wait(sm["mm"], s.val(sm["mm"]))
            C.V.tensor_copy(out=b.decT[:, 0:MH], in_=b.pG[0:128, 0:MH]).inc(sm["ve"], 1)
            C.T.wait(sm["ve"], s.val(sm["ve"]))
            C.T.transpose(b.pG[0:128, 0:MH], b.sdoc[0:MH, 0:128],
                          identity=b.ident_sb[0:MH, 0:MH]).inc(sm["mm"], 1)
            C.V.wait(sm["mm"], s.val(sm["mm"]))
            C.V.tensor_copy(out=b.sdoT[:, 0:MH], in_=b.pG[0:128, 0:MH]).inc(sm["ve"], 1)
            C.T.wait(sm["ve"], s.val(sm["ve"]))
            C.T.transpose(b.pG[0:1, 0:MH], b.sdoc[0:MH, CS - 1:CS],
                          identity=b.ident_sb[0:MH, 0:MH]).inc(sm["mm"], 1)
            C.V.wait(sm["mm"], s.val(sm["mm"]))
            C.V.tensor_copy(out=b.escrow[0:1, 0:MH], in_=b.pG[0:1, 0:MH]).inc(sm["ve"], 1)
            vesc = s.val(sm["ve"])
            C.T.wait(sm["ve"], vesc)
            C.T.matmul(b.pG[:, 0:MH], b.onesrow_f[0:1, :], b.escrow[0:1, 0:MH],
                       start=True, stop=True).inc(sm["mm"], 1)
            C.V.wait(sm["mm"], s.val(sm["mm"]))
            C.V.tensor_copy(out=b.escM[:, 0:MH], in_=b.pG[:, 0:MH]).inc(sm["ve"], 1)
            vescM = s.val(sm["ve"])

            if not (bi == 0 and u == 0):
                C.V.wait(sm["st"], b.mb_stored)
            C.S.wait(sm["mm"], s.val(sm["mm"]))
            C.V.wait(sm["sld"], v_ar)
            arr = b.arep[0]
            acT_rep = _ap3(b.acumT, [[b.acumT.ap().ap[0][0], 128], [1, MH], [0, 128]])
            C.V.tensor_tensor(out=b.scrA[:, 0:MH * 128], in0=arr[:, 0:MH * 128],
                              in1=acT_rep, op=ALU.subtract)
            C.V.tensor_scalar_min(out=b.scrA[:, 0:MH * 128], in0=b.scrA[:, 0:MH * 128],
                                  scalar1=0.0).inc(sm["ve"], 1)
            vdif = s.val(sm["ve"])
            b.mb_difdone = vdif
            if dbg is not None and bi == 0 and u == 0 and "dbg_dif2" in dbg:
                C.G.wait(sm["ve"], vdif)
                C.G.dma_start(out=dbg["dbg_dif2"], in_=b.scrA[:, 0:MH * 128]).inc(sm["gld"], 16)
                C.V.wait(sm["gld"], s.val(sm["gld"]))
            C.S.wait(sm["ve"], vdif)
            C.S.activation(out=b.expbf[:, 0:MH * 128], in_=b.scrA[:, 0:MH * 128],
                           func=AF.Exp).inc(sm["ac"], 1)
            vexp2 = s.val(sm["ac"])
            C.T.wait(sm["tr"], v_tr)
            C.T.wait(sm["ve"], s.val(sm["ve"]))
            C.T.matmul(b.pE2[:, 0:128], b.bchan[ub][:], b.cchan[ub][:],
                       start=True, stop=True).inc(sm["mm"], 1)
            vgm = s.val(sm["mm"])
            C.V.wait(sm["mm"], vgm)
            C.V.tensor_tensor(out=b.gmbf[:], in0=b.pE2[:, 0:128], in1=b.triu_sb[:],
                              op=ALU.mult).inc(sm["ve"], 1)
            vgmb = s.val(sm["ve"])
            if dbg is not None and bi == 0 and u == 0 and "dbg_gm" in dbg:
                C.G.wait(sm["ve"], vgmb)
                C.G.dma_start(out=dbg["dbg_gm"], in_=b.gmbf[:, 0:128]).inc(sm["gld"], 16)
                C.V.wait(sm["gld"], s.val(sm["gld"]))
            C.V.wait(sm["ac"], vexp2)
            gm_rep = _ap3(b.gmbf, [[b.gmbf.ap().ap[0][0], 128], [0, MH], [1, 128]])
            C.V.tensor_tensor(out=b.expbf[:, 0:MH * 128], in0=b.expbf[:, 0:MH * 128],
                              in1=gm_rep, op=ALU.mult).inc(sm["ve"], 1)
            vM = s.val(sm["ve"])

            C.V.wait(sm["tr"], v_tr)
            C.V.wait(sm["x"], v_ld)
            dt_rep = _ap3(b.dtk[ub], [[b.dtk[ub].ap().ap[0][0], 128], [1, MH], [0, MP]])
            C.V.tensor_tensor(out=b.xx[:, 0:INTER], in0=b.htok[ub][:, 0:INTER],
                              in1=dt_rep, op=ALU.mult)
            dec_rep = _ap3(b.decT, [[b.decT.ap().ap[0][0], 128], [1, MH], [0, MP]])
            C.V.tensor_tensor(out=b.xx2[:, 0:INTER], in0=b.xx[:, 0:INTER],
                              in1=dec_rep, op=ALU.mult).inc(sm["ve"], 1)
            vX = s.val(sm["ve"])

            for hf in range(2):
                hof = hf * (INTER // 2)
                pY = [b.pA, b.pB] if hf == 0 else [b.pA, b.pB]
                pO = [b.pC, b.pD]
                C.T.wait(sm["ve"], s.val(sm["ve"]))
                for hh in range(16):
                    h = hf * 16 + hh
                    bank = pY[hh // 8]
                    mm = C.T.matmul(bank[:, (hh % 8) * MP:(hh % 8) * MP + MP],
                                    b.expbf[:, h * 128:(h + 1) * 128],
                                    b.xx[:, h * MP:(h + 1) * MP],
                                    start=(hh % 8 == 0), stop=(hh % 8 == 7))
                    if hh == 15:
                        mm.inc(sm["mm"], 1)
                if u > 0:
                    for q in range(2):
                        mm = C.T.matmul(pO[q][:, 0:TC], b.cchan[ub][:],
                                        sp[:, hof + q * TC:hof + (q + 1) * TC],
                                        start=True, stop=True)
                        if q == 1:
                            mm.inc(sm["mm"], 1)
                vy = s.val(sm["mm"])
                C.V.wait(sm["mm"], vy)
                t1a = b.scrA[:, 0:TC]
                t1b = b.scrA[:, TC:2 * TC]
                if u > 0:
                    pst = b.sdoT.ap().ap[0][0]
                    sdo_rep = _ap3(b.sdoT, [[pst, 128], [1, 8], [0, MP]], offset=hf * 16)
                    C.V.tensor_tensor(out=t1a, in0=b.pC[:, 0:TC], in1=sdo_rep, op=ALU.mult)
                    sdo_rep2 = _ap3(b.sdoT, [[pst, 128], [1, 8], [0, MP]], offset=hf * 16 + 8)
                    C.V.tensor_tensor(out=t1b, in0=b.pD[:, 0:TC], in1=sdo_rep2, op=ALU.mult)
                C.G.wait(sm["ve"], s.val(sm["ve"]))
                C.G.tensor_tensor(out=b.stmp[:, 0:2 * TC], in0=b.htok[ub][:, hof:hof + 2 * TC],
                                  in1=b.drep[:, hof:hof + 2 * TC], op=ALU.mult).inc(sm["gp"], 1)
                vt2 = s.val(sm["gp"])
                for q in range(2):
                    pyq = [b.pA, b.pB][q]
                    ysl = b.scrA[:, (2 + q) * TC:(3 + q) * TC]
                    if u > 0:
                        tq = [t1a, t1b][q]
                        C.V.scalar_tensor_tensor(out=ysl, in0=pyq[:, 0:TC], scalar=1.0,
                                                 in1=tq, op0=ALU.mult, op1=ALU.add)
                    else:
                        C.V.tensor_copy(out=ysl, in_=pyq[:, 0:TC])
                C.V.wait(sm["gp"], vt2)
                for q in range(2):
                    ysl = b.scrA[:, (2 + q) * TC:(3 + q) * TC]
                    C.V.tensor_tensor(out=ysl, in0=ysl, in1=b.stmp[:, q * TC:(q + 1) * TC],
                                      op=ALU.add)
                    C.V.tensor_tensor(out=b.gbig[:, hof + q * TC:hof + (q + 1) * TC],
                                      in0=ysl, in1=b.gchk[ub][:, hof + q * TC:hof + (q + 1) * TC],
                                      op=ALU.mult).inc(sm["ve"], 1)
            vg = s.val(sm["ve"])
            if dbg is not None and bi == 0 and u == 0 and "dbg_m" in dbg:
                C.G.wait(sm["ve"], vg)
                C.G.dma_start(out=dbg["dbg_h"], in_=b.htok[ub][:, 0:INTER]).inc(sm["gld"], 16)
                C.G.dma_start(out=dbg["dbg_dt"], in_=b.dtk[ub][:, 0:MH]).inc(sm["gld"], 16)
                C.G.dma_start(out=dbg["dbg_m"], in_=b.expbf[:, 0:MH * 128]).inc(sm["gld"], 16)
                C.G.dma_start(out=dbg["dbg_x"], in_=b.xx[:, 0:INTER]).inc(sm["gld"], 16)
                C.G.dma_start(out=dbg["dbg_g"], in_=b.gbig[:, 0:INTER]).inc(sm["gld"], 16)
                C.G.dma_start(out=dbg["dbg_dif"], in_=b.scrA[:, 2 * TC:4 * TC]).inc(sm["gld"], 16)
                C.G.wait(sm["gld"], s.val(sm["gld"]))
                vtap = s.val(sm["gld"])
                C.V.wait(sm["gld"], vtap)
                C.S.wait(sm["gld"], vtap)
                C.T.wait(sm["gld"], vtap)
                C.Y.wait(sm["gld"], vtap)

            C.T.wait(sm["ve"], vg)
            C.T.wait(sm["tr"], v_tr)
            pS = [b.pA, b.pB, b.pC, b.pD]
            for q in range(4):
                mm = C.T.matmul(pS[q][:, 0:TC], b.btok[ub][:],
                                b.xx2[:, q * TC:(q + 1) * TC], start=True, stop=True)
                if q == 3:
                    mm.inc(sm["mm"], 1)
            vst = s.val(sm["mm"])
            C.G.wait(sm["ve"], s.val(sm["ve"]))
            esc_rep = _ap3(b.escM, [[b.escM.ap().ap[0][0], 128], [1, MH], [0, MP]])
            C.G.tensor_tensor(out=b.stmp[:, 0:INTER], in0=sp[:, 0:INTER],
                              in1=esc_rep, op=ALU.mult).inc(sm["gp"], 1)
            vsc = s.val(sm["gp"])
            C.V.wait(sm["mm"], vst)
            C.V.wait(sm["gp"], vsc)
            for q in range(4):
                C.V.tensor_tensor(out=sn[:, q * TC:(q + 1) * TC], in0=pS[q][:, 0:TC],
                                  in1=b.stmp[:, q * TC:(q + 1) * TC], op=ALU.add).inc(sm["ve"], 1)
            vS = s.val(sm["ve"])

            C.V.tensor_mul(out=b.scrA[:, 0:INTER], in0=b.gbig[:, 0:INTER],
                           in1=b.gbig[:, 0:INTER])
            C.V.reduce_sum(out=b.gsq[:, 0:1], in_=b.scrA[:, 0:INTER],
                           axis=mybir.AxisListType.X).inc(sm["ve"], 1)
            vms = s.val(sm["ve"])
            C.S.wait(sm["ve"], vms)
            C.S.activation(out=b.gsq[:, 0:1], in_=b.gsq[:, 0:1], func=AF.Ln,
                           scale=1.0 / INTER, bias=b.cpool[:, 0:1]).inc(sm["ac"], 1)
            C.S.wait(sm["ac"], s.val(sm["ac"]))
            C.S.activation(out=b.gsq[:, 0:1], in_=b.gsq[:, 0:1], func=AF.Exp,
                           scale=-0.5).inc(sm["ac"], 1)
            vrs = s.val(sm["ac"])
            C.V.wait(sm["ac"], vrs)
            C.V.wait(sm["ve"], v_nw)
            C.V.tensor_scalar_mul(out=b.gbig[:, 0:INTER], in0=b.gbig[:, 0:INTER],
                                  scalar1=b.gsq[:, 0:1])
            C.V.tensor_mul(out=b.gbig[:, 0:INTER], in0=b.gbig[:, 0:INTER],
                           in1=b.nwrep[:, 0:INTER]).inc(sm["ve"], 1)
            vgf = s.val(sm["ve"])
            b.mb_cons[(bi, u)] = vgf
            C.Y.wait(sm["ve"], vgf)
            gsp = _ap3(dr["g_scr"], [[INTER, 128], [1, INTER]], offset=tof * INTER)
            C.Y.dma_start(out=gsp, in_=b.gbig[:, 0:INTER]).inc(sm["st"], 16)
            b.mb_stored = s.val(sm["st"])
    C.Y.wait(sm["st"], b.mb_stored)


def emit_mamba_c(C, dr, b, sm, src, dst):
    s = C.s
    nd = D // 128
    for ci in range(NHT):
        sap = _ap3(dr["woutT"], [[D, 128], [1, D]], offset=ci * 128 * D)
        C.G.dma_start(out=b.wout[:, ci * D:(ci + 1) * D], in_=sap).inc(sm["gld"], 16)
    C.G.dma_start(out=b.boutrow[:], in_=dr["b_out"]).inc(sm["gld"], 16)
    v_w = s.val(sm["gld"])
    C.T.wait(sm["gld"], v_w)
    for c in range(NCH):
        cb = c % 2
        xch = b.xch[:, cb * nd * TC:(cb + 1) * nd * TC]
        if c >= 2:
            C.Y.wait(sm["ve"], b.mc_cons[c - 2])
        sap = _ap3(src, [[T, 128], [128 * T, nd], [1, TC]], offset=c * TC)
        C.Y.dma_start(out=xch, in_=sap).inc(sm["x"], 16)
        v_x = s.val(sm["x"])
        if c >= 1:
            C.Y.wait(sm["mm"], b.mc_mmdone[c - 1])
        for ci in range(NHT):
            gin = _ap3(dr["g_scr"], [[INTER, TC], [1, 128]], offset=c * TC * INTER + ci * 128)
            C.Y.dmat(out=b.gT[ci][:, 0:TC], in_=gin).inc(sm["tr"], 16)
        v_tr = s.val(sm["tr"])
        C.T.wait(sm["tr"], v_tr)
        pg = [b.pA, b.pB, b.pC, b.pD]
        for dt_ in range(nd):
            bank = pg[dt_ % 4]
            if dt_ >= 4:
                C.T.wait(sm["ve"], s.val(sm["ve"]))
            for ci in range(NHT):
                C.T.matmul(bank[:, 0:TC], b.wout[:, ci * D + dt_ * 128:ci * D + (dt_ + 1) * 128],
                           b.gT[ci][:, 0:TC], start=(ci == 0), stop=False)
            mm = C.T.matmul(bank[:, 0:TC], b.boutrow[0:1, dt_ * 128:(dt_ + 1) * 128],
                            b.onesrow_bf[0:1, 0:TC], start=False, stop=True)
            mm.inc(sm["mm"], 1)
            v_y = s.val(sm["mm"])
            C.V.wait(sm["mm"], v_y)
            if dt_ == 0:
                C.V.wait(sm["x"], v_x)
            if c >= 1 and dt_ == 0:
                C.V.wait(sm["st"], b.mc_stored[c - 1])
            C.V.scalar_tensor_tensor(out=b.rout[:, dt_ * TC:(dt_ + 1) * TC],
                                     in0=bank[:, 0:TC], scalar=1.0,
                                     in1=xch[:, dt_ * TC:(dt_ + 1) * TC],
                                     op0=ALU.mult, op1=ALU.add).inc(sm["ve"], 1)
        b.mc_mmdone[c] = s.val(sm["mm"])
        b.mc_cons[c] = s.val(sm["ve"])
        C.Y.wait(sm["ve"], b.mc_cons[c])
        dap = _ap3(dst, [[T, 128], [128 * T, nd], [1, TC]], offset=c * TC)
        C.Y.dma_start(out=dap, in_=b.rout[:, 0:nd * TC]).inc(sm["st"], 16)
        b.mc_stored[c] = s.val(sm["st"])
    C.Y.wait(sm["st"], b.mc_stored[NCH - 1])


def alloc_attn_persist(pa, b):
    b.fullk = pa.sb("fullk", [128, 8 * NSLOT], BF16)
    b.fullv = pa.sb("fullv", [128, 8 * NSLOT], BF16)
    b.vt = pa.sb("vt", [128, 4 * NH * (HD + 1)], BF16)
    b.pG2 = pa.ps("pG2", [128, TC])
    b.pT = pa.ps("pT", [128, HD], BF16)


def alloc_attn_a(pa, b):
    b.wA = pa.sb("wA", [128, 8 * D], BF16)
    b.wB = pa.sb("wB", [128, 8 * D], BF16)
    b.laA = pa.sb("laA", [128, 8 * LORA], BF16)
    b.lbA = pa.sb("lbA", [LORA, D], BF16)
    b.kc = pa.sb("kc", [128, 8 * TC], BF16)
    b.vc = pa.sb("vc", [128, 8 * TC], BF16)
    b.tmplA = pa.sb("tmplA", [LORA, TC], BF16)
    b.rpbA = pa.sb("rpbA", [128, TC], BF16)
    b.rkbA = pa.sb("rkbA", [128, TC], BF16)
    b.cosA = pa.sb("cosA", [128, TC], BF16)
    b.sinA = pa.sb("sinA", [128, TC], BF16)
    b.cdmaA = pa.sb("cdmaA", [128, MK], F32)
    b.identbf = pa.sb("identbf", [128, HD], BF16)


def alloc_attn_b(pa, b):
    b.wq_st = [pa.sb(f"wqst{i}", [128, 8 * 128], BF16) for i in range(2)]
    b.wo_st = [pa.sb(f"wost{i}", [128, 8 * 128], BF16) for i in range(2)]
    b.laB = pa.sb("laB", [128, 8 * LORA], BF16)
    b.lbB = pa.sb("lbB", [LORA, D], BF16)
    b.lc = pa.sb("lc", [128, 8 * LORA], BF16)
    b.ld_ = pa.sb("ld_", [LORA, D], BF16)
    b.kcB = pa.sb("kcB", [128, 8 * TC], BF16)
    b.vcB = pa.sb("vcB", [128, 8 * TC], BF16)
    b.cpkt = pa.sb("cpkt", [128, TC], BF16)
    b.cpvt = pa.sb("cpvt", [128, TC], BF16)
    b.aqb = pa.sb("aqb", [128, 8 * TC], BF16)
    b.o2 = pa.sb("o2", [128, 8 * TC], BF16)
    b.ebuf = pa.sb("ebuf", [128, 4 * TC], BF16)
    b.tmplB = pa.sb("tmplB", [LORA, TC], BF16)
    b.rpbB = pa.sb("rpbB", [128, TC], BF16)
    b.rkbB = pa.sb("rkbB", [128, TC], BF16)
    b.zrow1 = pa.sb("zrow1", [1, TC], F32)
    b.wcur = pa.sb("wcur", [1, TC], F32)
    b.zinv = pa.sb("zinv", [1, TC], F32)
    b.qcph = pa.sb("qcph", [128, TC], BF16)
    b.corr = pa.sb("corr", [128, TC], F32)
    b.cosB = pa.sb("cosB", [128, TC], BF16)
    b.sinB = pa.sb("sinB", [128, TC], BF16)
    b.qpaB = pa.sb("qpaB", [128, TC], BF16)
    b.qpoB = pa.sb("qpoB", [128, TC], BF16)
    b.bandB = pa.sb("bandB", [128, TC], F32)
    b.cdmaB = pa.sb("cdmaB", [128, MK], F32)


def _rope(C, b, sm, psum, cos_sl, sin_sl, rpb, rkb, outsl, vref):
    C.V.wait(sm["mm"], vref)
    for base in (0, 64):
        C.V.tensor_scalar_mul(out=rpb[base:base + 32, 0:TC],
                              in0=psum[base + 32:base + 64, 0:TC], scalar1=-1.0)
        C.V.tensor_copy(out=rpb[base + 32:base + 64, 0:TC],
                        in_=psum[base:base + 32, 0:TC])
    C.V.tensor_tensor(out=rkb[:, 0:TC], in0=psum[:, 0:TC], in1=cos_sl, op=ALU.mult)
    C.V.drain()
    C.V.tensor_tensor(out=rpb[:, 0:TC], in0=rpb[:, 0:TC], in1=sin_sl, op=ALU.mult)
    C.V.drain()
    C.V.tensor_tensor(out=outsl, in0=rkb[:, 0:TC], in1=rpb[:, 0:TC], op=ALU.add)


def _cdma_taps(C, b, sm, cdma, srcsl, dstsl, cumulative):
    if cumulative:
        C.V.tensor_scalar_mul(out=_ap3(dstsl, [[dstsl.ap[0][0], 128], [MK, 128]]),
                              in0=_ap3(srcsl, [[srcsl.ap[0][0], 128], [MK, 128]]),
                              scalar1=cdma[:, 0:1])
        for jj in range(1, MK):
            C.V.drain()
            C.V.scalar_tensor_tensor(
                out=_ap3(dstsl, [[dstsl.ap[0][0], 128], [MK, 128]], offset=jj),
                in0=_ap3(srcsl, [[srcsl.ap[0][0], 128], [MK, 128]], offset=jj),
                scalar=cdma[:, jj:jj + 1],
                in1=_ap3(dstsl, [[dstsl.ap[0][0], 128], [MK, 128]], offset=jj - 1),
                op0=ALU.mult, op1=ALU.add)
    else:
        C.V.tensor_scalar_mul(out=dstsl,
                              in0=_ap3(srcsl, [[srcsl.ap[0][0], 128], [MK, 128]]),
                              scalar1=cdma[:, 0:1])
        for jj in range(1, MK):
            C.V.drain()
            C.V.scalar_tensor_tensor(
                out=dstsl,
                in0=_ap3(srcsl, [[srcsl.ap[0][0], 128], [MK, 128]], offset=jj),
                scalar=cdma[:, jj:jj + 1], in1=dstsl, op0=ALU.mult, op1=ALU.add)


def emit_attn(C, dr, b, sm, src, dst):
    s = C.s
    nd = D // 128
    for bi in range(B_LOC):
        C.Y.dma_start(out=b.cdmaA[:], in_=dr["cdma_half"]).inc(sm["ld"], 16)
        C.G.dma_start(out=b.identbf[:], in_=dr["ident2"]).inc(sm["gld"], 16)
        v_c1 = s.val(sm["ld"])
        for e in (C.V, C.S, C.T):
            e.wait(sm["ld"], v_c1)
        for j in range(nd):
            C.G.dma_start(out=b.wA[:, j * D:(j + 1) * D],
                          in_=_ap3(dr["wk"], [[D, 128], [1, D]], offset=j * 128 * D)).inc(sm["gld"], 16)
            C.G.dma_start(out=b.wB[:, j * D:(j + 1) * D],
                          in_=_ap3(dr["wv"], [[D, 128], [1, D]], offset=j * 128 * D)).inc(sm["gld"], 16)
            C.G.dma_start(out=b.laA[:, j * LORA:(j + 1) * LORA],
                          in_=_ap3(dr["av"], [[LORA, 128], [1, LORA]], offset=j * 128 * LORA)).inc(sm["gld"], 16)
        C.G.dma_start(out=b.lbA[:], in_=dr["bv"]).inc(sm["gld"], 16)
        v_w3a = s.val(sm["gld"])
        C.T.wait(sm["gld"], v_w3a)
        C.V.memset(b.vt[:], 1.0).inc(sm["ve"], 1)

        for c in range(4):
            cg = bi * 4 + c
            cb = cg % 2
            xch = b.xch[:, cb * nd * TC:(cb + 1) * nd * TC]
            C.Y.wait(sm["ve"], b.at_cons.get(cg - 2, 0))
            C.Y.dma_start(out=xch, in_=_ap3(src, [[T, 128], [128 * T, nd], [1, TC]],
                                            offset=cg * TC)).inc(sm["x"], 16)
            C.G.wait(sm["ve"], b.at_tabfree.get(cg - 1, 0))
            C.G.dma_start(out=b.cosA[:], in_=_ap3(dr["cos_rep"], [[S, 128], [1, TC]],
                                                  offset=c * TC)).inc(sm["gld"], 16)
            C.G.dma_start(out=b.sinA[:], in_=_ap3(dr["sin_rep"], [[S, 128], [1, TC]],
                                                  offset=c * TC)).inc(sm["gld"], 16)
            v_tab = s.val(sm["gld"])
            v_x = s.val(sm["x"])
            C.S.wait(sm["x"], v_x)
            C.T.wait(sm["x"], v_x)
            C.V.wait(sm["x"], v_x)
            C.V.wait(sm["gld"], v_tab)
            v_xn = emit_ln_chunk(C, b, sm, xch, TC)
            b.at_cons[cg] = v_xn
            C.T.wait(sm["ve"], v_xn)
            C.T.wait(sm["ve"], s.val(sm["ve"]))
            for j in range(nd):
                mmL = C.T.matmul(b.pF[0:LORA, 0:TC], b.laA[:, j * LORA:(j + 1) * LORA],
                                 b.xn[:, j * TC:(j + 1) * TC], start=(j == 0), stop=(j == nd - 1))
                if j == nd - 1:
                    mmL.inc(sm["mm"], 1)
            C.V.wait(sm["mm"], s.val(sm["mm"]))
            C.V.tensor_copy(out=b.tmplA[:, 0:TC], in_=b.pF[0:LORA, 0:TC]).inc(sm["ve"], 1)
            v_tls = s.val(sm["ve"])
            for ct in range(nd):
                C.T.wait(sm["ve"], s.val(sm["ve"]))
                for j in range(nd):
                    mmK = C.T.matmul(b.pE[:, 0:TC], b.wA[:, j * D + ct * 128:j * D + (ct + 1) * 128],
                                     b.xn[:, j * TC:(j + 1) * TC], start=(j == 0), stop=(j == nd - 1))
                    if j == nd - 1:
                        mmK.inc(sm["mm"], 1)
                v_k = s.val(sm["mm"])
                _rope(C, b, sm, b.pE, b.cosA[:, 0:TC], b.sinA[:, 0:TC], b.rpbA, b.rkbA,
                      b.kc[:, ct * TC:(ct + 1) * TC], v_k)
                C.V.drain()
                _cdma_taps(C, b, sm, b.cdmaA, b.kc[:, ct * TC:(ct + 1) * TC],
                           b.fullk[:, ct * NSLOT + c * 128:ct * NSLOT + (c + 1) * 128], False)
                C.V.drain().inc(sm["ve"], 1)
                C.T.wait(sm["ve"], s.val(sm["ve"]))
                if ct == 0:
                    C.T.wait(sm["ve"], v_tls)
                for j in range(nd):
                    C.T.matmul(b.pF[:, 0:TC], b.wB[:, j * D + ct * 128:j * D + (ct + 1) * 128],
                               b.xn[:, j * TC:(j + 1) * TC], start=(j == 0), stop=False)
                C.T.matmul(b.pF[:, 0:TC], b.lbA[:, ct * 128:(ct + 1) * 128], b.tmplA[:, 0:TC],
                           start=False, stop=True).inc(sm["mm"], 1)
                v_v = s.val(sm["mm"])
                C.V.wait(sm["mm"], v_v)
                C.V.tensor_copy(out=b.vc[:, ct * TC:(ct + 1) * TC], in_=b.pF[:, 0:TC])
                C.V.drain()
                _cdma_taps(C, b, sm, b.cdmaA, b.vc[:, ct * TC:(ct + 1) * TC],
                           b.fullv[:, ct * NSLOT + c * 128:ct * NSLOT + (c + 1) * 128], False)
                C.V.drain().inc(sm["ve"], 1)
            v_kv = s.val(sm["ve"])
            b.at_tabfree[cg] = v_kv
            C.Y.wait(sm["ve"], v_kv)
            C.Y.dma_start(out=_ap3(dr["k_scr"], [[T, 128], [128 * T, nd], [1, TC]],
                                   offset=cg * TC), in_=b.kc[:, 0:nd * TC]).inc(sm["st"], 16)
            C.Y.dma_start(out=_ap3(dr["v_scr"], [[T, 128], [128 * T, nd], [1, TC]],
                                   offset=cg * TC), in_=b.vc[:, 0:nd * TC]).inc(sm["st"], 16)
            b.at_stored[("kv", cg)] = s.val(sm["st"])
        v_fv = s.val(sm["ve"])
        C.T.wait(sm["ve"], v_fv)
        for st_ in range(4):
            for h in range(NH):
                ct, par = h // 2, h % 2
                C.T.wait(sm["ve"], s.val(sm["ve"]))
                C.T.transpose(b.pT[0:128, 0:HD],
                              b.fullv[par * HD:(par + 1) * HD,
                                      ct * NSLOT + st_ * 128:ct * NSLOT + (st_ + 1) * 128],
                              identity=b.identbf[par * HD:(par + 1) * HD, 0:HD]).inc(sm["mm"], 1)
                C.V.wait(sm["mm"], s.val(sm["mm"]))
                col = (st_ * NH + h) * (HD + 1)
                C.V.tensor_copy(out=b.vt[:, col:col + HD], in_=b.pT[0:128, 0:HD]).inc(sm["ve"], 1)

        C.G.wait(sm["ve"], s.val(sm["ve"]))
        C.Y.dma_start(out=b.cdmaB[:], in_=dr["cdma_half"]).inc(sm["ld"], 16)
        C.Y.dma_start(out=b.bandB[:], in_=dr["band01"]).inc(sm["ld"], 16)
        v_cB = s.val(sm["ld"])
        for e in (C.V, C.S, C.T):
            e.wait(sm["ld"], v_cB)
        for j in range(nd):
            C.G.dma_start(out=b.laB[:, j * LORA:(j + 1) * LORA],
                          in_=_ap3(dr["aq"], [[LORA, 128], [1, LORA]], offset=j * 128 * LORA)).inc(sm["gld"], 16)
            C.G.dma_start(out=b.lc[:, j * LORA:(j + 1) * LORA],
                          in_=_ap3(dr["ao"], [[LORA, 128], [1, LORA]], offset=j * 128 * LORA)).inc(sm["gld"], 16)
        C.G.dma_start(out=b.lbB[:], in_=dr["bq"]).inc(sm["gld"], 16)
        C.G.dma_start(out=b.ld_[:], in_=dr["bo"]).inc(sm["gld"], 16)
        v_w3b = s.val(sm["gld"])
        C.T.wait(sm["gld"], v_w3b)

        for qc in range(4):
            cg = bi * 4 + qc
            cb = cg % 2
            xch = b.xch[:, cb * nd * TC:(cb + 1) * nd * TC]
            C.Y.wait(sm["ve"], b.at_cons.get(cg - 2, 0))
            C.Y.wait(sm["st"], b.at_stored.get(("r", cg - 1), 0))
            C.Y.dma_start(out=xch, in_=_ap3(src, [[T, 128], [128 * T, nd], [1, TC]],
                                            offset=cg * TC)).inc(sm["x"], 16)
            C.Y.wait(sm["ve"], b.at_kvfree.get(cg - 1, 0))
            C.Y.dma_start(out=b.kcB[:, 0:nd * TC],
                          in_=_ap3(dr["k_scr"], [[T, 128], [128 * T, nd], [1, TC]],
                                   offset=cg * TC)).inc(sm["x"], 16)
            C.Y.dma_start(out=b.vcB[:, 0:nd * TC],
                          in_=_ap3(dr["v_scr"], [[T, 128], [128 * T, nd], [1, TC]],
                                   offset=cg * TC)).inc(sm["x"], 16)
            C.G.wait(sm["ve"], b.at_tabfree.get(("b", cg - 1), 0))
            for nm, buf in (("cos_rep", b.cosB), ("sin_rep", b.sinB),
                            ("qpa", b.qpaB), ("qpo", b.qpoB)):
                C.G.dma_start(out=buf[:], in_=_ap3(dr[nm], [[S, 128], [1, TC]],
                                                   offset=qc * TC)).inc(sm["gld"], 16)
            v_tab = s.val(sm["gld"])
            v_x = s.val(sm["x"])
            C.S.wait(sm["x"], v_x)
            C.T.wait(sm["x"], v_x)
            C.V.wait(sm["x"], v_x)
            C.V.wait(sm["gld"], v_tab)
            v_xn = emit_ln_chunk(C, b, sm, xch, TC)
            b.at_cons[cg] = v_xn
            C.T.wait(sm["ve"], v_xn)
            C.T.wait(sm["ve"], s.val(sm["ve"]))
            for j in range(nd):
                mmq = C.T.matmul(b.pF[0:LORA, 0:TC], b.laB[:, j * LORA:(j + 1) * LORA],
                                 b.xn[:, j * TC:(j + 1) * TC], start=(j == 0), stop=(j == nd - 1))
                if j == nd - 1:
                    mmq.inc(sm["mm"], 1)
            C.V.wait(sm["mm"], s.val(sm["mm"]))
            C.V.tensor_copy(out=b.tmplB[:, 0:TC], in_=b.pF[0:LORA, 0:TC]).inc(sm["ve"], 1)
            v_tls = s.val(sm["ve"])
            for ct in range(nd):
                wq = b.wq_st[ct % 2]
                if len(b.at_wqmark) >= 2:
                    C.G.wait(sm["mm"], b.at_wqmark[-2])
                C.G.dma_start(out=wq[:, 0:8 * 128],
                              in_=_ap3(dr["wq"], [[D, 128], [128 * D, nd], [1, 128]],
                                       offset=ct * 128)).inc(sm["gld"], 16)
                v_wq = s.val(sm["gld"])
                pq = b.pE
                C.T.wait(sm["ve"], s.val(sm["ve"]))
                C.T.wait(sm["gld"], v_wq)
                if ct == 0:
                    C.T.wait(sm["ve"], v_tls)
                for j in range(nd):
                    C.T.matmul(pq[:, 0:TC], wq[:, j * 128:(j + 1) * 128],
                               b.xn[:, j * TC:(j + 1) * TC], start=(j == 0), stop=False)
                mmQ = C.T.matmul(pq[:, 0:TC], b.lbB[:, ct * 128:(ct + 1) * 128],
                                 b.tmplB[:, 0:TC], start=False, stop=True)
                mmQ.inc(sm["mm"], 1)
                b.at_wqmark.append(s.val(sm["mm"]))
                v_q = s.val(sm["mm"])
                _rope(C, b, sm, pq, b.cosB[:, 0:TC], b.sinB[:, 0:TC], b.rpbB, b.rkbB,
                      b.aqb[:, ct * TC:(ct + 1) * TC], v_q)
                C.V.drain()
                C.V.tensor_tensor(out=b.aqb[:, ct * TC:(ct + 1) * TC],
                                  in0=b.aqb[:, ct * TC:(ct + 1) * TC], in1=b.qpaB[:, 0:TC],
                                  op=ALU.mult)
                C.V.drain().inc(sm["ve"], 1)
            v_aq = s.val(sm["ve"])

            for h in range(NH):
                ct, par = h // 2, h % 2
                rb = par * HD
                if par == 0:
                    C.V.wait(sm["mm"], s.val(sm["mm"]))
                    _cdma_taps(C, b, sm, b.cdmaB, b.kcB[:, ct * TC:(ct + 1) * TC],
                               b.cpkt[:, 0:TC], True)
                    C.V.drain()
                    _cdma_taps(C, b, sm, b.cdmaB, b.vcB[:, ct * TC:(ct + 1) * TC],
                               b.cpvt[:, 0:TC], True)
                    C.V.drain().inc(sm["ve"], 1)
                v_cp = s.val(sm["ve"])
                pS = [b.pA, b.pB, b.pC, b.pD]
                C.T.wait(sm["ve"], v_aq)
                C.T.wait(sm["ac"], s.val(sm["ac"]))
                C.T.wait(sm["ve"], s.val(sm["ve"]))
                for st_ in range(qc + 1):
                    mmS = C.T.matmul(pS[st_][0:128, 0:TC],
                                     b.fullk[rb:rb + HD, ct * NSLOT + st_ * 128:ct * NSLOT + (st_ + 1) * 128],
                                     b.aqb[rb:rb + HD, ct * TC:(ct + 1) * TC],
                                     start=True, stop=True)
                    if st_ == qc:
                        mmS.inc(sm["mm"], 1)
                C.V.wait(sm["ve"], v_cp)
                C.V.wait(sm["mm"], s.val(sm["mm"]))
                C.V.tensor_tensor(out=b.qcph[rb:rb + HD, 0:TC],
                                  in0=b.aqb[rb:rb + HD, ct * TC:(ct + 1) * TC],
                                  in1=b.cpkt[rb:rb + HD, 0:TC], op=ALU.mult)
                C.V.drain().inc(sm["ve"], 1)
                v_qcp = s.val(sm["ve"])
                C.T.wait(sm["ve"], v_qcp)
                C.T.matmul(b.pG2[0:1, 0:TC], b.ones_bf[rb:rb + HD, 0:1],
                           b.qcph[rb:rb + HD, 0:TC], start=True, stop=True).inc(sm["mm"], 1)
                v_sc = s.val(sm["mm"])
                C.S.wait(sm["mm"], v_sc)
                C.S.wait(sm["ve"], s.val(sm["ve"]))
                for st_ in range(qc + 1):
                    C.S.activation(out=b.ebuf[:, st_ * TC:(st_ + 1) * TC],
                                   in_=pS[st_][0:128, 0:TC], func=AF.Exp).inc(sm["ac"], 1)
                C.S.activation(out=b.wcur[0:1, 0:TC], in_=b.pG2[0:1, 0:TC],
                               func=AF.Exp).inc(sm["ac"], 1)
                v_e = s.val(sm["ac"])
                C.V.wait(sm["ac"], v_e)
                C.V.tensor_tensor(out=b.ebuf[:, qc * TC:(qc + 1) * TC],
                                  in0=b.ebuf[:, qc * TC:(qc + 1) * TC],
                                  in1=b.bandB[:, 0:TC], op=ALU.mult)
                C.V.drain().inc(sm["ve"], 1)
                v_eb = s.val(sm["ve"])
                C.T.wait(sm["ve"], v_eb)
                for st_ in range(qc + 1):
                    col = (st_ * NH + h) * (HD + 1)
                    mmA = C.T.matmul(b.pE[0:HD + 1, 0:TC], b.vt[:, col:col + HD + 1],
                                     b.ebuf[:, st_ * TC:(st_ + 1) * TC],
                                     start=(st_ == 0), stop=(st_ == qc))
                    if st_ == qc:
                        mmA.inc(sm["mm"], 1)
                v_av = s.val(sm["mm"])
                C.V.wait(sm["mm"], v_av)
                C.V.tensor_copy(out=b.zrow1[0:1, 0:TC], in_=b.pE[HD:HD + 1, 0:TC])
                C.V.drain()
                C.V.tensor_tensor(out=b.zinv[0:1, 0:TC], in0=b.zrow1[0:1, 0:TC],
                                  in1=b.wcur[0:1, 0:TC], op=ALU.add)
                C.V.drain()
                C.V.reciprocal(out=b.zinv[0:1, 0:TC], in_=b.zinv[0:1, 0:TC])
                C.V.drain().inc(sm["ve"], 1)
                v_z = s.val(sm["ve"])
                C.T.wait(sm["ve"], v_z)
                C.T.matmul(b.pG2[0:HD, 0:TC], b.onesrow_f[0:1, 0:HD], b.zinv[0:1, 0:TC],
                           start=True, stop=True)
                C.T.matmul(b.pG2[HD:128, 0:TC], b.onesrow_f[0:1, 0:HD], b.wcur[0:1, 0:TC],
                           start=True, stop=True).inc(sm["mm"], 1)
                v_wm = s.val(sm["mm"])
                C.V.wait(sm["mm"], v_wm)
                C.V.tensor_tensor(out=b.corr[rb:rb + HD, 0:TC], in0=b.pG2[HD:128, 0:TC],
                                  in1=b.cpvt[rb:rb + HD, 0:TC], op=ALU.mult)
                C.V.drain()
                C.V.tensor_tensor(out=b.corr[rb:rb + HD, 0:TC], in0=b.corr[rb:rb + HD, 0:TC],
                                  in1=b.pE[0:HD, 0:TC], op=ALU.add)
                C.V.drain()
                C.V.tensor_tensor(out=b.corr[rb:rb + HD, 0:TC], in0=b.corr[rb:rb + HD, 0:TC],
                                  in1=b.pG2[0:HD, 0:TC], op=ALU.mult)
                C.V.drain()
                C.V.tensor_tensor(out=b.o2[rb:rb + HD, ct * TC:(ct + 1) * TC],
                                  in0=b.corr[rb:rb + HD, 0:TC],
                                  in1=b.qpoB[rb:rb + HD, 0:TC], op=ALU.mult)
                C.V.drain().inc(sm["ve"], 1)
            v_o2 = s.val(sm["ve"])
            b.at_kvfree[cg] = v_o2

            C.T.wait(sm["ve"], v_o2)
            for j in range(nd):
                mmo = C.T.matmul(b.pF[0:LORA, 0:TC], b.lc[:, j * LORA:(j + 1) * LORA],
                                 b.o2[:, j * TC:(j + 1) * TC], start=(j == 0), stop=(j == nd - 1))
                if j == nd - 1:
                    mmo.inc(sm["mm"], 1)
            C.V.wait(sm["mm"], s.val(sm["mm"]))
            C.V.tensor_copy(out=b.tmplB[:, 0:TC], in_=b.pF[0:LORA, 0:TC]).inc(sm["ve"], 1)
            v_t2 = s.val(sm["ve"])
            pgs = [b.pA, b.pB, b.pC, b.pD]
            for dt_ in range(nd):
                wo = b.wo_st[dt_ % 2]
                if len(b.at_womark) >= 2:
                    C.G.wait(sm["mm"], b.at_womark[-2])
                C.G.dma_start(out=wo[:, 0:8 * 128],
                              in_=_ap3(dr["wo"], [[D, 128], [128 * D, nd], [1, 128]],
                                       offset=dt_ * 128)).inc(sm["gld"], 16)
                v_wo = s.val(sm["gld"])
                bank = pgs[dt_ % 4]
                C.T.wait(sm["ve"], s.val(sm["ve"]))
                C.T.wait(sm["gld"], v_wo)
                if dt_ == 0:
                    C.T.wait(sm["ve"], v_t2)
                for j in range(nd):
                    C.T.matmul(bank[:, 0:TC], wo[:, j * 128:(j + 1) * 128],
                               b.o2[:, j * TC:(j + 1) * TC], start=(j == 0), stop=False)
                mmY = C.T.matmul(bank[:, 0:TC], b.ld_[:, dt_ * 128:(dt_ + 1) * 128],
                                 b.tmplB[:, 0:TC], start=False, stop=True)
                mmY.inc(sm["mm"], 1)
                b.at_womark.append(s.val(sm["mm"]))
                v_y = s.val(sm["mm"])
                C.V.wait(sm["mm"], v_y)
                C.V.scalar_tensor_tensor(out=b.rout[:, dt_ * TC:(dt_ + 1) * TC],
                                         in0=bank[:, 0:TC], scalar=1.0,
                                         in1=xch[:, dt_ * TC:(dt_ + 1) * TC],
                                         op0=ALU.mult, op1=ALU.add).inc(sm["ve"], 1)
            b.at_cons[cg] = s.val(sm["ve"])
            b.at_tabfree[("b", cg)] = s.val(sm["ve"])
            C.Y.wait(sm["ve"], b.at_cons[cg])
            C.Y.dma_start(out=_ap3(dst, [[T, 128], [128 * T, nd], [1, TC]], offset=cg * TC),
                          in_=b.rout[:, 0:nd * TC]).inc(sm["st"], 16)
            b.at_stored[("r", cg)] = s.val(sm["st"])
        C.barrier(sm["bar"])
    C.Y.wait(sm["st"], b.at_stored[("r", B_LOC * 4 - 1)])


def build_program(phases=("mamba", "ffn1", "attn", "ffn2"), debug_taps=()):
    import os as _os2
    nc = bass.Bass("TRN2", target_bir_lowering=False, detect_race_conditions=False,
                   enable_asserts=not _os2.environ.get("DEV_NOASSERTOFF"))
    C = Ctx(nc)
    dr = {}

    def inp(name, shape, dt=F32):
        dr[name] = nc.dram_tensor(name, list(shape), dt, kind="ExternalInput").ap()

    def scr(name, shape, dt):
        dr[name] = nc.dram_tensor(name, list(shape), dt, kind="Internal").ap()

    inp("x0", [D, T], F32)
    dr["out"] = nc.dram_tensor("out", [D, T], F32, kind="ExternalOutput").ap()
    inp("wfcT", [D, FF]); inp("wprojT", [FF, D])
    inp("consts", [128, 16])
    inp("ones_bf", [128, 1], BF16)
    inp("onesrow_f", [1, 128], F32)
    inp("onesrow_bf", [1, 512], BF16)
    inp("ident", [128, 128], F32)
    inp("triu01", [128, 128], F32)
    inp("winT", [D, PROJ])
    inp("b_in", [1, PROJ])
    inp("conv_w_t", [128, NJT * MK])
    inp("dtb_rep", [128, MH])
    inp("a_log", [MH, 1])
    inp("d_row", [1, INTER])
    inp("normw_row", [1, INTER])
    inp("woutT", [INTER, D])
    inp("b_out", [1, D])
    for w in ("wq", "wk", "wv", "wo"):
        inp(w, [D, D])
    inp("aq", [D, LORA]); inp("bq", [LORA, D])
    inp("av", [D, LORA]); inp("bv", [LORA, D])
    inp("ao", [D, LORA]); inp("bo", [LORA, D])
    inp("cos_rep", [128, S])
    inp("ident2", [128, HD])
    inp("sin_rep", [128, S])
    inp("qpa", [128, S])
    inp("qpo", [128, S])
    inp("cdma_half", [128, MK])
    inp("band01", [128, TC])
    scr("k_scr", [D, T], BF16)
    scr("v_scr", [D, T], BF16)
    scr("wproj_bf", [FF, D], BF16)
    scr("resid1", [D, T], F32)
    scr("resid2", [D, T], F32)
    scr("resid3", [D, T], F32)
    scr("h_scr", [INTER, T], BF16)
    scr("b_scr", [MN, T], BF16)
    scr("c_scr", [MN, T], BF16)
    scr("gate_scr", [T, INTER], BF16)
    scr("g_scr", [T, INTER], BF16)
    scr("dt_scr", [T, MH], F32)
    scr("acum_scr", [MH, T], F32)

    dbg = {}
    for name, shape, dt in debug_taps:
        dbg[name] = nc.dram_tensor(name, list(shape), dt, kind="ExternalOutput").ap()

    from contextlib import ExitStack
    with ExitStack() as es:
        sems = {}
        for name in ("ld", "gld", "sld", "x", "mm", "ve", "ac", "gp", "st", "tr", "bar"):
            sems[name] = es.enter_context(nc.semaphore(name))

        pa_shared = PA(nc)
        b = alloc_shared_bufs(pa_shared)
        pa_ffn = PA(nc)
        alloc_ffn_bufs(pa_ffn, b)
        pa_ffn.close()
        pa_ma = PA(nc)
        alloc_mamba_a_bufs(pa_ma, b)
        pa_ma.close()
        pa_mb = PA(nc)
        alloc_mamba_b_bufs(pa_mb, b)
        pa_mb.close()
        pa_mc = PA(nc)
        alloc_mamba_c_bufs(pa_mc, b)
        pa_mc.close()
        pa_atp = PA(nc)
        alloc_attn_persist(pa_atp, b)
        pa_at_a = PA(nc)
        alloc_attn_a(pa_at_a, b)
        pa_at_a.close()
        pa_at_b = PA(nc)
        alloc_attn_b(pa_at_b, b)
        pa_at_b.close()
        pa_atp.close()
        pa_shared.close()

        block = es.enter_context(nc.Block())

        scratch_of = {"mamba": "resid1", "ffn1": "resid2", "attn": "resid3",
                      "ffn2": "resid1", "mamba_a": "resid1", "mamba_ab": "resid1"}
        cur = "x0"
        plan = []
        for i, ph in enumerate(phases):
            dst = "out" if i == len(phases) - 1 else scratch_of[ph]
            plan.append((ph, cur, dst))
            cur = dst

        def run(engine_name):
            def f(handle):
                C.active = engine_name
                C.handle = handle
                C.s = Sched()
                b.ffn_wpmarks = []
                b.ffn_cons = {}
                b.ffn_stored = {}
                b.ma_cons = {}
                b.ma_stored = {}
                b.mb_cons = {}
                b.mb_stored = 0
                b.mb_difdone = 0
                b.mc_cons = {}
                b.mc_stored = {}
                b.mc_mmdone = {}
                b.at_cons = {}
                b.at_stored = {}
                b.at_kvfree = {}
                b.at_wqmark = []
                b.at_womark = []
                b.at_tabfree = {}
                emit_const_loads(C, dr, b, sems)
                first_ffn = True
                for ph, src, dst in plan:
                    if ph in ("ffn1", "ffn2"):
                        emit_ffn_phase(C, dr, b, sems, dr[src], dr[dst], first_ffn, dbg)
                        first_ffn = False
                    elif ph == "mamba":
                        emit_mamba_a(C, dr, b, sems, dr[src])
                        C.barrier(sems["bar"])
                        emit_mamba_b(C, dr, b, sems, dbg)
                        C.barrier(sems["bar"])
                        emit_mamba_c(C, dr, b, sems, dr[src], dr[dst])
                    elif ph == "mamba_a":
                        emit_mamba_a(C, dr, b, sems, dr[src])
                        C.barrier(sems["bar"])
                        C.G.dma_start(out=_ap3(dr["out"], [[T, 32], [1, T]]),
                                      in_=_ap3(dr["acum_scr"], [[T, 32], [1, T]])).inc(sems["gld"], 16)
                        C.G.dma_start(out=_ap3(dr["out"], [[T, 128], [1, T]], offset=32 * T),
                                      in_=_ap3(dr["b_scr"], [[T, 128], [1, T]])).inc(sems["gld"], 16)
                        C.G.dma_start(out=_ap3(dr["out"], [[T, 128], [1, T]], offset=160 * T),
                                      in_=_ap3(dr["c_scr"], [[T, 128], [1, T]])).inc(sems["gld"], 16)
                        C.G.dma_start(out=_ap3(dr["out"], [[T, 512], [1, T]], offset=288 * T),
                                      in_=_ap3(dr["h_scr"], [[T, 512], [1, T]])).inc(sems["gld"], 16)
                        C.G.dma_start(out=_ap3(dr["out"], [[T, 96], [1, T]], offset=800 * T),
                                      in_=_ap3(dr["h_scr"], [[T, 96], [1, T]], offset=1024 * T)).inc(sems["gld"], 16)
                        C.G.dma_start(out=_ap3(dr["out"], [[T, 32], [1, T]], offset=896 * T),
                                      in_=_ap3(dr["dt_scr"], [[T, 32], [1, T]])).inc(sems["gld"], 16)
                        C.G.wait(sems["gld"], C.s.val(sems["gld"]))
                    elif ph == "mamba_ab":
                        emit_mamba_a(C, dr, b, sems, dr[src])
                        C.barrier(sems["bar"])
                        emit_mamba_b(C, dr, b, sems, dbg)
                        C.barrier(sems["bar"])
                        C.G.dma_start(out=_ap3(dr["out"], [[T, 512], [1, T]]),
                                      in_=_ap3(dr["g_scr"], [[T, 512], [1, T]])).inc(sems["gld"], 16)
                        C.G.wait(sems["gld"], C.s.val(sems["gld"]))
                    elif ph == "attn":
                        emit_attn(C, dr, b, sems, dr[src], dr[dst])
                    C.barrier(sems["bar"])
            return f

        block.sync(run("sync"))
        block.vector(run("vector"))
        block.scalar(run("scalar"))
        block.tensor(run("tensor"))
        block.gpsimd(run("gpsimd"))

    return nc, dbg


def _host_consts():
    inv_freq = 1.0 / (10000.0 ** (np.arange(0, HD, 2, dtype=np.float64) / HD))
    pos = np.arange(S, dtype=np.float64)
    freqs = pos[:, None] * inv_freq[None, :]
    emb = np.concatenate([freqs, freqs], axis=-1)
    cos = np.cos(emb).astype(np.float32).T
    sin = np.sin(emb).astype(np.float32).T
    HAD = np.array([[1, 1, 1, 1], [1, -1, 1, -1], [1, 1, -1, -1], [1, -1, -1, 1]],
                   dtype=np.float32)
    CDMA = np.tile(HAD, (1, HD // 4))
    qphase = CDMA[np.arange(S) % KF]
    consts = np.zeros((128, 16), dtype=np.float32)
    consts[:, 0] = EPS
    for i, bp in enumerate(SPIKE_BPS):
        consts[:, 1 + i] = -np.float32(bp)
    consts[:, 6] = np.float32(SPIKE_BPS[3])
    consts[:, 7] = np.float32(SPIKE_BPS[4])
    consts[:, 8] = 1.0
    consts[:, 9] = -1.0
    return dict(
        cos_rep=np.ascontiguousarray(np.tile(cos, (2, 1))),
        sin_rep=np.ascontiguousarray(np.tile(sin, (2, 1))),
        qpa=np.ascontiguousarray(np.tile(qphase.T, (2, 1)) * 0.125).astype(np.float32),
        qpo=np.ascontiguousarray(np.tile(qphase.T, (2, 1)) * 2.0).astype(np.float32),
        cdma_half=np.ascontiguousarray(np.tile(CDMA.T, (2, 1)) * 0.5).astype(np.float32),
        band01=(np.arange(128)[:, None] < (np.arange(TC) // KF)[None, :]).astype(np.float32),
        triu01=(np.arange(128)[:, None] <= np.arange(128)[None, :]).astype(np.float32),
        ident=np.eye(128, dtype=np.float32),
        ident2=np.ascontiguousarray(np.tile(np.eye(HD, dtype=np.float32), (2, 1))),
        consts=consts,
        ones_bf=np.ones((128, 1), ml_dtypes.bfloat16),
        onesrow_f=np.ones((1, 128), np.float32),
        onesrow_bf=np.ones((1, 128), ml_dtypes.bfloat16),
    )


def _core_in_map(inputs, consts, core):
    x = inputs["L0_attn_a_x"][2 * core:2 * core + 2]
    xT = np.ascontiguousarray(x.reshape(T, D).T)
    m = {
        "x0": xT,
        "wfcT": consts["_wfcT"],
        "wprojT": consts["_wprojT"],
        "consts": consts["consts"],
        "ones_bf": consts["ones_bf"],
        "onesrow_f": consts["onesrow_f"],
        "onesrow_bf": consts["_onesrow_bf512"],
        "ident": consts["ident"],
        "ident2": consts["ident2"],
        "triu01": consts["triu01"],
        "winT": consts["_winT"],
        "b_in": consts["_b_in"],
        "conv_w_t": consts["_conv_w_t"],
        "dtb_rep": consts["_dtb_rep"],
        "a_log": consts["_a_log"],
        "d_row": consts["_d_row"],
        "normw_row": consts["_normw_row"],
        "woutT": consts["_woutT"],
        "b_out": consts["_b_out"],
        "wq": inputs["L0_attn_b_Wq"], "wk": inputs["L0_attn_b_Wk"],
        "wv": inputs["L0_attn_b_Wv"], "wo": inputs["L0_attn_b_Wo"],
        "aq": inputs["L0_attn_b_Aq"], "bq": inputs["L0_attn_b_Bq"],
        "av": inputs["L0_attn_b_Av"], "bv": inputs["L0_attn_b_Bv"],
        "ao": inputs["L0_attn_b_Ao"], "bo": inputs["L0_attn_b_Bo"],
        "cos_rep": consts["cos_rep"], "sin_rep": consts["sin_rep"],
        "qpa": consts["qpa"], "qpo": consts["qpo"],
        "cdma_half": consts["cdma_half"], "band01": consts["band01"],
    }
    return m


def kernel(**inputs):
    c = _host_consts()
    c["_wfcT"] = np.ascontiguousarray(inputs["L0_ffn_W_fc"].T)
    c["_wprojT"] = np.ascontiguousarray(inputs["L0_ffn_W_proj"].T)
    c["_winT"] = np.ascontiguousarray(inputs["L0_attn_a_W_in"].T)
    c["_b_in"] = np.ascontiguousarray(inputs["L0_attn_a_b_in"][None, :])
    c["_conv_w_t"] = np.ascontiguousarray(
        inputs["L0_attn_a_conv_w"].reshape(NJT, 128, MK).transpose(1, 0, 2).reshape(128, NJT * MK))
    c["_dtb_rep"] = np.ascontiguousarray(
        np.broadcast_to(inputs["L0_attn_a_dt_bias"][None, :], (128, MH)))
    c["_a_log"] = np.ascontiguousarray(inputs["L0_attn_a_A_log"][:, None])
    c["_d_row"] = np.ascontiguousarray(np.repeat(inputs["L0_attn_a_D"], MP))[None, :]
    c["_normw_row"] = np.ascontiguousarray(inputs["L0_attn_a_norm_w"][None, :])
    c["_woutT"] = np.ascontiguousarray(inputs["L0_attn_a_W_out"].T)
    c["_b_out"] = np.ascontiguousarray(inputs["L0_attn_a_b_out"][None, :])
    c["_onesrow_bf512"] = np.ones((1, 512), ml_dtypes.bfloat16)

    nc, _ = build_program(phases=("mamba", "ffn1", "attn", "ffn2"))
    in_maps = [_core_in_map(inputs, c, core) for core in range(NCORES)]
    res = run_bass_kernel_spmd(nc, in_maps, core_ids=list(range(NCORES)))
    outs = []
    for r in res.results:
        o = r["out"]
        outs.append(np.ascontiguousarray(o.T).reshape(B_LOC, S, D))
    return np.concatenate(outs, axis=0).astype(np.float32)


# version 03
# speedup vs baseline: 1.1909x; 1.1909x over previous
import numpy as np
import ml_dtypes
import concourse.bass as bass
import concourse.mybir as mybir
from concourse.bass_utils import run_bass_kernel_spmd

F32 = mybir.dt.float32
F32R = mybir.dt.float32r
BF16 = mybir.dt.bfloat16
AF = mybir.ActivationFunctionType
ALU = mybir.AluOpType

NCORES = 8
B_TOT, S, D = 16, 2048, 1024
B_LOC = B_TOT // NCORES
T = B_LOC * S
TC = 512
NCH = T // TC
MH, MP, MN, MK = 32, 64, 128, 4
INTER = MH * MP
CONV = INTER + 2 * MN
PROJ = INTER + CONV + MH
CS = 128
EPS = 1e-5
FF = 4096
NH, HD, KF = 16, 64, 4
NSLOT = S // KF
LORA = 64

ENGINES = ("sync", "vector", "scalar", "tensor", "gpsimd")


def _spike_breakpoints():
    BETA, TH, NS = 0.9, 1.0, 5

    def out_of(y):
        mem = spk = acc = 0.0
        for _ in range(NS):
            mem = BETA * mem + y - spk * TH
            spk = 1.0 if mem > TH else 0.0
            acc += spk
        return acc / NS

    bps, lo = [], -1.0
    for k in range(NS):
        target = (k + 1) / NS
        a, b = lo, 8.0
        for _ in range(200):
            m = 0.5 * (a + b)
            if out_of(m) >= target - 1e-9:
                b = m
            else:
                a = m
        bps.append(0.5 * (a + b))
        lo = bps[-1]
    return bps


SPIKE_BPS = _spike_breakpoints()


class Sched:
    def __init__(self):
        self.c = {}

    def inc(self, sem, n):
        k = sem.name
        self.c[k] = self.c.get(k, 0) + n
        return self.c[k]

    def val(self, sem):
        return self.c.get(sem.name, 0)


class _Instr:
    def __init__(self, ctx, inst, fence):
        self.ctx = ctx
        self.inst = inst
        self.fence = fence

    def inc(self, sem, n=1, fence=None):
        self.ctx.s.inc(sem, n)
        do_fence = self.fence if fence is None else (fence and self.fence)
        if self.inst is not None:
            if do_fence:
                self.ctx.handle.drain().then_inc(sem, n)
            else:
                self.inst.then_inc(sem, n)
        return self


import os as _os
_FENCE_ENGINES = () if _os.environ.get("DEV_NOFENCE") else ("vector", "scalar", "gpsimd")


class Eng:

    def __init__(self, ctx, name):
        self._ctx = ctx
        self._name = name

    @property
    def live(self):
        return self._ctx.active == self._name

    def __getattr__(self, op):
        ctx = self._ctx
        name = self._name

        def call(*args, **kwargs):
            fence = name in _FENCE_ENGINES and not op.startswith("dma")
            if self.live:
                return _Instr(ctx, getattr(ctx.handle, op)(*args, **kwargs), fence)
            return _Instr(ctx, None, fence)

        return call

    def wait(self, sem, val=None):
        if val is None:
            val = self._ctx.s.val(sem)
        if self.live:
            self._ctx.handle.wait_ge(sem, val)

    def dmat(self, out, in_):
        if self.live:
            return _Instr(self._ctx, self._ctx.handle.dma_start_transpose(out=out, in_=in_), False)
        return _Instr(self._ctx, None, False)


class Ctx:
    def __init__(self, nc):
        self.nc = nc
        self.s = None
        self.active = None
        self.handle = None
        self.Y = Eng(self, "sync")
        self.V = Eng(self, "vector")
        self.S = Eng(self, "scalar")
        self.T = Eng(self, "tensor")
        self.G = Eng(self, "gpsimd")

    def barrier(self, bar_sem):
        for e in (self.Y, self.V, self.S, self.T, self.G):
            e.nop().inc(bar_sem, 1)
        tgt = self.s.val(bar_sem)
        for e in (self.Y, self.V, self.S, self.T, self.G):
            e.wait(bar_sem, tgt)


class PA:

    def __init__(self, nc):
        from contextlib import ExitStack
        self.nc = nc
        self.es = ExitStack()

    def sb(self, name, shape, dt):
        return self.es.enter_context(self.nc.sbuf_tensor(name, list(shape), dt))

    def ps(self, name, shape, dt=F32):
        return self.es.enter_context(self.nc.psum_tensor(name, list(shape), dt))

    def close(self):
        self.es.close()


def _ap3(t, ap_dims, offset=0):
    base = t if hasattr(t, "tensor") else t.ap()
    return bass.AP(tensor=base.tensor, offset=base.offset + offset, ap=list(ap_dims))


def emit_ln_chunk(C, b, sm, xch, tc):
    s = C.s
    nd = D // 128
    C.T.wait(sm["ve"], s.val(sm["ve"]))
    C.V.wait(sm["mm"], s.val(sm["mm"]))
    C.V.tensor_copy(out=b.xbf[:, 0:nd * tc], in_=xch[:, 0:nd * tc])
    C.V.tensor_mul(out=b.xsqbf[:, 0:nd * tc], in0=b.xbf[:, 0:nd * tc],
                   in1=b.xbf[:, 0:nd * tc]).inc(sm["ve"], 1)
    v_sq = s.val(sm["ve"])
    C.T.wait(sm["ve"], v_sq)
    for j in range(nd):
        C.T.matmul(b.pA[0:1, 0:tc], b.ones_bf[:, 0:1], b.xbf[:, j * tc:(j + 1) * tc],
                   start=(j == 0), stop=(j == nd - 1))
    for j in range(nd):
        mm = C.T.matmul(b.pB[0:1, 0:tc], b.ones_bf[:, 0:1], b.xsqbf[:, j * tc:(j + 1) * tc],
                        start=(j == 0), stop=(j == nd - 1))
        if j == nd - 1:
            mm.inc(sm["mm"], 1)
    v_stats = s.val(sm["mm"])
    C.V.wait(sm["mm"], v_stats)
    r_mean = b.lnrow[0:1, 0 * TC:0 * TC + tc]
    r_ms = b.lnrow[0:1, 1 * TC:1 * TC + tc]
    r_m2 = b.lnrow[0:1, 2 * TC:2 * TC + tc]
    r_var = r_ms
    r_lnv = r_m2
    C.V.tensor_scalar_mul(out=r_mean, in0=b.pA[0:1, 0:tc], scalar1=1.0 / D)
    C.V.tensor_scalar_mul(out=r_ms, in0=b.pB[0:1, 0:tc], scalar1=1.0 / D)
    C.V.drain()
    C.V.tensor_mul(out=r_m2, in0=r_mean, in1=r_mean)
    C.V.tensor_sub(out=r_var, in0=r_ms, in1=r_m2).inc(sm["ve"], 1)
    v_var = s.val(sm["ve"])
    C.S.wait(sm["ve"], v_var)
    C.S.activation(out=r_lnv, in_=r_var, func=AF.Ln, bias=b.cpool[0:1, 0:1]).inc(sm["ac"], 1)
    C.S.wait(sm["ac"], s.val(sm["ac"]))
    C.S.activation(out=b.lnrow_r[0:1, 0:tc], in_=r_lnv, func=AF.Exp, scale=-0.5).inc(sm["ac"], 1)
    v_r = s.val(sm["ac"])
    C.V.wait(sm["ac"], v_r)
    C.V.tensor_mul(out=b.lnrow_r2[0:1, 0:tc], in0=r_mean, in1=b.lnrow_r[0:1, 0:tc]).inc(sm["ve"], 1)
    v_mr = s.val(sm["ve"])
    C.T.wait(sm["ac"], v_r)
    C.T.matmul(b.pC[:, 0:tc], b.onesrow_f[0:1, :], b.lnrow_r[0:1, 0:tc], start=True, stop=True)
    C.T.wait(sm["ve"], v_mr)
    C.T.matmul(b.pD[:, 0:tc], b.onesrow_f[0:1, :], b.lnrow_r2[0:1, 0:tc],
               start=True, stop=True).inc(sm["mm"], 1)
    v_out = s.val(sm["mm"])
    C.V.wait(sm["mm"], v_out)
    pR_rep = _ap3(b.pC, [[b.pC.ap().ap[0][0], 128], [0, nd], [1, tc]])
    pMR_rep = _ap3(b.pD, [[b.pD.ap().ap[0][0], 128], [0, nd], [1, tc]])
    C.V.tensor_tensor(out=b.scrA[:, 0:nd * tc], in0=xch[:, 0:nd * tc], in1=pR_rep, op=ALU.mult)
    C.V.tensor_tensor(out=b.xn[:, 0:nd * tc], in0=b.scrA[:, 0:nd * tc], in1=pMR_rep,
                      op=ALU.subtract).inc(sm["ve"], 1)
    return s.val(sm["ve"])


class NS:
    pass


DEV_FFN_LEVEL = 3
DEV_FFN_LEVEL2 = None


def emit_ffn_phase(C, dr, bufs, sm, src, dst, first_ffn, dbg=None):
    s = C.s
    b = bufs
    nd = D // 128
    NF = FF // 128
    LV = DEV_FFN_LEVEL if (first_ffn or DEV_FFN_LEVEL2 is None) else DEV_FFN_LEVEL2
    if LV == 3:
        LV = 99

    for j in range(nd):
        src_ap = _ap3(dr["wfcT"], [[FF, 128], [1, FF]], offset=j * 128 * FF)
        C.G.dma_start(out=b.wfc[:, j * FF:(j + 1) * FF], in_=src_ap).inc(sm["gld"], 16)
    if first_ffn:
        for i in range(NF):
            src_ap = _ap3(dr["wprojT"], [[D, 128], [1, D]], offset=i * 128 * D)
            C.G.dma_start(out=b.wpst[:, 0:D], in_=src_ap).inc(sm["gld"], 16)
            v = s.val(sm["gld"])
            C.G.wait(sm["gld"], v)
            dst_ap = _ap3(dr["wproj_bf"], [[D, 128], [1, D]], offset=i * 128 * D)
            C.G.dma_start(out=dst_ap, in_=b.wpst[:, 0:D]).inc(sm["gld"], 16)
    v_wload = s.val(sm["gld"])

    for c in range(NCH):
        cb = c % 2
        xch = b.xch[:, cb * nd * TC:(cb + 1) * nd * TC]
        if c >= 2:
            C.Y.wait(sm["ve"], b.ffn_cons[c - 2])
        src_ap = _ap3(src, [[T, 128], [128 * T, nd], [1, TC]], offset=c * TC)
        C.Y.dma_start(out=xch, in_=src_ap).inc(sm["x"], 16)
        v_x = s.val(sm["x"])
        C.S.wait(sm["x"], v_x)
        C.T.wait(sm["x"], v_x)
        C.V.wait(sm["x"], v_x)

        v_xn = emit_ln_chunk(C, b, sm, xch, TC)
        C.T.wait(sm["ve"], v_xn)
        if c == 0:
            C.T.wait(sm["gld"], v_wload)

        for fi in range(NF if LV >= 1 else 0):
            pf = b.pE if fi % 2 == 0 else b.pF
            if fi >= 2:
                C.T.wait(sm["ac"], s.val(sm["ac"]))
            for j in range(nd):
                mm = C.T.matmul(pf[:, 0:TC], b.wfc[:, j * FF + fi * 128:j * FF + (fi + 1) * 128],
                                b.xn[:, j * TC:(j + 1) * TC], start=(j == 0), stop=(j == nd - 1))
                if j == nd - 1:
                    mm.inc(sm["mm"], 1)
            C.S.wait(sm["mm"], s.val(sm["mm"]))
            C.S.wait(sm["ve"], s.val(sm["ve"]))
            C.S.activation(out=b.uff[:, (fi % 2) * TC:(fi % 2 + 1) * TC],
                           in_=pf[:, 0:TC], func=AF.Relu).inc(sm["ac"], 1)
            C.V.wait(sm["ac"], s.val(sm["ac"]))
            uslice = b.uff[:, (fi % 2) * TC:(fi % 2 + 1) * TC]
            C.V.tensor_mul(out=b.hbuf[:, fi * TC:(fi + 1) * TC], in0=uslice, in1=uslice).inc(sm["ve"], 1)
        v_h = s.val(sm["ve"])

        if c >= 1:
            C.V.wait(sm["st"], b.ffn_stored[c - 1])
        if LV < 2:
            C.V.tensor_copy(out=b.rout[:, 0:nd * TC], in_=xch).inc(sm["ve"], 1)
            b.ffn_cons[c] = s.val(sm["ve"])
            C.Y.wait(sm["ve"], b.ffn_cons[c])
            dst_ap = _ap3(dst, [[T, 128], [128 * T, nd], [1, TC]], offset=c * TC)
            C.Y.dma_start(out=dst_ap, in_=b.rout[:, 0:nd * TC]).inc(sm["st"], 16)
            b.ffn_stored[c] = s.val(sm["st"])
            continue
        for g in range(2):
            C.T.wait(sm["ve"], s.val(sm["ve"]))
            for fi in range(NF):
                wb = b.wpb0 if fi % 2 == 0 else b.wpb1
                if len(b.ffn_wpmarks) >= 2:
                    C.Y.wait(sm["mm"], b.ffn_wpmarks[-2])
                if c == 0 and g == 0 and fi == 0:
                    C.Y.wait(sm["gld"], s.val(sm["gld"]))
                ld_ap = _ap3(dr["wproj_bf"], [[D, 128], [1, TC]],
                             offset=fi * 128 * D + g * TC)
                C.Y.dma_start(out=wb[:, 0:TC], in_=ld_ap).inc(sm["sld"], 16)
                v_w = s.val(sm["sld"])
                C.T.wait(sm["sld"], v_w)
                if fi == 0:
                    C.T.wait(sm["ve"], v_h)
                if LV != 15:
                    for jo in range(4):
                        mm = C.T.matmul(b.py[jo][:, 0:TC], wb[:, jo * 128:(jo + 1) * 128],
                                        b.hbuf[:, fi * TC:(fi + 1) * TC],
                                        start=(fi == 0), stop=(fi == NF - 1))
                        if jo == 3:
                            mm.inc(sm["mm"], 1)
                else:
                    C.T.nop().inc(sm["mm"], 1)
                b.ffn_wpmarks.append(s.val(sm["mm"]))
            v_y = s.val(sm["mm"])
            if LV in (15, 17):
                continue
            if LV == 20:
                for jo in range(4):
                    jd = g * 4 + jo
                    C.V.wait(sm["mm"], v_y)
                    C.V.tensor_copy(out=b.rout[:, jd * TC:(jd + 1) * TC],
                                    in_=xch[:, jd * TC:(jd + 1) * TC]).inc(sm["ve"], 1)
                continue
            if LV == 18:
                for jo in range(4):
                    jd = g * 4 + jo
                    C.S.wait(sm["mm"], v_y)
                    C.S.activation(out=b.rout[:, jd * TC:(jd + 1) * TC],
                                   in_=b.py[jo][:, 0:TC], func=AF.Identity).inc(sm["ac"], 1)
                if True:
                    C.V.wait(sm["ac"], s.val(sm["ac"]))
                    C.V.engine_nop().inc(sm["ve"], 1)
                continue
            if LV < 3:
                for jo in range(4):
                    jd = g * 4 + jo
                    C.V.wait(sm["mm"], v_y)
                    C.V.tensor_copy(out=b.rout[:, jd * TC:(jd + 1) * TC],
                                    in_=b.py[jo][:, 0:TC]).inc(sm["ve"], 1)
                continue
            for jo in range(4):
                jd = g * 4 + jo
                pyj = b.py[jo]
                if dbg is not None and "dbg_y" in dbg:
                    C.V.wait(sm["mm"], v_y)
                    C.V.tensor_copy(out=b.rout[:, jd * TC:(jd + 1) * TC],
                                    in_=pyj[:, 0:TC]).inc(sm["ve"], 1)
                    C.Y.wait(sm["ve"], s.val(sm["ve"]))
                    dba = _ap3(dbg["dbg_y"], [[T, 128], [1, TC]],
                               offset=(jd * 128) * T + c * TC)
                    C.Y.dma_start(out=dba, in_=b.rout[:, jd * TC:(jd + 1) * TC]).inc(sm["st"], 16)
                    C.V.wait(sm["st"], s.val(sm["st"]))
                C.S.wait(sm["mm"], v_y)
                C.S.wait(sm["ve"], s.val(sm["ve"]))
                if LV >= 25:
                    for k in range(3):
                        act = C.S.activation(out=b.sgn[k][:, 0:TC], in_=pyj[:, 0:TC], func=AF.Sign,
                                             bias=b.cpool[:, 1 + k:2 + k])
                        if k == 2:
                            act.inc(sm["ac"], 1)
                else:
                    C.S.nop().inc(sm["ac"], 1)
                v_sgn = s.val(sm["ac"])
                C.V.wait(sm["mm"], v_y)
                C.V.wait(sm["ac"], v_sgn)
                if LV >= 26:
                    C.V.tensor_single_scalar(out=b.ind4[:, 0:TC], in_=pyj[:, 0:TC],
                                             scalar=b.cpool[:, 6:7], op=ALU.is_gt)
                    C.V.tensor_single_scalar(out=b.ind5[:, 0:TC], in_=pyj[:, 0:TC],
                                             scalar=b.cpool[:, 7:8], op=ALU.is_gt)
                if LV >= 27:
                    C.V.tensor_add(out=b.sa, in0=b.sgn[0][:, 0:TC], in1=b.sgn[1][:, 0:TC])
                    C.V.tensor_add(out=b.sa, in0=b.sa, in1=b.sgn[2][:, 0:TC])
                    C.V.tensor_add(out=b.sb_, in0=b.ind4[:, 0:TC], in1=b.ind5[:, 0:TC])
                xsl = xch[:, jd * TC:(jd + 1) * TC]
                rsl = b.rout[:, jd * TC:(jd + 1) * TC]
                if LV >= 28:
                    C.V.scalar_tensor_tensor(out=rsl, in0=b.sa, scalar=0.1,
                                             in1=xsl, op0=ALU.mult, op1=ALU.add)
                    C.V.scalar_tensor_tensor(out=rsl, in0=b.sb_, scalar=0.2,
                                             in1=rsl, op0=ALU.mult, op1=ALU.add)
                    C.V.tensor_scalar_add(out=rsl, in0=rsl, scalar1=0.3).inc(sm["ve"], 1)
                else:
                    C.V.tensor_copy(out=rsl, in_=xsl).inc(sm["ve"], 1)
        if LV in (15, 17):
            C.V.wait(sm["mm"], s.val(sm["mm"]))
            C.V.tensor_copy(out=b.rout[:, 0:nd * TC], in_=xch).inc(sm["ve"], 1)
        b.ffn_cons[c] = s.val(sm["ve"])
        C.Y.wait(sm["ve"], b.ffn_cons[c])
        dst_ap = _ap3(dst, [[T, 128], [128 * T, nd], [1, TC]], offset=c * TC)
        C.Y.dma_start(out=dst_ap, in_=b.rout[:, 0:nd * TC]).inc(sm["st"], 16)
        b.ffn_stored[c] = s.val(sm["st"])
        if dbg is not None and c == 0:
            pass
    C.Y.wait(sm["st"], b.ffn_stored[NCH - 1])


def alloc_ffn_bufs(pa, shared):
    b = shared
    b.wfc = pa.sb("wfc", [128, 8 * FF], BF16)
    b.wpst = pa.sb("wpst", [128, D], BF16)
    b.wpb0 = pa.sb("wpb0", [128, TC], BF16)
    b.wpb1 = pa.sb("wpb1", [128, TC], BF16)
    b.hbuf = pa.sb("hbuf", [128, FF // 128 * TC], BF16)
    b.uff = pa.sb("uff", [128, 2 * TC], BF16)
    b.sgn = [pa.sb(f"sgn{k}", [128, TC], BF16) for k in range(3)]
    b.ind4 = pa.sb("ind4", [128, TC], BF16)
    b.ind5 = pa.sb("ind5", [128, TC], BF16)
    import os
    if os.environ.get("DEV_NOALIAS2"):
        b.sa = pa.sb("sa2", [128, TC], F32)[:, 0:TC]
        b.sb_ = pa.sb("sb2", [128, TC], F32)[:, 0:TC]
    else:
        b.sa = b.scrA[:, 0:TC]
        b.sb_ = b.scrA[:, TC:2 * TC]
    b.py = [b.pA, b.pB, b.pC, b.pD]
    b.ffn_wpmarks = []
    b.ffn_cons = {}
    b.ffn_stored = {}
    return b


def alloc_shared_bufs(pa):
    b = NS()
    nd = D // 128
    b.xch = pa.sb("xch", [128, 2 * nd * TC], F32)
    b.scrA = pa.sb("scrA", [128, nd * TC], F32)
    b.xbf = pa.sb("xbf", [128, nd * TC], BF16)
    b.xn = pa.sb("xn", [128, nd * TC], BF16)
    import os
    if os.environ.get("DEV_NOALIAS"):
        b.xsqbf = pa.sb("xsqbf2", [128, nd * TC], BF16)
    else:
        b.xsqbf = b.xn
    b.rout = pa.sb("rout", [128, nd * TC], F32)
    b.lnrow = pa.sb("lnrow", [1, 3 * TC], F32)
    b.lnrow_r = pa.sb("lnrow_r", [1, TC], F32)
    b.lnrow_r2 = pa.sb("lnrow_r2", [1, TC], F32)
    b.cpool = pa.sb("sb_cpool", [128, 16], F32)
    b.ident_sb = pa.sb("sb_ident", [128, 128], F32)
    b.triu_sb = pa.sb("sb_triu", [128, 128], F32)
    b.dtbrep = pa.sb("sb_dtbrep", [128, MH], F32)
    b.ones_bf = pa.sb("sb_ones_bf", [128, 1], BF16)
    b.onesrow_f = pa.sb("sb_onesrow_f", [1, 128], F32)
    b.onesrow_bf = pa.sb("sb_onesrow_bf", [1, 512], BF16)
    b.pA = pa.ps("pA", [128, TC])
    b.pB = pa.ps("pB", [128, TC])
    b.pC = pa.ps("pC", [128, TC])
    b.pD = pa.ps("pD", [128, TC])
    b.pE = pa.ps("pE", [128, TC])
    b.pF = pa.ps("pF", [128, TC])
    return b


def emit_const_loads(C, dr, b, sm):
    C.Y.dma_start(out=b.cpool[:], in_=dr["consts"]).inc(sm["ld"], 16)
    C.Y.dma_start(out=b.ones_bf[:], in_=dr["ones_bf"]).inc(sm["ld"], 16)
    C.Y.dma_start(out=b.onesrow_f[:], in_=dr["onesrow_f"]).inc(sm["ld"], 16)
    C.Y.dma_start(out=b.onesrow_bf[:], in_=dr["onesrow_bf"]).inc(sm["ld"], 16)
    C.Y.dma_start(out=b.ident_sb[:], in_=dr["ident"]).inc(sm["ld"], 16)
    C.Y.dma_start(out=b.triu_sb[:], in_=dr["triu01"]).inc(sm["ld"], 16)
    C.Y.dma_start(out=b.dtbrep[:], in_=dr["dtb_rep"]).inc(sm["ld"], 16)
    v = C.s.val(sm["ld"])
    for e in (C.V, C.S, C.T, C.G):
        e.wait(sm["ld"], v)


NJT = CONV // 128
NHT = INTER // 128


def alloc_mamba_a_bufs(pa, b):
    b.win = pa.sb("win", [128, 8 * PROJ], BF16)
    b.binrow = pa.sb("binrow", [1, PROJ], BF16)
    b.cw = pa.sb("cw", [128, NJT * MK], F32)
    b.hbct = pa.sb("hbct", [128, MK - 1 + TC], F32)
    b.tails = pa.sb("tails", [128, NJT * (MK - 1)], F32)
    b.convo = pa.sb("convo", [128, TC], F32)
    b.hsp = pa.sb("hsp", [128, NHT * TC // 2], BF16)
    b.bcsp = pa.sb("bcsp", [128, 2 * TC], BF16)
    b.gbuf = pa.sb("gbuf", [128, INTER], BF16)
    b.dtt = pa.sb("dtt", [128, 4 * MH], F32)
    b.dts1 = pa.sb("dts1", [128, MH], F32)
    b.dts2 = pa.sb("dts2", [128, MH], F32)
    b.aexp = pa.sb("aexp", [MH, 1], F32)
    b.alog = pa.sb("alog", [MH, 1], F32)
    b.dtc = pa.sb("dtc", [32, TC], F32)
    b.acumc = pa.sb("acumc", [32, TC], F32)
    b.zrow = pa.sb("zrow", [32, 128], F32)
    b.pG = pa.ps("pG", [128, 128])


def alloc_mamba_b_bufs(pa, b):
    b.htok = [pa.sb(f"htok{i}", [128, INTER], BF16) for i in range(2)]
    b.btok = [pa.sb(f"btok{i}", [128, 128], BF16) for i in range(2)]
    b.bchan = [pa.sb(f"bchan{i}", [128, 128], BF16) for i in range(2)]
    b.cchan = [pa.sb(f"cchan{i}", [128, 128], BF16) for i in range(2)]
    b.gchk = [pa.sb(f"gchk{i}", [128, INTER], BF16) for i in range(2)]
    b.dtk = [pa.sb(f"dtk{i}", [128, MH], F32) for i in range(2)]
    b.acu = [pa.sb(f"acu{i}", [MH, 128], F32) for i in range(2)]
    b.arep = [pa.sb("arep0", [128, MH * 128], F32)]
    b.acumT = pa.sb("acumT", [128, MH], F32)
    b.decc = pa.sb("decc", [MH, 128], F32)
    b.sdoc = pa.sb("sdoc", [MH, 128], F32)
    b.decT = pa.sb("decT", [128, MH], F32)
    b.sdoT = pa.sb("sdoT", [128, MH], F32)
    b.escrow = pa.sb("escrow", [1, MH], F32)
    b.escM = pa.sb("escM", [128, MH], F32)
    b.expbf = pa.sb("expbf", [128, MH * 128], BF16)
    b.gmbf = pa.sb("gmbf", [128, 128], BF16)
    b.xx = pa.sb("xx", [128, INTER], BF16)
    b.xx2 = pa.sb("xx2", [128, INTER], BF16)
    b.ssb = [pa.sb(f"ssb{i}", [128, INTER], BF16) for i in range(2)]
    b.stmp = pa.sb("stmp", [128, INTER], BF16)
    b.gbig = pa.sb("gbig", [128, INTER], BF16)
    b.gsq = pa.sb("gsq", [128, 1], F32)
    b.drep = pa.sb("drep", [128, INTER], F32)
    b.nwrep = pa.sb("nwrep", [128, INTER], F32)
    b.droww = pa.sb("droww", [1, INTER], F32)
    b.nwrow = b.droww
    b.pG = pa.ps("pGb", [128, 128])
    b.pE2 = pa.ps("pE2", [128, 128])


def alloc_mamba_c_bufs(pa, b):
    b.wout = pa.sb("wout", [128, NHT * D], BF16)
    b.boutrow = pa.sb("boutrow", [1, D], BF16)
    b.gT = [pa.sb(f"gT{i}", [128, TC], BF16) for i in range(NHT)]


def emit_mamba_a(C, dr, b, sm, src):
    s = C.s
    nd = D // 128
    for j in range(nd):
        sap = _ap3(dr["winT"], [[PROJ, 128], [1, PROJ]], offset=j * 128 * PROJ)
        C.G.dma_start(out=b.win[:, j * PROJ:(j + 1) * PROJ], in_=sap).inc(sm["gld"], 16)
    C.G.dma_start(out=b.binrow[:], in_=dr["b_in"]).inc(sm["gld"], 16)
    C.Y.dma_start(out=b.cw[:], in_=dr["conv_w_t"]).inc(sm["ld"], 16)
    C.Y.dma_start(out=b.alog[:], in_=dr["a_log"]).inc(sm["ld"], 16)
    v_w = s.val(sm["gld"])
    v_l = s.val(sm["ld"])
    C.T.wait(sm["gld"], v_w)
    C.S.wait(sm["ld"], v_l)
    C.S.activation(out=b.aexp[:], in_=b.alog[:], func=AF.Exp).inc(sm["ac"], 1)
    C.V.wait(sm["ac"], s.val(sm["ac"]))
    C.V.tensor_scalar_mul(out=b.aexp[:], in0=b.aexp[:], scalar1=-1.0).inc(sm["ve"], 1)
    C.V.memset(b.zrow[:], 0.0)
    C.V.memset(b.tails[:], 0.0).inc(sm["ve"], 1)

    for c in range(NCH):
        cb = c % 2
        xch = b.xch[:, cb * nd * TC:(cb + 1) * nd * TC]
        if c >= 2:
            C.Y.wait(sm["ve"], b.ma_cons[c - 2])
        sap = _ap3(src, [[T, 128], [128 * T, nd], [1, TC]], offset=c * TC)
        C.Y.dma_start(out=xch, in_=sap).inc(sm["x"], 16)
        v_x = s.val(sm["x"])
        C.S.wait(sm["x"], v_x)
        C.T.wait(sm["x"], v_x)
        C.V.wait(sm["x"], v_x)
        v_xn = emit_ln_chunk(C, b, sm, xch, TC)
        b.ma_cons[c] = v_xn
        C.T.wait(sm["ve"], v_xn)
        if c == 0:
            C.T.wait(sm["gld"], v_w)

        if c >= 1:
            C.T.wait(sm["st"], b.ma_stored[c - 1])
            C.S.wait(sm["st"], b.ma_stored[c - 1])
            C.V.wait(sm["st"], b.ma_stored[c - 1])
        for jt in range(NJT):
            pf = b.pE if jt % 2 == 0 else b.pF
            co = INTER + jt * 128
            if jt >= 2:
                C.T.wait(sm["ve"], s.val(sm["ve"]))
            for j in range(nd):
                C.T.matmul(pf[:, 0:TC], b.win[:, j * PROJ + co:j * PROJ + co + 128],
                           b.xn[:, j * TC:(j + 1) * TC],
                           start=(j == 0), stop=False)
            C.T.matmul(pf[:, 0:TC], b.binrow[0:1, co:co + 128], b.onesrow_bf[0:1, 0:TC],
                       start=False, stop=True).inc(sm["mm"], 1)
            v_p = s.val(sm["mm"])
            C.V.wait(sm["mm"], v_p)
            C.V.wait(sm["ac"], s.val(sm["ac"]))
            C.V.tensor_copy(out=b.hbct[:, MK - 1:MK - 1 + TC], in_=pf[:, 0:TC])
            C.V.tensor_copy(out=b.hbct[:, 0:MK - 1], in_=b.tails[:, jt * (MK - 1):(jt + 1) * (MK - 1)])
            C.V.drain()
            C.V.tensor_scalar(out=b.convo[:, 0:TC], in0=b.hbct[:, 0:TC],
                              scalar1=b.cw[:, jt * MK:jt * MK + 1], scalar2=None, op0=ALU.mult)
            for i in range(1, MK):
                C.V.scalar_tensor_tensor(out=b.convo[:, 0:TC], in0=b.hbct[:, i:i + TC],
                                         scalar=b.cw[:, jt * MK + i:jt * MK + i + 1],
                                         in1=b.convo[:, 0:TC], op0=ALU.mult, op1=ALU.add)
            C.V.tensor_copy(out=b.tails[:, jt * (MK - 1):(jt + 1) * (MK - 1)],
                            in_=b.hbct[:, TC:TC + MK - 1]).inc(sm["ve"], 1)
            v_cv = s.val(sm["ve"])
            C.S.wait(sm["ve"], v_cv)
            if jt < NHT:
                if jt == 8 and True:
                    C.S.wait(sm["st"], s.val(sm["st"]))
                dst = b.hsp[:, (jt % 8) * TC:(jt % 8 + 1) * TC]
            elif jt == NHT:
                dst = b.bcsp[:, 0:TC]
            else:
                dst = b.bcsp[:, TC:2 * TC]
            C.S.activation(out=dst, in_=b.convo[:, 0:TC], func=AF.Silu).inc(sm["ac"], 1)
            if jt == 7 or jt == 15:
                vh = s.val(sm["ac"])
                C.Y.wait(sm["ac"], vh)
                half = 0 if jt == 7 else 1
                hap = _ap3(dr["h_scr"], [[T, 128], [128 * T, 8], [1, TC]],
                           offset=half * 8 * 128 * T + c * TC)
                C.Y.dma_start(out=hap, in_=b.hsp[:, 0:8 * TC]).inc(sm["st"], 16)
        v_sta = s.val(sm["ac"])

        if (c + 1) % (S // TC) == 0:
            C.V.wait(sm["ac"], v_sta)
            C.V.memset(b.tails[:], 0.0).inc(sm["ve"], 1)

        for tt in range(4):
            pg = [b.pA, b.pB, b.pC, b.pD]
            C.T.wait(sm["ac"], s.val(sm["ac"]))
            C.T.wait(sm["ve"], s.val(sm["ve"]))
            for bi in range(4):
                for j in range(nd):
                    C.T.matmul(pg[bi][:, 0:TC], b.xn[:, j * TC + tt * 128:j * TC + tt * 128 + 128],
                               b.win[:, j * PROJ + bi * TC:j * PROJ + (bi + 1) * TC],
                               start=(j == 0), stop=False)
                C.T.matmul(pg[bi][:, 0:TC], b.onesrow_bf[0:1, tt * 128:tt * 128 + 128],
                           b.binrow[0:1, bi * TC:(bi + 1) * TC],
                           start=False, stop=True).inc(sm["mm"], 1)
            for j in range(nd):
                C.T.matmul(b.pG[:, 0:MH], b.xn[:, j * TC + tt * 128:j * TC + tt * 128 + 128],
                           b.win[:, j * PROJ + INTER + CONV:j * PROJ + PROJ],
                           start=(j == 0), stop=False)
            C.T.matmul(b.pG[:, 0:MH], b.onesrow_bf[0:1, tt * 128:tt * 128 + 128],
                       b.binrow[0:1, INTER + CONV:PROJ], start=False, stop=True).inc(sm["mm"], 1)
            v_g = s.val(sm["mm"])
            C.S.wait(sm["mm"], v_g)
            if c > 0 or tt > 0:
                C.S.wait(sm["st"], s.val(sm["st"]))
            for bi in range(4):
                C.S.activation(out=b.gbuf[:, bi * TC:(bi + 1) * TC],
                               in_=pg[bi][:, 0:TC], func=AF.Silu).inc(sm["ac"], 1)
            v_gb = s.val(sm["ac"])
            C.Y.wait(sm["ac"], v_gb)
            gap = _ap3(dr["gate_scr"], [[INTER, 128], [1, INTER]],
                       offset=(c * 4 + tt) * 128 * INTER)
            C.Y.dma_start(out=gap, in_=b.gbuf[:, 0:INTER]).inc(sm["st"], 16)
            C.V.wait(sm["mm"], v_g)
            dsl = b.dtt[:, tt * MH:(tt + 1) * MH]
            C.V.tensor_tensor(out=b.dts2[:, 0:MH], in0=b.pG[:, 0:MH], in1=b.dtbrep[:, 0:MH],
                              op=ALU.add)
            C.V.drain()
            C.V.tensor_scalar_max(out=b.dts1[:, 0:MH], in0=b.dts2[:, 0:MH], scalar1=0.0).inc(sm["ve"], 1)
            v_d1 = s.val(sm["ve"])
            C.S.wait(sm["ve"], v_d1)
            C.S.activation(out=b.dts2[:, 0:MH], in_=b.dts2[:, 0:MH], func=AF.Abs).inc(sm["ac"], 1)
            C.S.wait(sm["ac"], s.val(sm["ac"]))
            C.S.activation(out=b.dts2[:, 0:MH], in_=b.dts2[:, 0:MH], func=AF.Exp,
                           scale=-1.0).inc(sm["ac"], 1)
            C.S.wait(sm["ac"], s.val(sm["ac"]))
            C.S.activation(out=b.dts2[:, 0:MH], in_=b.dts2[:, 0:MH], func=AF.Ln,
                           bias=b.cpool[:, 8:9]).inc(sm["ac"], 1)
            v_d2 = s.val(sm["ac"])
            C.V.wait(sm["ac"], v_d2)
            C.V.drain()
            C.V.tensor_add(out=dsl, in0=b.dts1[:, 0:MH], in1=b.dts2[:, 0:MH]).inc(sm["ve"], 1)
            v_dt = s.val(sm["ve"])
            C.T.wait(sm["ve"], v_dt)
            C.T.transpose(b.pG[0:MH, 0:128], dsl, identity=b.ident_sb[:, 0:128]).inc(sm["mm"], 1)
            v_tr = s.val(sm["mm"])
            C.V.wait(sm["mm"], v_tr)
            C.V.tensor_copy(out=b.dtc[:, tt * 128:(tt + 1) * 128], in_=b.pG[0:MH, 0:128])
            C.V.drain()
            C.V.tensor_scalar_mul(out=b.dtc[:, tt * 128:(tt + 1) * 128],
                                  in0=b.dtc[:, tt * 128:(tt + 1) * 128],
                                  scalar1=b.aexp[0:MH, 0:1])
            C.V.tensor_tensor_scan(out=b.acumc[0:MH, tt * 128:(tt + 1) * 128],
                                   data0=b.zrow[0:MH, 0:128],
                                   data1=b.dtc[0:MH, tt * 128:(tt + 1) * 128],
                                   initial=0.0, op0=ALU.add, op1=ALU.add).inc(sm["ve"], 1)
        v_chunk = s.val(sm["ve"])

        C.Y.wait(sm["ac"], v_sta)
        bap = _ap3(dr["b_scr"], [[T, 128], [1, TC]], offset=c * TC)
        C.Y.dma_start(out=bap, in_=b.bcsp[:, 0:TC]).inc(sm["st"], 16)
        cap = _ap3(dr["c_scr"], [[T, 128], [1, TC]], offset=c * TC)
        C.Y.dma_start(out=cap, in_=b.bcsp[:, TC:2 * TC]).inc(sm["st"], 16)
        C.Y.wait(sm["ve"], v_chunk)
        dap = _ap3(dr["dt_scr"], [[MH, 128], [128 * MH, 4], [1, MH]], offset=c * 4 * 128 * MH)
        C.Y.dma_start(out=dap, in_=b.dtt[:, 0:4 * MH]).inc(sm["st"], 16)
        aap = _ap3(dr["acum_scr"], [[T, MH], [1, TC]], offset=c * TC)
        C.Y.dma_start(out=aap, in_=b.acumc[0:MH, 0:TC]).inc(sm["st"], 16)
        b.ma_stored[c] = s.val(sm["st"])
    C.Y.wait(sm["st"], b.ma_stored[NCH - 1])


def emit_mamba_b(C, dr, b, sm, dbg=None):
    s = C.s
    NU = S // CS
    C.Y.dma_start(out=b.droww[:], in_=dr["d_row"]).inc(sm["ld"], 16)
    v_l = s.val(sm["ld"])
    C.T.wait(sm["ld"], v_l)
    C.T.matmul(b.pA[:, 0:TC], b.onesrow_f[0:1, :], b.droww[0:1, 0:INTER // 4], start=True, stop=True)
    C.T.matmul(b.pB[:, 0:TC], b.onesrow_f[0:1, :], b.droww[0:1, INTER // 4:INTER // 2], start=True, stop=True)
    C.T.matmul(b.pC[:, 0:TC], b.onesrow_f[0:1, :], b.droww[0:1, INTER // 2:3 * INTER // 4], start=True, stop=True)
    C.T.matmul(b.pD[:, 0:TC], b.onesrow_f[0:1, :], b.droww[0:1, 3 * INTER // 4:INTER], start=True, stop=True).inc(sm["mm"], 1)
    v_mm = s.val(sm["mm"])
    C.V.wait(sm["mm"], v_mm)
    for q in range(4):
        cp = C.V.tensor_copy(out=b.drep[:, q * TC:(q + 1) * TC],
                             in_=[b.pA, b.pB, b.pC, b.pD][q][:, 0:TC])
        if q == 3:
            cp.inc(sm["ve"], 1)
    v_dr = s.val(sm["ve"])
    C.Y.wait(sm["ve"], v_dr)
    C.Y.dma_start(out=b.nwrow[:], in_=dr["normw_row"]).inc(sm["ld"], 16)
    v_l2 = s.val(sm["ld"])
    C.T.wait(sm["ld"], v_l2)
    C.T.wait(sm["ve"], v_dr)
    C.T.matmul(b.pA[:, 0:TC], b.onesrow_f[0:1, :], b.nwrow[0:1, 0:TC], start=True, stop=True)
    C.T.matmul(b.pB[:, 0:TC], b.onesrow_f[0:1, :], b.nwrow[0:1, TC:2 * TC], start=True, stop=True)
    C.T.matmul(b.pC[:, 0:TC], b.onesrow_f[0:1, :], b.nwrow[0:1, 2 * TC:3 * TC], start=True, stop=True)
    C.T.matmul(b.pD[:, 0:TC], b.onesrow_f[0:1, :], b.nwrow[0:1, 3 * TC:INTER], start=True, stop=True).inc(sm["mm"], 1)
    v_mm2 = s.val(sm["mm"])
    C.V.wait(sm["mm"], v_mm2)
    for q in range(4):
        C.V.tensor_copy(out=b.nwrep[:, q * TC:(q + 1) * TC], in_=[b.pA, b.pB, b.pC, b.pD][q][:, 0:TC]).inc(sm["ve"], 1)
    v_nw = s.val(sm["ve"])

    for bi in range(B_LOC):
        C.V.wait(sm["mm"], s.val(sm["mm"]))
        C.V.memset(b.ssb[0][:], 0.0).inc(sm["ve"], 1)
        for u in range(NU):
            ub = u % 2
            tof = bi * S + u * CS
            sp, sn = b.ssb[u % 2], b.ssb[(u + 1) % 2]
            if u >= 2:
                C.Y.wait(sm["ve"], b.mb_cons[(bi, u - 2)])
                C.S.wait(sm["ve"], b.mb_cons[(bi, u - 2)])
            hin = _ap3(dr["h_scr"], [[T, INTER], [1, CS]], offset=tof)
            C.Y.dmat(out=b.htok[ub][:, 0:INTER], in_=hin).inc(sm["tr"], 16)
            bin_ = _ap3(dr["b_scr"], [[T, 128], [1, CS]], offset=tof)
            C.Y.dma_start(out=b.bchan[ub][:], in_=bin_).inc(sm["x"], 16)
            C.Y.dmat(out=b.btok[ub][:], in_=bin_).inc(sm["tr"], 16)
            cin = _ap3(dr["c_scr"], [[T, 128], [1, CS]], offset=tof)
            C.Y.dma_start(out=b.cchan[ub][:], in_=cin).inc(sm["x"], 16)
            gin = _ap3(dr["gate_scr"], [[INTER, 128], [1, INTER]], offset=tof * INTER)
            C.Y.dma_start(out=b.gchk[ub][:], in_=gin).inc(sm["x"], 16)
            din = _ap3(dr["dt_scr"], [[MH, 128], [1, MH]], offset=tof * MH)
            C.Y.dma_start(out=b.dtk[ub][:], in_=din).inc(sm["x"], 16)
            ain = _ap3(dr["acum_scr"], [[T, MH], [1, CS]], offset=tof)
            C.Y.dma_start(out=b.acu[ub][:], in_=ain).inc(sm["x"], 16)
            arin = _ap3(dr["acum_scr"], [[0, 128], [T, MH], [1, CS]], offset=tof)
            C.S.wait(sm["ve"], b.mb_difdone)
            C.S.dma_start(out=b.arep[0][:, 0:MH * 128], in_=arin).inc(sm["sld"], 16)
            v_ld = s.val(sm["x"])
            v_tr = s.val(sm["tr"])
            v_ar = s.val(sm["sld"])

            C.T.wait(sm["x"], v_ld)
            C.T.wait(sm["ve"], s.val(sm["ve"]))
            C.T.transpose(b.pG[0:128, 0:MH], b.acu[ub][0:MH, 0:128],
                          identity=b.ident_sb[0:MH, 0:MH]).inc(sm["mm"], 1)
            vt1 = s.val(sm["mm"])
            C.V.wait(sm["mm"], vt1)
            C.V.tensor_copy(out=b.acumT[:, 0:MH], in_=b.pG[0:128, 0:MH]).inc(sm["ve"], 1)
            vacT = s.val(sm["ve"])
            C.S.wait(sm["x"], v_ld)
            C.S.activation(out=b.decc[0:MH, 0:128], in_=b.acu[ub][0:MH, 0:128],
                           func=AF.Exp, scale=-1.0, bias=b.acu[ub][0:MH, CS - 1:CS])
            C.S.activation(out=b.sdoc[0:MH, 0:128], in_=b.acu[ub][0:MH, 0:128],
                           func=AF.Exp).inc(sm["ac"], 1)
            vexp = s.val(sm["ac"])
            C.T.wait(sm["ac"], vexp)
            C.T.wait(sm["ve"], vacT)
            C.T.transpose(b.pG[0:128, 0:MH], b.decc[0:MH, 0:128],
                          identity=b.ident_sb[0:MH, 0:MH]).inc(sm["mm"], 1)
            C.V.wait(sm["mm"], s.val(sm["mm"]))
            C.V.tensor_copy(out=b.decT[:, 0:MH], in_=b.pG[0:128, 0:MH]).inc(sm["ve"], 1)
            C.T.wait(sm["ve"], s.val(sm["ve"]))
            C.T.transpose(b.pG[0:128, 0:MH], b.sdoc[0:MH, 0:128],
                          identity=b.ident_sb[0:MH, 0:MH]).inc(sm["mm"], 1)
            C.V.wait(sm["mm"], s.val(sm["mm"]))
            C.V.tensor_copy(out=b.sdoT[:, 0:MH], in_=b.pG[0:128, 0:MH]).inc(sm["ve"], 1)
            C.T.wait(sm["ve"], s.val(sm["ve"]))
            C.T.transpose(b.pG[0:1, 0:MH], b.sdoc[0:MH, CS - 1:CS],
                          identity=b.ident_sb[0:MH, 0:MH]).inc(sm["mm"], 1)
            C.V.wait(sm["mm"], s.val(sm["mm"]))
            C.V.tensor_copy(out=b.escrow[0:1, 0:MH], in_=b.pG[0:1, 0:MH]).inc(sm["ve"], 1)
            vesc = s.val(sm["ve"])
            C.T.wait(sm["ve"], vesc)
            C.T.matmul(b.pG[:, 0:MH], b.onesrow_f[0:1, :], b.escrow[0:1, 0:MH],
                       start=True, stop=True).inc(sm["mm"], 1)
            C.V.wait(sm["mm"], s.val(sm["mm"]))
            C.V.tensor_copy(out=b.escM[:, 0:MH], in_=b.pG[:, 0:MH]).inc(sm["ve"], 1)
            vescM = s.val(sm["ve"])

            if not (bi == 0 and u == 0):
                C.V.wait(sm["st"], b.mb_stored)
            C.S.wait(sm["mm"], s.val(sm["mm"]))
            C.V.wait(sm["sld"], v_ar)
            arr = b.arep[0]
            acT_rep = _ap3(b.acumT, [[b.acumT.ap().ap[0][0], 128], [1, MH], [0, 128]])
            C.V.tensor_tensor(out=b.scrA[:, 0:MH * 128], in0=arr[:, 0:MH * 128],
                              in1=acT_rep, op=ALU.subtract)
            C.V.tensor_scalar_min(out=b.scrA[:, 0:MH * 128], in0=b.scrA[:, 0:MH * 128],
                                  scalar1=0.0).inc(sm["ve"], 1)
            vdif = s.val(sm["ve"])
            b.mb_difdone = vdif
            if dbg is not None and bi == 0 and u == 0 and "dbg_dif2" in dbg:
                C.G.wait(sm["ve"], vdif)
                C.G.dma_start(out=dbg["dbg_dif2"], in_=b.scrA[:, 0:MH * 128]).inc(sm["gld"], 16)
                C.V.wait(sm["gld"], s.val(sm["gld"]))
            C.S.wait(sm["ve"], vdif)
            C.S.activation(out=b.expbf[:, 0:MH * 128], in_=b.scrA[:, 0:MH * 128],
                           func=AF.Exp).inc(sm["ac"], 1)
            vexp2 = s.val(sm["ac"])
            C.T.wait(sm["tr"], v_tr)
            C.T.wait(sm["ve"], s.val(sm["ve"]))
            C.T.matmul(b.pE2[:, 0:128], b.bchan[ub][:], b.cchan[ub][:],
                       start=True, stop=True).inc(sm["mm"], 1)
            vgm = s.val(sm["mm"])
            C.V.wait(sm["mm"], vgm)
            C.V.tensor_tensor(out=b.gmbf[:], in0=b.pE2[:, 0:128], in1=b.triu_sb[:],
                              op=ALU.mult).inc(sm["ve"], 1)
            vgmb = s.val(sm["ve"])
            if dbg is not None and bi == 0 and u == 0 and "dbg_gm" in dbg:
                C.G.wait(sm["ve"], vgmb)
                C.G.dma_start(out=dbg["dbg_gm"], in_=b.gmbf[:, 0:128]).inc(sm["gld"], 16)
                C.V.wait(sm["gld"], s.val(sm["gld"]))
            C.V.wait(sm["ac"], vexp2)
            gm_rep = _ap3(b.gmbf, [[b.gmbf.ap().ap[0][0], 128], [0, MH], [1, 128]])
            C.V.tensor_tensor(out=b.expbf[:, 0:MH * 128], in0=b.expbf[:, 0:MH * 128],
                              in1=gm_rep, op=ALU.mult).inc(sm["ve"], 1)
            vM = s.val(sm["ve"])

            C.V.wait(sm["tr"], v_tr)
            C.V.wait(sm["x"], v_ld)
            dt_rep = _ap3(b.dtk[ub], [[b.dtk[ub].ap().ap[0][0], 128], [1, MH], [0, MP]])
            C.V.tensor_tensor(out=b.xx[:, 0:INTER], in0=b.htok[ub][:, 0:INTER],
                              in1=dt_rep, op=ALU.mult)
            dec_rep = _ap3(b.decT, [[b.decT.ap().ap[0][0], 128], [1, MH], [0, MP]])
            C.V.tensor_tensor(out=b.xx2[:, 0:INTER], in0=b.xx[:, 0:INTER],
                              in1=dec_rep, op=ALU.mult).inc(sm["ve"], 1)
            vX = s.val(sm["ve"])

            for hf in range(2):
                hof = hf * (INTER // 2)
                pY = [b.pA, b.pB] if hf == 0 else [b.pA, b.pB]
                pO = [b.pC, b.pD]
                C.T.wait(sm["ve"], s.val(sm["ve"]))
                for hh in range(16):
                    h = hf * 16 + hh
                    bank = pY[hh // 8]
                    mm = C.T.matmul(bank[:, (hh % 8) * MP:(hh % 8) * MP + MP],
                                    b.expbf[:, h * 128:(h + 1) * 128],
                                    b.xx[:, h * MP:(h + 1) * MP],
                                    start=(hh % 8 == 0), stop=(hh % 8 == 7))
                    if hh == 15:
                        mm.inc(sm["mm"], 1)
                if u > 0:
                    for q in range(2):
                        mm = C.T.matmul(pO[q][:, 0:TC], b.cchan[ub][:],
                                        sp[:, hof + q * TC:hof + (q + 1) * TC],
                                        start=True, stop=True)
                        if q == 1:
                            mm.inc(sm["mm"], 1)
                vy = s.val(sm["mm"])
                C.V.wait(sm["mm"], vy)
                t1a = b.scrA[:, 0:TC]
                t1b = b.scrA[:, TC:2 * TC]
                if u > 0:
                    pst = b.sdoT.ap().ap[0][0]
                    sdo_rep = _ap3(b.sdoT, [[pst, 128], [1, 8], [0, MP]], offset=hf * 16)
                    C.V.tensor_tensor(out=t1a, in0=b.pC[:, 0:TC], in1=sdo_rep, op=ALU.mult)
                    sdo_rep2 = _ap3(b.sdoT, [[pst, 128], [1, 8], [0, MP]], offset=hf * 16 + 8)
                    C.V.tensor_tensor(out=t1b, in0=b.pD[:, 0:TC], in1=sdo_rep2, op=ALU.mult)
                C.G.wait(sm["ve"], s.val(sm["ve"]))
                C.G.tensor_tensor(out=b.stmp[:, 0:2 * TC], in0=b.htok[ub][:, hof:hof + 2 * TC],
                                  in1=b.drep[:, hof:hof + 2 * TC], op=ALU.mult).inc(sm["gp"], 1)
                vt2 = s.val(sm["gp"])
                for q in range(2):
                    pyq = [b.pA, b.pB][q]
                    ysl = b.scrA[:, (2 + q) * TC:(3 + q) * TC]
                    if u > 0:
                        tq = [t1a, t1b][q]
                        C.V.scalar_tensor_tensor(out=ysl, in0=pyq[:, 0:TC], scalar=1.0,
                                                 in1=tq, op0=ALU.mult, op1=ALU.add)
                    else:
                        C.V.tensor_copy(out=ysl, in_=pyq[:, 0:TC])
                C.V.wait(sm["gp"], vt2)
                for q in range(2):
                    ysl = b.scrA[:, (2 + q) * TC:(3 + q) * TC]
                    C.V.tensor_tensor(out=ysl, in0=ysl, in1=b.stmp[:, q * TC:(q + 1) * TC],
                                      op=ALU.add)
                    C.V.tensor_tensor(out=b.gbig[:, hof + q * TC:hof + (q + 1) * TC],
                                      in0=ysl, in1=b.gchk[ub][:, hof + q * TC:hof + (q + 1) * TC],
                                      op=ALU.mult).inc(sm["ve"], 1)
            vg = s.val(sm["ve"])
            if dbg is not None and bi == 0 and u == 0 and "dbg_m" in dbg:
                C.G.wait(sm["ve"], vg)
                C.G.dma_start(out=dbg["dbg_h"], in_=b.htok[ub][:, 0:INTER]).inc(sm["gld"], 16)
                C.G.dma_start(out=dbg["dbg_dt"], in_=b.dtk[ub][:, 0:MH]).inc(sm["gld"], 16)
                C.G.dma_start(out=dbg["dbg_m"], in_=b.expbf[:, 0:MH * 128]).inc(sm["gld"], 16)
                C.G.dma_start(out=dbg["dbg_x"], in_=b.xx[:, 0:INTER]).inc(sm["gld"], 16)
                C.G.dma_start(out=dbg["dbg_g"], in_=b.gbig[:, 0:INTER]).inc(sm["gld"], 16)
                C.G.dma_start(out=dbg["dbg_dif"], in_=b.scrA[:, 2 * TC:4 * TC]).inc(sm["gld"], 16)
                C.G.wait(sm["gld"], s.val(sm["gld"]))
                vtap = s.val(sm["gld"])
                C.V.wait(sm["gld"], vtap)
                C.S.wait(sm["gld"], vtap)
                C.T.wait(sm["gld"], vtap)
                C.Y.wait(sm["gld"], vtap)

            C.T.wait(sm["ve"], vg)
            C.T.wait(sm["tr"], v_tr)
            pS = [b.pA, b.pB, b.pC, b.pD]
            for q in range(4):
                mm = C.T.matmul(pS[q][:, 0:TC], b.btok[ub][:],
                                b.xx2[:, q * TC:(q + 1) * TC], start=True, stop=True)
                if q == 3:
                    mm.inc(sm["mm"], 1)
            vst = s.val(sm["mm"])
            C.G.wait(sm["ve"], s.val(sm["ve"]))
            esc_rep = _ap3(b.escM, [[b.escM.ap().ap[0][0], 128], [1, MH], [0, MP]])
            C.G.tensor_tensor(out=b.stmp[:, 0:INTER], in0=sp[:, 0:INTER],
                              in1=esc_rep, op=ALU.mult).inc(sm["gp"], 1)
            vsc = s.val(sm["gp"])
            C.V.wait(sm["mm"], vst)
            C.V.wait(sm["gp"], vsc)
            for q in range(4):
                C.V.tensor_tensor(out=sn[:, q * TC:(q + 1) * TC], in0=pS[q][:, 0:TC],
                                  in1=b.stmp[:, q * TC:(q + 1) * TC], op=ALU.add).inc(sm["ve"], 1)
            vS = s.val(sm["ve"])

            C.V.tensor_mul(out=b.scrA[:, 0:INTER], in0=b.gbig[:, 0:INTER],
                           in1=b.gbig[:, 0:INTER])
            C.V.reduce_sum(out=b.gsq[:, 0:1], in_=b.scrA[:, 0:INTER],
                           axis=mybir.AxisListType.X).inc(sm["ve"], 1)
            vms = s.val(sm["ve"])
            C.S.wait(sm["ve"], vms)
            C.S.activation(out=b.gsq[:, 0:1], in_=b.gsq[:, 0:1], func=AF.Ln,
                           scale=1.0 / INTER, bias=b.cpool[:, 0:1]).inc(sm["ac"], 1)
            C.S.wait(sm["ac"], s.val(sm["ac"]))
            C.S.activation(out=b.gsq[:, 0:1], in_=b.gsq[:, 0:1], func=AF.Exp,
                           scale=-0.5).inc(sm["ac"], 1)
            vrs = s.val(sm["ac"])
            C.V.wait(sm["ac"], vrs)
            C.V.wait(sm["ve"], v_nw)
            C.V.tensor_scalar_mul(out=b.gbig[:, 0:INTER], in0=b.gbig[:, 0:INTER],
                                  scalar1=b.gsq[:, 0:1])
            C.V.tensor_mul(out=b.gbig[:, 0:INTER], in0=b.gbig[:, 0:INTER],
                           in1=b.nwrep[:, 0:INTER]).inc(sm["ve"], 1)
            vgf = s.val(sm["ve"])
            b.mb_cons[(bi, u)] = vgf
            C.Y.wait(sm["ve"], vgf)
            gsp = _ap3(dr["g_scr"], [[INTER, 128], [1, INTER]], offset=tof * INTER)
            C.Y.dma_start(out=gsp, in_=b.gbig[:, 0:INTER]).inc(sm["st"], 16)
            b.mb_stored = s.val(sm["st"])
    C.Y.wait(sm["st"], b.mb_stored)


def emit_mamba_c(C, dr, b, sm, src, dst):
    s = C.s
    nd = D // 128
    for ci in range(NHT):
        sap = _ap3(dr["woutT"], [[D, 128], [1, D]], offset=ci * 128 * D)
        C.G.dma_start(out=b.wout[:, ci * D:(ci + 1) * D], in_=sap).inc(sm["gld"], 16)
    C.G.dma_start(out=b.boutrow[:], in_=dr["b_out"]).inc(sm["gld"], 16)
    v_w = s.val(sm["gld"])
    C.T.wait(sm["gld"], v_w)
    for c in range(NCH):
        cb = c % 2
        xch = b.xch[:, cb * nd * TC:(cb + 1) * nd * TC]
        if c >= 2:
            C.Y.wait(sm["ve"], b.mc_cons[c - 2])
        sap = _ap3(src, [[T, 128], [128 * T, nd], [1, TC]], offset=c * TC)
        C.Y.dma_start(out=xch, in_=sap).inc(sm["x"], 16)
        v_x = s.val(sm["x"])
        if c >= 1:
            C.Y.wait(sm["mm"], b.mc_mmdone[c - 1])
        for ci in range(NHT):
            gin = _ap3(dr["g_scr"], [[INTER, TC], [1, 128]], offset=c * TC * INTER + ci * 128)
            C.Y.dmat(out=b.gT[ci][:, 0:TC], in_=gin).inc(sm["tr"], 16)
        v_tr = s.val(sm["tr"])
        C.T.wait(sm["tr"], v_tr)
        pg = [b.pA, b.pB, b.pC, b.pD]
        for dt_ in range(nd):
            bank = pg[dt_ % 4]
            if dt_ >= 4:
                C.T.wait(sm["ve"], s.val(sm["ve"]))
            for ci in range(NHT):
                C.T.matmul(bank[:, 0:TC], b.wout[:, ci * D + dt_ * 128:ci * D + (dt_ + 1) * 128],
                           b.gT[ci][:, 0:TC], start=(ci == 0), stop=False)
            mm = C.T.matmul(bank[:, 0:TC], b.boutrow[0:1, dt_ * 128:(dt_ + 1) * 128],
                            b.onesrow_bf[0:1, 0:TC], start=False, stop=True)
            mm.inc(sm["mm"], 1)
            v_y = s.val(sm["mm"])
            C.V.wait(sm["mm"], v_y)
            if dt_ == 0:
                C.V.wait(sm["x"], v_x)
            if c >= 1 and dt_ == 0:
                C.V.wait(sm["st"], b.mc_stored[c - 1])
            C.V.scalar_tensor_tensor(out=b.rout[:, dt_ * TC:(dt_ + 1) * TC],
                                     in0=bank[:, 0:TC], scalar=1.0,
                                     in1=xch[:, dt_ * TC:(dt_ + 1) * TC],
                                     op0=ALU.mult, op1=ALU.add).inc(sm["ve"], 1)
        b.mc_mmdone[c] = s.val(sm["mm"])
        b.mc_cons[c] = s.val(sm["ve"])
        C.Y.wait(sm["ve"], b.mc_cons[c])
        dap = _ap3(dst, [[T, 128], [128 * T, nd], [1, TC]], offset=c * TC)
        C.Y.dma_start(out=dap, in_=b.rout[:, 0:nd * TC]).inc(sm["st"], 16)
        b.mc_stored[c] = s.val(sm["st"])
    C.Y.wait(sm["st"], b.mc_stored[NCH - 1])


def alloc_attn_persist(pa, b):
    b.fullk = pa.sb("fullk", [128, 8 * NSLOT], BF16)
    b.fullv = pa.sb("fullv", [128, 8 * NSLOT], BF16)
    b.vt = pa.sb("vt", [128, 4 * NH * (HD + 1)], BF16)
    b.pG2 = pa.ps("pG2", [128, TC])
    b.pT = pa.ps("pT", [128, HD], BF16)


def alloc_attn_a(pa, b):
    b.wA = pa.sb("wA", [128, 8 * D], BF16)
    b.wB = pa.sb("wB", [128, 8 * D], BF16)
    b.laA = pa.sb("laA", [128, 8 * LORA], BF16)
    b.lbA = pa.sb("lbA", [LORA, D], BF16)
    b.kc = pa.sb("kc", [128, 8 * TC], BF16)
    b.vc = pa.sb("vc", [128, 8 * TC], BF16)
    b.tmplA = pa.sb("tmplA", [LORA, TC], BF16)
    b.rpbA = pa.sb("rpbA", [128, TC], BF16)
    b.rkbA = pa.sb("rkbA", [128, TC], BF16)
    b.cosA = pa.sb("cosA", [128, TC], BF16)
    b.sinA = pa.sb("sinA", [128, TC], BF16)
    b.cdmaA = pa.sb("cdmaA", [128, MK], F32)
    b.identbf = pa.sb("identbf", [128, HD], BF16)


def alloc_attn_b(pa, b):
    b.wq_st = [pa.sb(f"wqst{i}", [128, 8 * 128], BF16) for i in range(2)]
    b.wo_st = [pa.sb(f"wost{i}", [128, 8 * 128], BF16) for i in range(2)]
    b.laB = pa.sb("laB", [128, 8 * LORA], BF16)
    b.lbB = pa.sb("lbB", [LORA, D], BF16)
    b.lc = pa.sb("lc", [128, 8 * LORA], BF16)
    b.ld_ = pa.sb("ld_", [LORA, D], BF16)
    b.kcB = pa.sb("kcB", [128, 8 * TC], BF16)
    b.vcB = pa.sb("vcB", [128, 8 * TC], BF16)
    b.cpkt = pa.sb("cpkt", [128, TC], BF16)
    b.cpvt = pa.sb("cpvt", [128, TC], BF16)
    b.aqb = pa.sb("aqb", [128, 8 * TC], BF16)
    b.o2 = pa.sb("o2", [128, 8 * TC], BF16)
    b.ebuf = pa.sb("ebuf", [128, 4 * TC], BF16)
    b.tmplB = pa.sb("tmplB", [LORA, TC], BF16)
    b.rpbB = pa.sb("rpbB", [128, TC], BF16)
    b.rkbB = pa.sb("rkbB", [128, TC], BF16)
    b.zrow1 = pa.sb("zrow1", [1, TC], F32)
    b.wcur = pa.sb("wcur", [1, TC], F32)
    b.zinv = pa.sb("zinv", [1, TC], F32)
    b.qcph = pa.sb("qcph", [128, TC], BF16)
    b.corr = pa.sb("corr", [128, TC], F32)
    b.cosB = pa.sb("cosB", [128, TC], BF16)
    b.sinB = pa.sb("sinB", [128, TC], BF16)
    b.qpaB = pa.sb("qpaB", [128, TC], BF16)
    b.qpoB = pa.sb("qpoB", [128, TC], BF16)
    b.bandB = pa.sb("bandB", [128, TC], F32)
    b.cdmaB = pa.sb("cdmaB", [128, MK], F32)


def _rope(C, b, sm, psum, cos_sl, sin_sl, rpb, rkb, outsl, vref):
    C.V.wait(sm["mm"], vref)
    for base in (0, 64):
        C.V.tensor_scalar_mul(out=rpb[base:base + 32, 0:TC],
                              in0=psum[base + 32:base + 64, 0:TC], scalar1=-1.0)
        C.V.tensor_copy(out=rpb[base + 32:base + 64, 0:TC],
                        in_=psum[base:base + 32, 0:TC])
    C.V.tensor_tensor(out=rkb[:, 0:TC], in0=psum[:, 0:TC], in1=cos_sl, op=ALU.mult)
    C.V.tensor_tensor(out=rpb[:, 0:TC], in0=rpb[:, 0:TC], in1=sin_sl, op=ALU.mult)
    C.V.tensor_tensor(out=outsl, in0=rkb[:, 0:TC], in1=rpb[:, 0:TC], op=ALU.add)


def _cdma_taps(C, b, sm, cdma, srcsl, dstsl, cumulative):
    if cumulative:
        C.V.tensor_scalar_mul(out=_ap3(dstsl, [[dstsl.ap[0][0], 128], [MK, 128]]),
                              in0=_ap3(srcsl, [[srcsl.ap[0][0], 128], [MK, 128]]),
                              scalar1=cdma[:, 0:1])
        for jj in range(1, MK):
            C.V.scalar_tensor_tensor(
                out=_ap3(dstsl, [[dstsl.ap[0][0], 128], [MK, 128]], offset=jj),
                in0=_ap3(srcsl, [[srcsl.ap[0][0], 128], [MK, 128]], offset=jj),
                scalar=cdma[:, jj:jj + 1],
                in1=_ap3(dstsl, [[dstsl.ap[0][0], 128], [MK, 128]], offset=jj - 1),
                op0=ALU.mult, op1=ALU.add)
    else:
        C.V.tensor_scalar_mul(out=dstsl,
                              in0=_ap3(srcsl, [[srcsl.ap[0][0], 128], [MK, 128]]),
                              scalar1=cdma[:, 0:1])
        for jj in range(1, MK):
            C.V.scalar_tensor_tensor(
                out=dstsl,
                in0=_ap3(srcsl, [[srcsl.ap[0][0], 128], [MK, 128]], offset=jj),
                scalar=cdma[:, jj:jj + 1], in1=dstsl, op0=ALU.mult, op1=ALU.add)


def emit_attn(C, dr, b, sm, src, dst):
    s = C.s
    nd = D // 128
    for bi in range(B_LOC):
        C.Y.dma_start(out=b.cdmaA[:], in_=dr["cdma_half"]).inc(sm["ld"], 16)
        C.G.dma_start(out=b.identbf[:], in_=dr["ident2"]).inc(sm["gld"], 16)
        v_c1 = s.val(sm["ld"])
        for e in (C.V, C.S, C.T):
            e.wait(sm["ld"], v_c1)
        for j in range(nd):
            C.G.dma_start(out=b.wA[:, j * D:(j + 1) * D],
                          in_=_ap3(dr["wk"], [[D, 128], [1, D]], offset=j * 128 * D)).inc(sm["gld"], 16)
            C.G.dma_start(out=b.wB[:, j * D:(j + 1) * D],
                          in_=_ap3(dr["wv"], [[D, 128], [1, D]], offset=j * 128 * D)).inc(sm["gld"], 16)
            C.G.dma_start(out=b.laA[:, j * LORA:(j + 1) * LORA],
                          in_=_ap3(dr["av"], [[LORA, 128], [1, LORA]], offset=j * 128 * LORA)).inc(sm["gld"], 16)
        C.G.dma_start(out=b.lbA[:], in_=dr["bv"]).inc(sm["gld"], 16)
        v_w3a = s.val(sm["gld"])
        C.T.wait(sm["gld"], v_w3a)
        C.V.memset(b.vt[:], 1.0).inc(sm["ve"], 1)

        for c in range(4):
            cg = bi * 4 + c
            cb = cg % 2
            xch = b.xch[:, cb * nd * TC:(cb + 1) * nd * TC]
            C.Y.wait(sm["ve"], b.at_cons.get(cg - 2, 0))
            C.Y.dma_start(out=xch, in_=_ap3(src, [[T, 128], [128 * T, nd], [1, TC]],
                                            offset=cg * TC)).inc(sm["x"], 16)
            C.G.wait(sm["ve"], b.at_tabfree.get(cg - 1, 0))
            C.G.dma_start(out=b.cosA[:], in_=_ap3(dr["cos_rep"], [[S, 128], [1, TC]],
                                                  offset=c * TC)).inc(sm["gld"], 16)
            C.G.dma_start(out=b.sinA[:], in_=_ap3(dr["sin_rep"], [[S, 128], [1, TC]],
                                                  offset=c * TC)).inc(sm["gld"], 16)
            v_tab = s.val(sm["gld"])
            v_x = s.val(sm["x"])
            C.S.wait(sm["x"], v_x)
            C.T.wait(sm["x"], v_x)
            C.V.wait(sm["x"], v_x)
            C.V.wait(sm["gld"], v_tab)
            v_xn = emit_ln_chunk(C, b, sm, xch, TC)
            b.at_cons[cg] = v_xn
            C.T.wait(sm["ve"], v_xn)
            C.T.wait(sm["ve"], s.val(sm["ve"]))
            for j in range(nd):
                mmL = C.T.matmul(b.pF[0:LORA, 0:TC], b.laA[:, j * LORA:(j + 1) * LORA],
                                 b.xn[:, j * TC:(j + 1) * TC], start=(j == 0), stop=(j == nd - 1))
                if j == nd - 1:
                    mmL.inc(sm["mm"], 1)
            C.V.wait(sm["mm"], s.val(sm["mm"]))
            C.V.tensor_copy(out=b.tmplA[:, 0:TC], in_=b.pF[0:LORA, 0:TC]).inc(sm["ve"], 1)
            v_tls = s.val(sm["ve"])
            for ct in range(nd):
                C.T.wait(sm["ve"], s.val(sm["ve"]))
                for j in range(nd):
                    mmK = C.T.matmul(b.pE[:, 0:TC], b.wA[:, j * D + ct * 128:j * D + (ct + 1) * 128],
                                     b.xn[:, j * TC:(j + 1) * TC], start=(j == 0), stop=(j == nd - 1))
                    if j == nd - 1:
                        mmK.inc(sm["mm"], 1)
                v_k = s.val(sm["mm"])
                _rope(C, b, sm, b.pE, b.cosA[:, 0:TC], b.sinA[:, 0:TC], b.rpbA, b.rkbA,
                      b.kc[:, ct * TC:(ct + 1) * TC], v_k)
                _cdma_taps(C, b, sm, b.cdmaA, b.kc[:, ct * TC:(ct + 1) * TC],
                           b.fullk[:, ct * NSLOT + c * 128:ct * NSLOT + (c + 1) * 128], False)
                C.V.drain().inc(sm["ve"], 1)
                C.T.wait(sm["ve"], s.val(sm["ve"]))
                if ct == 0:
                    C.T.wait(sm["ve"], v_tls)
                for j in range(nd):
                    C.T.matmul(b.pF[:, 0:TC], b.wB[:, j * D + ct * 128:j * D + (ct + 1) * 128],
                               b.xn[:, j * TC:(j + 1) * TC], start=(j == 0), stop=False)
                C.T.matmul(b.pF[:, 0:TC], b.lbA[:, ct * 128:(ct + 1) * 128], b.tmplA[:, 0:TC],
                           start=False, stop=True).inc(sm["mm"], 1)
                v_v = s.val(sm["mm"])
                C.V.wait(sm["mm"], v_v)
                C.V.tensor_copy(out=b.vc[:, ct * TC:(ct + 1) * TC], in_=b.pF[:, 0:TC])
                C.V.drain()
                _cdma_taps(C, b, sm, b.cdmaA, b.vc[:, ct * TC:(ct + 1) * TC],
                           b.fullv[:, ct * NSLOT + c * 128:ct * NSLOT + (c + 1) * 128], False)
                C.V.drain().inc(sm["ve"], 1)
            v_kv = s.val(sm["ve"])
            b.at_tabfree[cg] = v_kv
            C.Y.wait(sm["ve"], v_kv)
            C.Y.dma_start(out=_ap3(dr["k_scr"], [[T, 128], [128 * T, nd], [1, TC]],
                                   offset=cg * TC), in_=b.kc[:, 0:nd * TC]).inc(sm["st"], 16)
            C.Y.dma_start(out=_ap3(dr["v_scr"], [[T, 128], [128 * T, nd], [1, TC]],
                                   offset=cg * TC), in_=b.vc[:, 0:nd * TC]).inc(sm["st"], 16)
            b.at_stored[("kv", cg)] = s.val(sm["st"])
        v_fv = s.val(sm["ve"])
        C.T.wait(sm["ve"], v_fv)
        for st_ in range(4):
            for h in range(NH):
                ct, par = h // 2, h % 2
                C.T.wait(sm["ve"], s.val(sm["ve"]))
                C.T.transpose(b.pT[0:128, 0:HD],
                              b.fullv[par * HD:(par + 1) * HD,
                                      ct * NSLOT + st_ * 128:ct * NSLOT + (st_ + 1) * 128],
                              identity=b.identbf[par * HD:(par + 1) * HD, 0:HD]).inc(sm["mm"], 1)
                C.V.wait(sm["mm"], s.val(sm["mm"]))
                col = (st_ * NH + h) * (HD + 1)
                C.V.tensor_copy(out=b.vt[:, col:col + HD], in_=b.pT[0:128, 0:HD]).inc(sm["ve"], 1)

        C.G.wait(sm["ve"], s.val(sm["ve"]))
        C.Y.dma_start(out=b.cdmaB[:], in_=dr["cdma_half"]).inc(sm["ld"], 16)
        C.Y.dma_start(out=b.bandB[:], in_=dr["band01"]).inc(sm["ld"], 16)
        v_cB = s.val(sm["ld"])
        for e in (C.V, C.S, C.T):
            e.wait(sm["ld"], v_cB)
        for j in range(nd):
            C.G.dma_start(out=b.laB[:, j * LORA:(j + 1) * LORA],
                          in_=_ap3(dr["aq"], [[LORA, 128], [1, LORA]], offset=j * 128 * LORA)).inc(sm["gld"], 16)
            C.G.dma_start(out=b.lc[:, j * LORA:(j + 1) * LORA],
                          in_=_ap3(dr["ao"], [[LORA, 128], [1, LORA]], offset=j * 128 * LORA)).inc(sm["gld"], 16)
        C.G.dma_start(out=b.lbB[:], in_=dr["bq"]).inc(sm["gld"], 16)
        C.G.dma_start(out=b.ld_[:], in_=dr["bo"]).inc(sm["gld"], 16)
        v_w3b = s.val(sm["gld"])
        C.T.wait(sm["gld"], v_w3b)

        for qc in range(4):
            cg = bi * 4 + qc
            cb = cg % 2
            xch = b.xch[:, cb * nd * TC:(cb + 1) * nd * TC]
            C.Y.wait(sm["ve"], b.at_cons.get(cg - 2, 0))
            C.Y.wait(sm["st"], b.at_stored.get(("r", cg - 1), 0))
            C.Y.dma_start(out=xch, in_=_ap3(src, [[T, 128], [128 * T, nd], [1, TC]],
                                            offset=cg * TC)).inc(sm["x"], 16)
            C.Y.wait(sm["ve"], b.at_kvfree.get(cg - 1, 0))
            C.Y.dma_start(out=b.kcB[:, 0:nd * TC],
                          in_=_ap3(dr["k_scr"], [[T, 128], [128 * T, nd], [1, TC]],
                                   offset=cg * TC)).inc(sm["x"], 16)
            C.Y.dma_start(out=b.vcB[:, 0:nd * TC],
                          in_=_ap3(dr["v_scr"], [[T, 128], [128 * T, nd], [1, TC]],
                                   offset=cg * TC)).inc(sm["x"], 16)
            C.G.wait(sm["ve"], b.at_tabfree.get(("b", cg - 1), 0))
            for nm, buf in (("cos_rep", b.cosB), ("sin_rep", b.sinB),
                            ("qpa", b.qpaB), ("qpo", b.qpoB)):
                C.G.dma_start(out=buf[:], in_=_ap3(dr[nm], [[S, 128], [1, TC]],
                                                   offset=qc * TC)).inc(sm["gld"], 16)
            v_tab = s.val(sm["gld"])
            v_x = s.val(sm["x"])
            C.S.wait(sm["x"], v_x)
            C.T.wait(sm["x"], v_x)
            C.V.wait(sm["x"], v_x)
            C.V.wait(sm["gld"], v_tab)
            v_xn = emit_ln_chunk(C, b, sm, xch, TC)
            b.at_cons[cg] = v_xn
            C.T.wait(sm["ve"], v_xn)
            C.T.wait(sm["ve"], s.val(sm["ve"]))
            for j in range(nd):
                mmq = C.T.matmul(b.pF[0:LORA, 0:TC], b.laB[:, j * LORA:(j + 1) * LORA],
                                 b.xn[:, j * TC:(j + 1) * TC], start=(j == 0), stop=(j == nd - 1))
                if j == nd - 1:
                    mmq.inc(sm["mm"], 1)
            C.V.wait(sm["mm"], s.val(sm["mm"]))
            C.V.tensor_copy(out=b.tmplB[:, 0:TC], in_=b.pF[0:LORA, 0:TC]).inc(sm["ve"], 1)
            v_tls = s.val(sm["ve"])
            for ct in range(nd):
                wq = b.wq_st[ct % 2]
                if len(b.at_wqmark) >= 2:
                    C.G.wait(sm["mm"], b.at_wqmark[-2])
                C.G.dma_start(out=wq[:, 0:8 * 128],
                              in_=_ap3(dr["wq"], [[D, 128], [128 * D, nd], [1, 128]],
                                       offset=ct * 128)).inc(sm["gld"], 16)
                v_wq = s.val(sm["gld"])
                pq = b.pE
                C.T.wait(sm["ve"], s.val(sm["ve"]))
                C.T.wait(sm["gld"], v_wq)
                if ct == 0:
                    C.T.wait(sm["ve"], v_tls)
                for j in range(nd):
                    C.T.matmul(pq[:, 0:TC], wq[:, j * 128:(j + 1) * 128],
                               b.xn[:, j * TC:(j + 1) * TC], start=(j == 0), stop=False)
                mmQ = C.T.matmul(pq[:, 0:TC], b.lbB[:, ct * 128:(ct + 1) * 128],
                                 b.tmplB[:, 0:TC], start=False, stop=True)
                mmQ.inc(sm["mm"], 1)
                b.at_wqmark.append(s.val(sm["mm"]))
                v_q = s.val(sm["mm"])
                _rope(C, b, sm, pq, b.cosB[:, 0:TC], b.sinB[:, 0:TC], b.rpbB, b.rkbB,
                      b.aqb[:, ct * TC:(ct + 1) * TC], v_q)
                C.V.tensor_tensor(out=b.aqb[:, ct * TC:(ct + 1) * TC],
                                  in0=b.aqb[:, ct * TC:(ct + 1) * TC], in1=b.qpaB[:, 0:TC],
                                  op=ALU.mult)
                C.V.drain().inc(sm["ve"], 1)
            v_aq = s.val(sm["ve"])

            for h in range(NH):
                ct, par = h // 2, h % 2
                rb = par * HD
                if par == 0:
                    C.V.wait(sm["mm"], s.val(sm["mm"]))
                    _cdma_taps(C, b, sm, b.cdmaB, b.kcB[:, ct * TC:(ct + 1) * TC],
                               b.cpkt[:, 0:TC], True)
                    C.V.drain()
                    _cdma_taps(C, b, sm, b.cdmaB, b.vcB[:, ct * TC:(ct + 1) * TC],
                               b.cpvt[:, 0:TC], True)
                    C.V.drain().inc(sm["ve"], 1)
                v_cp = s.val(sm["ve"])
                pS = [b.pA, b.pB, b.pC, b.pD]
                C.T.wait(sm["ve"], v_aq)
                C.T.wait(sm["ac"], s.val(sm["ac"]))
                C.T.wait(sm["ve"], s.val(sm["ve"]))
                for st_ in range(qc + 1):
                    mmS = C.T.matmul(pS[st_][0:128, 0:TC],
                                     b.fullk[rb:rb + HD, ct * NSLOT + st_ * 128:ct * NSLOT + (st_ + 1) * 128],
                                     b.aqb[rb:rb + HD, ct * TC:(ct + 1) * TC],
                                     start=True, stop=True)
                    if st_ == qc:
                        mmS.inc(sm["mm"], 1)
                C.V.wait(sm["ve"], v_cp)
                C.V.wait(sm["mm"], s.val(sm["mm"]))
                C.V.tensor_tensor(out=b.qcph[rb:rb + HD, 0:TC],
                                  in0=b.aqb[rb:rb + HD, ct * TC:(ct + 1) * TC],
                                  in1=b.cpkt[rb:rb + HD, 0:TC], op=ALU.mult)
                C.V.drain().inc(sm["ve"], 1)
                v_qcp = s.val(sm["ve"])
                C.T.wait(sm["ve"], v_qcp)
                C.T.matmul(b.pG2[0:1, 0:TC], b.ones_bf[rb:rb + HD, 0:1],
                           b.qcph[rb:rb + HD, 0:TC], start=True, stop=True).inc(sm["mm"], 1)
                v_sc = s.val(sm["mm"])
                C.S.wait(sm["mm"], v_sc)
                C.S.wait(sm["ve"], s.val(sm["ve"]))
                for st_ in range(qc + 1):
                    C.S.activation(out=b.ebuf[:, st_ * TC:(st_ + 1) * TC],
                                   in_=pS[st_][0:128, 0:TC], func=AF.Exp).inc(sm["ac"], 1)
                C.S.activation(out=b.wcur[0:1, 0:TC], in_=b.pG2[0:1, 0:TC],
                               func=AF.Exp).inc(sm["ac"], 1)
                v_e = s.val(sm["ac"])
                C.V.wait(sm["ac"], v_e)
                C.V.tensor_tensor(out=b.ebuf[:, qc * TC:(qc + 1) * TC],
                                  in0=b.ebuf[:, qc * TC:(qc + 1) * TC],
                                  in1=b.bandB[:, 0:TC], op=ALU.mult)
                C.V.drain().inc(sm["ve"], 1)
                v_eb = s.val(sm["ve"])
                C.T.wait(sm["ve"], v_eb)
                for st_ in range(qc + 1):
                    col = (st_ * NH + h) * (HD + 1)
                    mmA = C.T.matmul(b.pE[0:HD + 1, 0:TC], b.vt[:, col:col + HD + 1],
                                     b.ebuf[:, st_ * TC:(st_ + 1) * TC],
                                     start=(st_ == 0), stop=(st_ == qc))
                    if st_ == qc:
                        mmA.inc(sm["mm"], 1)
                v_av = s.val(sm["mm"])
                C.V.wait(sm["mm"], v_av)
                C.V.tensor_tensor(out=b.zinv[0:1, 0:TC], in0=b.pE[HD:HD + 1, 0:TC],
                                  in1=b.wcur[0:1, 0:TC], op=ALU.add)
                C.V.drain()
                C.V.reciprocal(out=b.zinv[0:1, 0:TC], in_=b.zinv[0:1, 0:TC])
                C.V.drain().inc(sm["ve"], 1)
                v_z = s.val(sm["ve"])
                C.T.wait(sm["ve"], v_z)
                C.T.matmul(b.pG2[0:HD, 0:TC], b.onesrow_f[0:1, 0:HD], b.zinv[0:1, 0:TC],
                           start=True, stop=True)
                C.T.matmul(b.pG2[HD:128, 0:TC], b.onesrow_f[0:1, 0:HD], b.wcur[0:1, 0:TC],
                           start=True, stop=True).inc(sm["mm"], 1)
                v_wm = s.val(sm["mm"])
                C.V.wait(sm["mm"], v_wm)
                C.V.tensor_tensor(out=b.corr[rb:rb + HD, 0:TC], in0=b.pG2[HD:128, 0:TC],
                                  in1=b.cpvt[rb:rb + HD, 0:TC], op=ALU.mult)
                C.V.tensor_tensor(out=b.corr[rb:rb + HD, 0:TC], in0=b.corr[rb:rb + HD, 0:TC],
                                  in1=b.pE[0:HD, 0:TC], op=ALU.add)
                C.V.tensor_tensor(out=b.corr[rb:rb + HD, 0:TC], in0=b.corr[rb:rb + HD, 0:TC],
                                  in1=b.pG2[0:HD, 0:TC], op=ALU.mult)
                C.V.tensor_tensor(out=b.o2[rb:rb + HD, ct * TC:(ct + 1) * TC],
                                  in0=b.corr[rb:rb + HD, 0:TC],
                                  in1=b.qpoB[rb:rb + HD, 0:TC], op=ALU.mult)
                C.V.drain().inc(sm["ve"], 1)
            v_o2 = s.val(sm["ve"])
            b.at_kvfree[cg] = v_o2

            C.T.wait(sm["ve"], v_o2)
            for j in range(nd):
                mmo = C.T.matmul(b.pF[0:LORA, 0:TC], b.lc[:, j * LORA:(j + 1) * LORA],
                                 b.o2[:, j * TC:(j + 1) * TC], start=(j == 0), stop=(j == nd - 1))
                if j == nd - 1:
                    mmo.inc(sm["mm"], 1)
            C.V.wait(sm["mm"], s.val(sm["mm"]))
            C.V.tensor_copy(out=b.tmplB[:, 0:TC], in_=b.pF[0:LORA, 0:TC]).inc(sm["ve"], 1)
            v_t2 = s.val(sm["ve"])
            pgs = [b.pA, b.pB, b.pC, b.pD]
            for dt_ in range(nd):
                wo = b.wo_st[dt_ % 2]
                if len(b.at_womark) >= 2:
                    C.G.wait(sm["mm"], b.at_womark[-2])
                C.G.dma_start(out=wo[:, 0:8 * 128],
                              in_=_ap3(dr["wo"], [[D, 128], [128 * D, nd], [1, 128]],
                                       offset=dt_ * 128)).inc(sm["gld"], 16)
                v_wo = s.val(sm["gld"])
                bank = pgs[dt_ % 4]
                C.T.wait(sm["ve"], s.val(sm["ve"]))
                C.T.wait(sm["gld"], v_wo)
                if dt_ == 0:
                    C.T.wait(sm["ve"], v_t2)
                for j in range(nd):
                    C.T.matmul(bank[:, 0:TC], wo[:, j * 128:(j + 1) * 128],
                               b.o2[:, j * TC:(j + 1) * TC], start=(j == 0), stop=False)
                mmY = C.T.matmul(bank[:, 0:TC], b.ld_[:, dt_ * 128:(dt_ + 1) * 128],
                                 b.tmplB[:, 0:TC], start=False, stop=True)
                mmY.inc(sm["mm"], 1)
                b.at_womark.append(s.val(sm["mm"]))
                v_y = s.val(sm["mm"])
                C.V.wait(sm["mm"], v_y)
                C.V.scalar_tensor_tensor(out=b.rout[:, dt_ * TC:(dt_ + 1) * TC],
                                         in0=bank[:, 0:TC], scalar=1.0,
                                         in1=xch[:, dt_ * TC:(dt_ + 1) * TC],
                                         op0=ALU.mult, op1=ALU.add).inc(sm["ve"], 1)
            b.at_cons[cg] = s.val(sm["ve"])
            b.at_tabfree[("b", cg)] = s.val(sm["ve"])
            C.Y.wait(sm["ve"], b.at_cons[cg])
            C.Y.dma_start(out=_ap3(dst, [[T, 128], [128 * T, nd], [1, TC]], offset=cg * TC),
                          in_=b.rout[:, 0:nd * TC]).inc(sm["st"], 16)
            b.at_stored[("r", cg)] = s.val(sm["st"])
        C.barrier(sm["bar"])
    C.Y.wait(sm["st"], b.at_stored[("r", B_LOC * 4 - 1)])


def build_program(phases=("mamba", "ffn1", "attn", "ffn2"), debug_taps=()):
    import os as _os2
    nc = bass.Bass("TRN2", target_bir_lowering=False, detect_race_conditions=False,
                   enable_asserts=not _os2.environ.get("DEV_NOASSERTOFF"))
    C = Ctx(nc)
    dr = {}

    def inp(name, shape, dt=F32):
        dr[name] = nc.dram_tensor(name, list(shape), dt, kind="ExternalInput").ap()

    def scr(name, shape, dt):
        dr[name] = nc.dram_tensor(name, list(shape), dt, kind="Internal").ap()

    inp("x0", [D, T], F32)
    dr["out"] = nc.dram_tensor("out", [D, T], F32, kind="ExternalOutput").ap()
    inp("wfcT", [D, FF]); inp("wprojT", [FF, D])
    inp("consts", [128, 16])
    inp("ones_bf", [128, 1], BF16)
    inp("onesrow_f", [1, 128], F32)
    inp("onesrow_bf", [1, 512], BF16)
    inp("ident", [128, 128], F32)
    inp("triu01", [128, 128], F32)
    inp("winT", [D, PROJ])
    inp("b_in", [1, PROJ])
    inp("conv_w_t", [128, NJT * MK])
    inp("dtb_rep", [128, MH])
    inp("a_log", [MH, 1])
    inp("d_row", [1, INTER])
    inp("normw_row", [1, INTER])
    inp("woutT", [INTER, D])
    inp("b_out", [1, D])
    for w in ("wq", "wk", "wv", "wo"):
        inp(w, [D, D])
    inp("aq", [D, LORA]); inp("bq", [LORA, D])
    inp("av", [D, LORA]); inp("bv", [LORA, D])
    inp("ao", [D, LORA]); inp("bo", [LORA, D])
    inp("cos_rep", [128, S])
    inp("ident2", [128, HD])
    inp("sin_rep", [128, S])
    inp("qpa", [128, S])
    inp("qpo", [128, S])
    inp("cdma_half", [128, MK])
    inp("band01", [128, TC])
    scr("k_scr", [D, T], BF16)
    scr("v_scr", [D, T], BF16)
    scr("wproj_bf", [FF, D], BF16)
    scr("resid1", [D, T], F32)
    scr("resid2", [D, T], F32)
    scr("resid3", [D, T], F32)
    scr("h_scr", [INTER, T], BF16)
    scr("b_scr", [MN, T], BF16)
    scr("c_scr", [MN, T], BF16)
    scr("gate_scr", [T, INTER], BF16)
    scr("g_scr", [T, INTER], BF16)
    scr("dt_scr", [T, MH], F32)
    scr("acum_scr", [MH, T], F32)

    dbg = {}
    for name, shape, dt in debug_taps:
        dbg[name] = nc.dram_tensor(name, list(shape), dt, kind="ExternalOutput").ap()

    from contextlib import ExitStack
    with ExitStack() as es:
        sems = {}
        for name in ("ld", "gld", "sld", "x", "mm", "ve", "ac", "gp", "st", "tr", "bar"):
            sems[name] = es.enter_context(nc.semaphore(name))

        pa_shared = PA(nc)
        b = alloc_shared_bufs(pa_shared)
        pa_ffn = PA(nc)
        alloc_ffn_bufs(pa_ffn, b)
        pa_ffn.close()
        pa_ma = PA(nc)
        alloc_mamba_a_bufs(pa_ma, b)
        pa_ma.close()
        pa_mb = PA(nc)
        alloc_mamba_b_bufs(pa_mb, b)
        pa_mb.close()
        pa_mc = PA(nc)
        alloc_mamba_c_bufs(pa_mc, b)
        pa_mc.close()
        pa_atp = PA(nc)
        alloc_attn_persist(pa_atp, b)
        pa_at_a = PA(nc)
        alloc_attn_a(pa_at_a, b)
        pa_at_a.close()
        pa_at_b = PA(nc)
        alloc_attn_b(pa_at_b, b)
        pa_at_b.close()
        pa_atp.close()
        pa_shared.close()

        block = es.enter_context(nc.Block())

        scratch_of = {"mamba": "resid1", "ffn1": "resid2", "attn": "resid3",
                      "ffn2": "resid1", "mamba_a": "resid1", "mamba_ab": "resid1"}
        cur = "x0"
        plan = []
        for i, ph in enumerate(phases):
            dst = "out" if i == len(phases) - 1 else scratch_of[ph]
            plan.append((ph, cur, dst))
            cur = dst

        def run(engine_name):
            def f(handle):
                C.active = engine_name
                C.handle = handle
                C.s = Sched()
                b.ffn_wpmarks = []
                b.ffn_cons = {}
                b.ffn_stored = {}
                b.ma_cons = {}
                b.ma_stored = {}
                b.mb_cons = {}
                b.mb_stored = 0
                b.mb_difdone = 0
                b.mc_cons = {}
                b.mc_stored = {}
                b.mc_mmdone = {}
                b.at_cons = {}
                b.at_stored = {}
                b.at_kvfree = {}
                b.at_wqmark = []
                b.at_womark = []
                b.at_tabfree = {}
                emit_const_loads(C, dr, b, sems)
                first_ffn = True
                for ph, src, dst in plan:
                    if ph in ("ffn1", "ffn2"):
                        emit_ffn_phase(C, dr, b, sems, dr[src], dr[dst], first_ffn, dbg)
                        first_ffn = False
                    elif ph == "mamba":
                        emit_mamba_a(C, dr, b, sems, dr[src])
                        C.barrier(sems["bar"])
                        emit_mamba_b(C, dr, b, sems, dbg)
                        C.barrier(sems["bar"])
                        emit_mamba_c(C, dr, b, sems, dr[src], dr[dst])
                    elif ph == "mamba_a":
                        emit_mamba_a(C, dr, b, sems, dr[src])
                        C.barrier(sems["bar"])
                        C.G.dma_start(out=_ap3(dr["out"], [[T, 32], [1, T]]),
                                      in_=_ap3(dr["acum_scr"], [[T, 32], [1, T]])).inc(sems["gld"], 16)
                        C.G.dma_start(out=_ap3(dr["out"], [[T, 128], [1, T]], offset=32 * T),
                                      in_=_ap3(dr["b_scr"], [[T, 128], [1, T]])).inc(sems["gld"], 16)
                        C.G.dma_start(out=_ap3(dr["out"], [[T, 128], [1, T]], offset=160 * T),
                                      in_=_ap3(dr["c_scr"], [[T, 128], [1, T]])).inc(sems["gld"], 16)
                        C.G.dma_start(out=_ap3(dr["out"], [[T, 512], [1, T]], offset=288 * T),
                                      in_=_ap3(dr["h_scr"], [[T, 512], [1, T]])).inc(sems["gld"], 16)
                        C.G.dma_start(out=_ap3(dr["out"], [[T, 96], [1, T]], offset=800 * T),
                                      in_=_ap3(dr["h_scr"], [[T, 96], [1, T]], offset=1024 * T)).inc(sems["gld"], 16)
                        C.G.dma_start(out=_ap3(dr["out"], [[T, 32], [1, T]], offset=896 * T),
                                      in_=_ap3(dr["dt_scr"], [[T, 32], [1, T]])).inc(sems["gld"], 16)
                        C.G.wait(sems["gld"], C.s.val(sems["gld"]))
                    elif ph == "mamba_ab":
                        emit_mamba_a(C, dr, b, sems, dr[src])
                        C.barrier(sems["bar"])
                        emit_mamba_b(C, dr, b, sems, dbg)
                        C.barrier(sems["bar"])
                        C.G.dma_start(out=_ap3(dr["out"], [[T, 512], [1, T]]),
                                      in_=_ap3(dr["g_scr"], [[T, 512], [1, T]])).inc(sems["gld"], 16)
                        C.G.wait(sems["gld"], C.s.val(sems["gld"]))
                    elif ph == "attn":
                        emit_attn(C, dr, b, sems, dr[src], dr[dst])
                    C.barrier(sems["bar"])
            return f

        block.sync(run("sync"))
        block.vector(run("vector"))
        block.scalar(run("scalar"))
        block.tensor(run("tensor"))
        block.gpsimd(run("gpsimd"))

    return nc, dbg


def _host_consts():
    inv_freq = 1.0 / (10000.0 ** (np.arange(0, HD, 2, dtype=np.float64) / HD))
    pos = np.arange(S, dtype=np.float64)
    freqs = pos[:, None] * inv_freq[None, :]
    emb = np.concatenate([freqs, freqs], axis=-1)
    cos = np.cos(emb).astype(np.float32).T
    sin = np.sin(emb).astype(np.float32).T
    HAD = np.array([[1, 1, 1, 1], [1, -1, 1, -1], [1, 1, -1, -1], [1, -1, -1, 1]],
                   dtype=np.float32)
    CDMA = np.tile(HAD, (1, HD // 4))
    qphase = CDMA[np.arange(S) % KF]
    consts = np.zeros((128, 16), dtype=np.float32)
    consts[:, 0] = EPS
    for i, bp in enumerate(SPIKE_BPS):
        consts[:, 1 + i] = -np.float32(bp)
    consts[:, 6] = np.float32(SPIKE_BPS[3])
    consts[:, 7] = np.float32(SPIKE_BPS[4])
    consts[:, 8] = 1.0
    consts[:, 9] = -1.0
    return dict(
        cos_rep=np.ascontiguousarray(np.tile(cos, (2, 1))),
        sin_rep=np.ascontiguousarray(np.tile(sin, (2, 1))),
        qpa=np.ascontiguousarray(np.tile(qphase.T, (2, 1)) * 0.125).astype(np.float32),
        qpo=np.ascontiguousarray(np.tile(qphase.T, (2, 1)) * 2.0).astype(np.float32),
        cdma_half=np.ascontiguousarray(np.tile(CDMA.T, (2, 1)) * 0.5).astype(np.float32),
        band01=(np.arange(128)[:, None] < (np.arange(TC) // KF)[None, :]).astype(np.float32),
        triu01=(np.arange(128)[:, None] <= np.arange(128)[None, :]).astype(np.float32),
        ident=np.eye(128, dtype=np.float32),
        ident2=np.ascontiguousarray(np.tile(np.eye(HD, dtype=np.float32), (2, 1))),
        consts=consts,
        ones_bf=np.ones((128, 1), ml_dtypes.bfloat16),
        onesrow_f=np.ones((1, 128), np.float32),
        onesrow_bf=np.ones((1, 128), ml_dtypes.bfloat16),
    )


def _core_in_map(inputs, consts, core):
    x = inputs["L0_attn_a_x"][2 * core:2 * core + 2]
    xT = np.ascontiguousarray(x.reshape(T, D).T)
    m = {
        "x0": xT,
        "wfcT": consts["_wfcT"],
        "wprojT": consts["_wprojT"],
        "consts": consts["consts"],
        "ones_bf": consts["ones_bf"],
        "onesrow_f": consts["onesrow_f"],
        "onesrow_bf": consts["_onesrow_bf512"],
        "ident": consts["ident"],
        "ident2": consts["ident2"],
        "triu01": consts["triu01"],
        "winT": consts["_winT"],
        "b_in": consts["_b_in"],
        "conv_w_t": consts["_conv_w_t"],
        "dtb_rep": consts["_dtb_rep"],
        "a_log": consts["_a_log"],
        "d_row": consts["_d_row"],
        "normw_row": consts["_normw_row"],
        "woutT": consts["_woutT"],
        "b_out": consts["_b_out"],
        "wq": inputs["L0_attn_b_Wq"], "wk": inputs["L0_attn_b_Wk"],
        "wv": inputs["L0_attn_b_Wv"], "wo": inputs["L0_attn_b_Wo"],
        "aq": inputs["L0_attn_b_Aq"], "bq": inputs["L0_attn_b_Bq"],
        "av": inputs["L0_attn_b_Av"], "bv": inputs["L0_attn_b_Bv"],
        "ao": inputs["L0_attn_b_Ao"], "bo": inputs["L0_attn_b_Bo"],
        "cos_rep": consts["cos_rep"], "sin_rep": consts["sin_rep"],
        "qpa": consts["qpa"], "qpo": consts["qpo"],
        "cdma_half": consts["cdma_half"], "band01": consts["band01"],
    }
    return m


def kernel(**inputs):
    c = _host_consts()
    c["_wfcT"] = np.ascontiguousarray(inputs["L0_ffn_W_fc"].T)
    c["_wprojT"] = np.ascontiguousarray(inputs["L0_ffn_W_proj"].T)
    c["_winT"] = np.ascontiguousarray(inputs["L0_attn_a_W_in"].T)
    c["_b_in"] = np.ascontiguousarray(inputs["L0_attn_a_b_in"][None, :])
    c["_conv_w_t"] = np.ascontiguousarray(
        inputs["L0_attn_a_conv_w"].reshape(NJT, 128, MK).transpose(1, 0, 2).reshape(128, NJT * MK))
    c["_dtb_rep"] = np.ascontiguousarray(
        np.broadcast_to(inputs["L0_attn_a_dt_bias"][None, :], (128, MH)))
    c["_a_log"] = np.ascontiguousarray(inputs["L0_attn_a_A_log"][:, None])
    c["_d_row"] = np.ascontiguousarray(np.repeat(inputs["L0_attn_a_D"], MP))[None, :]
    c["_normw_row"] = np.ascontiguousarray(inputs["L0_attn_a_norm_w"][None, :])
    c["_woutT"] = np.ascontiguousarray(inputs["L0_attn_a_W_out"].T)
    c["_b_out"] = np.ascontiguousarray(inputs["L0_attn_a_b_out"][None, :])
    c["_onesrow_bf512"] = np.ones((1, 512), ml_dtypes.bfloat16)

    nc, _ = build_program(phases=("mamba", "ffn1", "attn", "ffn2"))
    in_maps = [_core_in_map(inputs, c, core) for core in range(NCORES)]
    res = run_bass_kernel_spmd(nc, in_maps, core_ids=list(range(NCORES)))
    outs = []
    for r in res.results:
        o = r["out"]
        outs.append(np.ascontiguousarray(o.T).reshape(B_LOC, S, D))
    return np.concatenate(outs, axis=0).astype(np.float32)


# version 04
# speedup vs baseline: 1.2362x; 1.0381x over previous
import numpy as np
import ml_dtypes
import concourse.bass as bass
import concourse.mybir as mybir
from concourse.bass_utils import run_bass_kernel_spmd

F32 = mybir.dt.float32
F32R = mybir.dt.float32r
BF16 = mybir.dt.bfloat16
AF = mybir.ActivationFunctionType
ALU = mybir.AluOpType

NCORES = 8
B_TOT, S, D = 16, 2048, 1024
B_LOC = B_TOT // NCORES
T = B_LOC * S
TC = 512
NCH = T // TC
MH, MP, MN, MK = 32, 64, 128, 4
INTER = MH * MP
CONV = INTER + 2 * MN
PROJ = INTER + CONV + MH
CS = 128
EPS = 1e-5
FF = 4096
NH, HD, KF = 16, 64, 4
NSLOT = S // KF
LORA = 64

ENGINES = ("sync", "vector", "scalar", "tensor", "gpsimd")


def _spike_breakpoints():
    BETA, TH, NS = 0.9, 1.0, 5

    def out_of(y):
        mem = spk = acc = 0.0
        for _ in range(NS):
            mem = BETA * mem + y - spk * TH
            spk = 1.0 if mem > TH else 0.0
            acc += spk
        return acc / NS

    bps, lo = [], -1.0
    for k in range(NS):
        target = (k + 1) / NS
        a, b = lo, 8.0
        for _ in range(200):
            m = 0.5 * (a + b)
            if out_of(m) >= target - 1e-9:
                b = m
            else:
                a = m
        bps.append(0.5 * (a + b))
        lo = bps[-1]
    return bps


SPIKE_BPS = _spike_breakpoints()


class Sched:
    def __init__(self):
        self.c = {}

    def inc(self, sem, n):
        k = sem.name
        self.c[k] = self.c.get(k, 0) + n
        return self.c[k]

    def val(self, sem):
        return self.c.get(sem.name, 0)


class _Instr:
    def __init__(self, ctx, inst, fence):
        self.ctx = ctx
        self.inst = inst
        self.fence = fence

    def inc(self, sem, n=1, fence=None):
        self.ctx.s.inc(sem, n)
        do_fence = self.fence if fence is None else (fence and self.fence)
        if self.inst is not None:
            if do_fence:
                self.ctx.handle.drain().then_inc(sem, n)
            else:
                self.inst.then_inc(sem, n)
        return self


import os as _os
_FENCE_ENGINES = () if _os.environ.get("DEV_NOFENCE") else ("vector", "scalar", "gpsimd")


class Eng:

    def __init__(self, ctx, name):
        self._ctx = ctx
        self._name = name

    @property
    def live(self):
        return self._ctx.active == self._name

    def __getattr__(self, op):
        ctx = self._ctx
        name = self._name

        def call(*args, **kwargs):
            fence = name in _FENCE_ENGINES and not op.startswith("dma")
            if self.live:
                return _Instr(ctx, getattr(ctx.handle, op)(*args, **kwargs), fence)
            return _Instr(ctx, None, fence)

        return call

    def wait(self, sem, val=None):
        if val is None:
            val = self._ctx.s.val(sem)
        if self.live:
            self._ctx.handle.wait_ge(sem, val)

    def dmat(self, out, in_):
        if self.live:
            return _Instr(self._ctx, self._ctx.handle.dma_start_transpose(out=out, in_=in_), False)
        return _Instr(self._ctx, None, False)


class Ctx:
    def __init__(self, nc):
        self.nc = nc
        self.s = None
        self.active = None
        self.handle = None
        self.Y = Eng(self, "sync")
        self.V = Eng(self, "vector")
        self.S = Eng(self, "scalar")
        self.T = Eng(self, "tensor")
        self.G = Eng(self, "gpsimd")

    def barrier(self, bar_sem):
        for e in (self.Y, self.V, self.S, self.T, self.G):
            e.nop().inc(bar_sem, 1)
        tgt = self.s.val(bar_sem)
        for e in (self.Y, self.V, self.S, self.T, self.G):
            e.wait(bar_sem, tgt)


class PA:

    def __init__(self, nc):
        from contextlib import ExitStack
        self.nc = nc
        self.es = ExitStack()

    def sb(self, name, shape, dt):
        return self.es.enter_context(self.nc.sbuf_tensor(name, list(shape), dt))

    def ps(self, name, shape, dt=F32):
        return self.es.enter_context(self.nc.psum_tensor(name, list(shape), dt))

    def close(self):
        self.es.close()


def _ap3(t, ap_dims, offset=0):
    base = t if hasattr(t, "tensor") else t.ap()
    return bass.AP(tensor=base.tensor, offset=base.offset + offset, ap=list(ap_dims))


def emit_ln_chunk(C, b, sm, xch, tc):
    s = C.s
    nd = D // 128
    C.T.wait(sm["ve"], s.val(sm["ve"]))
    C.V.wait(sm["mm"], s.val(sm["mm"]))
    C.V.tensor_copy(out=b.xbf[:, 0:nd * tc], in_=xch[:, 0:nd * tc])
    C.V.tensor_mul(out=b.xsqbf[:, 0:nd * tc], in0=b.xbf[:, 0:nd * tc],
                   in1=b.xbf[:, 0:nd * tc]).inc(sm["ve"], 1)
    v_sq = s.val(sm["ve"])
    C.T.wait(sm["ve"], v_sq)
    for j in range(nd):
        C.T.matmul(b.pA[0:1, 0:tc], b.ones_bf[:, 0:1], b.xbf[:, j * tc:(j + 1) * tc],
                   start=(j == 0), stop=(j == nd - 1))
    for j in range(nd):
        mm = C.T.matmul(b.pB[0:1, 0:tc], b.ones_bf[:, 0:1], b.xsqbf[:, j * tc:(j + 1) * tc],
                        start=(j == 0), stop=(j == nd - 1))
        if j == nd - 1:
            mm.inc(sm["mm"], 1)
    v_stats = s.val(sm["mm"])
    C.V.wait(sm["mm"], v_stats)
    r_mean = b.lnrow[0:1, 0 * TC:0 * TC + tc]
    r_ms = b.lnrow[0:1, 1 * TC:1 * TC + tc]
    r_m2 = b.lnrow[0:1, 2 * TC:2 * TC + tc]
    r_var = r_ms
    r_lnv = r_m2
    C.V.tensor_scalar_mul(out=r_mean, in0=b.pA[0:1, 0:tc], scalar1=1.0 / D)
    C.V.tensor_scalar_mul(out=r_ms, in0=b.pB[0:1, 0:tc], scalar1=1.0 / D)
    C.V.drain()
    C.V.tensor_mul(out=r_m2, in0=r_mean, in1=r_mean)
    C.V.tensor_sub(out=r_var, in0=r_ms, in1=r_m2).inc(sm["ve"], 1)
    v_var = s.val(sm["ve"])
    C.S.wait(sm["ve"], v_var)
    C.S.activation(out=r_lnv, in_=r_var, func=AF.Ln, bias=b.cpool[0:1, 0:1]).inc(sm["ac"], 1)
    C.S.wait(sm["ac"], s.val(sm["ac"]))
    C.S.activation(out=b.lnrow_r[0:1, 0:tc], in_=r_lnv, func=AF.Exp, scale=-0.5).inc(sm["ac"], 1)
    v_r = s.val(sm["ac"])
    C.V.wait(sm["ac"], v_r)
    C.V.tensor_mul(out=b.lnrow_r2[0:1, 0:tc], in0=r_mean, in1=b.lnrow_r[0:1, 0:tc]).inc(sm["ve"], 1)
    v_mr = s.val(sm["ve"])
    C.T.wait(sm["ac"], v_r)
    C.T.matmul(b.pC[:, 0:tc], b.onesrow_f[0:1, :], b.lnrow_r[0:1, 0:tc], start=True, stop=True)
    C.T.wait(sm["ve"], v_mr)
    C.T.matmul(b.pD[:, 0:tc], b.onesrow_f[0:1, :], b.lnrow_r2[0:1, 0:tc],
               start=True, stop=True).inc(sm["mm"], 1)
    v_out = s.val(sm["mm"])
    C.V.wait(sm["mm"], v_out)
    pR_rep = _ap3(b.pC, [[b.pC.ap().ap[0][0], 128], [0, nd], [1, tc]])
    pMR_rep = _ap3(b.pD, [[b.pD.ap().ap[0][0], 128], [0, nd], [1, tc]])
    C.V.tensor_tensor(out=b.scrA[:, 0:nd * tc], in0=xch[:, 0:nd * tc], in1=pR_rep, op=ALU.mult)
    C.V.tensor_tensor(out=b.xn[:, 0:nd * tc], in0=b.scrA[:, 0:nd * tc], in1=pMR_rep,
                      op=ALU.subtract).inc(sm["ve"], 1)
    return s.val(sm["ve"])


class NS:
    pass


DEV_FFN_LEVEL = 3
DEV_FFN_LEVEL2 = None


def emit_ffn_phase(C, dr, bufs, sm, src, dst, first_ffn, dbg=None):
    s = C.s
    b = bufs
    nd = D // 128
    NF = FF // 128
    LV = DEV_FFN_LEVEL if (first_ffn or DEV_FFN_LEVEL2 is None) else DEV_FFN_LEVEL2
    if LV == 3:
        LV = 99

    for j in range(nd):
        src_ap = _ap3(dr["wfcT"], [[FF, 128], [1, FF]], offset=j * 128 * FF)
        C.G.dma_start(out=b.wfc[:, j * FF:(j + 1) * FF], in_=src_ap).inc(sm["gld"], 16)
    if first_ffn:
        for i in range(NF):
            src_ap = _ap3(dr["wprojT"], [[D, 128], [1, D]], offset=i * 128 * D)
            C.G.dma_start(out=b.wpst[:, 0:D], in_=src_ap).inc(sm["gld"], 16)
            v = s.val(sm["gld"])
            C.G.wait(sm["gld"], v)
            dst_ap = _ap3(dr["wproj_bf"], [[D, 128], [1, D]], offset=i * 128 * D)
            C.G.dma_start(out=dst_ap, in_=b.wpst[:, 0:D]).inc(sm["gld"], 16)
    v_wload = s.val(sm["gld"])

    for c in range(NCH):
        cb = c % 2
        xch = b.xch[:, cb * nd * TC:(cb + 1) * nd * TC]
        if c >= 2:
            C.Y.wait(sm["ve"], b.ffn_cons[c - 2])
        src_ap = _ap3(src, [[T, 128], [128 * T, nd], [1, TC]], offset=c * TC)
        C.Y.dma_start(out=xch, in_=src_ap).inc(sm["x"], 16)
        v_x = s.val(sm["x"])
        C.S.wait(sm["x"], v_x)
        C.T.wait(sm["x"], v_x)
        C.V.wait(sm["x"], v_x)

        v_xn = emit_ln_chunk(C, b, sm, xch, TC)
        C.T.wait(sm["ve"], v_xn)
        if c == 0:
            C.T.wait(sm["gld"], v_wload)

        b.ffn_relu_marks = []
        b.ffn_sq_marks = []
        fcbanks = [b.pE, b.pF, b.pA, b.pB]
        for fi in range(NF if LV >= 1 else 0):
            pf = fcbanks[fi % 4]
            if fi >= 4:
                C.T.wait(sm["ac"], b.ffn_relu_marks[fi - 4])
            for j in range(nd):
                mm = C.T.matmul(pf[:, 0:TC], b.wfc[:, j * FF + fi * 128:j * FF + (fi + 1) * 128],
                                b.xn[:, j * TC:(j + 1) * TC], start=(j == 0), stop=(j == nd - 1))
                if j == nd - 1:
                    mm.inc(sm["mm"], 1)
            C.S.wait(sm["mm"], s.val(sm["mm"]))
            if fi >= 4:
                C.S.wait(sm["ve"], b.ffn_sq_marks[fi - 4])
            C.S.activation(out=b.uff[:, (fi % 4) * TC:(fi % 4 + 1) * TC],
                           in_=pf[:, 0:TC], func=AF.Relu).inc(sm["ac"], 1)
            b.ffn_relu_marks.append(s.val(sm["ac"]))
            C.V.wait(sm["ac"], s.val(sm["ac"]))
            uslice = b.uff[:, (fi % 4) * TC:(fi % 4 + 1) * TC]
            C.V.tensor_mul(out=b.hbuf[:, fi * TC:(fi + 1) * TC], in0=uslice,
                           in1=uslice).inc(sm["ve"], 1)
            b.ffn_sq_marks.append(s.val(sm["ve"]))
        v_h = s.val(sm["ve"])

        if c >= 1:
            C.V.wait(sm["st"], b.ffn_stored[c - 1])
        if LV < 2:
            C.V.tensor_copy(out=b.rout[:, 0:nd * TC], in_=xch).inc(sm["ve"], 1)
            b.ffn_cons[c] = s.val(sm["ve"])
            C.Y.wait(sm["ve"], b.ffn_cons[c])
            dst_ap = _ap3(dst, [[T, 128], [128 * T, nd], [1, TC]], offset=c * TC)
            C.Y.dma_start(out=dst_ap, in_=b.rout[:, 0:nd * TC]).inc(sm["st"], 16)
            b.ffn_stored[c] = s.val(sm["st"])
            continue
        for g in range(2):
            C.T.wait(sm["ve"], s.val(sm["ve"]))
            if LV >= 1 and g == 0:
                C.T.wait(sm["ac"], s.val(sm["ac"]))
            for fi in range(NF):
                wb = b.wpb0 if fi % 2 == 0 else b.wpb1
                if len(b.ffn_wpmarks) >= 2:
                    C.Y.wait(sm["mm"], b.ffn_wpmarks[-2])
                if c == 0 and g == 0 and fi == 0:
                    C.Y.wait(sm["gld"], s.val(sm["gld"]))
                ld_ap = _ap3(dr["wproj_bf"], [[D, 128], [1, TC]],
                             offset=fi * 128 * D + g * TC)
                C.Y.dma_start(out=wb[:, 0:TC], in_=ld_ap).inc(sm["sld"], 16)
                v_w = s.val(sm["sld"])
                C.T.wait(sm["sld"], v_w)
                if fi == 0:
                    C.T.wait(sm["ve"], v_h)
                if LV != 15:
                    for jo in range(4):
                        mm = C.T.matmul(b.py[jo][:, 0:TC], wb[:, jo * 128:(jo + 1) * 128],
                                        b.hbuf[:, fi * TC:(fi + 1) * TC],
                                        start=(fi == 0), stop=(fi == NF - 1))
                        if jo == 3:
                            mm.inc(sm["mm"], 1)
                else:
                    C.T.nop().inc(sm["mm"], 1)
                b.ffn_wpmarks.append(s.val(sm["mm"]))
            v_y = s.val(sm["mm"])
            if LV in (15, 17):
                continue
            if LV == 20:
                for jo in range(4):
                    jd = g * 4 + jo
                    C.V.wait(sm["mm"], v_y)
                    C.V.tensor_copy(out=b.rout[:, jd * TC:(jd + 1) * TC],
                                    in_=xch[:, jd * TC:(jd + 1) * TC]).inc(sm["ve"], 1)
                continue
            if LV == 18:
                for jo in range(4):
                    jd = g * 4 + jo
                    C.S.wait(sm["mm"], v_y)
                    C.S.activation(out=b.rout[:, jd * TC:(jd + 1) * TC],
                                   in_=b.py[jo][:, 0:TC], func=AF.Identity).inc(sm["ac"], 1)
                if True:
                    C.V.wait(sm["ac"], s.val(sm["ac"]))
                    C.V.engine_nop().inc(sm["ve"], 1)
                continue
            if LV < 3:
                for jo in range(4):
                    jd = g * 4 + jo
                    C.V.wait(sm["mm"], v_y)
                    C.V.tensor_copy(out=b.rout[:, jd * TC:(jd + 1) * TC],
                                    in_=b.py[jo][:, 0:TC]).inc(sm["ve"], 1)
                continue
            for jo in range(4):
                jd = g * 4 + jo
                pyj = b.py[jo]
                if dbg is not None and "dbg_y" in dbg:
                    C.V.wait(sm["mm"], v_y)
                    C.V.tensor_copy(out=b.rout[:, jd * TC:(jd + 1) * TC],
                                    in_=pyj[:, 0:TC]).inc(sm["ve"], 1)
                    C.Y.wait(sm["ve"], s.val(sm["ve"]))
                    dba = _ap3(dbg["dbg_y"], [[T, 128], [1, TC]],
                               offset=(jd * 128) * T + c * TC)
                    C.Y.dma_start(out=dba, in_=b.rout[:, jd * TC:(jd + 1) * TC]).inc(sm["st"], 16)
                    C.V.wait(sm["st"], s.val(sm["st"]))
                C.S.wait(sm["mm"], v_y)
                C.S.wait(sm["ve"], s.val(sm["ve"]))
                if LV >= 25:
                    for k in range(3):
                        act = C.S.activation(out=b.sgn[k][:, 0:TC], in_=pyj[:, 0:TC], func=AF.Sign,
                                             bias=b.cpool[:, 1 + k:2 + k])
                        if k == 2:
                            act.inc(sm["ac"], 1)
                else:
                    C.S.nop().inc(sm["ac"], 1)
                v_sgn = s.val(sm["ac"])
                C.V.wait(sm["mm"], v_y)
                C.V.wait(sm["ac"], v_sgn)
                if LV >= 26:
                    C.V.tensor_single_scalar(out=b.ind4[:, 0:TC], in_=pyj[:, 0:TC],
                                             scalar=b.cpool[:, 6:7], op=ALU.is_gt)
                    C.V.tensor_single_scalar(out=b.ind5[:, 0:TC], in_=pyj[:, 0:TC],
                                             scalar=b.cpool[:, 7:8], op=ALU.is_gt)
                if LV >= 27:
                    C.V.tensor_add(out=b.sa, in0=b.sgn[0][:, 0:TC], in1=b.sgn[1][:, 0:TC])
                    C.V.tensor_add(out=b.sa, in0=b.sa, in1=b.sgn[2][:, 0:TC])
                    C.V.tensor_add(out=b.sb_, in0=b.ind4[:, 0:TC], in1=b.ind5[:, 0:TC])
                xsl = xch[:, jd * TC:(jd + 1) * TC]
                rsl = b.rout[:, jd * TC:(jd + 1) * TC]
                if LV >= 28:
                    C.V.scalar_tensor_tensor(out=rsl, in0=b.sa, scalar=0.1,
                                             in1=xsl, op0=ALU.mult, op1=ALU.add)
                    C.V.scalar_tensor_tensor(out=rsl, in0=b.sb_, scalar=0.2,
                                             in1=rsl, op0=ALU.mult, op1=ALU.add)
                    C.V.tensor_scalar_add(out=rsl, in0=rsl, scalar1=0.3).inc(sm["ve"], 1)
                else:
                    C.V.tensor_copy(out=rsl, in_=xsl).inc(sm["ve"], 1)
        if LV in (15, 17):
            C.V.wait(sm["mm"], s.val(sm["mm"]))
            C.V.tensor_copy(out=b.rout[:, 0:nd * TC], in_=xch).inc(sm["ve"], 1)
        b.ffn_cons[c] = s.val(sm["ve"])
        C.Y.wait(sm["ve"], b.ffn_cons[c])
        dst_ap = _ap3(dst, [[T, 128], [128 * T, nd], [1, TC]], offset=c * TC)
        C.Y.dma_start(out=dst_ap, in_=b.rout[:, 0:nd * TC]).inc(sm["st"], 16)
        b.ffn_stored[c] = s.val(sm["st"])
        if dbg is not None and c == 0:
            pass
    C.Y.wait(sm["st"], b.ffn_stored[NCH - 1])


def alloc_ffn_bufs(pa, shared):
    b = shared
    b.wfc = pa.sb("wfc", [128, 8 * FF], BF16)
    b.wpst = pa.sb("wpst", [128, D], BF16)
    b.wpb0 = pa.sb("wpb0", [128, TC], BF16)
    b.wpb1 = pa.sb("wpb1", [128, TC], BF16)
    b.hbuf = pa.sb("hbuf", [128, FF // 128 * TC], BF16)
    b.uff = pa.sb("uff", [128, 4 * TC], BF16)
    b.sgn = [pa.sb(f"sgn{k}", [128, TC], BF16) for k in range(3)]
    b.ind4 = pa.sb("ind4", [128, TC], BF16)
    b.ind5 = pa.sb("ind5", [128, TC], BF16)
    import os
    if os.environ.get("DEV_NOALIAS2"):
        b.sa = pa.sb("sa2", [128, TC], F32)[:, 0:TC]
        b.sb_ = pa.sb("sb2", [128, TC], F32)[:, 0:TC]
    else:
        b.sa = b.scrA[:, 0:TC]
        b.sb_ = b.scrA[:, TC:2 * TC]
    b.py = [b.pA, b.pB, b.pC, b.pD]
    b.ffn_wpmarks = []
    b.ffn_cons = {}
    b.ffn_stored = {}
    return b


def alloc_shared_bufs(pa):
    b = NS()
    nd = D // 128
    b.xch = pa.sb("xch", [128, 2 * nd * TC], F32)
    b.scrA = pa.sb("scrA", [128, nd * TC], F32)
    b.xbf = pa.sb("xbf", [128, nd * TC], BF16)
    b.xn = pa.sb("xn", [128, nd * TC], BF16)
    import os
    if os.environ.get("DEV_NOALIAS"):
        b.xsqbf = pa.sb("xsqbf2", [128, nd * TC], BF16)
    else:
        b.xsqbf = b.xn
    b.rout = pa.sb("rout", [128, nd * TC], F32)
    b.lnrow = pa.sb("lnrow", [1, 3 * TC], F32)
    b.lnrow_r = pa.sb("lnrow_r", [1, TC], F32)
    b.lnrow_r2 = pa.sb("lnrow_r2", [1, TC], F32)
    b.cpool = pa.sb("sb_cpool", [128, 16], F32)
    b.ident_sb = pa.sb("sb_ident", [128, 128], F32)
    b.triu_sb = pa.sb("sb_triu", [128, 128], F32)
    b.dtbrep = pa.sb("sb_dtbrep", [128, MH], F32)
    b.ones_bf = pa.sb("sb_ones_bf", [128, 1], BF16)
    b.onesrow_f = pa.sb("sb_onesrow_f", [1, 128], F32)
    b.onesrow_bf = pa.sb("sb_onesrow_bf", [1, 512], BF16)
    b.pA = pa.ps("pA", [128, TC])
    b.pB = pa.ps("pB", [128, TC])
    b.pC = pa.ps("pC", [128, TC])
    b.pD = pa.ps("pD", [128, TC])
    b.pE = pa.ps("pE", [128, TC])
    b.pF = pa.ps("pF", [128, TC])
    return b


def emit_const_loads(C, dr, b, sm):
    C.Y.dma_start(out=b.cpool[:], in_=dr["consts"]).inc(sm["ld"], 16)
    C.Y.dma_start(out=b.ones_bf[:], in_=dr["ones_bf"]).inc(sm["ld"], 16)
    C.Y.dma_start(out=b.onesrow_f[:], in_=dr["onesrow_f"]).inc(sm["ld"], 16)
    C.Y.dma_start(out=b.onesrow_bf[:], in_=dr["onesrow_bf"]).inc(sm["ld"], 16)
    C.Y.dma_start(out=b.ident_sb[:], in_=dr["ident"]).inc(sm["ld"], 16)
    C.Y.dma_start(out=b.triu_sb[:], in_=dr["triu01"]).inc(sm["ld"], 16)
    C.Y.dma_start(out=b.dtbrep[:], in_=dr["dtb_rep"]).inc(sm["ld"], 16)
    v = C.s.val(sm["ld"])
    for e in (C.V, C.S, C.T, C.G):
        e.wait(sm["ld"], v)


NJT = CONV // 128
NHT = INTER // 128


def alloc_mamba_a_bufs(pa, b):
    b.win = pa.sb("win", [128, 8 * PROJ], BF16)
    b.binrow = pa.sb("binrow", [1, PROJ], BF16)
    b.cw = pa.sb("cw", [128, NJT * MK], F32)
    b.hbct = pa.sb("hbct", [128, MK - 1 + TC], F32)
    b.tails = pa.sb("tails", [128, NJT * (MK - 1)], F32)
    b.convo = pa.sb("convo", [128, TC], F32)
    b.hsp = pa.sb("hsp", [128, NHT * TC // 2], BF16)
    b.bcsp = pa.sb("bcsp", [128, 2 * TC], BF16)
    b.gbuf = pa.sb("gbuf", [128, INTER], BF16)
    b.dtt = pa.sb("dtt", [128, 4 * MH], F32)
    b.dts1 = pa.sb("dts1", [128, MH], F32)
    b.dts2 = pa.sb("dts2", [128, MH], F32)
    b.aexp = pa.sb("aexp", [MH, 1], F32)
    b.alog = pa.sb("alog", [MH, 1], F32)
    b.dtc = pa.sb("dtc", [32, TC], F32)
    b.acumc = pa.sb("acumc", [32, TC], F32)
    b.zrow = pa.sb("zrow", [32, 128], F32)
    b.pG = pa.ps("pG", [128, 128])


def alloc_mamba_b_bufs(pa, b):
    b.htok = [pa.sb(f"htok{i}", [128, INTER], BF16) for i in range(2)]
    b.btok = [pa.sb(f"btok{i}", [128, 128], BF16) for i in range(2)]
    b.bchan = [pa.sb(f"bchan{i}", [128, 128], BF16) for i in range(2)]
    b.cchan = [pa.sb(f"cchan{i}", [128, 128], BF16) for i in range(2)]
    b.gchk = [pa.sb(f"gchk{i}", [128, INTER], BF16) for i in range(2)]
    b.dtk = [pa.sb(f"dtk{i}", [128, MH], F32) for i in range(2)]
    b.acu = [pa.sb(f"acu{i}", [MH, 128], F32) for i in range(2)]
    b.arep = [pa.sb("arep0", [128, MH * 128], F32)]
    b.acumT = pa.sb("acumT", [128, MH], F32)
    b.decc = pa.sb("decc", [MH, 128], F32)
    b.sdoc = pa.sb("sdoc", [MH, 128], F32)
    b.decT = pa.sb("decT", [128, MH], F32)
    b.sdoT = pa.sb("sdoT", [128, MH], F32)
    b.escrow = pa.sb("escrow", [1, MH], F32)
    b.escM = pa.sb("escM", [128, MH], F32)
    b.expbf = pa.sb("expbf", [128, MH * 128], BF16)
    b.gmbf = pa.sb("gmbf", [128, 128], BF16)
    b.xx = pa.sb("xx", [128, INTER], BF16)
    b.xx2 = pa.sb("xx2", [128, INTER], BF16)
    b.ssb = [pa.sb(f"ssb{i}", [128, INTER], BF16) for i in range(2)]
    b.stmp = pa.sb("stmp", [128, INTER], BF16)
    b.gbig = pa.sb("gbig", [128, INTER], BF16)
    b.gsq = pa.sb("gsq", [128, 1], F32)
    b.drep = pa.sb("drep", [128, INTER], F32)
    b.nwrep = pa.sb("nwrep", [128, INTER], F32)
    b.droww = pa.sb("droww", [1, INTER], F32)
    b.nwrow = b.droww
    b.pG = pa.ps("pGb", [128, 128])
    b.pE2 = pa.ps("pE2", [128, 128])


def alloc_mamba_c_bufs(pa, b):
    b.wout = pa.sb("wout", [128, NHT * D], BF16)
    b.boutrow = pa.sb("boutrow", [1, D], BF16)
    b.gT = [pa.sb(f"gT{i}", [128, TC], BF16) for i in range(NHT)]


def emit_mamba_a(C, dr, b, sm, src):
    s = C.s
    nd = D // 128
    for j in range(nd):
        sap = _ap3(dr["winT"], [[PROJ, 128], [1, PROJ]], offset=j * 128 * PROJ)
        C.G.dma_start(out=b.win[:, j * PROJ:(j + 1) * PROJ], in_=sap).inc(sm["gld"], 16)
    C.G.dma_start(out=b.binrow[:], in_=dr["b_in"]).inc(sm["gld"], 16)
    C.Y.dma_start(out=b.cw[:], in_=dr["conv_w_t"]).inc(sm["ld"], 16)
    C.Y.dma_start(out=b.alog[:], in_=dr["a_log"]).inc(sm["ld"], 16)
    v_w = s.val(sm["gld"])
    v_l = s.val(sm["ld"])
    C.T.wait(sm["gld"], v_w)
    C.S.wait(sm["ld"], v_l)
    C.S.activation(out=b.aexp[:], in_=b.alog[:], func=AF.Exp).inc(sm["ac"], 1)
    C.V.wait(sm["ac"], s.val(sm["ac"]))
    C.V.tensor_scalar_mul(out=b.aexp[:], in0=b.aexp[:], scalar1=-1.0).inc(sm["ve"], 1)
    C.V.memset(b.zrow[:], 0.0)
    C.V.memset(b.tails[:], 0.0).inc(sm["ve"], 1)

    for c in range(NCH):
        cb = c % 2
        xch = b.xch[:, cb * nd * TC:(cb + 1) * nd * TC]
        if c >= 2:
            C.Y.wait(sm["ve"], b.ma_cons[c - 2])
        sap = _ap3(src, [[T, 128], [128 * T, nd], [1, TC]], offset=c * TC)
        C.Y.dma_start(out=xch, in_=sap).inc(sm["x"], 16)
        v_x = s.val(sm["x"])
        C.S.wait(sm["x"], v_x)
        C.T.wait(sm["x"], v_x)
        C.V.wait(sm["x"], v_x)
        v_xn = emit_ln_chunk(C, b, sm, xch, TC)
        b.ma_cons[c] = v_xn
        C.T.wait(sm["ve"], v_xn)
        if c == 0:
            C.T.wait(sm["gld"], v_w)

        if c >= 1:
            C.T.wait(sm["st"], b.ma_stored[c - 1])
            C.S.wait(sm["st"], b.ma_stored[c - 1])
            C.V.wait(sm["st"], b.ma_stored[c - 1])
        for jt in range(NJT):
            pf = b.pE if jt % 2 == 0 else b.pF
            co = INTER + jt * 128
            if jt >= 2:
                C.T.wait(sm["ve"], s.val(sm["ve"]))
            for j in range(nd):
                C.T.matmul(pf[:, 0:TC], b.win[:, j * PROJ + co:j * PROJ + co + 128],
                           b.xn[:, j * TC:(j + 1) * TC],
                           start=(j == 0), stop=False)
            C.T.matmul(pf[:, 0:TC], b.binrow[0:1, co:co + 128], b.onesrow_bf[0:1, 0:TC],
                       start=False, stop=True).inc(sm["mm"], 1)
            v_p = s.val(sm["mm"])
            C.V.wait(sm["mm"], v_p)
            C.V.wait(sm["ac"], s.val(sm["ac"]))
            C.V.tensor_copy(out=b.hbct[:, MK - 1:MK - 1 + TC], in_=pf[:, 0:TC])
            C.V.tensor_copy(out=b.hbct[:, 0:MK - 1], in_=b.tails[:, jt * (MK - 1):(jt + 1) * (MK - 1)])
            C.V.drain()
            C.V.tensor_scalar(out=b.convo[:, 0:TC], in0=b.hbct[:, 0:TC],
                              scalar1=b.cw[:, jt * MK:jt * MK + 1], scalar2=None, op0=ALU.mult)
            for i in range(1, MK):
                C.V.scalar_tensor_tensor(out=b.convo[:, 0:TC], in0=b.hbct[:, i:i + TC],
                                         scalar=b.cw[:, jt * MK + i:jt * MK + i + 1],
                                         in1=b.convo[:, 0:TC], op0=ALU.mult, op1=ALU.add)
            C.V.tensor_copy(out=b.tails[:, jt * (MK - 1):(jt + 1) * (MK - 1)],
                            in_=b.hbct[:, TC:TC + MK - 1]).inc(sm["ve"], 1)
            v_cv = s.val(sm["ve"])
            C.S.wait(sm["ve"], v_cv)
            if jt < NHT:
                if jt == 8 and True:
                    C.S.wait(sm["st"], s.val(sm["st"]))
                dst = b.hsp[:, (jt % 8) * TC:(jt % 8 + 1) * TC]
            elif jt == NHT:
                dst = b.bcsp[:, 0:TC]
            else:
                dst = b.bcsp[:, TC:2 * TC]
            C.S.activation(out=dst, in_=b.convo[:, 0:TC], func=AF.Silu).inc(sm["ac"], 1)
            if jt == 7 or jt == 15:
                vh = s.val(sm["ac"])
                C.Y.wait(sm["ac"], vh)
                half = 0 if jt == 7 else 1
                hap = _ap3(dr["h_scr"], [[T, 128], [128 * T, 8], [1, TC]],
                           offset=half * 8 * 128 * T + c * TC)
                C.Y.dma_start(out=hap, in_=b.hsp[:, 0:8 * TC]).inc(sm["st"], 16)
        v_sta = s.val(sm["ac"])

        if (c + 1) % (S // TC) == 0:
            C.V.wait(sm["ac"], v_sta)
            C.V.memset(b.tails[:], 0.0).inc(sm["ve"], 1)

        for tt in range(4):
            pg = [b.pA, b.pB, b.pC, b.pD]
            C.T.wait(sm["ac"], s.val(sm["ac"]))
            C.T.wait(sm["ve"], s.val(sm["ve"]))
            for bi in range(4):
                for j in range(nd):
                    C.T.matmul(pg[bi][:, 0:TC], b.xn[:, j * TC + tt * 128:j * TC + tt * 128 + 128],
                               b.win[:, j * PROJ + bi * TC:j * PROJ + (bi + 1) * TC],
                               start=(j == 0), stop=False)
                C.T.matmul(pg[bi][:, 0:TC], b.onesrow_bf[0:1, tt * 128:tt * 128 + 128],
                           b.binrow[0:1, bi * TC:(bi + 1) * TC],
                           start=False, stop=True).inc(sm["mm"], 1)
            for j in range(nd):
                C.T.matmul(b.pG[:, 0:MH], b.xn[:, j * TC + tt * 128:j * TC + tt * 128 + 128],
                           b.win[:, j * PROJ + INTER + CONV:j * PROJ + PROJ],
                           start=(j == 0), stop=False)
            C.T.matmul(b.pG[:, 0:MH], b.onesrow_bf[0:1, tt * 128:tt * 128 + 128],
                       b.binrow[0:1, INTER + CONV:PROJ], start=False, stop=True).inc(sm["mm"], 1)
            v_g = s.val(sm["mm"])
            C.S.wait(sm["mm"], v_g)
            if c > 0 or tt > 0:
                C.S.wait(sm["st"], s.val(sm["st"]))
            for bi in range(4):
                C.S.activation(out=b.gbuf[:, bi * TC:(bi + 1) * TC],
                               in_=pg[bi][:, 0:TC], func=AF.Silu).inc(sm["ac"], 1)
            v_gb = s.val(sm["ac"])
            C.Y.wait(sm["ac"], v_gb)
            gap = _ap3(dr["gate_scr"], [[INTER, 128], [1, INTER]],
                       offset=(c * 4 + tt) * 128 * INTER)
            C.Y.dma_start(out=gap, in_=b.gbuf[:, 0:INTER]).inc(sm["st"], 16)
            C.V.wait(sm["mm"], v_g)
            dsl = b.dtt[:, tt * MH:(tt + 1) * MH]
            C.V.tensor_tensor(out=b.dts2[:, 0:MH], in0=b.pG[:, 0:MH], in1=b.dtbrep[:, 0:MH],
                              op=ALU.add)
            C.V.drain()
            C.V.tensor_scalar_max(out=b.dts1[:, 0:MH], in0=b.dts2[:, 0:MH], scalar1=0.0).inc(sm["ve"], 1)
            v_d1 = s.val(sm["ve"])
            C.S.wait(sm["ve"], v_d1)
            C.S.activation(out=b.dts2[:, 0:MH], in_=b.dts2[:, 0:MH], func=AF.Abs).inc(sm["ac"], 1)
            C.S.wait(sm["ac"], s.val(sm["ac"]))
            C.S.activation(out=b.dts2[:, 0:MH], in_=b.dts2[:, 0:MH], func=AF.Exp,
                           scale=-1.0).inc(sm["ac"], 1)
            C.S.wait(sm["ac"], s.val(sm["ac"]))
            C.S.activation(out=b.dts2[:, 0:MH], in_=b.dts2[:, 0:MH], func=AF.Ln,
                           bias=b.cpool[:, 8:9]).inc(sm["ac"], 1)
            v_d2 = s.val(sm["ac"])
            C.V.wait(sm["ac"], v_d2)
            C.V.drain()
            C.V.tensor_add(out=dsl, in0=b.dts1[:, 0:MH], in1=b.dts2[:, 0:MH]).inc(sm["ve"], 1)
            v_dt = s.val(sm["ve"])
            C.T.wait(sm["ve"], v_dt)
            C.T.transpose(b.pG[0:MH, 0:128], dsl, identity=b.ident_sb[:, 0:128]).inc(sm["mm"], 1)
            v_tr = s.val(sm["mm"])
            C.V.wait(sm["mm"], v_tr)
            C.V.tensor_copy(out=b.dtc[:, tt * 128:(tt + 1) * 128], in_=b.pG[0:MH, 0:128])
            C.V.drain()
            C.V.tensor_scalar_mul(out=b.dtc[:, tt * 128:(tt + 1) * 128],
                                  in0=b.dtc[:, tt * 128:(tt + 1) * 128],
                                  scalar1=b.aexp[0:MH, 0:1])
            C.V.tensor_tensor_scan(out=b.acumc[0:MH, tt * 128:(tt + 1) * 128],
                                   data0=b.zrow[0:MH, 0:128],
                                   data1=b.dtc[0:MH, tt * 128:(tt + 1) * 128],
                                   initial=0.0, op0=ALU.add, op1=ALU.add).inc(sm["ve"], 1)
        v_chunk = s.val(sm["ve"])

        C.Y.wait(sm["ac"], v_sta)
        bap = _ap3(dr["b_scr"], [[T, 128], [1, TC]], offset=c * TC)
        C.Y.dma_start(out=bap, in_=b.bcsp[:, 0:TC]).inc(sm["st"], 16)
        cap = _ap3(dr["c_scr"], [[T, 128], [1, TC]], offset=c * TC)
        C.Y.dma_start(out=cap, in_=b.bcsp[:, TC:2 * TC]).inc(sm["st"], 16)
        C.Y.wait(sm["ve"], v_chunk)
        dap = _ap3(dr["dt_scr"], [[MH, 128], [128 * MH, 4], [1, MH]], offset=c * 4 * 128 * MH)
        C.Y.dma_start(out=dap, in_=b.dtt[:, 0:4 * MH]).inc(sm["st"], 16)
        aap = _ap3(dr["acum_scr"], [[T, MH], [1, TC]], offset=c * TC)
        C.Y.dma_start(out=aap, in_=b.acumc[0:MH, 0:TC]).inc(sm["st"], 16)
        b.ma_stored[c] = s.val(sm["st"])
    C.Y.wait(sm["st"], b.ma_stored[NCH - 1])


def emit_mamba_b(C, dr, b, sm, dbg=None):
    s = C.s
    NU = S // CS
    C.Y.dma_start(out=b.droww[:], in_=dr["d_row"]).inc(sm["ld"], 16)
    v_l = s.val(sm["ld"])
    C.T.wait(sm["ld"], v_l)
    C.T.matmul(b.pA[:, 0:TC], b.onesrow_f[0:1, :], b.droww[0:1, 0:INTER // 4], start=True, stop=True)
    C.T.matmul(b.pB[:, 0:TC], b.onesrow_f[0:1, :], b.droww[0:1, INTER // 4:INTER // 2], start=True, stop=True)
    C.T.matmul(b.pC[:, 0:TC], b.onesrow_f[0:1, :], b.droww[0:1, INTER // 2:3 * INTER // 4], start=True, stop=True)
    C.T.matmul(b.pD[:, 0:TC], b.onesrow_f[0:1, :], b.droww[0:1, 3 * INTER // 4:INTER], start=True, stop=True).inc(sm["mm"], 1)
    v_mm = s.val(sm["mm"])
    C.V.wait(sm["mm"], v_mm)
    for q in range(4):
        cp = C.V.tensor_copy(out=b.drep[:, q * TC:(q + 1) * TC],
                             in_=[b.pA, b.pB, b.pC, b.pD][q][:, 0:TC])
        if q == 3:
            cp.inc(sm["ve"], 1)
    v_dr = s.val(sm["ve"])
    C.Y.wait(sm["ve"], v_dr)
    C.Y.dma_start(out=b.nwrow[:], in_=dr["normw_row"]).inc(sm["ld"], 16)
    v_l2 = s.val(sm["ld"])
    C.T.wait(sm["ld"], v_l2)
    C.T.wait(sm["ve"], v_dr)
    C.T.matmul(b.pA[:, 0:TC], b.onesrow_f[0:1, :], b.nwrow[0:1, 0:TC], start=True, stop=True)
    C.T.matmul(b.pB[:, 0:TC], b.onesrow_f[0:1, :], b.nwrow[0:1, TC:2 * TC], start=True, stop=True)
    C.T.matmul(b.pC[:, 0:TC], b.onesrow_f[0:1, :], b.nwrow[0:1, 2 * TC:3 * TC], start=True, stop=True)
    C.T.matmul(b.pD[:, 0:TC], b.onesrow_f[0:1, :], b.nwrow[0:1, 3 * TC:INTER], start=True, stop=True).inc(sm["mm"], 1)
    v_mm2 = s.val(sm["mm"])
    C.V.wait(sm["mm"], v_mm2)
    for q in range(4):
        C.V.tensor_copy(out=b.nwrep[:, q * TC:(q + 1) * TC], in_=[b.pA, b.pB, b.pC, b.pD][q][:, 0:TC]).inc(sm["ve"], 1)
    v_nw = s.val(sm["ve"])

    for bi in range(B_LOC):
        C.V.wait(sm["mm"], s.val(sm["mm"]))
        C.V.memset(b.ssb[0][:], 0.0).inc(sm["ve"], 1)
        for u in range(NU):
            ub = u % 2
            tof = bi * S + u * CS
            sp, sn = b.ssb[u % 2], b.ssb[(u + 1) % 2]
            if u >= 2:
                C.Y.wait(sm["ve"], b.mb_cons[(bi, u - 2)])
                C.S.wait(sm["ve"], b.mb_cons[(bi, u - 2)])
            hin = _ap3(dr["h_scr"], [[T, INTER], [1, CS]], offset=tof)
            C.Y.dmat(out=b.htok[ub][:, 0:INTER], in_=hin).inc(sm["tr"], 16)
            bin_ = _ap3(dr["b_scr"], [[T, 128], [1, CS]], offset=tof)
            C.Y.dma_start(out=b.bchan[ub][:], in_=bin_).inc(sm["x"], 16)
            C.Y.dmat(out=b.btok[ub][:], in_=bin_).inc(sm["tr"], 16)
            cin = _ap3(dr["c_scr"], [[T, 128], [1, CS]], offset=tof)
            C.Y.dma_start(out=b.cchan[ub][:], in_=cin).inc(sm["x"], 16)
            gin = _ap3(dr["gate_scr"], [[INTER, 128], [1, INTER]], offset=tof * INTER)
            C.Y.dma_start(out=b.gchk[ub][:], in_=gin).inc(sm["x"], 16)
            din = _ap3(dr["dt_scr"], [[MH, 128], [1, MH]], offset=tof * MH)
            C.Y.dma_start(out=b.dtk[ub][:], in_=din).inc(sm["x"], 16)
            ain = _ap3(dr["acum_scr"], [[T, MH], [1, CS]], offset=tof)
            C.Y.dma_start(out=b.acu[ub][:], in_=ain).inc(sm["x"], 16)
            arin = _ap3(dr["acum_scr"], [[0, 128], [T, MH], [1, CS]], offset=tof)
            C.S.wait(sm["ve"], b.mb_difdone)
            C.S.dma_start(out=b.arep[0][:, 0:MH * 128], in_=arin).inc(sm["sld"], 16)
            v_ld = s.val(sm["x"])
            v_tr = s.val(sm["tr"])
            v_ar = s.val(sm["sld"])

            C.T.wait(sm["x"], v_ld)
            C.T.wait(sm["ve"], s.val(sm["ve"]))
            C.T.transpose(b.pG[0:128, 0:MH], b.acu[ub][0:MH, 0:128],
                          identity=b.ident_sb[0:MH, 0:MH]).inc(sm["mm"], 1)
            vt1 = s.val(sm["mm"])
            C.V.wait(sm["mm"], vt1)
            C.V.tensor_copy(out=b.acumT[:, 0:MH], in_=b.pG[0:128, 0:MH]).inc(sm["ve"], 1)
            vacT = s.val(sm["ve"])
            C.S.wait(sm["x"], v_ld)
            C.S.activation(out=b.decc[0:MH, 0:128], in_=b.acu[ub][0:MH, 0:128],
                           func=AF.Exp, scale=-1.0, bias=b.acu[ub][0:MH, CS - 1:CS])
            C.S.activation(out=b.sdoc[0:MH, 0:128], in_=b.acu[ub][0:MH, 0:128],
                           func=AF.Exp).inc(sm["ac"], 1)
            vexp = s.val(sm["ac"])
            C.T.wait(sm["ac"], vexp)
            C.T.wait(sm["ve"], vacT)
            C.T.transpose(b.pG[0:128, 0:MH], b.decc[0:MH, 0:128],
                          identity=b.ident_sb[0:MH, 0:MH]).inc(sm["mm"], 1)
            C.V.wait(sm["mm"], s.val(sm["mm"]))
            C.V.tensor_copy(out=b.decT[:, 0:MH], in_=b.pG[0:128, 0:MH]).inc(sm["ve"], 1)
            C.T.wait(sm["ve"], s.val(sm["ve"]))
            C.T.transpose(b.pG[0:128, 0:MH], b.sdoc[0:MH, 0:128],
                          identity=b.ident_sb[0:MH, 0:MH]).inc(sm["mm"], 1)
            C.V.wait(sm["mm"], s.val(sm["mm"]))
            C.V.tensor_copy(out=b.sdoT[:, 0:MH], in_=b.pG[0:128, 0:MH]).inc(sm["ve"], 1)
            C.T.wait(sm["ve"], s.val(sm["ve"]))
            C.T.transpose(b.pG[0:1, 0:MH], b.sdoc[0:MH, CS - 1:CS],
                          identity=b.ident_sb[0:MH, 0:MH]).inc(sm["mm"], 1)
            C.V.wait(sm["mm"], s.val(sm["mm"]))
            C.V.tensor_copy(out=b.escrow[0:1, 0:MH], in_=b.pG[0:1, 0:MH]).inc(sm["ve"], 1)
            vesc = s.val(sm["ve"])
            C.T.wait(sm["ve"], vesc)
            C.T.matmul(b.pG[:, 0:MH], b.onesrow_f[0:1, :], b.escrow[0:1, 0:MH],
                       start=True, stop=True).inc(sm["mm"], 1)
            C.V.wait(sm["mm"], s.val(sm["mm"]))
            C.V.tensor_copy(out=b.escM[:, 0:MH], in_=b.pG[:, 0:MH]).inc(sm["ve"], 1)
            vescM = s.val(sm["ve"])

            if not (bi == 0 and u == 0):
                C.V.wait(sm["st"], b.mb_stored)
            C.S.wait(sm["mm"], s.val(sm["mm"]))
            C.V.wait(sm["sld"], v_ar)
            arr = b.arep[0]
            acT_rep = _ap3(b.acumT, [[b.acumT.ap().ap[0][0], 128], [1, MH], [0, 128]])
            C.V.tensor_tensor(out=b.scrA[:, 0:MH * 128], in0=arr[:, 0:MH * 128],
                              in1=acT_rep, op=ALU.subtract)
            C.V.tensor_scalar_min(out=b.scrA[:, 0:MH * 128], in0=b.scrA[:, 0:MH * 128],
                                  scalar1=0.0).inc(sm["ve"], 1)
            vdif = s.val(sm["ve"])
            b.mb_difdone = vdif
            if dbg is not None and bi == 0 and u == 0 and "dbg_dif2" in dbg:
                C.G.wait(sm["ve"], vdif)
                C.G.dma_start(out=dbg["dbg_dif2"], in_=b.scrA[:, 0:MH * 128]).inc(sm["gld"], 16)
                C.V.wait(sm["gld"], s.val(sm["gld"]))
            C.S.wait(sm["ve"], vdif)
            C.S.activation(out=b.expbf[:, 0:MH * 128], in_=b.scrA[:, 0:MH * 128],
                           func=AF.Exp).inc(sm["ac"], 1)
            vexp2 = s.val(sm["ac"])
            C.T.wait(sm["tr"], v_tr)
            C.T.wait(sm["ve"], s.val(sm["ve"]))
            C.T.matmul(b.pE2[:, 0:128], b.bchan[ub][:], b.cchan[ub][:],
                       start=True, stop=True).inc(sm["mm"], 1)
            vgm = s.val(sm["mm"])
            C.V.wait(sm["mm"], vgm)
            C.V.tensor_tensor(out=b.gmbf[:], in0=b.pE2[:, 0:128], in1=b.triu_sb[:],
                              op=ALU.mult).inc(sm["ve"], 1)
            vgmb = s.val(sm["ve"])
            if dbg is not None and bi == 0 and u == 0 and "dbg_gm" in dbg:
                C.G.wait(sm["ve"], vgmb)
                C.G.dma_start(out=dbg["dbg_gm"], in_=b.gmbf[:, 0:128]).inc(sm["gld"], 16)
                C.V.wait(sm["gld"], s.val(sm["gld"]))
            C.V.wait(sm["ac"], vexp2)
            gm_rep = _ap3(b.gmbf, [[b.gmbf.ap().ap[0][0], 128], [0, MH], [1, 128]])
            C.V.tensor_tensor(out=b.expbf[:, 0:MH * 128], in0=b.expbf[:, 0:MH * 128],
                              in1=gm_rep, op=ALU.mult).inc(sm["ve"], 1)
            vM = s.val(sm["ve"])

            C.V.wait(sm["tr"], v_tr)
            C.V.wait(sm["x"], v_ld)
            dt_rep = _ap3(b.dtk[ub], [[b.dtk[ub].ap().ap[0][0], 128], [1, MH], [0, MP]])
            C.V.tensor_tensor(out=b.xx[:, 0:INTER], in0=b.htok[ub][:, 0:INTER],
                              in1=dt_rep, op=ALU.mult)
            dec_rep = _ap3(b.decT, [[b.decT.ap().ap[0][0], 128], [1, MH], [0, MP]])
            C.V.tensor_tensor(out=b.xx2[:, 0:INTER], in0=b.xx[:, 0:INTER],
                              in1=dec_rep, op=ALU.mult).inc(sm["ve"], 1)
            vX = s.val(sm["ve"])

            for hf in range(2):
                hof = hf * (INTER // 2)
                pY = [b.pA, b.pB] if hf == 0 else [b.pA, b.pB]
                pO = [b.pC, b.pD]
                C.T.wait(sm["ve"], s.val(sm["ve"]))
                for hh in range(16):
                    h = hf * 16 + hh
                    bank = pY[hh // 8]
                    mm = C.T.matmul(bank[:, (hh % 8) * MP:(hh % 8) * MP + MP],
                                    b.expbf[:, h * 128:(h + 1) * 128],
                                    b.xx[:, h * MP:(h + 1) * MP],
                                    start=(hh % 8 == 0), stop=(hh % 8 == 7))
                    if hh == 15:
                        mm.inc(sm["mm"], 1)
                if u > 0:
                    for q in range(2):
                        mm = C.T.matmul(pO[q][:, 0:TC], b.cchan[ub][:],
                                        sp[:, hof + q * TC:hof + (q + 1) * TC],
                                        start=True, stop=True)
                        if q == 1:
                            mm.inc(sm["mm"], 1)
                vy = s.val(sm["mm"])
                C.V.wait(sm["mm"], vy)
                t1a = b.scrA[:, 0:TC]
                t1b = b.scrA[:, TC:2 * TC]
                if u > 0:
                    pst = b.sdoT.ap().ap[0][0]
                    sdo_rep = _ap3(b.sdoT, [[pst, 128], [1, 8], [0, MP]], offset=hf * 16)
                    C.V.tensor_tensor(out=t1a, in0=b.pC[:, 0:TC], in1=sdo_rep, op=ALU.mult)
                    sdo_rep2 = _ap3(b.sdoT, [[pst, 128], [1, 8], [0, MP]], offset=hf * 16 + 8)
                    C.V.tensor_tensor(out=t1b, in0=b.pD[:, 0:TC], in1=sdo_rep2, op=ALU.mult)
                C.G.wait(sm["ve"], s.val(sm["ve"]))
                C.G.tensor_tensor(out=b.stmp[:, 0:2 * TC], in0=b.htok[ub][:, hof:hof + 2 * TC],
                                  in1=b.drep[:, hof:hof + 2 * TC], op=ALU.mult).inc(sm["gp"], 1)
                vt2 = s.val(sm["gp"])
                for q in range(2):
                    pyq = [b.pA, b.pB][q]
                    ysl = b.scrA[:, (2 + q) * TC:(3 + q) * TC]
                    if u > 0:
                        tq = [t1a, t1b][q]
                        C.V.scalar_tensor_tensor(out=ysl, in0=pyq[:, 0:TC], scalar=1.0,
                                                 in1=tq, op0=ALU.mult, op1=ALU.add)
                    else:
                        C.V.tensor_copy(out=ysl, in_=pyq[:, 0:TC])
                C.V.wait(sm["gp"], vt2)
                for q in range(2):
                    ysl = b.scrA[:, (2 + q) * TC:(3 + q) * TC]
                    C.V.tensor_tensor(out=ysl, in0=ysl, in1=b.stmp[:, q * TC:(q + 1) * TC],
                                      op=ALU.add)
                    C.V.tensor_tensor(out=b.gbig[:, hof + q * TC:hof + (q + 1) * TC],
                                      in0=ysl, in1=b.gchk[ub][:, hof + q * TC:hof + (q + 1) * TC],
                                      op=ALU.mult).inc(sm["ve"], 1)
            vg = s.val(sm["ve"])
            if dbg is not None and bi == 0 and u == 0 and "dbg_m" in dbg:
                C.G.wait(sm["ve"], vg)
                C.G.dma_start(out=dbg["dbg_h"], in_=b.htok[ub][:, 0:INTER]).inc(sm["gld"], 16)
                C.G.dma_start(out=dbg["dbg_dt"], in_=b.dtk[ub][:, 0:MH]).inc(sm["gld"], 16)
                C.G.dma_start(out=dbg["dbg_m"], in_=b.expbf[:, 0:MH * 128]).inc(sm["gld"], 16)
                C.G.dma_start(out=dbg["dbg_x"], in_=b.xx[:, 0:INTER]).inc(sm["gld"], 16)
                C.G.dma_start(out=dbg["dbg_g"], in_=b.gbig[:, 0:INTER]).inc(sm["gld"], 16)
                C.G.dma_start(out=dbg["dbg_dif"], in_=b.scrA[:, 2 * TC:4 * TC]).inc(sm["gld"], 16)
                C.G.wait(sm["gld"], s.val(sm["gld"]))
                vtap = s.val(sm["gld"])
                C.V.wait(sm["gld"], vtap)
                C.S.wait(sm["gld"], vtap)
                C.T.wait(sm["gld"], vtap)
                C.Y.wait(sm["gld"], vtap)

            C.T.wait(sm["ve"], vg)
            C.T.wait(sm["tr"], v_tr)
            pS = [b.pA, b.pB, b.pC, b.pD]
            for q in range(4):
                mm = C.T.matmul(pS[q][:, 0:TC], b.btok[ub][:],
                                b.xx2[:, q * TC:(q + 1) * TC], start=True, stop=True)
                if q == 3:
                    mm.inc(sm["mm"], 1)
            vst = s.val(sm["mm"])
            C.G.wait(sm["ve"], s.val(sm["ve"]))
            esc_rep = _ap3(b.escM, [[b.escM.ap().ap[0][0], 128], [1, MH], [0, MP]])
            C.G.tensor_tensor(out=b.stmp[:, 0:INTER], in0=sp[:, 0:INTER],
                              in1=esc_rep, op=ALU.mult).inc(sm["gp"], 1)
            vsc = s.val(sm["gp"])
            C.V.wait(sm["mm"], vst)
            C.V.wait(sm["gp"], vsc)
            for q in range(4):
                C.V.tensor_tensor(out=sn[:, q * TC:(q + 1) * TC], in0=pS[q][:, 0:TC],
                                  in1=b.stmp[:, q * TC:(q + 1) * TC], op=ALU.add).inc(sm["ve"], 1)
            vS = s.val(sm["ve"])

            C.V.tensor_mul(out=b.scrA[:, 0:INTER], in0=b.gbig[:, 0:INTER],
                           in1=b.gbig[:, 0:INTER])
            C.V.reduce_sum(out=b.gsq[:, 0:1], in_=b.scrA[:, 0:INTER],
                           axis=mybir.AxisListType.X).inc(sm["ve"], 1)
            vms = s.val(sm["ve"])
            C.S.wait(sm["ve"], vms)
            C.S.activation(out=b.gsq[:, 0:1], in_=b.gsq[:, 0:1], func=AF.Ln,
                           scale=1.0 / INTER, bias=b.cpool[:, 0:1]).inc(sm["ac"], 1)
            C.S.wait(sm["ac"], s.val(sm["ac"]))
            C.S.activation(out=b.gsq[:, 0:1], in_=b.gsq[:, 0:1], func=AF.Exp,
                           scale=-0.5).inc(sm["ac"], 1)
            vrs = s.val(sm["ac"])
            C.V.wait(sm["ac"], vrs)
            C.V.wait(sm["ve"], v_nw)
            C.V.tensor_scalar_mul(out=b.gbig[:, 0:INTER], in0=b.gbig[:, 0:INTER],
                                  scalar1=b.gsq[:, 0:1])
            C.V.tensor_mul(out=b.gbig[:, 0:INTER], in0=b.gbig[:, 0:INTER],
                           in1=b.nwrep[:, 0:INTER]).inc(sm["ve"], 1)
            vgf = s.val(sm["ve"])
            b.mb_cons[(bi, u)] = vgf
            C.Y.wait(sm["ve"], vgf)
            gsp = _ap3(dr["g_scr"], [[INTER, 128], [1, INTER]], offset=tof * INTER)
            C.Y.dma_start(out=gsp, in_=b.gbig[:, 0:INTER]).inc(sm["st"], 16)
            b.mb_stored = s.val(sm["st"])
    C.Y.wait(sm["st"], b.mb_stored)


def emit_mamba_c(C, dr, b, sm, src, dst):
    s = C.s
    nd = D // 128
    for ci in range(NHT):
        sap = _ap3(dr["woutT"], [[D, 128], [1, D]], offset=ci * 128 * D)
        C.G.dma_start(out=b.wout[:, ci * D:(ci + 1) * D], in_=sap).inc(sm["gld"], 16)
    C.G.dma_start(out=b.boutrow[:], in_=dr["b_out"]).inc(sm["gld"], 16)
    v_w = s.val(sm["gld"])
    C.T.wait(sm["gld"], v_w)
    for c in range(NCH):
        cb = c % 2
        xch = b.xch[:, cb * nd * TC:(cb + 1) * nd * TC]
        if c >= 2:
            C.Y.wait(sm["ve"], b.mc_cons[c - 2])
        sap = _ap3(src, [[T, 128], [128 * T, nd], [1, TC]], offset=c * TC)
        C.Y.dma_start(out=xch, in_=sap).inc(sm["x"], 16)
        v_x = s.val(sm["x"])
        if c >= 1:
            C.Y.wait(sm["mm"], b.mc_mmdone[c - 1])
        for ci in range(NHT):
            gin = _ap3(dr["g_scr"], [[INTER, TC], [1, 128]], offset=c * TC * INTER + ci * 128)
            C.Y.dmat(out=b.gT[ci][:, 0:TC], in_=gin).inc(sm["tr"], 16)
        v_tr = s.val(sm["tr"])
        C.T.wait(sm["tr"], v_tr)
        pg = [b.pA, b.pB, b.pC, b.pD]
        for dt_ in range(nd):
            bank = pg[dt_ % 4]
            if dt_ >= 4:
                C.T.wait(sm["ve"], s.val(sm["ve"]))
            for ci in range(NHT):
                C.T.matmul(bank[:, 0:TC], b.wout[:, ci * D + dt_ * 128:ci * D + (dt_ + 1) * 128],
                           b.gT[ci][:, 0:TC], start=(ci == 0), stop=False)
            mm = C.T.matmul(bank[:, 0:TC], b.boutrow[0:1, dt_ * 128:(dt_ + 1) * 128],
                            b.onesrow_bf[0:1, 0:TC], start=False, stop=True)
            mm.inc(sm["mm"], 1)
            v_y = s.val(sm["mm"])
            C.V.wait(sm["mm"], v_y)
            if dt_ == 0:
                C.V.wait(sm["x"], v_x)
            if c >= 1 and dt_ == 0:
                C.V.wait(sm["st"], b.mc_stored[c - 1])
            C.V.scalar_tensor_tensor(out=b.rout[:, dt_ * TC:(dt_ + 1) * TC],
                                     in0=bank[:, 0:TC], scalar=1.0,
                                     in1=xch[:, dt_ * TC:(dt_ + 1) * TC],
                                     op0=ALU.mult, op1=ALU.add).inc(sm["ve"], 1)
        b.mc_mmdone[c] = s.val(sm["mm"])
        b.mc_cons[c] = s.val(sm["ve"])
        C.Y.wait(sm["ve"], b.mc_cons[c])
        dap = _ap3(dst, [[T, 128], [128 * T, nd], [1, TC]], offset=c * TC)
        C.Y.dma_start(out=dap, in_=b.rout[:, 0:nd * TC]).inc(sm["st"], 16)
        b.mc_stored[c] = s.val(sm["st"])
    C.Y.wait(sm["st"], b.mc_stored[NCH - 1])


def alloc_attn_persist(pa, b):
    b.fullk = pa.sb("fullk", [128, 8 * NSLOT], BF16)
    b.fullv = pa.sb("fullv", [128, 8 * NSLOT], BF16)
    b.vt = pa.sb("vt", [128, 4 * NH * (HD + 1)], BF16)
    b.pG2 = pa.ps("pG2", [128, TC])
    b.pT = pa.ps("pT", [128, HD], BF16)


def alloc_attn_a(pa, b):
    b.wA = pa.sb("wA", [128, 8 * D], BF16)
    b.wB = pa.sb("wB", [128, 8 * D], BF16)
    b.laA = pa.sb("laA", [128, 8 * LORA], BF16)
    b.lbA = pa.sb("lbA", [LORA, D], BF16)
    b.kc = pa.sb("kc", [128, 8 * TC], BF16)
    b.vc = pa.sb("vc", [128, 8 * TC], BF16)
    b.tmplA = pa.sb("tmplA", [LORA, TC], BF16)
    b.rpbA = pa.sb("rpbA", [128, TC], BF16)
    b.rkbA = pa.sb("rkbA", [128, TC], BF16)
    b.cosA = pa.sb("cosA", [128, TC], BF16)
    b.sinA = pa.sb("sinA", [128, TC], BF16)
    b.cdmaA = pa.sb("cdmaA", [128, MK], F32)
    b.identbf = pa.sb("identbf", [128, HD], BF16)


def alloc_attn_b(pa, b):
    b.wq_st = [pa.sb(f"wqst{i}", [128, 8 * 128], BF16) for i in range(2)]
    b.wo_st = [pa.sb(f"wost{i}", [128, 8 * 128], BF16) for i in range(2)]
    b.laB = pa.sb("laB", [128, 8 * LORA], BF16)
    b.lbB = pa.sb("lbB", [LORA, D], BF16)
    b.lc = pa.sb("lc", [128, 8 * LORA], BF16)
    b.ld_ = pa.sb("ld_", [LORA, D], BF16)
    b.kcB = pa.sb("kcB", [128, 8 * TC], BF16)
    b.vcB = pa.sb("vcB", [128, 8 * TC], BF16)
    b.cpkt = pa.sb("cpkt", [128, TC], BF16)
    b.cpvt = pa.sb("cpvt", [128, TC], BF16)
    b.aqb = pa.sb("aqb", [128, 8 * TC], BF16)
    b.o2 = pa.sb("o2", [128, 8 * TC], BF16)
    b.ebuf = pa.sb("ebuf", [128, 4 * TC], BF16)
    b.tmplB = pa.sb("tmplB", [LORA, TC], BF16)
    b.rpbB = pa.sb("rpbB", [128, TC], BF16)
    b.rkbB = pa.sb("rkbB", [128, TC], BF16)
    b.zrow1 = pa.sb("zrow1", [1, TC], F32)
    b.wcur = pa.sb("wcur", [1, TC], F32)
    b.zinv = pa.sb("zinv", [1, TC], F32)
    b.qcph = pa.sb("qcph", [128, TC], BF16)
    b.corr = pa.sb("corr", [128, TC], F32)
    b.cosB = pa.sb("cosB", [128, TC], BF16)
    b.sinB = pa.sb("sinB", [128, TC], BF16)
    b.qpaB = pa.sb("qpaB", [128, TC], BF16)
    b.qpoB = pa.sb("qpoB", [128, TC], BF16)
    b.bandB = pa.sb("bandB", [128, TC], F32)
    b.cdmaB = pa.sb("cdmaB", [128, MK], F32)


def _rope(C, b, sm, psum, cos_sl, sin_sl, rpb, rkb, outsl, vref):
    C.V.wait(sm["mm"], vref)
    for base in (0, 64):
        C.V.tensor_scalar_mul(out=rpb[base:base + 32, 0:TC],
                              in0=psum[base + 32:base + 64, 0:TC], scalar1=-1.0)
        C.V.tensor_copy(out=rpb[base + 32:base + 64, 0:TC],
                        in_=psum[base:base + 32, 0:TC])
    C.V.tensor_tensor(out=rkb[:, 0:TC], in0=psum[:, 0:TC], in1=cos_sl, op=ALU.mult)
    C.V.tensor_tensor(out=rpb[:, 0:TC], in0=rpb[:, 0:TC], in1=sin_sl, op=ALU.mult)
    C.V.tensor_tensor(out=outsl, in0=rkb[:, 0:TC], in1=rpb[:, 0:TC], op=ALU.add)


def _cdma_taps(C, b, sm, cdma, srcsl, dstsl, cumulative):
    if cumulative:
        C.V.tensor_scalar_mul(out=_ap3(dstsl, [[dstsl.ap[0][0], 128], [MK, 128]]),
                              in0=_ap3(srcsl, [[srcsl.ap[0][0], 128], [MK, 128]]),
                              scalar1=cdma[:, 0:1])
        for jj in range(1, MK):
            C.V.scalar_tensor_tensor(
                out=_ap3(dstsl, [[dstsl.ap[0][0], 128], [MK, 128]], offset=jj),
                in0=_ap3(srcsl, [[srcsl.ap[0][0], 128], [MK, 128]], offset=jj),
                scalar=cdma[:, jj:jj + 1],
                in1=_ap3(dstsl, [[dstsl.ap[0][0], 128], [MK, 128]], offset=jj - 1),
                op0=ALU.mult, op1=ALU.add)
    else:
        C.V.tensor_scalar_mul(out=dstsl,
                              in0=_ap3(srcsl, [[srcsl.ap[0][0], 128], [MK, 128]]),
                              scalar1=cdma[:, 0:1])
        for jj in range(1, MK):
            C.V.scalar_tensor_tensor(
                out=dstsl,
                in0=_ap3(srcsl, [[srcsl.ap[0][0], 128], [MK, 128]], offset=jj),
                scalar=cdma[:, jj:jj + 1], in1=dstsl, op0=ALU.mult, op1=ALU.add)


def emit_attn(C, dr, b, sm, src, dst):
    s = C.s
    nd = D // 128
    for bi in range(B_LOC):
        C.Y.dma_start(out=b.cdmaA[:], in_=dr["cdma_half"]).inc(sm["ld"], 16)
        C.G.dma_start(out=b.identbf[:], in_=dr["ident2"]).inc(sm["gld"], 16)
        v_c1 = s.val(sm["ld"])
        for e in (C.V, C.S, C.T):
            e.wait(sm["ld"], v_c1)
        for j in range(nd):
            C.G.dma_start(out=b.wA[:, j * D:(j + 1) * D],
                          in_=_ap3(dr["wk"], [[D, 128], [1, D]], offset=j * 128 * D)).inc(sm["gld"], 16)
            C.G.dma_start(out=b.wB[:, j * D:(j + 1) * D],
                          in_=_ap3(dr["wv"], [[D, 128], [1, D]], offset=j * 128 * D)).inc(sm["gld"], 16)
            C.G.dma_start(out=b.laA[:, j * LORA:(j + 1) * LORA],
                          in_=_ap3(dr["av"], [[LORA, 128], [1, LORA]], offset=j * 128 * LORA)).inc(sm["gld"], 16)
        C.G.dma_start(out=b.lbA[:], in_=dr["bv"]).inc(sm["gld"], 16)
        v_w3a = s.val(sm["gld"])
        C.T.wait(sm["gld"], v_w3a)
        C.V.memset(b.vt[:], 1.0).inc(sm["ve"], 1)

        for c in range(4):
            cg = bi * 4 + c
            cb = cg % 2
            xch = b.xch[:, cb * nd * TC:(cb + 1) * nd * TC]
            C.Y.wait(sm["ve"], b.at_cons.get(cg - 2, 0))
            C.Y.dma_start(out=xch, in_=_ap3(src, [[T, 128], [128 * T, nd], [1, TC]],
                                            offset=cg * TC)).inc(sm["x"], 16)
            C.G.wait(sm["ve"], b.at_tabfree.get(cg - 1, 0))
            C.G.dma_start(out=b.cosA[:], in_=_ap3(dr["cos_rep"], [[S, 128], [1, TC]],
                                                  offset=c * TC)).inc(sm["gld"], 16)
            C.G.dma_start(out=b.sinA[:], in_=_ap3(dr["sin_rep"], [[S, 128], [1, TC]],
                                                  offset=c * TC)).inc(sm["gld"], 16)
            v_tab = s.val(sm["gld"])
            v_x = s.val(sm["x"])
            C.S.wait(sm["x"], v_x)
            C.T.wait(sm["x"], v_x)
            C.V.wait(sm["x"], v_x)
            C.V.wait(sm["gld"], v_tab)
            v_xn = emit_ln_chunk(C, b, sm, xch, TC)
            b.at_cons[cg] = v_xn
            C.T.wait(sm["ve"], v_xn)
            C.T.wait(sm["ve"], s.val(sm["ve"]))
            for j in range(nd):
                mmL = C.T.matmul(b.pF[0:LORA, 0:TC], b.laA[:, j * LORA:(j + 1) * LORA],
                                 b.xn[:, j * TC:(j + 1) * TC], start=(j == 0), stop=(j == nd - 1))
                if j == nd - 1:
                    mmL.inc(sm["mm"], 1)
            C.V.wait(sm["mm"], s.val(sm["mm"]))
            C.V.tensor_copy(out=b.tmplA[:, 0:TC], in_=b.pF[0:LORA, 0:TC]).inc(sm["ve"], 1)
            v_tls = s.val(sm["ve"])
            for ct in range(nd):
                C.T.wait(sm["ve"], s.val(sm["ve"]))
                for j in range(nd):
                    mmK = C.T.matmul(b.pE[:, 0:TC], b.wA[:, j * D + ct * 128:j * D + (ct + 1) * 128],
                                     b.xn[:, j * TC:(j + 1) * TC], start=(j == 0), stop=(j == nd - 1))
                    if j == nd - 1:
                        mmK.inc(sm["mm"], 1)
                v_k = s.val(sm["mm"])
                _rope(C, b, sm, b.pE, b.cosA[:, 0:TC], b.sinA[:, 0:TC], b.rpbA, b.rkbA,
                      b.kc[:, ct * TC:(ct + 1) * TC], v_k)
                _cdma_taps(C, b, sm, b.cdmaA, b.kc[:, ct * TC:(ct + 1) * TC],
                           b.fullk[:, ct * NSLOT + c * 128:ct * NSLOT + (c + 1) * 128], False)
                C.V.drain().inc(sm["ve"], 1)
                C.T.wait(sm["ve"], s.val(sm["ve"]))
                if ct == 0:
                    C.T.wait(sm["ve"], v_tls)
                for j in range(nd):
                    C.T.matmul(b.pF[:, 0:TC], b.wB[:, j * D + ct * 128:j * D + (ct + 1) * 128],
                               b.xn[:, j * TC:(j + 1) * TC], start=(j == 0), stop=False)
                C.T.matmul(b.pF[:, 0:TC], b.lbA[:, ct * 128:(ct + 1) * 128], b.tmplA[:, 0:TC],
                           start=False, stop=True).inc(sm["mm"], 1)
                v_v = s.val(sm["mm"])
                C.V.wait(sm["mm"], v_v)
                C.V.tensor_copy(out=b.vc[:, ct * TC:(ct + 1) * TC], in_=b.pF[:, 0:TC])
                C.V.drain()
                _cdma_taps(C, b, sm, b.cdmaA, b.vc[:, ct * TC:(ct + 1) * TC],
                           b.fullv[:, ct * NSLOT + c * 128:ct * NSLOT + (c + 1) * 128], False)
                C.V.drain().inc(sm["ve"], 1)
            v_kv = s.val(sm["ve"])
            b.at_tabfree[cg] = v_kv
            C.Y.wait(sm["ve"], v_kv)
            C.Y.dma_start(out=_ap3(dr["k_scr"], [[T, 128], [128 * T, nd], [1, TC]],
                                   offset=cg * TC), in_=b.kc[:, 0:nd * TC]).inc(sm["st"], 16)
            C.Y.dma_start(out=_ap3(dr["v_scr"], [[T, 128], [128 * T, nd], [1, TC]],
                                   offset=cg * TC), in_=b.vc[:, 0:nd * TC]).inc(sm["st"], 16)
            b.at_stored[("kv", cg)] = s.val(sm["st"])
        v_fv = s.val(sm["ve"])
        C.T.wait(sm["ve"], v_fv)
        for st_ in range(4):
            for h in range(NH):
                ct, par = h // 2, h % 2
                C.T.wait(sm["ve"], s.val(sm["ve"]))
                C.T.transpose(b.pT[0:128, 0:HD],
                              b.fullv[par * HD:(par + 1) * HD,
                                      ct * NSLOT + st_ * 128:ct * NSLOT + (st_ + 1) * 128],
                              identity=b.identbf[par * HD:(par + 1) * HD, 0:HD]).inc(sm["mm"], 1)
                C.V.wait(sm["mm"], s.val(sm["mm"]))
                col = (st_ * NH + h) * (HD + 1)
                C.V.tensor_copy(out=b.vt[:, col:col + HD], in_=b.pT[0:128, 0:HD]).inc(sm["ve"], 1)

        C.G.wait(sm["ve"], s.val(sm["ve"]))
        C.Y.dma_start(out=b.cdmaB[:], in_=dr["cdma_half"]).inc(sm["ld"], 16)
        C.Y.dma_start(out=b.bandB[:], in_=dr["band01"]).inc(sm["ld"], 16)
        v_cB = s.val(sm["ld"])
        for e in (C.V, C.S, C.T):
            e.wait(sm["ld"], v_cB)
        for j in range(nd):
            C.G.dma_start(out=b.laB[:, j * LORA:(j + 1) * LORA],
                          in_=_ap3(dr["aq"], [[LORA, 128], [1, LORA]], offset=j * 128 * LORA)).inc(sm["gld"], 16)
            C.G.dma_start(out=b.lc[:, j * LORA:(j + 1) * LORA],
                          in_=_ap3(dr["ao"], [[LORA, 128], [1, LORA]], offset=j * 128 * LORA)).inc(sm["gld"], 16)
        C.G.dma_start(out=b.lbB[:], in_=dr["bq"]).inc(sm["gld"], 16)
        C.G.dma_start(out=b.ld_[:], in_=dr["bo"]).inc(sm["gld"], 16)
        v_w3b = s.val(sm["gld"])
        C.T.wait(sm["gld"], v_w3b)

        for qc in range(4):
            cg = bi * 4 + qc
            cb = cg % 2
            xch = b.xch[:, cb * nd * TC:(cb + 1) * nd * TC]
            C.Y.wait(sm["ve"], b.at_cons.get(cg - 2, 0))
            C.Y.wait(sm["st"], b.at_stored.get(("r", cg - 1), 0))
            C.Y.dma_start(out=xch, in_=_ap3(src, [[T, 128], [128 * T, nd], [1, TC]],
                                            offset=cg * TC)).inc(sm["x"], 16)
            C.Y.wait(sm["ve"], b.at_kvfree.get(cg - 1, 0))
            C.Y.dma_start(out=b.kcB[:, 0:nd * TC],
                          in_=_ap3(dr["k_scr"], [[T, 128], [128 * T, nd], [1, TC]],
                                   offset=cg * TC)).inc(sm["x"], 16)
            C.Y.dma_start(out=b.vcB[:, 0:nd * TC],
                          in_=_ap3(dr["v_scr"], [[T, 128], [128 * T, nd], [1, TC]],
                                   offset=cg * TC)).inc(sm["x"], 16)
            C.G.wait(sm["ve"], b.at_tabfree.get(("b", cg - 1), 0))
            for nm, buf in (("cos_rep", b.cosB), ("sin_rep", b.sinB),
                            ("qpa", b.qpaB), ("qpo", b.qpoB)):
                C.G.dma_start(out=buf[:], in_=_ap3(dr[nm], [[S, 128], [1, TC]],
                                                   offset=qc * TC)).inc(sm["gld"], 16)
            v_tab = s.val(sm["gld"])
            v_x = s.val(sm["x"])
            C.S.wait(sm["x"], v_x)
            C.T.wait(sm["x"], v_x)
            C.V.wait(sm["x"], v_x)
            C.V.wait(sm["gld"], v_tab)
            v_xn = emit_ln_chunk(C, b, sm, xch, TC)
            b.at_cons[cg] = v_xn
            C.T.wait(sm["ve"], v_xn)
            C.T.wait(sm["ve"], s.val(sm["ve"]))
            for j in range(nd):
                mmq = C.T.matmul(b.pF[0:LORA, 0:TC], b.laB[:, j * LORA:(j + 1) * LORA],
                                 b.xn[:, j * TC:(j + 1) * TC], start=(j == 0), stop=(j == nd - 1))
                if j == nd - 1:
                    mmq.inc(sm["mm"], 1)
            C.V.wait(sm["mm"], s.val(sm["mm"]))
            C.V.tensor_copy(out=b.tmplB[:, 0:TC], in_=b.pF[0:LORA, 0:TC]).inc(sm["ve"], 1)
            v_tls = s.val(sm["ve"])
            for ct in range(nd):
                wq = b.wq_st[ct % 2]
                if len(b.at_wqmark) >= 2:
                    C.G.wait(sm["mm"], b.at_wqmark[-2])
                C.G.dma_start(out=wq[:, 0:8 * 128],
                              in_=_ap3(dr["wq"], [[D, 128], [128 * D, nd], [1, 128]],
                                       offset=ct * 128)).inc(sm["gld"], 16)
                v_wq = s.val(sm["gld"])
                pq = b.pE
                C.T.wait(sm["ve"], s.val(sm["ve"]))
                C.T.wait(sm["gld"], v_wq)
                if ct == 0:
                    C.T.wait(sm["ve"], v_tls)
                for j in range(nd):
                    C.T.matmul(pq[:, 0:TC], wq[:, j * 128:(j + 1) * 128],
                               b.xn[:, j * TC:(j + 1) * TC], start=(j == 0), stop=False)
                mmQ = C.T.matmul(pq[:, 0:TC], b.lbB[:, ct * 128:(ct + 1) * 128],
                                 b.tmplB[:, 0:TC], start=False, stop=True)
                mmQ.inc(sm["mm"], 1)
                b.at_wqmark.append(s.val(sm["mm"]))
                v_q = s.val(sm["mm"])
                _rope(C, b, sm, pq, b.cosB[:, 0:TC], b.sinB[:, 0:TC], b.rpbB, b.rkbB,
                      b.aqb[:, ct * TC:(ct + 1) * TC], v_q)
                C.V.tensor_tensor(out=b.aqb[:, ct * TC:(ct + 1) * TC],
                                  in0=b.aqb[:, ct * TC:(ct + 1) * TC], in1=b.qpaB[:, 0:TC],
                                  op=ALU.mult)
                C.V.drain().inc(sm["ve"], 1)
            v_aq = s.val(sm["ve"])

            for h in range(NH):
                ct, par = h // 2, h % 2
                rb = par * HD
                if par == 0:
                    C.V.wait(sm["mm"], s.val(sm["mm"]))
                    _cdma_taps(C, b, sm, b.cdmaB, b.kcB[:, ct * TC:(ct + 1) * TC],
                               b.cpkt[:, 0:TC], True)
                    C.V.drain()
                    _cdma_taps(C, b, sm, b.cdmaB, b.vcB[:, ct * TC:(ct + 1) * TC],
                               b.cpvt[:, 0:TC], True)
                    C.V.drain().inc(sm["ve"], 1)
                v_cp = s.val(sm["ve"])
                pS = [b.pA, b.pB, b.pC, b.pD]
                C.T.wait(sm["ve"], v_aq)
                C.T.wait(sm["ac"], s.val(sm["ac"]))
                C.T.wait(sm["ve"], s.val(sm["ve"]))
                for st_ in range(qc + 1):
                    mmS = C.T.matmul(pS[st_][0:128, 0:TC],
                                     b.fullk[rb:rb + HD, ct * NSLOT + st_ * 128:ct * NSLOT + (st_ + 1) * 128],
                                     b.aqb[rb:rb + HD, ct * TC:(ct + 1) * TC],
                                     start=True, stop=True)
                    if st_ == qc:
                        mmS.inc(sm["mm"], 1)
                C.V.wait(sm["ve"], v_cp)
                C.V.wait(sm["mm"], s.val(sm["mm"]))
                C.V.tensor_tensor(out=b.qcph[rb:rb + HD, 0:TC],
                                  in0=b.aqb[rb:rb + HD, ct * TC:(ct + 1) * TC],
                                  in1=b.cpkt[rb:rb + HD, 0:TC], op=ALU.mult)
                C.V.drain().inc(sm["ve"], 1)
                v_qcp = s.val(sm["ve"])
                C.T.wait(sm["ve"], v_qcp)
                C.T.matmul(b.pG2[0:1, 0:TC], b.ones_bf[rb:rb + HD, 0:1],
                           b.qcph[rb:rb + HD, 0:TC], start=True, stop=True).inc(sm["mm"], 1)
                v_sc = s.val(sm["mm"])
                C.S.wait(sm["mm"], v_sc)
                C.S.wait(sm["ve"], s.val(sm["ve"]))
                for st_ in range(qc + 1):
                    C.S.activation(out=b.ebuf[:, st_ * TC:(st_ + 1) * TC],
                                   in_=pS[st_][0:128, 0:TC], func=AF.Exp).inc(sm["ac"], 1)
                C.S.activation(out=b.wcur[0:1, 0:TC], in_=b.pG2[0:1, 0:TC],
                               func=AF.Exp).inc(sm["ac"], 1)
                v_e = s.val(sm["ac"])
                C.V.wait(sm["ac"], v_e)
                C.V.tensor_tensor(out=b.ebuf[:, qc * TC:(qc + 1) * TC],
                                  in0=b.ebuf[:, qc * TC:(qc + 1) * TC],
                                  in1=b.bandB[:, 0:TC], op=ALU.mult)
                C.V.drain().inc(sm["ve"], 1)
                v_eb = s.val(sm["ve"])
                C.T.wait(sm["ve"], v_eb)
                for st_ in range(qc + 1):
                    col = (st_ * NH + h) * (HD + 1)
                    mmA = C.T.matmul(b.pE[0:HD + 1, 0:TC], b.vt[:, col:col + HD + 1],
                                     b.ebuf[:, st_ * TC:(st_ + 1) * TC],
                                     start=(st_ == 0), stop=(st_ == qc))
                    if st_ == qc:
                        mmA.inc(sm["mm"], 1)
                v_av = s.val(sm["mm"])
                C.V.wait(sm["mm"], v_av)
                C.V.tensor_tensor(out=b.zinv[0:1, 0:TC], in0=b.pE[HD:HD + 1, 0:TC],
                                  in1=b.wcur[0:1, 0:TC], op=ALU.add)
                C.V.drain()
                C.V.reciprocal(out=b.zinv[0:1, 0:TC], in_=b.zinv[0:1, 0:TC])
                C.V.drain().inc(sm["ve"], 1)
                v_z = s.val(sm["ve"])
                C.T.wait(sm["ve"], v_z)
                C.T.matmul(b.pG2[0:HD, 0:TC], b.onesrow_f[0:1, 0:HD], b.zinv[0:1, 0:TC],
                           start=True, stop=True)
                C.T.matmul(b.pG2[HD:128, 0:TC], b.onesrow_f[0:1, 0:HD], b.wcur[0:1, 0:TC],
                           start=True, stop=True).inc(sm["mm"], 1)
                v_wm = s.val(sm["mm"])
                C.V.wait(sm["mm"], v_wm)
                C.V.tensor_tensor(out=b.corr[rb:rb + HD, 0:TC], in0=b.pG2[HD:128, 0:TC],
                                  in1=b.cpvt[rb:rb + HD, 0:TC], op=ALU.mult)
                C.V.tensor_tensor(out=b.corr[rb:rb + HD, 0:TC], in0=b.corr[rb:rb + HD, 0:TC],
                                  in1=b.pE[0:HD, 0:TC], op=ALU.add)
                C.V.tensor_tensor(out=b.corr[rb:rb + HD, 0:TC], in0=b.corr[rb:rb + HD, 0:TC],
                                  in1=b.pG2[0:HD, 0:TC], op=ALU.mult)
                C.V.tensor_tensor(out=b.o2[rb:rb + HD, ct * TC:(ct + 1) * TC],
                                  in0=b.corr[rb:rb + HD, 0:TC],
                                  in1=b.qpoB[rb:rb + HD, 0:TC], op=ALU.mult)
                C.V.drain().inc(sm["ve"], 1)
            v_o2 = s.val(sm["ve"])
            b.at_kvfree[cg] = v_o2

            C.T.wait(sm["ve"], v_o2)
            for j in range(nd):
                mmo = C.T.matmul(b.pF[0:LORA, 0:TC], b.lc[:, j * LORA:(j + 1) * LORA],
                                 b.o2[:, j * TC:(j + 1) * TC], start=(j == 0), stop=(j == nd - 1))
                if j == nd - 1:
                    mmo.inc(sm["mm"], 1)
            C.V.wait(sm["mm"], s.val(sm["mm"]))
            C.V.tensor_copy(out=b.tmplB[:, 0:TC], in_=b.pF[0:LORA, 0:TC]).inc(sm["ve"], 1)
            v_t2 = s.val(sm["ve"])
            pgs = [b.pA, b.pB, b.pC, b.pD]
            for dt_ in range(nd):
                wo = b.wo_st[dt_ % 2]
                if len(b.at_womark) >= 2:
                    C.G.wait(sm["mm"], b.at_womark[-2])
                C.G.dma_start(out=wo[:, 0:8 * 128],
                              in_=_ap3(dr["wo"], [[D, 128], [128 * D, nd], [1, 128]],
                                       offset=dt_ * 128)).inc(sm["gld"], 16)
                v_wo = s.val(sm["gld"])
                bank = pgs[dt_ % 4]
                C.T.wait(sm["ve"], s.val(sm["ve"]))
                C.T.wait(sm["gld"], v_wo)
                if dt_ == 0:
                    C.T.wait(sm["ve"], v_t2)
                for j in range(nd):
                    C.T.matmul(bank[:, 0:TC], wo[:, j * 128:(j + 1) * 128],
                               b.o2[:, j * TC:(j + 1) * TC], start=(j == 0), stop=False)
                mmY = C.T.matmul(bank[:, 0:TC], b.ld_[:, dt_ * 128:(dt_ + 1) * 128],
                                 b.tmplB[:, 0:TC], start=False, stop=True)
                mmY.inc(sm["mm"], 1)
                b.at_womark.append(s.val(sm["mm"]))
                v_y = s.val(sm["mm"])
                C.V.wait(sm["mm"], v_y)
                C.V.scalar_tensor_tensor(out=b.rout[:, dt_ * TC:(dt_ + 1) * TC],
                                         in0=bank[:, 0:TC], scalar=1.0,
                                         in1=xch[:, dt_ * TC:(dt_ + 1) * TC],
                                         op0=ALU.mult, op1=ALU.add).inc(sm["ve"], 1)
            b.at_cons[cg] = s.val(sm["ve"])
            b.at_tabfree[("b", cg)] = s.val(sm["ve"])
            C.Y.wait(sm["ve"], b.at_cons[cg])
            C.Y.dma_start(out=_ap3(dst, [[T, 128], [128 * T, nd], [1, TC]], offset=cg * TC),
                          in_=b.rout[:, 0:nd * TC]).inc(sm["st"], 16)
            b.at_stored[("r", cg)] = s.val(sm["st"])
        C.barrier(sm["bar"])
    C.Y.wait(sm["st"], b.at_stored[("r", B_LOC * 4 - 1)])


def build_program(phases=("mamba", "ffn1", "attn", "ffn2"), debug_taps=()):
    import os as _os2
    nc = bass.Bass("TRN2", target_bir_lowering=False, detect_race_conditions=False,
                   enable_asserts=not _os2.environ.get("DEV_NOASSERTOFF"))
    C = Ctx(nc)
    dr = {}

    def inp(name, shape, dt=F32):
        dr[name] = nc.dram_tensor(name, list(shape), dt, kind="ExternalInput").ap()

    def scr(name, shape, dt):
        dr[name] = nc.dram_tensor(name, list(shape), dt, kind="Internal").ap()

    inp("x0", [D, T], F32)
    dr["out"] = nc.dram_tensor("out", [D, T], F32, kind="ExternalOutput").ap()
    inp("wfcT", [D, FF]); inp("wprojT", [FF, D])
    inp("consts", [128, 16])
    inp("ones_bf", [128, 1], BF16)
    inp("onesrow_f", [1, 128], F32)
    inp("onesrow_bf", [1, 512], BF16)
    inp("ident", [128, 128], F32)
    inp("triu01", [128, 128], F32)
    inp("winT", [D, PROJ])
    inp("b_in", [1, PROJ])
    inp("conv_w_t", [128, NJT * MK])
    inp("dtb_rep", [128, MH])
    inp("a_log", [MH, 1])
    inp("d_row", [1, INTER])
    inp("normw_row", [1, INTER])
    inp("woutT", [INTER, D])
    inp("b_out", [1, D])
    for w in ("wq", "wk", "wv", "wo"):
        inp(w, [D, D])
    inp("aq", [D, LORA]); inp("bq", [LORA, D])
    inp("av", [D, LORA]); inp("bv", [LORA, D])
    inp("ao", [D, LORA]); inp("bo", [LORA, D])
    inp("cos_rep", [128, S])
    inp("ident2", [128, HD])
    inp("sin_rep", [128, S])
    inp("qpa", [128, S])
    inp("qpo", [128, S])
    inp("cdma_half", [128, MK])
    inp("band01", [128, TC])
    scr("k_scr", [D, T], BF16)
    scr("v_scr", [D, T], BF16)
    scr("wproj_bf", [FF, D], BF16)
    scr("resid1", [D, T], F32)
    scr("resid2", [D, T], F32)
    scr("resid3", [D, T], F32)
    scr("h_scr", [INTER, T], BF16)
    scr("b_scr", [MN, T], BF16)
    scr("c_scr", [MN, T], BF16)
    scr("gate_scr", [T, INTER], BF16)
    scr("g_scr", [T, INTER], BF16)
    scr("dt_scr", [T, MH], F32)
    scr("acum_scr", [MH, T], F32)

    dbg = {}
    for name, shape, dt in debug_taps:
        dbg[name] = nc.dram_tensor(name, list(shape), dt, kind="ExternalOutput").ap()

    from contextlib import ExitStack
    with ExitStack() as es:
        sems = {}
        for name in ("ld", "gld", "sld", "x", "mm", "ve", "ac", "gp", "st", "tr", "bar"):
            sems[name] = es.enter_context(nc.semaphore(name))

        pa_shared = PA(nc)
        b = alloc_shared_bufs(pa_shared)
        pa_ffn = PA(nc)
        alloc_ffn_bufs(pa_ffn, b)
        pa_ffn.close()
        pa_ma = PA(nc)
        alloc_mamba_a_bufs(pa_ma, b)
        pa_ma.close()
        pa_mb = PA(nc)
        alloc_mamba_b_bufs(pa_mb, b)
        pa_mb.close()
        pa_mc = PA(nc)
        alloc_mamba_c_bufs(pa_mc, b)
        pa_mc.close()
        pa_atp = PA(nc)
        alloc_attn_persist(pa_atp, b)
        pa_at_a = PA(nc)
        alloc_attn_a(pa_at_a, b)
        pa_at_a.close()
        pa_at_b = PA(nc)
        alloc_attn_b(pa_at_b, b)
        pa_at_b.close()
        pa_atp.close()
        pa_shared.close()

        block = es.enter_context(nc.Block())

        scratch_of = {"mamba": "resid1", "ffn1": "resid2", "attn": "resid3",
                      "ffn2": "resid1", "mamba_a": "resid1", "mamba_ab": "resid1"}
        cur = "x0"
        plan = []
        for i, ph in enumerate(phases):
            dst = "out" if i == len(phases) - 1 else scratch_of[ph]
            plan.append((ph, cur, dst))
            cur = dst

        def run(engine_name):
            def f(handle):
                C.active = engine_name
                C.handle = handle
                C.s = Sched()
                b.ffn_wpmarks = []
                b.ffn_cons = {}
                b.ffn_stored = {}
                b.ma_cons = {}
                b.ma_stored = {}
                b.mb_cons = {}
                b.mb_stored = 0
                b.mb_difdone = 0
                b.mc_cons = {}
                b.mc_stored = {}
                b.mc_mmdone = {}
                b.at_cons = {}
                b.at_stored = {}
                b.at_kvfree = {}
                b.at_wqmark = []
                b.at_womark = []
                b.at_tabfree = {}
                emit_const_loads(C, dr, b, sems)
                first_ffn = True
                for ph, src, dst in plan:
                    if ph in ("ffn1", "ffn2"):
                        emit_ffn_phase(C, dr, b, sems, dr[src], dr[dst], first_ffn, dbg)
                        first_ffn = False
                    elif ph == "mamba":
                        emit_mamba_a(C, dr, b, sems, dr[src])
                        C.barrier(sems["bar"])
                        emit_mamba_b(C, dr, b, sems, dbg)
                        C.barrier(sems["bar"])
                        emit_mamba_c(C, dr, b, sems, dr[src], dr[dst])
                    elif ph == "mamba_a":
                        emit_mamba_a(C, dr, b, sems, dr[src])
                        C.barrier(sems["bar"])
                        C.G.dma_start(out=_ap3(dr["out"], [[T, 32], [1, T]]),
                                      in_=_ap3(dr["acum_scr"], [[T, 32], [1, T]])).inc(sems["gld"], 16)
                        C.G.dma_start(out=_ap3(dr["out"], [[T, 128], [1, T]], offset=32 * T),
                                      in_=_ap3(dr["b_scr"], [[T, 128], [1, T]])).inc(sems["gld"], 16)
                        C.G.dma_start(out=_ap3(dr["out"], [[T, 128], [1, T]], offset=160 * T),
                                      in_=_ap3(dr["c_scr"], [[T, 128], [1, T]])).inc(sems["gld"], 16)
                        C.G.dma_start(out=_ap3(dr["out"], [[T, 512], [1, T]], offset=288 * T),
                                      in_=_ap3(dr["h_scr"], [[T, 512], [1, T]])).inc(sems["gld"], 16)
                        C.G.dma_start(out=_ap3(dr["out"], [[T, 96], [1, T]], offset=800 * T),
                                      in_=_ap3(dr["h_scr"], [[T, 96], [1, T]], offset=1024 * T)).inc(sems["gld"], 16)
                        C.G.dma_start(out=_ap3(dr["out"], [[T, 32], [1, T]], offset=896 * T),
                                      in_=_ap3(dr["dt_scr"], [[T, 32], [1, T]])).inc(sems["gld"], 16)
                        C.G.wait(sems["gld"], C.s.val(sems["gld"]))
                    elif ph == "mamba_ab":
                        emit_mamba_a(C, dr, b, sems, dr[src])
                        C.barrier(sems["bar"])
                        emit_mamba_b(C, dr, b, sems, dbg)
                        C.barrier(sems["bar"])
                        C.G.dma_start(out=_ap3(dr["out"], [[T, 512], [1, T]]),
                                      in_=_ap3(dr["g_scr"], [[T, 512], [1, T]])).inc(sems["gld"], 16)
                        C.G.wait(sems["gld"], C.s.val(sems["gld"]))
                    elif ph == "attn":
                        emit_attn(C, dr, b, sems, dr[src], dr[dst])
                    C.barrier(sems["bar"])
            return f

        block.sync(run("sync"))
        block.vector(run("vector"))
        block.scalar(run("scalar"))
        block.tensor(run("tensor"))
        block.gpsimd(run("gpsimd"))

    return nc, dbg


def _host_consts():
    inv_freq = 1.0 / (10000.0 ** (np.arange(0, HD, 2, dtype=np.float64) / HD))
    pos = np.arange(S, dtype=np.float64)
    freqs = pos[:, None] * inv_freq[None, :]
    emb = np.concatenate([freqs, freqs], axis=-1)
    cos = np.cos(emb).astype(np.float32).T
    sin = np.sin(emb).astype(np.float32).T
    HAD = np.array([[1, 1, 1, 1], [1, -1, 1, -1], [1, 1, -1, -1], [1, -1, -1, 1]],
                   dtype=np.float32)
    CDMA = np.tile(HAD, (1, HD // 4))
    qphase = CDMA[np.arange(S) % KF]
    consts = np.zeros((128, 16), dtype=np.float32)
    consts[:, 0] = EPS
    for i, bp in enumerate(SPIKE_BPS):
        consts[:, 1 + i] = -np.float32(bp)
    consts[:, 6] = np.float32(SPIKE_BPS[3])
    consts[:, 7] = np.float32(SPIKE_BPS[4])
    consts[:, 8] = 1.0
    consts[:, 9] = -1.0
    return dict(
        cos_rep=np.ascontiguousarray(np.tile(cos, (2, 1))),
        sin_rep=np.ascontiguousarray(np.tile(sin, (2, 1))),
        qpa=np.ascontiguousarray(np.tile(qphase.T, (2, 1)) * 0.125).astype(np.float32),
        qpo=np.ascontiguousarray(np.tile(qphase.T, (2, 1)) * 2.0).astype(np.float32),
        cdma_half=np.ascontiguousarray(np.tile(CDMA.T, (2, 1)) * 0.5).astype(np.float32),
        band01=(np.arange(128)[:, None] < (np.arange(TC) // KF)[None, :]).astype(np.float32),
        triu01=(np.arange(128)[:, None] <= np.arange(128)[None, :]).astype(np.float32),
        ident=np.eye(128, dtype=np.float32),
        ident2=np.ascontiguousarray(np.tile(np.eye(HD, dtype=np.float32), (2, 1))),
        consts=consts,
        ones_bf=np.ones((128, 1), ml_dtypes.bfloat16),
        onesrow_f=np.ones((1, 128), np.float32),
        onesrow_bf=np.ones((1, 128), ml_dtypes.bfloat16),
    )


def _core_in_map(inputs, consts, core):
    x = inputs["L0_attn_a_x"][2 * core:2 * core + 2]
    xT = np.ascontiguousarray(x.reshape(T, D).T)
    m = {
        "x0": xT,
        "wfcT": consts["_wfcT"],
        "wprojT": consts["_wprojT"],
        "consts": consts["consts"],
        "ones_bf": consts["ones_bf"],
        "onesrow_f": consts["onesrow_f"],
        "onesrow_bf": consts["_onesrow_bf512"],
        "ident": consts["ident"],
        "ident2": consts["ident2"],
        "triu01": consts["triu01"],
        "winT": consts["_winT"],
        "b_in": consts["_b_in"],
        "conv_w_t": consts["_conv_w_t"],
        "dtb_rep": consts["_dtb_rep"],
        "a_log": consts["_a_log"],
        "d_row": consts["_d_row"],
        "normw_row": consts["_normw_row"],
        "woutT": consts["_woutT"],
        "b_out": consts["_b_out"],
        "wq": inputs["L0_attn_b_Wq"], "wk": inputs["L0_attn_b_Wk"],
        "wv": inputs["L0_attn_b_Wv"], "wo": inputs["L0_attn_b_Wo"],
        "aq": inputs["L0_attn_b_Aq"], "bq": inputs["L0_attn_b_Bq"],
        "av": inputs["L0_attn_b_Av"], "bv": inputs["L0_attn_b_Bv"],
        "ao": inputs["L0_attn_b_Ao"], "bo": inputs["L0_attn_b_Bo"],
        "cos_rep": consts["cos_rep"], "sin_rep": consts["sin_rep"],
        "qpa": consts["qpa"], "qpo": consts["qpo"],
        "cdma_half": consts["cdma_half"], "band01": consts["band01"],
    }
    return m


def kernel(**inputs):
    c = _host_consts()
    c["_wfcT"] = np.ascontiguousarray(inputs["L0_ffn_W_fc"].T)
    c["_wprojT"] = np.ascontiguousarray(inputs["L0_ffn_W_proj"].T)
    c["_winT"] = np.ascontiguousarray(inputs["L0_attn_a_W_in"].T)
    c["_b_in"] = np.ascontiguousarray(inputs["L0_attn_a_b_in"][None, :])
    c["_conv_w_t"] = np.ascontiguousarray(
        inputs["L0_attn_a_conv_w"].reshape(NJT, 128, MK).transpose(1, 0, 2).reshape(128, NJT * MK))
    c["_dtb_rep"] = np.ascontiguousarray(
        np.broadcast_to(inputs["L0_attn_a_dt_bias"][None, :], (128, MH)))
    c["_a_log"] = np.ascontiguousarray(inputs["L0_attn_a_A_log"][:, None])
    c["_d_row"] = np.ascontiguousarray(np.repeat(inputs["L0_attn_a_D"], MP))[None, :]
    c["_normw_row"] = np.ascontiguousarray(inputs["L0_attn_a_norm_w"][None, :])
    c["_woutT"] = np.ascontiguousarray(inputs["L0_attn_a_W_out"].T)
    c["_b_out"] = np.ascontiguousarray(inputs["L0_attn_a_b_out"][None, :])
    c["_onesrow_bf512"] = np.ones((1, 512), ml_dtypes.bfloat16)

    nc, _ = build_program(phases=("mamba", "ffn1", "attn", "ffn2"))
    in_maps = [_core_in_map(inputs, c, core) for core in range(NCORES)]
    res = run_bass_kernel_spmd(nc, in_maps, core_ids=list(range(NCORES)))
    outs = []
    for r in res.results:
        o = r["out"]
        outs.append(np.ascontiguousarray(o.T).reshape(B_LOC, S, D))
    return np.concatenate(outs, axis=0).astype(np.float32)


# version 06
# speedup vs baseline: 1.2511x; 1.0121x over previous
import numpy as np
import ml_dtypes
import concourse.bass as bass
import concourse.mybir as mybir
from concourse.bass_utils import run_bass_kernel_spmd

F32 = mybir.dt.float32
F32R = mybir.dt.float32r
BF16 = mybir.dt.bfloat16
AF = mybir.ActivationFunctionType
ALU = mybir.AluOpType

NCORES = 8
B_TOT, S, D = 16, 2048, 1024
B_LOC = B_TOT // NCORES
T = B_LOC * S
TC = 512
NCH = T // TC
MH, MP, MN, MK = 32, 64, 128, 4
INTER = MH * MP
CONV = INTER + 2 * MN
PROJ = INTER + CONV + MH
CS = 128
EPS = 1e-5
FF = 4096
NH, HD, KF = 16, 64, 4
NSLOT = S // KF
LORA = 64

ENGINES = ("sync", "vector", "scalar", "tensor", "gpsimd")


def _spike_breakpoints():
    BETA, TH, NS = 0.9, 1.0, 5

    def out_of(y):
        mem = spk = acc = 0.0
        for _ in range(NS):
            mem = BETA * mem + y - spk * TH
            spk = 1.0 if mem > TH else 0.0
            acc += spk
        return acc / NS

    bps, lo = [], -1.0
    for k in range(NS):
        target = (k + 1) / NS
        a, b = lo, 8.0
        for _ in range(200):
            m = 0.5 * (a + b)
            if out_of(m) >= target - 1e-9:
                b = m
            else:
                a = m
        bps.append(0.5 * (a + b))
        lo = bps[-1]
    return bps


SPIKE_BPS = _spike_breakpoints()


class Sched:
    def __init__(self):
        self.c = {}

    def inc(self, sem, n):
        k = sem.name
        self.c[k] = self.c.get(k, 0) + n
        return self.c[k]

    def val(self, sem):
        return self.c.get(sem.name, 0)


class _Instr:
    def __init__(self, ctx, inst, fence):
        self.ctx = ctx
        self.inst = inst
        self.fence = fence

    def inc(self, sem, n=1, fence=None):
        self.ctx.s.inc(sem, n)
        do_fence = self.fence if fence is None else (fence and self.fence)
        if self.inst is not None:
            if do_fence:
                self.ctx.handle.drain().then_inc(sem, n)
            else:
                self.inst.then_inc(sem, n)
        return self


import os as _os
_FENCE_ENGINES = () if _os.environ.get("DEV_NOFENCE") else ("vector", "scalar", "gpsimd")


class Eng:

    def __init__(self, ctx, name):
        self._ctx = ctx
        self._name = name

    @property
    def live(self):
        return self._ctx.active == self._name

    def __getattr__(self, op):
        ctx = self._ctx
        name = self._name

        def call(*args, **kwargs):
            fence = name in _FENCE_ENGINES and not op.startswith("dma")
            if self.live:
                return _Instr(ctx, getattr(ctx.handle, op)(*args, **kwargs), fence)
            return _Instr(ctx, None, fence)

        return call

    def wait(self, sem, val=None):
        if val is None:
            val = self._ctx.s.val(sem)
        if self.live:
            self._ctx.handle.wait_ge(sem, val)

    def dmat(self, out, in_):
        if self.live:
            return _Instr(self._ctx, self._ctx.handle.dma_start_transpose(out=out, in_=in_), False)
        return _Instr(self._ctx, None, False)


class Ctx:
    def __init__(self, nc):
        self.nc = nc
        self.s = None
        self.active = None
        self.handle = None
        self.Y = Eng(self, "sync")
        self.V = Eng(self, "vector")
        self.S = Eng(self, "scalar")
        self.T = Eng(self, "tensor")
        self.G = Eng(self, "gpsimd")

    def barrier(self, bar_sem):
        for e in (self.Y, self.V, self.S, self.T, self.G):
            e.nop().inc(bar_sem, 1)
        tgt = self.s.val(bar_sem)
        for e in (self.Y, self.V, self.S, self.T, self.G):
            e.wait(bar_sem, tgt)


class PA:

    def __init__(self, nc):
        from contextlib import ExitStack
        self.nc = nc
        self.es = ExitStack()

    def sb(self, name, shape, dt):
        return self.es.enter_context(self.nc.sbuf_tensor(name, list(shape), dt))

    def ps(self, name, shape, dt=F32):
        return self.es.enter_context(self.nc.psum_tensor(name, list(shape), dt))

    def close(self):
        self.es.close()


def _ap3(t, ap_dims, offset=0):
    base = t if hasattr(t, "tensor") else t.ap()
    return bass.AP(tensor=base.tensor, offset=base.offset + offset, ap=list(ap_dims))


def emit_ln_chunk(C, b, sm, xch, tc):
    s = C.s
    nd = D // 128
    C.T.wait(sm["ve"], s.val(sm["ve"]))
    C.V.wait(sm["mm"], s.val(sm["mm"]))
    C.V.tensor_copy(out=b.xbf[:, 0:nd * tc], in_=xch[:, 0:nd * tc])
    C.V.tensor_mul(out=b.xsqbf[:, 0:nd * tc], in0=b.xbf[:, 0:nd * tc],
                   in1=b.xbf[:, 0:nd * tc]).inc(sm["ve"], 1)
    v_sq = s.val(sm["ve"])
    C.T.wait(sm["ve"], v_sq)
    for j in range(nd):
        C.T.matmul(b.pA[0:1, 0:tc], b.ones_bf[:, 0:1], b.xbf[:, j * tc:(j + 1) * tc],
                   start=(j == 0), stop=(j == nd - 1))
    for j in range(nd):
        mm = C.T.matmul(b.pB[0:1, 0:tc], b.ones_bf[:, 0:1], b.xsqbf[:, j * tc:(j + 1) * tc],
                        start=(j == 0), stop=(j == nd - 1))
        if j == nd - 1:
            mm.inc(sm["mm"], 1)
    v_stats = s.val(sm["mm"])
    C.V.wait(sm["mm"], v_stats)
    r_mean = b.lnrow[0:1, 0 * TC:0 * TC + tc]
    r_ms = b.lnrow[0:1, 1 * TC:1 * TC + tc]
    r_m2 = b.lnrow[0:1, 2 * TC:2 * TC + tc]
    r_var = r_ms
    r_lnv = r_m2
    C.V.tensor_scalar_mul(out=r_mean, in0=b.pA[0:1, 0:tc], scalar1=1.0 / D)
    C.V.tensor_scalar_mul(out=r_ms, in0=b.pB[0:1, 0:tc], scalar1=1.0 / D)
    C.V.drain()
    C.V.tensor_mul(out=r_m2, in0=r_mean, in1=r_mean)
    C.V.tensor_sub(out=r_var, in0=r_ms, in1=r_m2).inc(sm["ve"], 1)
    v_var = s.val(sm["ve"])
    C.S.wait(sm["ve"], v_var)
    C.S.activation(out=r_lnv, in_=r_var, func=AF.Ln, bias=b.cpool[0:1, 0:1]).inc(sm["ac"], 1)
    C.S.wait(sm["ac"], s.val(sm["ac"]))
    C.S.activation(out=b.lnrow_r[0:1, 0:tc], in_=r_lnv, func=AF.Exp, scale=-0.5).inc(sm["ac"], 1)
    v_r = s.val(sm["ac"])
    C.V.wait(sm["ac"], v_r)
    C.V.tensor_mul(out=b.lnrow_r2[0:1, 0:tc], in0=r_mean, in1=b.lnrow_r[0:1, 0:tc]).inc(sm["ve"], 1)
    v_mr = s.val(sm["ve"])
    C.T.wait(sm["ac"], v_r)
    C.T.matmul(b.pC[:, 0:tc], b.onesrow_f[0:1, :], b.lnrow_r[0:1, 0:tc], start=True, stop=True)
    C.T.wait(sm["ve"], v_mr)
    C.T.matmul(b.pD[:, 0:tc], b.onesrow_f[0:1, :], b.lnrow_r2[0:1, 0:tc],
               start=True, stop=True).inc(sm["mm"], 1)
    v_out = s.val(sm["mm"])
    C.V.wait(sm["mm"], v_out)
    pR_rep = _ap3(b.pC, [[b.pC.ap().ap[0][0], 128], [0, nd], [1, tc]])
    pMR_rep = _ap3(b.pD, [[b.pD.ap().ap[0][0], 128], [0, nd], [1, tc]])
    C.V.tensor_tensor(out=b.scrA[:, 0:nd * tc], in0=xch[:, 0:nd * tc], in1=pR_rep, op=ALU.mult)
    C.V.tensor_tensor(out=b.xn[:, 0:nd * tc], in0=b.scrA[:, 0:nd * tc], in1=pMR_rep,
                      op=ALU.subtract).inc(sm["ve"], 1)
    return s.val(sm["ve"])


class NS:
    pass


DEV_FFN_LEVEL = 3
DEV_FFN_LEVEL2 = None


def emit_ffn_phase(C, dr, bufs, sm, src, dst, first_ffn, dbg=None):
    s = C.s
    b = bufs
    nd = D // 128
    NF = FF // 128
    LV = DEV_FFN_LEVEL if (first_ffn or DEV_FFN_LEVEL2 is None) else DEV_FFN_LEVEL2
    if LV == 3:
        LV = 99

    for j in range(nd):
        src_ap = _ap3(dr["wfcT"], [[FF, 128], [1, FF]], offset=j * 128 * FF)
        C.G.dma_start(out=b.wfc[:, j * FF:(j + 1) * FF], in_=src_ap).inc(sm["gld"], 16)
    if first_ffn:
        for i in range(NF):
            src_ap = _ap3(dr["wprojT"], [[D, 128], [1, D]], offset=i * 128 * D)
            C.G.dma_start(out=b.wpst[:, 0:D], in_=src_ap).inc(sm["gld"], 16)
            v = s.val(sm["gld"])
            C.G.wait(sm["gld"], v)
            dst_ap = _ap3(dr["wproj_bf"], [[D, 128], [1, D]], offset=i * 128 * D)
            C.G.dma_start(out=dst_ap, in_=b.wpst[:, 0:D]).inc(sm["gld"], 16)
    v_wload = s.val(sm["gld"])

    for c in range(NCH):
        cb = c % 2
        xch = b.xch[:, cb * nd * TC:(cb + 1) * nd * TC]
        if c >= 2:
            C.Y.wait(sm["ve"], b.ffn_cons[c - 2])
        src_ap = _ap3(src, [[T, 128], [128 * T, nd], [1, TC]], offset=c * TC)
        C.Y.dma_start(out=xch, in_=src_ap).inc(sm["x"], 16)
        v_x = s.val(sm["x"])
        C.S.wait(sm["x"], v_x)
        C.T.wait(sm["x"], v_x)
        C.V.wait(sm["x"], v_x)

        v_xn = emit_ln_chunk(C, b, sm, xch, TC)
        C.T.wait(sm["ve"], v_xn)
        if c == 0:
            C.T.wait(sm["gld"], v_wload)

        b.ffn_relu_marks = []
        b.ffn_sq_marks = []
        fcbanks = [b.pE, b.pF, b.pA, b.pB]
        for fi in range(NF if LV >= 1 else 0):
            pf = fcbanks[fi % 4]
            if fi >= 4:
                C.T.wait(sm["ac"], b.ffn_relu_marks[fi - 4])
            for j in range(nd):
                mm = C.T.matmul(pf[:, 0:TC], b.wfc[:, j * FF + fi * 128:j * FF + (fi + 1) * 128],
                                b.xn[:, j * TC:(j + 1) * TC], start=(j == 0), stop=(j == nd - 1))
                if j == nd - 1:
                    mm.inc(sm["mm"], 1)
            C.S.wait(sm["mm"], s.val(sm["mm"]))
            if fi >= 4:
                C.S.wait(sm["ve"], b.ffn_sq_marks[fi - 4])
            C.S.activation(out=b.uff[:, (fi % 4) * TC:(fi % 4 + 1) * TC],
                           in_=pf[:, 0:TC], func=AF.Relu).inc(sm["ac"], 1)
            b.ffn_relu_marks.append(s.val(sm["ac"]))
            C.V.wait(sm["ac"], s.val(sm["ac"]))
            uslice = b.uff[:, (fi % 4) * TC:(fi % 4 + 1) * TC]
            C.V.tensor_mul(out=b.hbuf[:, fi * TC:(fi + 1) * TC], in0=uslice,
                           in1=uslice).inc(sm["ve"], 1)
            b.ffn_sq_marks.append(s.val(sm["ve"]))
        v_h = s.val(sm["ve"])

        if c >= 1:
            C.V.wait(sm["st"], b.ffn_stored[c - 1])
        if LV < 2:
            C.V.tensor_copy(out=b.rout[:, 0:nd * TC], in_=xch).inc(sm["ve"], 1)
            b.ffn_cons[c] = s.val(sm["ve"])
            C.Y.wait(sm["ve"], b.ffn_cons[c])
            dst_ap = _ap3(dst, [[T, 128], [128 * T, nd], [1, TC]], offset=c * TC)
            C.Y.dma_start(out=dst_ap, in_=b.rout[:, 0:nd * TC]).inc(sm["st"], 16)
            b.ffn_stored[c] = s.val(sm["st"])
            continue
        for g in range(2):
            C.T.wait(sm["ve"], s.val(sm["ve"]))
            if LV >= 1 and g == 0:
                C.T.wait(sm["ac"], s.val(sm["ac"]))
            for fi in range(NF):
                wb = b.wpb0 if fi % 2 == 0 else b.wpb1
                if len(b.ffn_wpmarks) >= 2:
                    C.Y.wait(sm["mm"], b.ffn_wpmarks[-2])
                if c == 0 and g == 0 and fi == 0:
                    C.Y.wait(sm["gld"], s.val(sm["gld"]))
                ld_ap = _ap3(dr["wproj_bf"], [[D, 128], [1, TC]],
                             offset=fi * 128 * D + g * TC)
                C.Y.dma_start(out=wb[:, 0:TC], in_=ld_ap).inc(sm["sld"], 16)
                v_w = s.val(sm["sld"])
                C.T.wait(sm["sld"], v_w)
                if fi == 0:
                    C.T.wait(sm["ve"], v_h)
                if LV != 15:
                    for jo in range(4):
                        mm = C.T.matmul(b.py[jo][:, 0:TC], wb[:, jo * 128:(jo + 1) * 128],
                                        b.hbuf[:, fi * TC:(fi + 1) * TC],
                                        start=(fi == 0), stop=(fi == NF - 1))
                        if jo == 3:
                            mm.inc(sm["mm"], 1)
                else:
                    C.T.nop().inc(sm["mm"], 1)
                b.ffn_wpmarks.append(s.val(sm["mm"]))
            v_y = s.val(sm["mm"])
            if LV in (15, 17):
                continue
            if LV == 20:
                for jo in range(4):
                    jd = g * 4 + jo
                    C.V.wait(sm["mm"], v_y)
                    C.V.tensor_copy(out=b.rout[:, jd * TC:(jd + 1) * TC],
                                    in_=xch[:, jd * TC:(jd + 1) * TC]).inc(sm["ve"], 1)
                continue
            if LV == 18:
                for jo in range(4):
                    jd = g * 4 + jo
                    C.S.wait(sm["mm"], v_y)
                    C.S.activation(out=b.rout[:, jd * TC:(jd + 1) * TC],
                                   in_=b.py[jo][:, 0:TC], func=AF.Identity).inc(sm["ac"], 1)
                if True:
                    C.V.wait(sm["ac"], s.val(sm["ac"]))
                    C.V.engine_nop().inc(sm["ve"], 1)
                continue
            if LV < 3:
                for jo in range(4):
                    jd = g * 4 + jo
                    C.V.wait(sm["mm"], v_y)
                    C.V.tensor_copy(out=b.rout[:, jd * TC:(jd + 1) * TC],
                                    in_=b.py[jo][:, 0:TC]).inc(sm["ve"], 1)
                continue
            for jo in range(4):
                jd = g * 4 + jo
                pyj = b.py[jo]
                if dbg is not None and "dbg_y" in dbg:
                    C.V.wait(sm["mm"], v_y)
                    C.V.tensor_copy(out=b.rout[:, jd * TC:(jd + 1) * TC],
                                    in_=pyj[:, 0:TC]).inc(sm["ve"], 1)
                    C.Y.wait(sm["ve"], s.val(sm["ve"]))
                    dba = _ap3(dbg["dbg_y"], [[T, 128], [1, TC]],
                               offset=(jd * 128) * T + c * TC)
                    C.Y.dma_start(out=dba, in_=b.rout[:, jd * TC:(jd + 1) * TC]).inc(sm["st"], 16)
                    C.V.wait(sm["st"], s.val(sm["st"]))
                C.S.wait(sm["mm"], v_y)
                C.S.wait(sm["ve"], s.val(sm["ve"]))
                if LV >= 25:
                    for k in range(3):
                        act = C.S.activation(out=b.sgn[k][:, 0:TC], in_=pyj[:, 0:TC], func=AF.Sign,
                                             bias=b.cpool[:, 1 + k:2 + k])
                        if k == 2:
                            act.inc(sm["ac"], 1)
                else:
                    C.S.nop().inc(sm["ac"], 1)
                v_sgn = s.val(sm["ac"])
                C.V.wait(sm["mm"], v_y)
                C.V.wait(sm["ac"], v_sgn)
                if LV >= 26:
                    C.V.tensor_single_scalar(out=b.ind4[:, 0:TC], in_=pyj[:, 0:TC],
                                             scalar=b.cpool[:, 6:7], op=ALU.is_gt)
                    C.V.tensor_single_scalar(out=b.ind5[:, 0:TC], in_=pyj[:, 0:TC],
                                             scalar=b.cpool[:, 7:8], op=ALU.is_gt)
                if LV >= 27:
                    C.V.tensor_add(out=b.sa, in0=b.sgn[0][:, 0:TC], in1=b.sgn[1][:, 0:TC])
                    C.V.tensor_add(out=b.sa, in0=b.sa, in1=b.sgn[2][:, 0:TC])
                    C.V.tensor_add(out=b.sb_, in0=b.ind4[:, 0:TC], in1=b.ind5[:, 0:TC])
                xsl = xch[:, jd * TC:(jd + 1) * TC]
                rsl = b.rout[:, jd * TC:(jd + 1) * TC]
                if LV >= 28:
                    C.V.scalar_tensor_tensor(out=rsl, in0=b.sa, scalar=0.1,
                                             in1=xsl, op0=ALU.mult, op1=ALU.add)
                    C.V.scalar_tensor_tensor(out=rsl, in0=b.sb_, scalar=0.2,
                                             in1=rsl, op0=ALU.mult, op1=ALU.add)
                    C.V.tensor_scalar_add(out=rsl, in0=rsl, scalar1=0.3).inc(sm["ve"], 1)
                else:
                    C.V.tensor_copy(out=rsl, in_=xsl).inc(sm["ve"], 1)
        if LV in (15, 17):
            C.V.wait(sm["mm"], s.val(sm["mm"]))
            C.V.tensor_copy(out=b.rout[:, 0:nd * TC], in_=xch).inc(sm["ve"], 1)
        b.ffn_cons[c] = s.val(sm["ve"])
        C.Y.wait(sm["ve"], b.ffn_cons[c])
        dst_ap = _ap3(dst, [[T, 128], [128 * T, nd], [1, TC]], offset=c * TC)
        C.Y.dma_start(out=dst_ap, in_=b.rout[:, 0:nd * TC]).inc(sm["st"], 16)
        b.ffn_stored[c] = s.val(sm["st"])
        if dbg is not None and c == 0:
            pass
    C.Y.wait(sm["st"], b.ffn_stored[NCH - 1])


def alloc_ffn_bufs(pa, shared):
    b = shared
    b.wfc = pa.sb("wfc", [128, 8 * FF], BF16)
    b.wpst = pa.sb("wpst", [128, D], BF16)
    b.wpb0 = pa.sb("wpb0", [128, TC], BF16)
    b.wpb1 = pa.sb("wpb1", [128, TC], BF16)
    b.hbuf = pa.sb("hbuf", [128, FF // 128 * TC], BF16)
    b.uff = pa.sb("uff", [128, 4 * TC], BF16)
    b.sgn = [pa.sb(f"sgn{k}", [128, TC], BF16) for k in range(3)]
    b.ind4 = pa.sb("ind4", [128, TC], BF16)
    b.ind5 = pa.sb("ind5", [128, TC], BF16)
    import os
    if os.environ.get("DEV_NOALIAS2"):
        b.sa = pa.sb("sa2", [128, TC], F32)[:, 0:TC]
        b.sb_ = pa.sb("sb2", [128, TC], F32)[:, 0:TC]
    else:
        b.sa = b.scrA[:, 0:TC]
        b.sb_ = b.scrA[:, TC:2 * TC]
    b.py = [b.pA, b.pB, b.pC, b.pD]
    b.ffn_wpmarks = []
    b.ffn_cons = {}
    b.ffn_stored = {}
    return b


def alloc_shared_bufs(pa):
    b = NS()
    nd = D // 128
    b.xch = pa.sb("xch", [128, 2 * nd * TC], F32)
    b.scrA = pa.sb("scrA", [128, nd * TC], F32)
    b.xbf = pa.sb("xbf", [128, nd * TC], BF16)
    b.xn = pa.sb("xn", [128, nd * TC], BF16)
    import os
    if os.environ.get("DEV_NOALIAS"):
        b.xsqbf = pa.sb("xsqbf2", [128, nd * TC], BF16)
    else:
        b.xsqbf = b.xn
    b.rout = pa.sb("rout", [128, nd * TC], F32)
    b.lnrow = pa.sb("lnrow", [1, 3 * TC], F32)
    b.lnrow_r = pa.sb("lnrow_r", [1, TC], F32)
    b.lnrow_r2 = pa.sb("lnrow_r2", [1, TC], F32)
    b.cpool = pa.sb("sb_cpool", [128, 16], F32)
    b.ident_sb = pa.sb("sb_ident", [128, 128], F32)
    b.triu_sb = pa.sb("sb_triu", [128, 128], F32)
    b.dtbrep = pa.sb("sb_dtbrep", [128, MH], F32)
    b.ones_bf = pa.sb("sb_ones_bf", [128, 1], BF16)
    b.onesrow_f = pa.sb("sb_onesrow_f", [1, 128], F32)
    b.onesrow_bf = pa.sb("sb_onesrow_bf", [1, 512], BF16)
    b.pA = pa.ps("pA", [128, TC])
    b.pB = pa.ps("pB", [128, TC])
    b.pC = pa.ps("pC", [128, TC])
    b.pD = pa.ps("pD", [128, TC])
    b.pE = pa.ps("pE", [128, TC])
    b.pF = pa.ps("pF", [128, TC])
    return b


def emit_const_loads(C, dr, b, sm):
    C.Y.dma_start(out=b.cpool[:], in_=dr["consts"]).inc(sm["ld"], 16)
    C.Y.dma_start(out=b.ones_bf[:], in_=dr["ones_bf"]).inc(sm["ld"], 16)
    C.Y.dma_start(out=b.onesrow_f[:], in_=dr["onesrow_f"]).inc(sm["ld"], 16)
    C.Y.dma_start(out=b.onesrow_bf[:], in_=dr["onesrow_bf"]).inc(sm["ld"], 16)
    C.Y.dma_start(out=b.ident_sb[:], in_=dr["ident"]).inc(sm["ld"], 16)
    C.Y.dma_start(out=b.triu_sb[:], in_=dr["triu01"]).inc(sm["ld"], 16)
    C.Y.dma_start(out=b.dtbrep[:], in_=dr["dtb_rep"]).inc(sm["ld"], 16)
    v = C.s.val(sm["ld"])
    for e in (C.V, C.S, C.T, C.G):
        e.wait(sm["ld"], v)


NJT = CONV // 128
NHT = INTER // 128


def alloc_mamba_a_bufs(pa, b):
    b.win = pa.sb("win", [128, 8 * PROJ], BF16)
    b.binrow = pa.sb("binrow", [1, PROJ], BF16)
    b.cw = pa.sb("cw", [128, NJT * MK], F32)
    b.hbct = pa.sb("hbct", [128, MK - 1 + TC], F32)
    b.tails = pa.sb("tails", [128, NJT * (MK - 1)], F32)
    b.convo = pa.sb("convo", [128, TC], F32)
    b.hsp = pa.sb("hsp", [128, NHT * TC // 2], BF16)
    b.bcsp = pa.sb("bcsp", [128, 2 * TC], BF16)
    b.gbuf = pa.sb("gbuf", [128, INTER], BF16)
    b.dtt = pa.sb("dtt", [128, 4 * MH], F32)
    b.dts1 = pa.sb("dts1", [128, MH], F32)
    b.dts2 = pa.sb("dts2", [128, MH], F32)
    b.aexp = pa.sb("aexp", [MH, 1], F32)
    b.alog = pa.sb("alog", [MH, 1], F32)
    b.dtc = pa.sb("dtc", [32, TC], F32)
    b.acumc = pa.sb("acumc", [32, TC], F32)
    b.zrow = pa.sb("zrow", [32, 128], F32)
    b.pG = pa.ps("pG", [128, 128])


def alloc_mamba_b_bufs(pa, b):
    b.htok = [pa.sb(f"htok{i}", [128, INTER], BF16) for i in range(2)]
    b.btok = [pa.sb(f"btok{i}", [128, 128], BF16) for i in range(2)]
    b.bchan = [pa.sb(f"bchan{i}", [128, 128], BF16) for i in range(2)]
    b.cchan = [pa.sb(f"cchan{i}", [128, 128], BF16) for i in range(2)]
    b.gchk = [pa.sb(f"gchk{i}", [128, INTER], BF16) for i in range(2)]
    b.dtk = [pa.sb(f"dtk{i}", [128, MH], F32) for i in range(2)]
    b.acu = [pa.sb(f"acu{i}", [MH, 128], F32) for i in range(2)]
    b.arep = [pa.sb("arep0", [128, MH * 128], F32)]
    b.acumT = pa.sb("acumT", [128, MH], F32)
    b.decc = pa.sb("decc", [MH, 128], F32)
    b.sdoc = pa.sb("sdoc", [MH, 128], F32)
    b.decT = pa.sb("decT", [128, MH], F32)
    b.sdoT = pa.sb("sdoT", [128, MH], F32)
    b.escrow = pa.sb("escrow", [1, MH], F32)
    b.escM = pa.sb("escM", [128, MH], F32)
    b.expbf = pa.sb("expbf", [128, MH * 128], BF16)
    b.gmbf = pa.sb("gmbf", [128, 128], BF16)
    b.xx = pa.sb("xx", [128, INTER], BF16)
    b.xx2 = pa.sb("xx2", [128, INTER], BF16)
    b.ssb = [pa.sb(f"ssb{i}", [128, INTER], BF16) for i in range(2)]
    b.stmp = pa.sb("stmp", [128, INTER], BF16)
    b.gbig = pa.sb("gbig", [128, INTER], BF16)
    b.gsq = pa.sb("gsq", [128, 1], F32)
    b.drep = pa.sb("drep", [128, INTER], F32)
    b.nwrep = pa.sb("nwrep", [128, INTER], F32)
    b.droww = pa.sb("droww", [1, INTER], F32)
    b.nwrow = b.droww
    b.pG = pa.ps("pGb", [128, 128])
    b.pE2 = pa.ps("pE2", [128, 128])


def alloc_mamba_c_bufs(pa, b):
    b.wout = pa.sb("wout", [128, NHT * D], BF16)
    b.boutrow = pa.sb("boutrow", [1, D], BF16)
    b.gT = [pa.sb(f"gT{i}", [128, TC], BF16) for i in range(NHT)]


def emit_mamba_a(C, dr, b, sm, src):
    s = C.s
    nd = D // 128
    for j in range(nd):
        sap = _ap3(dr["winT"], [[PROJ, 128], [1, PROJ]], offset=j * 128 * PROJ)
        C.G.dma_start(out=b.win[:, j * PROJ:(j + 1) * PROJ], in_=sap).inc(sm["gld"], 16)
    C.G.dma_start(out=b.binrow[:], in_=dr["b_in"]).inc(sm["gld"], 16)
    C.Y.dma_start(out=b.cw[:], in_=dr["conv_w_t"]).inc(sm["ld"], 16)
    C.Y.dma_start(out=b.alog[:], in_=dr["a_log"]).inc(sm["ld"], 16)
    v_w = s.val(sm["gld"])
    v_l = s.val(sm["ld"])
    C.T.wait(sm["gld"], v_w)
    C.S.wait(sm["ld"], v_l)
    C.S.activation(out=b.aexp[:], in_=b.alog[:], func=AF.Exp).inc(sm["ac"], 1)
    C.V.wait(sm["ac"], s.val(sm["ac"]))
    C.V.tensor_scalar_mul(out=b.aexp[:], in0=b.aexp[:], scalar1=-1.0).inc(sm["ve"], 1)
    C.V.memset(b.zrow[:], 0.0)
    C.V.memset(b.tails[:], 0.0).inc(sm["ve"], 1)

    for c in range(NCH):
        cb = c % 2
        xch = b.xch[:, cb * nd * TC:(cb + 1) * nd * TC]
        if c >= 2:
            C.Y.wait(sm["ve"], b.ma_cons[c - 2])
        sap = _ap3(src, [[T, 128], [128 * T, nd], [1, TC]], offset=c * TC)
        C.Y.dma_start(out=xch, in_=sap).inc(sm["x"], 16)
        v_x = s.val(sm["x"])
        C.S.wait(sm["x"], v_x)
        C.T.wait(sm["x"], v_x)
        C.V.wait(sm["x"], v_x)
        v_xn = emit_ln_chunk(C, b, sm, xch, TC)
        b.ma_cons[c] = v_xn
        C.T.wait(sm["ve"], v_xn)
        if c == 0:
            C.T.wait(sm["gld"], v_w)

        if c >= 1:
            C.T.wait(sm["st"], b.ma_stored[c - 1])
            C.S.wait(sm["st"], b.ma_stored[c - 1])
            C.V.wait(sm["st"], b.ma_stored[c - 1])
        for jt in range(NJT):
            pf = b.pE if jt % 2 == 0 else b.pF
            co = INTER + jt * 128
            if jt >= 2:
                C.T.wait(sm["ve"], s.val(sm["ve"]))
            for j in range(nd):
                C.T.matmul(pf[:, 0:TC], b.win[:, j * PROJ + co:j * PROJ + co + 128],
                           b.xn[:, j * TC:(j + 1) * TC],
                           start=(j == 0), stop=False)
            C.T.matmul(pf[:, 0:TC], b.binrow[0:1, co:co + 128], b.onesrow_bf[0:1, 0:TC],
                       start=False, stop=True).inc(sm["mm"], 1)
            v_p = s.val(sm["mm"])
            C.V.wait(sm["mm"], v_p)
            C.V.wait(sm["ac"], s.val(sm["ac"]))
            C.V.tensor_copy(out=b.hbct[:, MK - 1:MK - 1 + TC], in_=pf[:, 0:TC])
            C.V.tensor_copy(out=b.hbct[:, 0:MK - 1], in_=b.tails[:, jt * (MK - 1):(jt + 1) * (MK - 1)])
            C.V.drain()
            C.V.tensor_scalar(out=b.convo[:, 0:TC], in0=b.hbct[:, 0:TC],
                              scalar1=b.cw[:, jt * MK:jt * MK + 1], scalar2=None, op0=ALU.mult)
            for i in range(1, MK):
                C.V.scalar_tensor_tensor(out=b.convo[:, 0:TC], in0=b.hbct[:, i:i + TC],
                                         scalar=b.cw[:, jt * MK + i:jt * MK + i + 1],
                                         in1=b.convo[:, 0:TC], op0=ALU.mult, op1=ALU.add)
            C.V.tensor_copy(out=b.tails[:, jt * (MK - 1):(jt + 1) * (MK - 1)],
                            in_=b.hbct[:, TC:TC + MK - 1]).inc(sm["ve"], 1)
            v_cv = s.val(sm["ve"])
            C.S.wait(sm["ve"], v_cv)
            if jt < NHT:
                if jt == 8 and True:
                    C.S.wait(sm["st"], s.val(sm["st"]))
                dst = b.hsp[:, (jt % 8) * TC:(jt % 8 + 1) * TC]
            elif jt == NHT:
                dst = b.bcsp[:, 0:TC]
            else:
                dst = b.bcsp[:, TC:2 * TC]
            C.S.activation(out=dst, in_=b.convo[:, 0:TC], func=AF.Silu).inc(sm["ac"], 1)
            if jt == 7 or jt == 15:
                vh = s.val(sm["ac"])
                C.Y.wait(sm["ac"], vh)
                half = 0 if jt == 7 else 1
                hap = _ap3(dr["h_scr"], [[T, 128], [128 * T, 8], [1, TC]],
                           offset=half * 8 * 128 * T + c * TC)
                C.Y.dma_start(out=hap, in_=b.hsp[:, 0:8 * TC]).inc(sm["st"], 16)
        v_sta = s.val(sm["ac"])

        if (c + 1) % (S // TC) == 0:
            C.V.wait(sm["ac"], v_sta)
            C.V.memset(b.tails[:], 0.0).inc(sm["ve"], 1)

        for tt in range(4):
            pg = [b.pA, b.pB, b.pC, b.pD]
            C.T.wait(sm["ac"], s.val(sm["ac"]))
            C.T.wait(sm["ve"], s.val(sm["ve"]))
            for bi in range(4):
                for j in range(nd):
                    C.T.matmul(pg[bi][:, 0:TC], b.xn[:, j * TC + tt * 128:j * TC + tt * 128 + 128],
                               b.win[:, j * PROJ + bi * TC:j * PROJ + (bi + 1) * TC],
                               start=(j == 0), stop=False)
                C.T.matmul(pg[bi][:, 0:TC], b.onesrow_bf[0:1, tt * 128:tt * 128 + 128],
                           b.binrow[0:1, bi * TC:(bi + 1) * TC],
                           start=False, stop=True).inc(sm["mm"], 1)
            for j in range(nd):
                C.T.matmul(b.pG[:, 0:MH], b.xn[:, j * TC + tt * 128:j * TC + tt * 128 + 128],
                           b.win[:, j * PROJ + INTER + CONV:j * PROJ + PROJ],
                           start=(j == 0), stop=False)
            C.T.matmul(b.pG[:, 0:MH], b.onesrow_bf[0:1, tt * 128:tt * 128 + 128],
                       b.binrow[0:1, INTER + CONV:PROJ], start=False, stop=True).inc(sm["mm"], 1)
            v_g = s.val(sm["mm"])
            C.S.wait(sm["mm"], v_g)
            if c > 0 or tt > 0:
                C.S.wait(sm["st"], s.val(sm["st"]))
            for bi in range(4):
                C.S.activation(out=b.gbuf[:, bi * TC:(bi + 1) * TC],
                               in_=pg[bi][:, 0:TC], func=AF.Silu).inc(sm["ac"], 1)
            v_gb = s.val(sm["ac"])
            C.Y.wait(sm["ac"], v_gb)
            gap = _ap3(dr["gate_scr"], [[INTER, 128], [1, INTER]],
                       offset=(c * 4 + tt) * 128 * INTER)
            C.Y.dma_start(out=gap, in_=b.gbuf[:, 0:INTER]).inc(sm["st"], 16)
            C.V.wait(sm["mm"], v_g)
            dsl = b.dtt[:, tt * MH:(tt + 1) * MH]
            C.V.tensor_tensor(out=b.dts2[:, 0:MH], in0=b.pG[:, 0:MH], in1=b.dtbrep[:, 0:MH],
                              op=ALU.add)
            C.V.drain()
            C.V.tensor_scalar_max(out=b.dts1[:, 0:MH], in0=b.dts2[:, 0:MH], scalar1=0.0).inc(sm["ve"], 1)
            v_d1 = s.val(sm["ve"])
            C.S.wait(sm["ve"], v_d1)
            C.S.activation(out=b.dts2[:, 0:MH], in_=b.dts2[:, 0:MH], func=AF.Abs).inc(sm["ac"], 1)
            C.S.wait(sm["ac"], s.val(sm["ac"]))
            C.S.activation(out=b.dts2[:, 0:MH], in_=b.dts2[:, 0:MH], func=AF.Exp,
                           scale=-1.0).inc(sm["ac"], 1)
            C.S.wait(sm["ac"], s.val(sm["ac"]))
            C.S.activation(out=b.dts2[:, 0:MH], in_=b.dts2[:, 0:MH], func=AF.Ln,
                           bias=b.cpool[:, 8:9]).inc(sm["ac"], 1)
            v_d2 = s.val(sm["ac"])
            C.V.wait(sm["ac"], v_d2)
            C.V.drain()
            C.V.tensor_add(out=dsl, in0=b.dts1[:, 0:MH], in1=b.dts2[:, 0:MH]).inc(sm["ve"], 1)
            v_dt = s.val(sm["ve"])
            C.T.wait(sm["ve"], v_dt)
            C.T.transpose(b.pG[0:MH, 0:128], dsl, identity=b.ident_sb[:, 0:128]).inc(sm["mm"], 1)
            v_tr = s.val(sm["mm"])
            C.V.wait(sm["mm"], v_tr)
            C.V.tensor_copy(out=b.dtc[:, tt * 128:(tt + 1) * 128], in_=b.pG[0:MH, 0:128])
            C.V.drain()
            C.V.tensor_scalar_mul(out=b.dtc[:, tt * 128:(tt + 1) * 128],
                                  in0=b.dtc[:, tt * 128:(tt + 1) * 128],
                                  scalar1=b.aexp[0:MH, 0:1])
            C.V.tensor_tensor_scan(out=b.acumc[0:MH, tt * 128:(tt + 1) * 128],
                                   data0=b.zrow[0:MH, 0:128],
                                   data1=b.dtc[0:MH, tt * 128:(tt + 1) * 128],
                                   initial=0.0, op0=ALU.add, op1=ALU.add).inc(sm["ve"], 1)
        v_chunk = s.val(sm["ve"])

        C.Y.wait(sm["ac"], v_sta)
        bap = _ap3(dr["b_scr"], [[T, 128], [1, TC]], offset=c * TC)
        C.Y.dma_start(out=bap, in_=b.bcsp[:, 0:TC]).inc(sm["st"], 16)
        cap = _ap3(dr["c_scr"], [[T, 128], [1, TC]], offset=c * TC)
        C.Y.dma_start(out=cap, in_=b.bcsp[:, TC:2 * TC]).inc(sm["st"], 16)
        C.Y.wait(sm["ve"], v_chunk)
        dap = _ap3(dr["dt_scr"], [[MH, 128], [128 * MH, 4], [1, MH]], offset=c * 4 * 128 * MH)
        C.Y.dma_start(out=dap, in_=b.dtt[:, 0:4 * MH]).inc(sm["st"], 16)
        aap = _ap3(dr["acum_scr"], [[T, MH], [1, TC]], offset=c * TC)
        C.Y.dma_start(out=aap, in_=b.acumc[0:MH, 0:TC]).inc(sm["st"], 16)
        b.ma_stored[c] = s.val(sm["st"])
    C.Y.wait(sm["st"], b.ma_stored[NCH - 1])


def emit_mamba_b(C, dr, b, sm, dbg=None):
    s = C.s
    NU = S // CS
    C.Y.dma_start(out=b.droww[:], in_=dr["d_row"]).inc(sm["ld"], 16)
    v_l = s.val(sm["ld"])
    C.T.wait(sm["ld"], v_l)
    C.T.matmul(b.pA[:, 0:TC], b.onesrow_f[0:1, :], b.droww[0:1, 0:INTER // 4], start=True, stop=True)
    C.T.matmul(b.pB[:, 0:TC], b.onesrow_f[0:1, :], b.droww[0:1, INTER // 4:INTER // 2], start=True, stop=True)
    C.T.matmul(b.pC[:, 0:TC], b.onesrow_f[0:1, :], b.droww[0:1, INTER // 2:3 * INTER // 4], start=True, stop=True)
    C.T.matmul(b.pD[:, 0:TC], b.onesrow_f[0:1, :], b.droww[0:1, 3 * INTER // 4:INTER], start=True, stop=True).inc(sm["mm"], 1)
    v_mm = s.val(sm["mm"])
    C.V.wait(sm["mm"], v_mm)
    for q in range(4):
        cp = C.V.tensor_copy(out=b.drep[:, q * TC:(q + 1) * TC],
                             in_=[b.pA, b.pB, b.pC, b.pD][q][:, 0:TC])
        if q == 3:
            cp.inc(sm["ve"], 1)
    v_dr = s.val(sm["ve"])
    C.Y.wait(sm["ve"], v_dr)
    C.Y.dma_start(out=b.nwrow[:], in_=dr["normw_row"]).inc(sm["ld"], 16)
    v_l2 = s.val(sm["ld"])
    C.T.wait(sm["ld"], v_l2)
    C.T.wait(sm["ve"], v_dr)
    C.T.matmul(b.pA[:, 0:TC], b.onesrow_f[0:1, :], b.nwrow[0:1, 0:TC], start=True, stop=True)
    C.T.matmul(b.pB[:, 0:TC], b.onesrow_f[0:1, :], b.nwrow[0:1, TC:2 * TC], start=True, stop=True)
    C.T.matmul(b.pC[:, 0:TC], b.onesrow_f[0:1, :], b.nwrow[0:1, 2 * TC:3 * TC], start=True, stop=True)
    C.T.matmul(b.pD[:, 0:TC], b.onesrow_f[0:1, :], b.nwrow[0:1, 3 * TC:INTER], start=True, stop=True).inc(sm["mm"], 1)
    v_mm2 = s.val(sm["mm"])
    C.V.wait(sm["mm"], v_mm2)
    for q in range(4):
        C.V.tensor_copy(out=b.nwrep[:, q * TC:(q + 1) * TC], in_=[b.pA, b.pB, b.pC, b.pD][q][:, 0:TC]).inc(sm["ve"], 1)
    v_nw = s.val(sm["ve"])

    for bi in range(B_LOC):
        C.V.wait(sm["mm"], s.val(sm["mm"]))
        C.V.memset(b.ssb[0][:], 0.0).inc(sm["ve"], 1)
        for u in range(NU):
            ub = u % 2
            tof = bi * S + u * CS
            sp, sn = b.ssb[u % 2], b.ssb[(u + 1) % 2]
            if u >= 2:
                C.Y.wait(sm["ve"], b.mb_cons[(bi, u - 2)])
                C.S.wait(sm["ve"], b.mb_cons[(bi, u - 2)])
            hin = _ap3(dr["h_scr"], [[T, INTER], [1, CS]], offset=tof)
            C.Y.dmat(out=b.htok[ub][:, 0:INTER], in_=hin).inc(sm["tr"], 16)
            bin_ = _ap3(dr["b_scr"], [[T, 128], [1, CS]], offset=tof)
            C.Y.dma_start(out=b.bchan[ub][:], in_=bin_).inc(sm["x"], 16)
            C.Y.dmat(out=b.btok[ub][:], in_=bin_).inc(sm["tr"], 16)
            cin = _ap3(dr["c_scr"], [[T, 128], [1, CS]], offset=tof)
            C.Y.dma_start(out=b.cchan[ub][:], in_=cin).inc(sm["x"], 16)
            gin = _ap3(dr["gate_scr"], [[INTER, 128], [1, INTER]], offset=tof * INTER)
            C.Y.dma_start(out=b.gchk[ub][:], in_=gin).inc(sm["x"], 16)
            din = _ap3(dr["dt_scr"], [[MH, 128], [1, MH]], offset=tof * MH)
            C.Y.dma_start(out=b.dtk[ub][:], in_=din).inc(sm["x"], 16)
            ain = _ap3(dr["acum_scr"], [[T, MH], [1, CS]], offset=tof)
            C.Y.dma_start(out=b.acu[ub][:], in_=ain).inc(sm["x"], 16)
            arin = _ap3(dr["acum_scr"], [[0, 128], [T, MH], [1, CS]], offset=tof)
            C.S.wait(sm["ve"], b.mb_difdone)
            C.S.dma_start(out=b.arep[0][:, 0:MH * 128], in_=arin).inc(sm["sld"], 16)
            v_ld = s.val(sm["x"])
            v_tr = s.val(sm["tr"])
            v_ar = s.val(sm["sld"])

            C.T.wait(sm["x"], v_ld)
            C.T.wait(sm["ve"], s.val(sm["ve"]))
            C.T.transpose(b.pG[0:128, 0:MH], b.acu[ub][0:MH, 0:128],
                          identity=b.ident_sb[0:MH, 0:MH]).inc(sm["mm"], 1)
            vt1 = s.val(sm["mm"])
            C.V.wait(sm["mm"], vt1)
            C.V.tensor_copy(out=b.acumT[:, 0:MH], in_=b.pG[0:128, 0:MH]).inc(sm["ve"], 1)
            vacT = s.val(sm["ve"])
            C.S.wait(sm["x"], v_ld)
            C.S.activation(out=b.decc[0:MH, 0:128], in_=b.acu[ub][0:MH, 0:128],
                           func=AF.Exp, scale=-1.0, bias=b.acu[ub][0:MH, CS - 1:CS])
            C.S.activation(out=b.sdoc[0:MH, 0:128], in_=b.acu[ub][0:MH, 0:128],
                           func=AF.Exp).inc(sm["ac"], 1)
            vexp = s.val(sm["ac"])
            C.T.wait(sm["ac"], vexp)
            C.T.wait(sm["ve"], vacT)
            C.T.transpose(b.pG[0:128, 0:MH], b.decc[0:MH, 0:128],
                          identity=b.ident_sb[0:MH, 0:MH]).inc(sm["mm"], 1)
            C.V.wait(sm["mm"], s.val(sm["mm"]))
            C.V.tensor_copy(out=b.decT[:, 0:MH], in_=b.pG[0:128, 0:MH]).inc(sm["ve"], 1)
            C.T.wait(sm["ve"], s.val(sm["ve"]))
            C.T.transpose(b.pG[0:128, 0:MH], b.sdoc[0:MH, 0:128],
                          identity=b.ident_sb[0:MH, 0:MH]).inc(sm["mm"], 1)
            C.V.wait(sm["mm"], s.val(sm["mm"]))
            C.V.tensor_copy(out=b.sdoT[:, 0:MH], in_=b.pG[0:128, 0:MH]).inc(sm["ve"], 1)
            C.T.wait(sm["ve"], s.val(sm["ve"]))
            C.T.transpose(b.pG[0:1, 0:MH], b.sdoc[0:MH, CS - 1:CS],
                          identity=b.ident_sb[0:MH, 0:MH]).inc(sm["mm"], 1)
            C.V.wait(sm["mm"], s.val(sm["mm"]))
            C.V.tensor_copy(out=b.escrow[0:1, 0:MH], in_=b.pG[0:1, 0:MH]).inc(sm["ve"], 1)
            vesc = s.val(sm["ve"])
            C.T.wait(sm["ve"], vesc)
            C.T.matmul(b.pG[:, 0:MH], b.onesrow_f[0:1, :], b.escrow[0:1, 0:MH],
                       start=True, stop=True).inc(sm["mm"], 1)
            C.V.wait(sm["mm"], s.val(sm["mm"]))
            C.V.tensor_copy(out=b.escM[:, 0:MH], in_=b.pG[:, 0:MH]).inc(sm["ve"], 1)
            vescM = s.val(sm["ve"])

            if not (bi == 0 and u == 0):
                C.V.wait(sm["st"], b.mb_stored)
            C.S.wait(sm["mm"], s.val(sm["mm"]))
            C.V.wait(sm["sld"], v_ar)
            arr = b.arep[0]
            acT_rep = _ap3(b.acumT, [[b.acumT.ap().ap[0][0], 128], [1, MH], [0, 128]])
            C.V.tensor_tensor(out=b.scrA[:, 0:MH * 128], in0=arr[:, 0:MH * 128],
                              in1=acT_rep, op=ALU.subtract)
            C.V.tensor_scalar_min(out=b.scrA[:, 0:MH * 128], in0=b.scrA[:, 0:MH * 128],
                                  scalar1=0.0).inc(sm["ve"], 1)
            vdif = s.val(sm["ve"])
            b.mb_difdone = vdif
            if dbg is not None and bi == 0 and u == 0 and "dbg_dif2" in dbg:
                C.G.wait(sm["ve"], vdif)
                C.G.dma_start(out=dbg["dbg_dif2"], in_=b.scrA[:, 0:MH * 128]).inc(sm["gld"], 16)
                C.V.wait(sm["gld"], s.val(sm["gld"]))
            C.S.wait(sm["ve"], vdif)
            C.S.activation(out=b.expbf[:, 0:MH * 128], in_=b.scrA[:, 0:MH * 128],
                           func=AF.Exp).inc(sm["ac"], 1)
            vexp2 = s.val(sm["ac"])
            C.T.wait(sm["tr"], v_tr)
            C.T.wait(sm["ve"], s.val(sm["ve"]))
            C.T.matmul(b.pE2[:, 0:128], b.bchan[ub][:], b.cchan[ub][:],
                       start=True, stop=True).inc(sm["mm"], 1)
            vgm = s.val(sm["mm"])
            C.V.wait(sm["mm"], vgm)
            C.V.tensor_tensor(out=b.gmbf[:], in0=b.pE2[:, 0:128], in1=b.triu_sb[:],
                              op=ALU.mult).inc(sm["ve"], 1)
            vgmb = s.val(sm["ve"])
            if dbg is not None and bi == 0 and u == 0 and "dbg_gm" in dbg:
                C.G.wait(sm["ve"], vgmb)
                C.G.dma_start(out=dbg["dbg_gm"], in_=b.gmbf[:, 0:128]).inc(sm["gld"], 16)
                C.V.wait(sm["gld"], s.val(sm["gld"]))
            C.V.wait(sm["ac"], vexp2)
            gm_rep = _ap3(b.gmbf, [[b.gmbf.ap().ap[0][0], 128], [0, MH], [1, 128]])
            C.V.tensor_tensor(out=b.expbf[:, 0:MH * 128], in0=b.expbf[:, 0:MH * 128],
                              in1=gm_rep, op=ALU.mult).inc(sm["ve"], 1)
            vM = s.val(sm["ve"])

            C.V.wait(sm["tr"], v_tr)
            C.V.wait(sm["x"], v_ld)
            dt_rep = _ap3(b.dtk[ub], [[b.dtk[ub].ap().ap[0][0], 128], [1, MH], [0, MP]])
            C.V.tensor_tensor(out=b.xx[:, 0:INTER], in0=b.htok[ub][:, 0:INTER],
                              in1=dt_rep, op=ALU.mult)
            dec_rep = _ap3(b.decT, [[b.decT.ap().ap[0][0], 128], [1, MH], [0, MP]])
            C.V.tensor_tensor(out=b.xx2[:, 0:INTER], in0=b.xx[:, 0:INTER],
                              in1=dec_rep, op=ALU.mult).inc(sm["ve"], 1)
            vX = s.val(sm["ve"])

            for hf in range(2):
                hof = hf * (INTER // 2)
                pY = [b.pA, b.pB] if hf == 0 else [b.pA, b.pB]
                pO = [b.pC, b.pD]
                C.T.wait(sm["ve"], s.val(sm["ve"]))
                for hh in range(16):
                    h = hf * 16 + hh
                    bank = pY[hh // 8]
                    mm = C.T.matmul(bank[:, (hh % 8) * MP:(hh % 8) * MP + MP],
                                    b.expbf[:, h * 128:(h + 1) * 128],
                                    b.xx[:, h * MP:(h + 1) * MP],
                                    start=(hh % 8 == 0), stop=(hh % 8 == 7))
                    if hh == 15:
                        mm.inc(sm["mm"], 1)
                if u > 0:
                    for q in range(2):
                        mm = C.T.matmul(pO[q][:, 0:TC], b.cchan[ub][:],
                                        sp[:, hof + q * TC:hof + (q + 1) * TC],
                                        start=True, stop=True)
                        if q == 1:
                            mm.inc(sm["mm"], 1)
                vy = s.val(sm["mm"])
                C.V.wait(sm["mm"], vy)
                t1a = b.scrA[:, 0:TC]
                t1b = b.scrA[:, TC:2 * TC]
                if u > 0:
                    pst = b.sdoT.ap().ap[0][0]
                    sdo_rep = _ap3(b.sdoT, [[pst, 128], [1, 8], [0, MP]], offset=hf * 16)
                    C.V.tensor_tensor(out=t1a, in0=b.pC[:, 0:TC], in1=sdo_rep, op=ALU.mult)
                    sdo_rep2 = _ap3(b.sdoT, [[pst, 128], [1, 8], [0, MP]], offset=hf * 16 + 8)
                    C.V.tensor_tensor(out=t1b, in0=b.pD[:, 0:TC], in1=sdo_rep2, op=ALU.mult)
                C.G.wait(sm["ve"], s.val(sm["ve"]))
                C.G.tensor_tensor(out=b.stmp[:, 0:2 * TC], in0=b.htok[ub][:, hof:hof + 2 * TC],
                                  in1=b.drep[:, hof:hof + 2 * TC], op=ALU.mult).inc(sm["gp"], 1)
                vt2 = s.val(sm["gp"])
                for q in range(2):
                    pyq = [b.pA, b.pB][q]
                    ysl = b.scrA[:, (2 + q) * TC:(3 + q) * TC]
                    if u > 0:
                        tq = [t1a, t1b][q]
                        C.V.scalar_tensor_tensor(out=ysl, in0=pyq[:, 0:TC], scalar=1.0,
                                                 in1=tq, op0=ALU.mult, op1=ALU.add)
                    else:
                        C.V.tensor_copy(out=ysl, in_=pyq[:, 0:TC])
                C.V.wait(sm["gp"], vt2)
                for q in range(2):
                    ysl = b.scrA[:, (2 + q) * TC:(3 + q) * TC]
                    C.V.tensor_tensor(out=ysl, in0=ysl, in1=b.stmp[:, q * TC:(q + 1) * TC],
                                      op=ALU.add)
                    C.V.tensor_tensor(out=b.gbig[:, hof + q * TC:hof + (q + 1) * TC],
                                      in0=ysl, in1=b.gchk[ub][:, hof + q * TC:hof + (q + 1) * TC],
                                      op=ALU.mult).inc(sm["ve"], 1)
            vg = s.val(sm["ve"])
            if dbg is not None and bi == 0 and u == 0 and "dbg_m" in dbg:
                C.G.wait(sm["ve"], vg)
                C.G.dma_start(out=dbg["dbg_h"], in_=b.htok[ub][:, 0:INTER]).inc(sm["gld"], 16)
                C.G.dma_start(out=dbg["dbg_dt"], in_=b.dtk[ub][:, 0:MH]).inc(sm["gld"], 16)
                C.G.dma_start(out=dbg["dbg_m"], in_=b.expbf[:, 0:MH * 128]).inc(sm["gld"], 16)
                C.G.dma_start(out=dbg["dbg_x"], in_=b.xx[:, 0:INTER]).inc(sm["gld"], 16)
                C.G.dma_start(out=dbg["dbg_g"], in_=b.gbig[:, 0:INTER]).inc(sm["gld"], 16)
                C.G.dma_start(out=dbg["dbg_dif"], in_=b.scrA[:, 2 * TC:4 * TC]).inc(sm["gld"], 16)
                C.G.wait(sm["gld"], s.val(sm["gld"]))
                vtap = s.val(sm["gld"])
                C.V.wait(sm["gld"], vtap)
                C.S.wait(sm["gld"], vtap)
                C.T.wait(sm["gld"], vtap)
                C.Y.wait(sm["gld"], vtap)

            C.T.wait(sm["ve"], vg)
            C.T.wait(sm["tr"], v_tr)
            pS = [b.pA, b.pB, b.pC, b.pD]
            for q in range(4):
                mm = C.T.matmul(pS[q][:, 0:TC], b.btok[ub][:],
                                b.xx2[:, q * TC:(q + 1) * TC], start=True, stop=True)
                if q == 3:
                    mm.inc(sm["mm"], 1)
            vst = s.val(sm["mm"])
            C.G.wait(sm["ve"], s.val(sm["ve"]))
            esc_rep = _ap3(b.escM, [[b.escM.ap().ap[0][0], 128], [1, MH], [0, MP]])
            C.G.tensor_tensor(out=b.stmp[:, 0:INTER], in0=sp[:, 0:INTER],
                              in1=esc_rep, op=ALU.mult).inc(sm["gp"], 1)
            vsc = s.val(sm["gp"])
            C.V.wait(sm["mm"], vst)
            C.V.wait(sm["gp"], vsc)
            for q in range(4):
                C.V.tensor_tensor(out=sn[:, q * TC:(q + 1) * TC], in0=pS[q][:, 0:TC],
                                  in1=b.stmp[:, q * TC:(q + 1) * TC], op=ALU.add).inc(sm["ve"], 1)
            vS = s.val(sm["ve"])

            C.V.tensor_mul(out=b.scrA[:, 0:INTER], in0=b.gbig[:, 0:INTER],
                           in1=b.gbig[:, 0:INTER])
            C.V.reduce_sum(out=b.gsq[:, 0:1], in_=b.scrA[:, 0:INTER],
                           axis=mybir.AxisListType.X).inc(sm["ve"], 1)
            vms = s.val(sm["ve"])
            C.S.wait(sm["ve"], vms)
            C.S.activation(out=b.gsq[:, 0:1], in_=b.gsq[:, 0:1], func=AF.Ln,
                           scale=1.0 / INTER, bias=b.cpool[:, 0:1]).inc(sm["ac"], 1)
            C.S.wait(sm["ac"], s.val(sm["ac"]))
            C.S.activation(out=b.gsq[:, 0:1], in_=b.gsq[:, 0:1], func=AF.Exp,
                           scale=-0.5).inc(sm["ac"], 1)
            vrs = s.val(sm["ac"])
            C.V.wait(sm["ac"], vrs)
            C.V.wait(sm["ve"], v_nw)
            C.V.tensor_scalar_mul(out=b.gbig[:, 0:INTER], in0=b.gbig[:, 0:INTER],
                                  scalar1=b.gsq[:, 0:1])
            C.V.tensor_mul(out=b.gbig[:, 0:INTER], in0=b.gbig[:, 0:INTER],
                           in1=b.nwrep[:, 0:INTER]).inc(sm["ve"], 1)
            vgf = s.val(sm["ve"])
            b.mb_cons[(bi, u)] = vgf
            C.Y.wait(sm["ve"], vgf)
            gsp = _ap3(dr["g_scr"], [[INTER, 128], [1, INTER]], offset=tof * INTER)
            C.Y.dma_start(out=gsp, in_=b.gbig[:, 0:INTER]).inc(sm["st"], 16)
            b.mb_stored = s.val(sm["st"])
    C.Y.wait(sm["st"], b.mb_stored)


def emit_mamba_c(C, dr, b, sm, src, dst):
    s = C.s
    nd = D // 128
    for ci in range(NHT):
        sap = _ap3(dr["woutT"], [[D, 128], [1, D]], offset=ci * 128 * D)
        C.G.dma_start(out=b.wout[:, ci * D:(ci + 1) * D], in_=sap).inc(sm["gld"], 16)
    C.G.dma_start(out=b.boutrow[:], in_=dr["b_out"]).inc(sm["gld"], 16)
    v_w = s.val(sm["gld"])
    C.T.wait(sm["gld"], v_w)
    for c in range(NCH):
        cb = c % 2
        xch = b.xch[:, cb * nd * TC:(cb + 1) * nd * TC]
        if c >= 2:
            C.Y.wait(sm["ve"], b.mc_cons[c - 2])
        sap = _ap3(src, [[T, 128], [128 * T, nd], [1, TC]], offset=c * TC)
        C.Y.dma_start(out=xch, in_=sap).inc(sm["x"], 16)
        v_x = s.val(sm["x"])
        if c >= 1:
            C.Y.wait(sm["mm"], b.mc_mmdone[c - 1])
        for ci in range(NHT):
            gin = _ap3(dr["g_scr"], [[INTER, TC], [1, 128]], offset=c * TC * INTER + ci * 128)
            C.Y.dmat(out=b.gT[ci][:, 0:TC], in_=gin).inc(sm["tr"], 16)
        v_tr = s.val(sm["tr"])
        C.T.wait(sm["tr"], v_tr)
        pg = [b.pA, b.pB, b.pC, b.pD]
        for dt_ in range(nd):
            bank = pg[dt_ % 4]
            if dt_ >= 4:
                C.T.wait(sm["ve"], s.val(sm["ve"]))
            for ci in range(NHT):
                C.T.matmul(bank[:, 0:TC], b.wout[:, ci * D + dt_ * 128:ci * D + (dt_ + 1) * 128],
                           b.gT[ci][:, 0:TC], start=(ci == 0), stop=False)
            mm = C.T.matmul(bank[:, 0:TC], b.boutrow[0:1, dt_ * 128:(dt_ + 1) * 128],
                            b.onesrow_bf[0:1, 0:TC], start=False, stop=True)
            mm.inc(sm["mm"], 1)
            v_y = s.val(sm["mm"])
            C.V.wait(sm["mm"], v_y)
            if dt_ == 0:
                C.V.wait(sm["x"], v_x)
            if c >= 1 and dt_ == 0:
                C.V.wait(sm["st"], b.mc_stored[c - 1])
            C.V.scalar_tensor_tensor(out=b.rout[:, dt_ * TC:(dt_ + 1) * TC],
                                     in0=bank[:, 0:TC], scalar=1.0,
                                     in1=xch[:, dt_ * TC:(dt_ + 1) * TC],
                                     op0=ALU.mult, op1=ALU.add).inc(sm["ve"], 1)
        b.mc_mmdone[c] = s.val(sm["mm"])
        b.mc_cons[c] = s.val(sm["ve"])
        C.Y.wait(sm["ve"], b.mc_cons[c])
        dap = _ap3(dst, [[T, 128], [128 * T, nd], [1, TC]], offset=c * TC)
        C.Y.dma_start(out=dap, in_=b.rout[:, 0:nd * TC]).inc(sm["st"], 16)
        b.mc_stored[c] = s.val(sm["st"])
    C.Y.wait(sm["st"], b.mc_stored[NCH - 1])


def alloc_attn_persist(pa, b):
    b.fullk = pa.sb("fullk", [128, 8 * NSLOT], BF16)
    b.fullv = pa.sb("fullv", [128, 8 * NSLOT], BF16)
    b.vt = pa.sb("vt", [128, 4 * NH * (HD + 1)], BF16)
    b.pG2 = pa.ps("pG2", [128, TC])
    b.pT = pa.ps("pT", [128, HD], BF16)


def alloc_attn_a(pa, b):
    b.wA = pa.sb("wA", [128, 8 * D], BF16)
    b.wB = pa.sb("wB", [128, 8 * D], BF16)
    b.laA = pa.sb("laA", [128, 8 * LORA], BF16)
    b.lbA = pa.sb("lbA", [LORA, D], BF16)
    b.kc = pa.sb("kc", [128, 8 * TC], BF16)
    b.vc = pa.sb("vc", [128, 8 * TC], BF16)
    b.tmplA = pa.sb("tmplA", [LORA, TC], BF16)
    b.rpbA = pa.sb("rpbA", [128, TC], BF16)
    b.rkbA = pa.sb("rkbA", [128, TC], BF16)
    b.cosA = pa.sb("cosA", [128, TC], BF16)
    b.sinA = pa.sb("sinA", [128, TC], BF16)
    b.cdmaA = pa.sb("cdmaA", [128, MK], F32)
    b.identbf = pa.sb("identbf", [128, HD], BF16)


def alloc_attn_b(pa, b):
    b.wq_st = [pa.sb(f"wqst{i}", [128, 8 * 128], BF16) for i in range(2)]
    b.wo_st = [pa.sb(f"wost{i}", [128, 8 * 128], BF16) for i in range(2)]
    b.laB = pa.sb("laB", [128, 8 * LORA], BF16)
    b.lbB = pa.sb("lbB", [LORA, D], BF16)
    b.lc = pa.sb("lc", [128, 8 * LORA], BF16)
    b.ld_ = pa.sb("ld_", [LORA, D], BF16)
    b.kcB = pa.sb("kcB", [128, 8 * TC], BF16)
    b.vcB = pa.sb("vcB", [128, 8 * TC], BF16)
    b.cpkt = pa.sb("cpkt", [128, 8 * TC], BF16)
    b.cpvt = pa.sb("cpvt", [128, 8 * TC], BF16)
    b.qcp = pa.sb("qcp", [128, 8 * TC], BF16)
    b.aqb = pa.sb("aqb", [128, 8 * TC], BF16)
    b.o2 = pa.sb("o2", [128, 8 * TC], BF16)
    b.ebuf = pa.sb("ebuf", [128, 4 * TC], BF16)
    b.tmplB = pa.sb("tmplB", [LORA, TC], BF16)
    b.rpbB = pa.sb("rpbB", [128, TC], BF16)
    b.rkbB = pa.sb("rkbB", [128, TC], BF16)
    b.zrow1 = pa.sb("zrow1", [1, TC], F32)
    b.wcur = pa.sb("wcur", [1, TC], BF16)
    b.e65 = pa.sb("e65", [1, HD + 1], BF16)
    b.zinv = pa.sb("zinv", [1, TC], F32)
    b.qcph = pa.sb("qcph", [128, TC], BF16)
    b.corr = pa.sb("corr", [128, TC], F32)
    b.cosB = pa.sb("cosB", [128, TC], BF16)
    b.sinB = pa.sb("sinB", [128, TC], BF16)
    b.qpaB = pa.sb("qpaB", [128, TC], BF16)
    b.qpoB = pa.sb("qpoB", [128, TC], BF16)
    b.bandB = pa.sb("bandB", [128, TC], BF16)
    b.cdmaB = pa.sb("cdmaB", [128, MK], F32)


def _rope(C, b, sm, psum, cos_sl, sin_sl, rpb, rkb, outsl, vref):
    C.V.wait(sm["mm"], vref)
    for base in (0, 64):
        C.V.tensor_scalar_mul(out=rpb[base:base + 32, 0:TC],
                              in0=psum[base + 32:base + 64, 0:TC], scalar1=-1.0)
        C.V.tensor_copy(out=rpb[base + 32:base + 64, 0:TC],
                        in_=psum[base:base + 32, 0:TC])
    C.V.tensor_tensor(out=rkb[:, 0:TC], in0=psum[:, 0:TC], in1=cos_sl, op=ALU.mult)
    C.V.tensor_tensor(out=rpb[:, 0:TC], in0=rpb[:, 0:TC], in1=sin_sl, op=ALU.mult)
    C.V.tensor_tensor(out=outsl, in0=rkb[:, 0:TC], in1=rpb[:, 0:TC], op=ALU.add)


def _cdma_taps(C, b, sm, cdma, srcsl, dstsl, cumulative):
    if cumulative:
        C.V.tensor_scalar_mul(out=_ap3(dstsl, [[dstsl.ap[0][0], 128], [MK, 128]]),
                              in0=_ap3(srcsl, [[srcsl.ap[0][0], 128], [MK, 128]]),
                              scalar1=cdma[:, 0:1])
        for jj in range(1, MK):
            C.V.scalar_tensor_tensor(
                out=_ap3(dstsl, [[dstsl.ap[0][0], 128], [MK, 128]], offset=jj),
                in0=_ap3(srcsl, [[srcsl.ap[0][0], 128], [MK, 128]], offset=jj),
                scalar=cdma[:, jj:jj + 1],
                in1=_ap3(dstsl, [[dstsl.ap[0][0], 128], [MK, 128]], offset=jj - 1),
                op0=ALU.mult, op1=ALU.add)
    else:
        C.V.tensor_scalar_mul(out=dstsl,
                              in0=_ap3(srcsl, [[srcsl.ap[0][0], 128], [MK, 128]]),
                              scalar1=cdma[:, 0:1])
        for jj in range(1, MK):
            C.V.scalar_tensor_tensor(
                out=dstsl,
                in0=_ap3(srcsl, [[srcsl.ap[0][0], 128], [MK, 128]], offset=jj),
                scalar=cdma[:, jj:jj + 1], in1=dstsl, op0=ALU.mult, op1=ALU.add)


def emit_attn(C, dr, b, sm, src, dst):
    s = C.s
    nd = D // 128
    for bi in range(B_LOC):
        C.Y.dma_start(out=b.cdmaA[:], in_=dr["cdma_half"]).inc(sm["ld"], 16)
        C.G.dma_start(out=b.identbf[:], in_=dr["ident2"]).inc(sm["gld"], 16)
        v_c1 = s.val(sm["ld"])
        for e in (C.V, C.S, C.T):
            e.wait(sm["ld"], v_c1)
        for j in range(nd):
            C.G.dma_start(out=b.wA[:, j * D:(j + 1) * D],
                          in_=_ap3(dr["wk"], [[D, 128], [1, D]], offset=j * 128 * D)).inc(sm["gld"], 16)
            C.G.dma_start(out=b.wB[:, j * D:(j + 1) * D],
                          in_=_ap3(dr["wv"], [[D, 128], [1, D]], offset=j * 128 * D)).inc(sm["gld"], 16)
            C.G.dma_start(out=b.laA[:, j * LORA:(j + 1) * LORA],
                          in_=_ap3(dr["av"], [[LORA, 128], [1, LORA]], offset=j * 128 * LORA)).inc(sm["gld"], 16)
        C.G.dma_start(out=b.lbA[:], in_=dr["bv"]).inc(sm["gld"], 16)
        v_w3a = s.val(sm["gld"])
        C.T.wait(sm["gld"], v_w3a)
        C.V.memset(b.vt[:], 1.0).inc(sm["ve"], 1)

        for c in range(4):
            cg = bi * 4 + c
            cb = cg % 2
            xch = b.xch[:, cb * nd * TC:(cb + 1) * nd * TC]
            C.Y.wait(sm["ve"], b.at_cons.get(cg - 2, 0))
            C.Y.dma_start(out=xch, in_=_ap3(src, [[T, 128], [128 * T, nd], [1, TC]],
                                            offset=cg * TC)).inc(sm["x"], 16)
            C.G.wait(sm["ve"], b.at_tabfree.get(cg - 1, 0))
            C.G.dma_start(out=b.cosA[:], in_=_ap3(dr["cos_rep"], [[S, 128], [1, TC]],
                                                  offset=c * TC)).inc(sm["gld"], 16)
            C.G.dma_start(out=b.sinA[:], in_=_ap3(dr["sin_rep"], [[S, 128], [1, TC]],
                                                  offset=c * TC)).inc(sm["gld"], 16)
            v_tab = s.val(sm["gld"])
            v_x = s.val(sm["x"])
            C.S.wait(sm["x"], v_x)
            C.T.wait(sm["x"], v_x)
            C.V.wait(sm["x"], v_x)
            C.V.wait(sm["gld"], v_tab)
            v_xn = emit_ln_chunk(C, b, sm, xch, TC)
            b.at_cons[cg] = v_xn
            C.T.wait(sm["ve"], v_xn)
            C.T.wait(sm["ve"], s.val(sm["ve"]))
            for j in range(nd):
                mmL = C.T.matmul(b.pF[0:LORA, 0:TC], b.laA[:, j * LORA:(j + 1) * LORA],
                                 b.xn[:, j * TC:(j + 1) * TC], start=(j == 0), stop=(j == nd - 1))
                if j == nd - 1:
                    mmL.inc(sm["mm"], 1)
            C.V.wait(sm["mm"], s.val(sm["mm"]))
            C.V.tensor_copy(out=b.tmplA[:, 0:TC], in_=b.pF[0:LORA, 0:TC]).inc(sm["ve"], 1)
            v_tls = s.val(sm["ve"])
            for ct in range(nd):
                C.T.wait(sm["ve"], s.val(sm["ve"]))
                for j in range(nd):
                    mmK = C.T.matmul(b.pE[:, 0:TC], b.wA[:, j * D + ct * 128:j * D + (ct + 1) * 128],
                                     b.xn[:, j * TC:(j + 1) * TC], start=(j == 0), stop=(j == nd - 1))
                    if j == nd - 1:
                        mmK.inc(sm["mm"], 1)
                v_k = s.val(sm["mm"])
                _rope(C, b, sm, b.pE, b.cosA[:, 0:TC], b.sinA[:, 0:TC], b.rpbA, b.rkbA,
                      b.kc[:, ct * TC:(ct + 1) * TC], v_k)
                _cdma_taps(C, b, sm, b.cdmaA, b.kc[:, ct * TC:(ct + 1) * TC],
                           b.fullk[:, ct * NSLOT + c * 128:ct * NSLOT + (c + 1) * 128], False)
                C.V.drain().inc(sm["ve"], 1)
                C.T.wait(sm["ve"], s.val(sm["ve"]))
                if ct == 0:
                    C.T.wait(sm["ve"], v_tls)
                for j in range(nd):
                    C.T.matmul(b.pF[:, 0:TC], b.wB[:, j * D + ct * 128:j * D + (ct + 1) * 128],
                               b.xn[:, j * TC:(j + 1) * TC], start=(j == 0), stop=False)
                C.T.matmul(b.pF[:, 0:TC], b.lbA[:, ct * 128:(ct + 1) * 128], b.tmplA[:, 0:TC],
                           start=False, stop=True).inc(sm["mm"], 1)
                v_v = s.val(sm["mm"])
                C.V.wait(sm["mm"], v_v)
                C.V.tensor_copy(out=b.vc[:, ct * TC:(ct + 1) * TC], in_=b.pF[:, 0:TC])
                C.V.drain()
                _cdma_taps(C, b, sm, b.cdmaA, b.vc[:, ct * TC:(ct + 1) * TC],
                           b.fullv[:, ct * NSLOT + c * 128:ct * NSLOT + (c + 1) * 128], False)
                C.V.drain().inc(sm["ve"], 1)
            v_kv = s.val(sm["ve"])
            b.at_tabfree[cg] = v_kv
            C.Y.wait(sm["ve"], v_kv)
            C.Y.dma_start(out=_ap3(dr["k_scr"], [[T, 128], [128 * T, nd], [1, TC]],
                                   offset=cg * TC), in_=b.kc[:, 0:nd * TC]).inc(sm["st"], 16)
            C.Y.dma_start(out=_ap3(dr["v_scr"], [[T, 128], [128 * T, nd], [1, TC]],
                                   offset=cg * TC), in_=b.vc[:, 0:nd * TC]).inc(sm["st"], 16)
            b.at_stored[("kv", cg)] = s.val(sm["st"])
        v_fv = s.val(sm["ve"])
        C.T.wait(sm["ve"], v_fv)
        for st_ in range(4):
            for h in range(NH):
                ct, par = h // 2, h % 2
                C.T.wait(sm["ve"], s.val(sm["ve"]))
                C.T.transpose(b.pT[0:128, 0:HD],
                              b.fullv[par * HD:(par + 1) * HD,
                                      ct * NSLOT + st_ * 128:ct * NSLOT + (st_ + 1) * 128],
                              identity=b.identbf[par * HD:(par + 1) * HD, 0:HD]).inc(sm["mm"], 1)
                C.V.wait(sm["mm"], s.val(sm["mm"]))
                col = (st_ * NH + h) * (HD + 1)
                C.V.tensor_copy(out=b.vt[:, col:col + HD], in_=b.pT[0:128, 0:HD]).inc(sm["ve"], 1)

        C.G.wait(sm["ve"], s.val(sm["ve"]))
        C.Y.dma_start(out=b.cdmaB[:], in_=dr["cdma_half"]).inc(sm["ld"], 16)
        C.G.dma_start(out=b.bandB[:], in_=dr["band01"]).inc(sm["gld"], 16)
        C.G.dma_start(out=b.e65[:], in_=dr["e65row"]).inc(sm["gld"], 16)
        v_cB = s.val(sm["ld"])
        v_cB2 = s.val(sm["gld"])
        for e in (C.V, C.S, C.T):
            e.wait(sm["ld"], v_cB)
            e.wait(sm["gld"], v_cB2)
        for j in range(nd):
            C.G.dma_start(out=b.laB[:, j * LORA:(j + 1) * LORA],
                          in_=_ap3(dr["aq"], [[LORA, 128], [1, LORA]], offset=j * 128 * LORA)).inc(sm["gld"], 16)
            C.G.dma_start(out=b.lc[:, j * LORA:(j + 1) * LORA],
                          in_=_ap3(dr["ao"], [[LORA, 128], [1, LORA]], offset=j * 128 * LORA)).inc(sm["gld"], 16)
        C.G.dma_start(out=b.lbB[:], in_=dr["bq"]).inc(sm["gld"], 16)
        C.G.dma_start(out=b.ld_[:], in_=dr["bo"]).inc(sm["gld"], 16)
        v_w3b = s.val(sm["gld"])
        C.T.wait(sm["gld"], v_w3b)

        for qc in range(4):
            cg = bi * 4 + qc
            cb = cg % 2
            xch = b.xch[:, cb * nd * TC:(cb + 1) * nd * TC]
            C.Y.wait(sm["ve"], b.at_cons.get(cg - 2, 0))
            C.Y.wait(sm["st"], b.at_stored.get(("r", cg - 1), 0))
            C.Y.dma_start(out=xch, in_=_ap3(src, [[T, 128], [128 * T, nd], [1, TC]],
                                            offset=cg * TC)).inc(sm["x"], 16)
            C.Y.wait(sm["ve"], b.at_kvfree.get(cg - 1, 0))
            C.Y.dma_start(out=b.kcB[:, 0:nd * TC],
                          in_=_ap3(dr["k_scr"], [[T, 128], [128 * T, nd], [1, TC]],
                                   offset=cg * TC)).inc(sm["x"], 16)
            C.Y.dma_start(out=b.vcB[:, 0:nd * TC],
                          in_=_ap3(dr["v_scr"], [[T, 128], [128 * T, nd], [1, TC]],
                                   offset=cg * TC)).inc(sm["x"], 16)
            C.G.wait(sm["ve"], b.at_tabfree.get(("b", cg - 1), 0))
            for nm, buf in (("cos_rep", b.cosB), ("sin_rep", b.sinB),
                            ("qpa", b.qpaB), ("qpo", b.qpoB)):
                C.G.dma_start(out=buf[:], in_=_ap3(dr[nm], [[S, 128], [1, TC]],
                                                   offset=qc * TC)).inc(sm["gld"], 16)
            v_tab = s.val(sm["gld"])
            v_x = s.val(sm["x"])
            C.S.wait(sm["x"], v_x)
            C.T.wait(sm["x"], v_x)
            C.V.wait(sm["x"], v_x)
            C.V.wait(sm["gld"], v_tab)
            v_xn = emit_ln_chunk(C, b, sm, xch, TC)
            b.at_cons[cg] = v_xn
            C.T.wait(sm["ve"], v_xn)
            C.T.wait(sm["ve"], s.val(sm["ve"]))
            for j in range(nd):
                mmq = C.T.matmul(b.pF[0:LORA, 0:TC], b.laB[:, j * LORA:(j + 1) * LORA],
                                 b.xn[:, j * TC:(j + 1) * TC], start=(j == 0), stop=(j == nd - 1))
                if j == nd - 1:
                    mmq.inc(sm["mm"], 1)
            C.V.wait(sm["mm"], s.val(sm["mm"]))
            C.V.tensor_copy(out=b.tmplB[:, 0:TC], in_=b.pF[0:LORA, 0:TC]).inc(sm["ve"], 1)
            v_tls = s.val(sm["ve"])
            for ct in range(nd):
                wq = b.wq_st[ct % 2]
                if len(b.at_wqmark) >= 2:
                    C.G.wait(sm["mm"], b.at_wqmark[-2])
                C.G.dma_start(out=wq[:, 0:8 * 128],
                              in_=_ap3(dr["wq"], [[D, 128], [128 * D, nd], [1, 128]],
                                       offset=ct * 128)).inc(sm["gld"], 16)
                v_wq = s.val(sm["gld"])
                pq = b.pE
                C.T.wait(sm["ve"], s.val(sm["ve"]))
                C.T.wait(sm["gld"], v_wq)
                if ct == 0:
                    C.T.wait(sm["ve"], v_tls)
                for j in range(nd):
                    C.T.matmul(pq[:, 0:TC], wq[:, j * 128:(j + 1) * 128],
                               b.xn[:, j * TC:(j + 1) * TC], start=(j == 0), stop=False)
                mmQ = C.T.matmul(pq[:, 0:TC], b.lbB[:, ct * 128:(ct + 1) * 128],
                                 b.tmplB[:, 0:TC], start=False, stop=True)
                mmQ.inc(sm["mm"], 1)
                b.at_wqmark.append(s.val(sm["mm"]))
                v_q = s.val(sm["mm"])
                _rope(C, b, sm, pq, b.cosB[:, 0:TC], b.sinB[:, 0:TC], b.rpbB, b.rkbB,
                      b.aqb[:, ct * TC:(ct + 1) * TC], v_q)
                C.V.tensor_tensor(out=b.aqb[:, ct * TC:(ct + 1) * TC],
                                  in0=b.aqb[:, ct * TC:(ct + 1) * TC], in1=b.qpaB[:, 0:TC],
                                  op=ALU.mult)
                C.V.drain().inc(sm["ve"], 1)
            v_aq = s.val(sm["ve"])

            C.V.wait(sm["mm"], s.val(sm["mm"]))
            for ct in range(nd):
                _cdma_taps(C, b, sm, b.cdmaB, b.kcB[:, ct * TC:(ct + 1) * TC],
                           b.cpkt[:, ct * TC:(ct + 1) * TC], True)
                _cdma_taps(C, b, sm, b.cdmaB, b.vcB[:, ct * TC:(ct + 1) * TC],
                           b.cpvt[:, ct * TC:(ct + 1) * TC], True)
            C.V.wait(sm["ve"], v_aq)
            C.V.tensor_tensor(out=b.qcp[:, 0:nd * TC], in0=b.aqb[:, 0:nd * TC],
                              in1=b.cpkt[:, 0:nd * TC], op=ALU.mult)
            C.V.drain().inc(sm["ve"], 1)
            v_cp = s.val(sm["ve"])
            b.at_kvfree[cg] = v_cp
            C.T.wait(sm["ve"], v_cp)

            expm = {}
            avdone = {}
            o2done = {}
            outdone = {}
            wcm = {}
            for h in range(NH):
                ct, par = h // 2, h % 2
                rb = par * HD
                pAV = b.pE if h % 2 == 0 else b.pF
                pOut = b.pG2 if h % 2 == 0 else b.pC
                C.T.wait(sm["ve"], o2done.get(h - 2, 0))
                C.T.matmul(pOut[0:1, 0:TC], b.ones_bf[rb:rb + HD, 0:1],
                           b.qcp[rb:rb + HD, ct * TC:(ct + 1) * TC],
                           start=True, stop=True).inc(sm["mm"], 1)
                v_cur = s.val(sm["mm"])
                C.S.wait(sm["mm"], v_cur)
                C.S.wait(sm["ve"], o2done.get(h - 1, 0))
                C.S.activation(out=b.wcur[0:1, 0:TC], in_=pOut[0:1, 0:TC],
                               func=AF.Exp).inc(sm["ac"], 1)
                wcm[h] = s.val(sm["ac"])
                for st_ in range(qc + 1):
                    pS = b.pA if st_ % 2 == 0 else b.pB
                    prev = expm.get((h, st_ - 2), 0)
                    if st_ < 2 and h > 0:
                        cand = [v for (hh, ss), v in expm.items()
                                if hh == h - 1 and ss % 2 == st_ % 2]
                        if cand:
                            prev = max(prev, max(cand))
                    C.T.wait(sm["ac"], prev)
                    mmS = C.T.matmul(pS[0:128, 0:TC],
                                     b.fullk[rb:rb + HD, ct * NSLOT + st_ * 128:ct * NSLOT + (st_ + 1) * 128],
                                     b.aqb[rb:rb + HD, ct * TC:(ct + 1) * TC],
                                     start=True, stop=True)
                    mmS.inc(sm["mm"], 1)
                    v_s1 = s.val(sm["mm"])
                    C.S.wait(sm["mm"], v_s1)
                    C.S.activation(out=b.ebuf[:, st_ * TC:(st_ + 1) * TC],
                                   in_=pS[0:128, 0:TC], func=AF.Exp).inc(sm["ac"], 1)
                    expm[(h, st_)] = s.val(sm["ac"])
                    if st_ == qc:
                        C.V.wait(sm["ac"], expm[(h, st_)])
                        C.V.tensor_tensor(out=b.ebuf[:, qc * TC:(qc + 1) * TC],
                                          in0=b.ebuf[:, qc * TC:(qc + 1) * TC],
                                          in1=b.bandB[:, 0:TC], op=ALU.mult)
                        C.V.drain().inc(sm["ve"], 1)
                        v_eb = s.val(sm["ve"])
                    if st_ == 0:
                        C.T.wait(sm["ve"], o2done.get(h - 2, 0))
                    if st_ == qc:
                        C.T.wait(sm["ve"], v_eb)
                    else:
                        C.T.wait(sm["ac"], expm[(h, st_)])
                    C.T.matmul(pAV[0:HD + 1, 0:TC], b.vt[:, (st_ * NH + h) * (HD + 1):(st_ * NH + h + 1) * (HD + 1)],
                               b.ebuf[:, st_ * TC:(st_ + 1) * TC],
                               start=(st_ == 0), stop=False)
                C.T.wait(sm["ac"], wcm[h])
                C.T.matmul(pAV[0:HD + 1, 0:TC], b.e65[0:1, 0:HD + 1], b.wcur[0:1, 0:TC],
                           start=False, stop=True).inc(sm["mm"], 1)
                avdone[h] = s.val(sm["mm"])
                C.V.wait(sm["mm"], avdone[h])
                C.V.reciprocal(out=b.zinv[0:1, 0:TC], in_=pAV[HD:HD + 1, 0:TC])
                C.V.drain().inc(sm["ve"], 1)
                v_z = s.val(sm["ve"])
                C.T.wait(sm["ve"], v_z)
                C.T.matmul(pOut[0:HD, 0:TC], b.onesrow_f[0:1, 0:HD], b.zinv[0:1, 0:TC],
                           start=True, stop=True)
                C.T.matmul(pOut[HD:128, 0:TC], b.onesrow_bf[0:1, 0:HD],
                           b.wcur[0:1, 0:TC], start=True, stop=True).inc(sm["mm"], 1)
                outdone[h] = s.val(sm["mm"])
                C.V.wait(sm["mm"], outdone[h])
                C.V.tensor_tensor(out=b.corr[rb:rb + HD, 0:TC], in0=pOut[HD:128, 0:TC],
                                  in1=b.cpvt[rb:rb + HD, ct * TC:(ct + 1) * TC], op=ALU.mult)
                C.V.tensor_tensor(out=b.corr[rb:rb + HD, 0:TC], in0=b.corr[rb:rb + HD, 0:TC],
                                  in1=pAV[0:HD, 0:TC], op=ALU.add)
                C.V.tensor_tensor(out=b.corr[rb:rb + HD, 0:TC], in0=b.corr[rb:rb + HD, 0:TC],
                                  in1=pOut[0:HD, 0:TC], op=ALU.mult)
                C.V.tensor_tensor(out=b.o2[rb:rb + HD, ct * TC:(ct + 1) * TC],
                                  in0=b.corr[rb:rb + HD, 0:TC],
                                  in1=b.qpoB[rb:rb + HD, 0:TC], op=ALU.mult)
                C.V.drain().inc(sm["ve"], 1)
                o2done[h] = s.val(sm["ve"])
            v_o2 = s.val(sm["ve"])
            b.at_kvfree[cg] = v_o2

            C.T.wait(sm["ve"], v_o2)
            for j in range(nd):
                mmo = C.T.matmul(b.pF[0:LORA, 0:TC], b.lc[:, j * LORA:(j + 1) * LORA],
                                 b.o2[:, j * TC:(j + 1) * TC], start=(j == 0), stop=(j == nd - 1))
                if j == nd - 1:
                    mmo.inc(sm["mm"], 1)
            C.V.wait(sm["mm"], s.val(sm["mm"]))
            C.V.tensor_copy(out=b.tmplB[:, 0:TC], in_=b.pF[0:LORA, 0:TC]).inc(sm["ve"], 1)
            v_t2 = s.val(sm["ve"])
            pgs = [b.pA, b.pB, b.pC, b.pD]
            for dt_ in range(nd):
                wo = b.wo_st[dt_ % 2]
                if len(b.at_womark) >= 2:
                    C.G.wait(sm["mm"], b.at_womark[-2])
                C.G.dma_start(out=wo[:, 0:8 * 128],
                              in_=_ap3(dr["wo"], [[D, 128], [128 * D, nd], [1, 128]],
                                       offset=dt_ * 128)).inc(sm["gld"], 16)
                v_wo = s.val(sm["gld"])
                bank = pgs[dt_ % 4]
                C.T.wait(sm["ve"], s.val(sm["ve"]))
                C.T.wait(sm["gld"], v_wo)
                if dt_ == 0:
                    C.T.wait(sm["ve"], v_t2)
                for j in range(nd):
                    C.T.matmul(bank[:, 0:TC], wo[:, j * 128:(j + 1) * 128],
                               b.o2[:, j * TC:(j + 1) * TC], start=(j == 0), stop=False)
                mmY = C.T.matmul(bank[:, 0:TC], b.ld_[:, dt_ * 128:(dt_ + 1) * 128],
                                 b.tmplB[:, 0:TC], start=False, stop=True)
                mmY.inc(sm["mm"], 1)
                b.at_womark.append(s.val(sm["mm"]))
                v_y = s.val(sm["mm"])
                C.V.wait(sm["mm"], v_y)
                C.V.scalar_tensor_tensor(out=b.rout[:, dt_ * TC:(dt_ + 1) * TC],
                                         in0=bank[:, 0:TC], scalar=1.0,
                                         in1=xch[:, dt_ * TC:(dt_ + 1) * TC],
                                         op0=ALU.mult, op1=ALU.add).inc(sm["ve"], 1)
            b.at_cons[cg] = s.val(sm["ve"])
            b.at_tabfree[("b", cg)] = s.val(sm["ve"])
            C.Y.wait(sm["ve"], b.at_cons[cg])
            C.Y.dma_start(out=_ap3(dst, [[T, 128], [128 * T, nd], [1, TC]], offset=cg * TC),
                          in_=b.rout[:, 0:nd * TC]).inc(sm["st"], 16)
            b.at_stored[("r", cg)] = s.val(sm["st"])
        C.barrier(sm["bar"])
    C.Y.wait(sm["st"], b.at_stored[("r", B_LOC * 4 - 1)])


def build_program(phases=("mamba", "ffn1", "attn", "ffn2"), debug_taps=()):
    import os as _os2
    nc = bass.Bass("TRN2", target_bir_lowering=False, detect_race_conditions=False,
                   enable_asserts=not _os2.environ.get("DEV_NOASSERTOFF"))
    C = Ctx(nc)
    dr = {}

    def inp(name, shape, dt=F32):
        dr[name] = nc.dram_tensor(name, list(shape), dt, kind="ExternalInput").ap()

    def scr(name, shape, dt):
        dr[name] = nc.dram_tensor(name, list(shape), dt, kind="Internal").ap()

    inp("x0", [D, T], F32)
    dr["out"] = nc.dram_tensor("out", [D, T], F32, kind="ExternalOutput").ap()
    inp("wfcT", [D, FF]); inp("wprojT", [FF, D])
    inp("consts", [128, 16])
    inp("ones_bf", [128, 1], BF16)
    inp("onesrow_f", [1, 128], F32)
    inp("onesrow_bf", [1, 512], BF16)
    inp("ident", [128, 128], F32)
    inp("triu01", [128, 128], F32)
    inp("winT", [D, PROJ])
    inp("b_in", [1, PROJ])
    inp("conv_w_t", [128, NJT * MK])
    inp("dtb_rep", [128, MH])
    inp("a_log", [MH, 1])
    inp("d_row", [1, INTER])
    inp("normw_row", [1, INTER])
    inp("woutT", [INTER, D])
    inp("b_out", [1, D])
    for w in ("wq", "wk", "wv", "wo"):
        inp(w, [D, D])
    inp("aq", [D, LORA]); inp("bq", [LORA, D])
    inp("av", [D, LORA]); inp("bv", [LORA, D])
    inp("ao", [D, LORA]); inp("bo", [LORA, D])
    inp("cos_rep", [128, S])
    inp("ident2", [128, HD])
    inp("sin_rep", [128, S])
    inp("qpa", [128, S])
    inp("qpo", [128, S])
    inp("cdma_half", [128, MK])
    inp("band01", [128, TC])
    inp("e65row", [1, HD + 1])
    scr("k_scr", [D, T], BF16)
    scr("v_scr", [D, T], BF16)
    scr("wproj_bf", [FF, D], BF16)
    scr("resid1", [D, T], F32)
    scr("resid2", [D, T], F32)
    scr("resid3", [D, T], F32)
    scr("h_scr", [INTER, T], BF16)
    scr("b_scr", [MN, T], BF16)
    scr("c_scr", [MN, T], BF16)
    scr("gate_scr", [T, INTER], BF16)
    scr("g_scr", [T, INTER], BF16)
    scr("dt_scr", [T, MH], F32)
    scr("acum_scr", [MH, T], F32)

    dbg = {}
    for name, shape, dt in debug_taps:
        dbg[name] = nc.dram_tensor(name, list(shape), dt, kind="ExternalOutput").ap()

    from contextlib import ExitStack
    with ExitStack() as es:
        sems = {}
        for name in ("ld", "gld", "sld", "x", "mm", "ve", "ac", "gp", "st", "tr", "bar"):
            sems[name] = es.enter_context(nc.semaphore(name))

        pa_shared = PA(nc)
        b = alloc_shared_bufs(pa_shared)
        pa_ffn = PA(nc)
        alloc_ffn_bufs(pa_ffn, b)
        pa_ffn.close()
        pa_ma = PA(nc)
        alloc_mamba_a_bufs(pa_ma, b)
        pa_ma.close()
        pa_mb = PA(nc)
        alloc_mamba_b_bufs(pa_mb, b)
        pa_mb.close()
        pa_mc = PA(nc)
        alloc_mamba_c_bufs(pa_mc, b)
        pa_mc.close()
        pa_atp = PA(nc)
        alloc_attn_persist(pa_atp, b)
        pa_at_a = PA(nc)
        alloc_attn_a(pa_at_a, b)
        pa_at_a.close()
        pa_at_b = PA(nc)
        alloc_attn_b(pa_at_b, b)
        pa_at_b.close()
        pa_atp.close()
        pa_shared.close()

        block = es.enter_context(nc.Block())

        scratch_of = {"mamba": "resid1", "ffn1": "resid2", "attn": "resid3",
                      "ffn2": "resid1", "mamba_a": "resid1", "mamba_ab": "resid1"}
        cur = "x0"
        plan = []
        for i, ph in enumerate(phases):
            dst = "out" if i == len(phases) - 1 else scratch_of[ph]
            plan.append((ph, cur, dst))
            cur = dst

        def run(engine_name):
            def f(handle):
                C.active = engine_name
                C.handle = handle
                C.s = Sched()
                b.ffn_wpmarks = []
                b.ffn_cons = {}
                b.ffn_stored = {}
                b.ma_cons = {}
                b.ma_stored = {}
                b.mb_cons = {}
                b.mb_stored = 0
                b.mb_difdone = 0
                b.mc_cons = {}
                b.mc_stored = {}
                b.mc_mmdone = {}
                b.at_cons = {}
                b.at_stored = {}
                b.at_kvfree = {}
                b.at_wqmark = []
                b.at_womark = []
                b.at_tabfree = {}
                emit_const_loads(C, dr, b, sems)
                first_ffn = True
                for ph, src, dst in plan:
                    if ph in ("ffn1", "ffn2"):
                        emit_ffn_phase(C, dr, b, sems, dr[src], dr[dst], first_ffn, dbg)
                        first_ffn = False
                    elif ph == "mamba":
                        emit_mamba_a(C, dr, b, sems, dr[src])
                        C.barrier(sems["bar"])
                        emit_mamba_b(C, dr, b, sems, dbg)
                        C.barrier(sems["bar"])
                        emit_mamba_c(C, dr, b, sems, dr[src], dr[dst])
                    elif ph == "mamba_a":
                        emit_mamba_a(C, dr, b, sems, dr[src])
                        C.barrier(sems["bar"])
                        C.G.dma_start(out=_ap3(dr["out"], [[T, 32], [1, T]]),
                                      in_=_ap3(dr["acum_scr"], [[T, 32], [1, T]])).inc(sems["gld"], 16)
                        C.G.dma_start(out=_ap3(dr["out"], [[T, 128], [1, T]], offset=32 * T),
                                      in_=_ap3(dr["b_scr"], [[T, 128], [1, T]])).inc(sems["gld"], 16)
                        C.G.dma_start(out=_ap3(dr["out"], [[T, 128], [1, T]], offset=160 * T),
                                      in_=_ap3(dr["c_scr"], [[T, 128], [1, T]])).inc(sems["gld"], 16)
                        C.G.dma_start(out=_ap3(dr["out"], [[T, 512], [1, T]], offset=288 * T),
                                      in_=_ap3(dr["h_scr"], [[T, 512], [1, T]])).inc(sems["gld"], 16)
                        C.G.dma_start(out=_ap3(dr["out"], [[T, 96], [1, T]], offset=800 * T),
                                      in_=_ap3(dr["h_scr"], [[T, 96], [1, T]], offset=1024 * T)).inc(sems["gld"], 16)
                        C.G.dma_start(out=_ap3(dr["out"], [[T, 32], [1, T]], offset=896 * T),
                                      in_=_ap3(dr["dt_scr"], [[T, 32], [1, T]])).inc(sems["gld"], 16)
                        C.G.wait(sems["gld"], C.s.val(sems["gld"]))
                    elif ph == "mamba_ab":
                        emit_mamba_a(C, dr, b, sems, dr[src])
                        C.barrier(sems["bar"])
                        emit_mamba_b(C, dr, b, sems, dbg)
                        C.barrier(sems["bar"])
                        C.G.dma_start(out=_ap3(dr["out"], [[T, 512], [1, T]]),
                                      in_=_ap3(dr["g_scr"], [[T, 512], [1, T]])).inc(sems["gld"], 16)
                        C.G.wait(sems["gld"], C.s.val(sems["gld"]))
                    elif ph == "attn":
                        emit_attn(C, dr, b, sems, dr[src], dr[dst])
                    C.barrier(sems["bar"])
            return f

        block.sync(run("sync"))
        block.vector(run("vector"))
        block.scalar(run("scalar"))
        block.tensor(run("tensor"))
        block.gpsimd(run("gpsimd"))

    return nc, dbg


def _host_consts():
    inv_freq = 1.0 / (10000.0 ** (np.arange(0, HD, 2, dtype=np.float64) / HD))
    pos = np.arange(S, dtype=np.float64)
    freqs = pos[:, None] * inv_freq[None, :]
    emb = np.concatenate([freqs, freqs], axis=-1)
    cos = np.cos(emb).astype(np.float32).T
    sin = np.sin(emb).astype(np.float32).T
    HAD = np.array([[1, 1, 1, 1], [1, -1, 1, -1], [1, 1, -1, -1], [1, -1, -1, 1]],
                   dtype=np.float32)
    CDMA = np.tile(HAD, (1, HD // 4))
    qphase = CDMA[np.arange(S) % KF]
    consts = np.zeros((128, 16), dtype=np.float32)
    consts[:, 0] = EPS
    for i, bp in enumerate(SPIKE_BPS):
        consts[:, 1 + i] = -np.float32(bp)
    consts[:, 6] = np.float32(SPIKE_BPS[3])
    consts[:, 7] = np.float32(SPIKE_BPS[4])
    consts[:, 8] = 1.0
    consts[:, 9] = -1.0
    return dict(
        cos_rep=np.ascontiguousarray(np.tile(cos, (2, 1))),
        sin_rep=np.ascontiguousarray(np.tile(sin, (2, 1))),
        qpa=np.ascontiguousarray(np.tile(qphase.T, (2, 1)) * 0.125).astype(np.float32),
        qpo=np.ascontiguousarray(np.tile(qphase.T, (2, 1)) * 2.0).astype(np.float32),
        cdma_half=np.ascontiguousarray(np.tile(CDMA.T, (2, 1)) * 0.5).astype(np.float32),
        band01=(np.arange(128)[:, None] < (np.arange(TC) // KF)[None, :]).astype(np.float32),
        triu01=(np.arange(128)[:, None] <= np.arange(128)[None, :]).astype(np.float32),
        ident=np.eye(128, dtype=np.float32),
        ident2=np.ascontiguousarray(np.tile(np.eye(HD, dtype=np.float32), (2, 1))),
        e65row=np.eye(1, HD + 1, HD, dtype=np.float32),
        consts=consts,
        ones_bf=np.ones((128, 1), ml_dtypes.bfloat16),
        onesrow_f=np.ones((1, 128), np.float32),
        onesrow_bf=np.ones((1, 128), ml_dtypes.bfloat16),
    )


def _core_in_map(inputs, consts, core):
    x = inputs["L0_attn_a_x"][2 * core:2 * core + 2]
    xT = np.ascontiguousarray(x.reshape(T, D).T)
    m = {
        "x0": xT,
        "wfcT": consts["_wfcT"],
        "wprojT": consts["_wprojT"],
        "consts": consts["consts"],
        "ones_bf": consts["ones_bf"],
        "onesrow_f": consts["onesrow_f"],
        "onesrow_bf": consts["_onesrow_bf512"],
        "ident": consts["ident"],
        "ident2": consts["ident2"],
        "triu01": consts["triu01"],
        "winT": consts["_winT"],
        "b_in": consts["_b_in"],
        "conv_w_t": consts["_conv_w_t"],
        "dtb_rep": consts["_dtb_rep"],
        "a_log": consts["_a_log"],
        "d_row": consts["_d_row"],
        "normw_row": consts["_normw_row"],
        "woutT": consts["_woutT"],
        "b_out": consts["_b_out"],
        "wq": inputs["L0_attn_b_Wq"], "wk": inputs["L0_attn_b_Wk"],
        "wv": inputs["L0_attn_b_Wv"], "wo": inputs["L0_attn_b_Wo"],
        "aq": inputs["L0_attn_b_Aq"], "bq": inputs["L0_attn_b_Bq"],
        "av": inputs["L0_attn_b_Av"], "bv": inputs["L0_attn_b_Bv"],
        "ao": inputs["L0_attn_b_Ao"], "bo": inputs["L0_attn_b_Bo"],
        "cos_rep": consts["cos_rep"], "sin_rep": consts["sin_rep"],
        "qpa": consts["qpa"], "qpo": consts["qpo"],
        "cdma_half": consts["cdma_half"], "band01": consts["band01"],
        "ident2": consts["ident2"], "e65row": consts["e65row"],
    }
    return m


def kernel(**inputs):
    c = _host_consts()
    c["_wfcT"] = np.ascontiguousarray(inputs["L0_ffn_W_fc"].T)
    c["_wprojT"] = np.ascontiguousarray(inputs["L0_ffn_W_proj"].T)
    c["_winT"] = np.ascontiguousarray(inputs["L0_attn_a_W_in"].T)
    c["_b_in"] = np.ascontiguousarray(inputs["L0_attn_a_b_in"][None, :])
    c["_conv_w_t"] = np.ascontiguousarray(
        inputs["L0_attn_a_conv_w"].reshape(NJT, 128, MK).transpose(1, 0, 2).reshape(128, NJT * MK))
    c["_dtb_rep"] = np.ascontiguousarray(
        np.broadcast_to(inputs["L0_attn_a_dt_bias"][None, :], (128, MH)))
    c["_a_log"] = np.ascontiguousarray(inputs["L0_attn_a_A_log"][:, None])
    c["_d_row"] = np.ascontiguousarray(np.repeat(inputs["L0_attn_a_D"], MP))[None, :]
    c["_normw_row"] = np.ascontiguousarray(inputs["L0_attn_a_norm_w"][None, :])
    c["_woutT"] = np.ascontiguousarray(inputs["L0_attn_a_W_out"].T)
    c["_b_out"] = np.ascontiguousarray(inputs["L0_attn_a_b_out"][None, :])
    c["_onesrow_bf512"] = np.ones((1, 512), ml_dtypes.bfloat16)

    nc, _ = build_program(phases=("mamba", "ffn1", "attn", "ffn2"))
    in_maps = [_core_in_map(inputs, c, core) for core in range(NCORES)]
    res = run_bass_kernel_spmd(nc, in_maps, core_ids=list(range(NCORES)))
    outs = []
    for r in res.results:
        o = r["out"]
        outs.append(np.ascontiguousarray(o.T).reshape(B_LOC, S, D))
    return np.concatenate(outs, axis=0).astype(np.float32)
